# version 03
# baseline (speedup 1.0000x reference)
import math

import numpy as np

import concourse.bass as bass
import concourse.mybir as mybir
from concourse.bass_utils import run_bass_kernel_spmd

F32 = mybir.dt.float32
I32 = mybir.dt.int32
U32 = mybir.dt.uint32
AF = mybir.ActivationFunctionType
ALU = mybir.AluOpType
AX = mybir.AxisListType.X

NCORES = 8
DM = 1024
DI = 2048
DS = 16
DCV = 4
E = 8
FF = 4096
Q = 128
NDK = DM // 128
NDC = DI // 128
NFK = FF // 128


class Cfg:
    def __init__(self, NB=4, L=2048, G=256, TT=256, CCAP=12288):
        self.NB, self.L, self.G, self.TT, self.CCAP = NB, L, G, TT, CCAP
        self.TOKS = NB * L
        self.GPS = L // G
        self.NG = NB * self.GPS
        self.GC = G // Q
        self.NCH = self.TOKS // Q
        self.NT = self.TOKS // TT + E
        self.NSLOT = self.NT * TT
        self.TCH = min(1024, self.TOKS)
        self.NTC = self.TOKS // self.TCH
        assert L % G == 0 and G % Q == 0 and TT % 128 == 0
        assert self.TOKS % self.TCH == 0 and self.TCH % Q == 0


class Sem:
    def __init__(self, nc, name):
        self.h = nc.alloc_semaphore(name)
        self.n = 0
        self.name = name

    def inc(self, inst, k=1):
        inst.then_inc(self.h, k)
        self.n += k
        return inst


class Pool:

    def __init__(self, nc, name, elems_per_part):
        self.base = nc.alloc_sbuf_tensor(name, [128, elems_per_part], F32).ap()
        self.off = 0
        self.cap = elems_per_part

    def take(self, nelem, dtype=F32, parts=128):
        o = self.off
        self.off += (nelem + 15) & ~15
        assert self.off <= self.cap, f"pool overflow {self.off} > {self.cap}"
        v = self.base[0:parts, o:o + nelem]
        if dtype is not F32:
            v = v.bitcast(dtype)
        return v

    def mark(self):
        return self.off

    def reset(self, m=0):
        self.off = m


def build(cfg: Cfg):
    nc = bass.Bass("TRN2", target_bir_lowering=False, debug=False, num_devices=NCORES)
    NB, L, G, TT, CCAP = cfg.NB, cfg.L, cfg.G, cfg.TT, cfg.CCAP
    TOKS, GPS, NG, GC, NCH = cfg.TOKS, cfg.GPS, cfg.NG, cfg.GC, cfg.NCH
    NT, NSLOT, TCH, NTC = cfg.NT, cfg.NSLOT, cfg.TCH, cfg.NTC
    CONVW = G + DCV - 1
    NSUB = TT // 128
    NROWB = TOKS // 128
    assert NROWB <= 128 and NT <= 128

    x0_d = nc.dram_tensor("x0", [TOKS, DM], F32, kind="ExternalInput").ap()
    wi_d = nc.dram_tensor("w_in", [DM, 2 * DI], F32, kind="ExternalInput").ap()
    cw_d = nc.dram_tensor("conv_w", [128, NDC * DCV], F32, kind="ExternalInput").ap()
    cb_d = nc.dram_tensor("conv_b", [128, NDC], F32, kind="ExternalInput").ap()
    wxd_d = nc.dram_tensor("wx_d", [128, NDC], F32, kind="ExternalInput").ap()
    wxb_d = nc.dram_tensor("wx_b", [128, NDC * DS], F32, kind="ExternalInput").ap()
    wxc_d = nc.dram_tensor("wx_c", [128, NDC * DS], F32, kind="ExternalInput").ap()
    alog_d = nc.dram_tensor("a_log", [1, DS], F32, kind="ExternalInput").ap()
    dp_d = nc.dram_tensor("d_param", [128, NDC], F32, kind="ExternalInput").ap()
    wo_d = nc.dram_tensor("w_out", [DI, DM], F32, kind="ExternalInput").ap()
    rwt_d = nc.dram_tensor("rw_t", [128, NDK * E], F32, kind="ExternalInput").ap()
    w1_d = nc.dram_tensor("w1", [E * DM, FF], F32, kind="ExternalInput").ap()
    w2_d = nc.dram_tensor("w2", [E * FF, DM], F32, kind="ExternalInput").ap()
    b1_d = nc.dram_tensor("b1t", [E * 128, NFK], F32, kind="ExternalInput").ap()
    b2_d = nc.dram_tensor("b2t", [E * 128, NDK], F32, kind="ExternalInput").ap()
    r1hot_d = nc.dram_tensor("r1hot", [1, NCORES], F32, kind="ExternalInput").ap()
    rpfx_d = nc.dram_tensor("rpfx", [NCORES, 1], F32, kind="ExternalInput").ap()

    out_d = nc.dram_tensor("out", [TOKS, DM], F32, kind="ExternalOutput").ap()

    sx1_d = nc.dram_tensor("s_x1", [TOKS, DM], F32, kind="Internal").ap()
    sxln_d = nc.dram_tensor("s_xln", [TOKS, DM], F32, kind="Internal").ap()
    sxs_d = nc.dram_tensor("s_xs", [NSLOT, DM], F32, kind="Internal").ap()
    smoe_d = nc.dram_tensor("s_moe", [NSLOT, DM], F32, kind="Internal").ap()
    w1h_d = nc.dram_tensor("w1h", [E * DM, FF], mybir.dt.bfloat16, kind="Internal").ap()
    w1l_d = nc.dram_tensor("w1l", [E * DM, FF], mybir.dt.bfloat16, kind="Internal").ap()
    w2h_d = nc.dram_tensor("w2h", [E * FF, DM], mybir.dt.bfloat16, kind="Internal").ap()
    w2l_d = nc.dram_tensor("w2l", [E * FF, DM], mybir.dt.bfloat16, kind="Internal").ap()
    ccin_d = nc.dram_tensor("cc_in", [NCORES, E], F32, kind="Internal")
    ccout_d = nc.dram_tensor("cc_out", [NCORES, E], F32, kind="Internal")

    ident_c = nc.inline_tensor(np.eye(128, dtype=np.float32), name="c_ident")
    mS = np.where(np.arange(128)[None, :] >= np.arange(128)[:, None], 0.0, 1e9).astype(np.float32)
    maskS_c = nc.inline_tensor(mS, name="c_maskS")
    iota8c_c = nc.inline_tensor(np.arange(E, dtype=np.float32).reshape(E, 1), name="c_iota8c")
    rev8_c = nc.inline_tensor(np.tile(np.arange(E, 0, -1).astype(np.float32), (128, 1)), name="c_rev8")
    tri8_c = nc.inline_tensor(np.triu(np.ones((E, E), np.float32), 1), name="c_tri8")
    ktile_c = nc.inline_tensor(np.tile((np.arange(NT) * TT).astype(np.float32), (E, 1)), name="c_ktile")

    def sb(name, shape, dtype=F32):
        return nc.alloc_sbuf_tensor(name, shape, dtype).ap()

    ident = sb("ident", [128, 128])
    maskS = sb("maskS", [128, 128])
    iota8c = sb("iota8c", [E, 1])
    rev8 = sb("rev8", [128, E])
    tri8 = sb("tri8", [E, E])
    ktile = sb("ktile", [E, NT])
    convw = sb("convw", [128, NDC * DCV])
    convb = sb("convb", [128, NDC])
    wxd = sb("wxd", [128, NDC])
    wxb = sb("wxb", [128, NDC * DS])
    wxc = sb("wxc", [128, NDC * DS])
    dcol = sb("dcol", [128, NDC])
    rwt = sb("rwt", [128, NDK * E])
    alog = sb("alog", [1, DS])
    arow = sb("arow", [1, DS])
    abc = sb("abc", [128, DS])
    hst = sb("hst", [DS, DI])
    ctail = sb("ctail", [128, NDC * (DCV - 1)])
    ecol = sb("ecol", [128, NCH])
    pcol = sb("pcol", [128, NCH])
    onesr = sb("onesr", [1, 512])
    onesc8 = sb("onesc8", [E, 1])
    ones8t = sb("ones8t", [E, 2048])
    zt = sb("zt", [128, 1024])
    scols = sb("scols", [128, NCH])
    scoli = sb("scoli", [128, NCH], I32)
    kcols = sb("kcols", [128, NCH])
    scalec = sb("scalec", [128, NCH])
    cnts = sb("cnts", [E, 1])
    carry = sb("carry", [E, 1])
    basec = sb("basec", [E, 1])
    poffm1 = sb("poffm1", [E, 1])
    cuminc = sb("cuminc", [E, 1])
    kcst = sb("kcst", [128, 4])
    r1hot = sb("r1hot_s", [1, NCORES])
    rpfx = sb("rpfx_s", [NCORES, 1])

    pool = Pool(nc, "pool", (nc.sbuf_bytes_remaining - 512) // 4)

    x0t = [pool.take(DM) for _ in range(2)]
    xnt = [pool.take(DM) for _ in range(2)]
    sqt = pool.take(DM)
    lnm = [pool.take(1) for _ in range(2)]
    lni = [pool.take(1) for _ in range(2)]
    lns = pool.take(8)
    xnfm = pool.take(NDK * G)
    wiblk = [pool.take(1024) for _ in range(2)]
    xssm = pool.take(NDC * CONVW)
    szfm = pool.take(NDC * G)
    xcsfm = pool.take(NDC * G)
    xcdtm = pool.take(DI)
    drow = pool.take(G, parts=1)
    dabs = pool.take(G, parts=1)
    dexp = pool.take(G, parts=1)
    dln = pool.take(G, parts=1)
    drel = pool.take(G, parts=1)
    bfm = pool.take(G, parts=DS)
    cfm = pool.take(G, parts=DS)
    crow = pool.take(Q, parts=1)
    negc = pool.take(Q, parts=1)
    argrow = pool.take(2 * Q + 1, parts=1)
    ekall = pool.take(2 * Q + 1, parts=DS)
    dmt = pool.take(128)
    estk = pool.take((DS // 2) * Q)
    cflat = pool.take(DS * Q, parts=1)
    bttm = pool.take(DS)
    kbfm = pool.take(Q, parts=DS)
    qbfm = pool.take(Q, parts=DS)
    kbtm = pool.take(DS)
    dcolq = pool.take(1)
    mtmp = pool.take((DS // 2) * Q)
    mtt = pool.take(128)
    y3fm = pool.take(NDC * G)
    woblk = [pool.take(DM) for _ in range(2)]
    x1t = [pool.take(DM) for _ in range(2)]
    xlnt = [pool.take(DM) for _ in range(2)]
    xlnfm = pool.take(NDK * G)
    lgu = pool.take(E, U32)
    lgf = pool.take(E)
    lgw = pool.take(E)
    lgm = pool.take(1)
    lgm2 = pool.take(1)
    lgnm = pool.take(1)
    lge = pool.take(E)
    lgs = pool.take(1)
    wsp_in = pool.take(512)
    wsp_h = pool.take(256).bitcast(mybir.dt.bfloat16)
    wsp_l = pool.take(256).bitcast(mybir.dt.bfloat16)
    a_mark = pool.mark()

    pool.reset(0)
    erow_c = pool.take(TCH, parts=1)
    ind_c = pool.take(TCH, parts=E)
    po_c = pool.take(TCH, parts=E)
    tmp_c = pool.take(TCH, parts=E)
    tmq_c = pool.take(TCH, dtype=U32, parts=E)
    ef16 = pool.take(128, parts=16)
    srow_c = pool.take(TCH, parts=1)
    krow_c = pool.take(TCH, parts=1)
    srows = pool.take(128, parts=NROWB)
    krows = pool.take(128, parts=NROWB)
    cntr = pool.take(E, parts=1)
    ccsb = pool.take(E, parts=NCORES)
    ccall = pool.take(E, parts=NCORES)
    baser = pool.take(E, parts=1)
    pcq = pool.take(4, parts=E)
    eti = sb("eti", [1, NT], I32)
    etu = pool.take(NT, dtype=U32, parts=E)
    etf = pool.take(NT, parts=E)
    etr = pool.take(NT, parts=1)
    etfr = pool.take(NT, parts=1)
    xlt = [pool.take(DM) for _ in range(2)]
    w1blk = [pool.take(NDK * 512) for _ in range(2)]
    w2blk = [pool.take(4 * DM) for _ in range(2)]
    xsfm = pool.take(NDK * TT)
    hsb = pool.take(4 * TT)
    eofm = pool.take(NDK * TT)
    eotm = [pool.take(DM) for _ in range(NSUB)]
    b1sb = pool.take(NFK)
    b2sb = pool.take(NDK)
    mot = [pool.take(DM) for _ in range(2)]
    x1f = [pool.take(DM) for _ in range(2)]
    outt = [pool.take(DM) for _ in range(2)]
    b_mark = pool.mark()
    assert max(a_mark, b_mark) <= pool.cap

    PSA = nc.alloc_psum_tensor("PSA", [128, 4096], F32).ap()

    def psb(bank, nelem, parts=128):
        return PSA[0:parts, bank * 512: bank * 512 + nelem]

    def TR(tensor, out, in_):
        p = in_.shape[0]
        return tensor.transpose(out, in_, ident[0:p, 0:p])

    U8 = mybir.dt.uint8

    def dyndma(eng, dst, src_ap, idx_ap, scale_b, add_b, max_b, dims):
        with eng.register() as r:
            eng.reg_load(r, idx_ap)
            eng.reg_alu(r, r, scale_b, ALU.mult)
            if add_b:
                eng.reg_alu(r, r, add_b, ALU.add)
            rv = eng.snap(r, min_val=0, max_val=max_b)
            src = bass.AP(src_ap.tensor, rv, dims)
            ins = eng.dma_start(out=dst.bitcast(U8), in_=src)
            try:
                for reg in rv.val:
                    eng.free_register(reg)
            except TypeError:
                eng.free_register(rv.val)
        return ins

    w1_b = w1_d.bitcast(mybir.dt.uint8)
    w2_b = w2_d.bitcast(mybir.dt.uint8)
    b1_b = b1_d.bitcast(mybir.dt.uint8)
    b2_b = b2_d.bitcast(mybir.dt.uint8)

    s_ms = Sem(nc, "s_ms")
    s_ws = Sem(nc, "s_ws")
    s_wt = Sem(nc, "s_wt")
    s_gm = Sem(nc, "s_gm")
    s_x0 = Sem(nc, "s_x0")
    s_wi = [Sem(nc, "s_wiA"), Sem(nc, "s_wiB")]
    s_wo = [Sem(nc, "s_woA"), Sem(nc, "s_woB")]
    s_st1 = Sem(nc, "s_st1")
    s_cf = Sem(nc, "s_cf")
    s_zx = Sem(nc, "s_zx")
    s_xl = [Sem(nc, "s_xlA"), Sem(nc, "s_xlB")]
    s_sc = Sem(nc, "s_sc")
    s_bw = Sem(nc, "s_bw")
    s_w1 = [Sem(nc, "s_w1A"), Sem(nc, "s_w1B")]
    s_w2 = [Sem(nc, "s_w2A"), Sem(nc, "s_w2B")]
    s_mo = Sem(nc, "s_mo")
    s_x1f = [Sem(nc, "s_x1fA"), Sem(nc, "s_x1fB")]
    s_mg = [Sem(nc, "s_mgA"), Sem(nc, "s_mgB")]
    pe = Sem(nc, "pe")
    dv = Sem(nc, "dv")
    ac = Sem(nc, "ac")
    cc = Sem(nc, "cc")
    sfin = Sem(nc, "sfin")

    waited = {}

    def W(eng, en, sem, val=None):
        v = sem.n if val is None else val
        if v <= 0:
            return
        k = (en, sem.name)
        if waited.get(k, -1) < v:
            eng.wait_ge(sem.h, v)
            waited[k] = v

    with nc.Block() as block:
        RUN = {"sync": block.sync, "scalar": block.scalar, "vector": block.vector,
               "tensor": block.tensor, "gpsimd": block.gpsimd}

        def S(name, fn):
            RUN[name](fn)

        def _(sync):
            for dst, src in [(ident, ident_c.ap()), (maskS, maskS_c.ap()),
                             (iota8c, iota8c_c.ap()), (rev8, rev8_c.ap()),
                             (tri8, tri8_c.ap()), (ktile, ktile_c.ap()),
                             (convw, cw_d), (convb, cb_d), (wxd, wxd_d),
                             (wxb, wxb_d), (wxc, wxc_d), (dcol, dp_d),
                             (rwt, rwt_d), (alog, alog_d),
                             (r1hot, r1hot_d), (rpfx, rpfx_d)]:
                s_ms.inc(sync.dma_start(out=dst, in_=src), 16)
        S("sync", _)

        def _(vector):
            W(vector, "vector", s_ms)
            vector.memset(onesr[:], 1.0)
            vector.memset(kcst[:, 0:1], 1e-5)
            vector.memset(kcst[:, 1:2], 1.0)
            vector.memset(kcst[:, 2:3], -1.0)
            vector.memset(onesc8[:], 1.0)
            vector.memset(ones8t[:], 1.0)
            vector.memset(zt[:], 0.0)
            vector.memset(ctail[:], 0.0)
            dv.inc(vector.memset(hst[:], 0.0))
        S("vector", _)

        def _(scalar):
            W(scalar, "scalar", s_ms)
            ac.inc(scalar.activation(arow[:], alog[:], AF.Exp))
        S("scalar", _)

        def _(vector):
            W(vector, "vector", ac)
            dv.inc(vector.tensor_scalar_mul(arow[:], arow[:], -1.0))
        S("vector", _)

        def _(tensor):
            W(tensor, "tensor", dv)
            pe.inc(tensor.matmul(psb(6, DS), onesr[0:1, 0:128], arow[:], start=True, stop=True))
        S("tensor", _)

        def _(vector):
            W(vector, "vector", pe)
            dv.inc(vector.tensor_copy(abc[:], psb(6, DS)))
        S("vector", _)
        abc_done = dv.n

        def _(gpsimd):
            W(gpsimd, "gpsimd", dv, abc_done)
            rp = (128 * 1024) // DM
            for i in range((NSLOT + rp - 1) // rp):
                r0, r1 = i * rp, min(NSLOT, (i + 1) * rp)
                s_zx.inc(gpsimd.dma_start(out=sxs_d[r0:r1, :],
                                           in_=zt[0:(r1 - r0) * DM // 1024, :]), 16)
        S("gpsimd", _)
        sxs_zeroed = s_zx.n

        x1_done_mark = [0]
        prev_marks = {"wi": 0, "wo": 0, "x1st": 0, "xlnst": 0}

        for g in range(NG):
            b = g // GPS
            first = (g % GPS) == 0
            t0 = g * G
            ci0 = g * GC

            def _(sync, t0=t0):
                W(sync, "sync", dv, x1_done_mark[0])
                for i in range(GC):
                    s_x0.inc(sync.dma_start(out=x0t[i],
                                            in_=x0_d[t0 + i * 128: t0 + (i + 1) * 128, :]), 16)
            S("sync", _)
            x0_ld = s_x0.n

            for i in range(GC):
                def _(scalar, i=i, x0_ld=x0_ld):
                    W(scalar, "scalar", s_x0, x0_ld)
                    W(scalar, "scalar", dv)
                    ac.inc(scalar.activation(sqt, x0t[i], AF.Square))
                S("scalar", _)

                def _(vector, i=i, x0_ld=x0_ld):
                    W(vector, "vector", s_x0, x0_ld)
                    W(vector, "vector", ac)
                    vector.reduce_sum(lns[:, 0:1], x0t[i], axis=AX)
                    dv.inc(vector.reduce_sum(lns[:, 1:2], sqt, axis=AX))
                    W(vector, "vector", dv)
                    vector.tensor_scalar(lnm[i], lns[:, 0:1], 1.0 / DM, None, ALU.mult)
                    dv.inc(vector.tensor_scalar(lns[:, 3:4], lns[:, 1:2], 1.0 / DM, None, ALU.mult))
                    W(vector, "vector", dv)
                    dv.inc(vector.tensor_tensor(lns[:, 2:3], lnm[i], lnm[i], ALU.mult))
                    W(vector, "vector", dv)
                    dv.inc(vector.tensor_tensor(lns[:, 4:5], lns[:, 3:4], lns[:, 2:3], ALU.subtract))
                S("vector", _)

                def _(scalar, i=i):
                    W(scalar, "scalar", dv)
                    ac.inc(scalar.activation(lns[:, 5:6], lns[:, 4:5], AF.Sqrt, bias=kcst[:, 0:1]))
                S("scalar", _)

                def _(vector, i=i):
                    W(vector, "vector", ac)
                    dv.inc(vector.reciprocal(lni[i], lns[:, 5:6]))
                    W(vector, "vector", dv)
                    dv.inc(vector.tensor_scalar(xnt[i], x0t[i], lnm[i], lni[i],
                                                ALU.subtract, ALU.mult))
                S("vector", _)
            xn_done = dv.n

            items = [(xnt[i][:, k * 128:(k + 1) * 128], xnfm[:, k * G + i * 128: k * G + (i + 1) * 128])
                     for k in range(NDK) for i in range(GC)]
            pe0, dv0 = pe.n, dv.n

            def _(tensor, items=items, xn_done=xn_done, dv0=dv0):
                W(tensor, "tensor", dv, xn_done)
                for j, (src, _dst) in enumerate(items):
                    if j >= 2:
                        W(tensor, "tensor", dv, dv0 + j - 1)
                    pe.inc(TR(tensor, psb(6 + j % 2, 128), src))
            S("tensor", _)

            def _(vector, items=items, pe0=pe0):
                for j, (_src, dst) in enumerate(items):
                    W(vector, "vector", pe, pe0 + j + 1)
                    dv.inc(vector.tensor_copy(dst, psb(6 + j % 2, 128)))
            S("vector", _)
            xnfm_done = dv.n

            wi_pe0 = pe.n
            wi_base = [s_wi[0].n, s_wi[1].n]
            ev_marks = list(prev_marks.get("wi_ev", [None, None]))
            for r in range(4):
                roff = (r % 2) * 2048

                def _(sync, r=r, wi_pe0=wi_pe0):
                    for k in range(NDK):
                        flat = r * NDK + k
                        if flat >= 2:
                            W(sync, "sync", pe, wi_pe0 + flat - 1)
                        else:
                            W(sync, "sync", pe, prev_marks["wi"])
                        ins = sync.dma_start(out=wiblk[k % 2],
                                             in_=wi_d[k * 128:(k + 1) * 128,
                                                      r * 1024:(r + 1) * 1024])
                        s_wi[k % 2].inc(ins, 16)
                S("sync", _)

                def _(tensor, r=r, roff=roff, wi_base=wi_base, xnfm_done=xnfm_done,
                      em=ev_marks[r]):
                    W(tensor, "tensor", dv, xnfm_done)
                    if em is not None:
                        W(tensor, "tensor", em[0], em[1])
                    for k in range(NDK):
                        flat = r * NDK + k
                        W(tensor, "tensor", s_wi[k % 2],
                          wi_base[k % 2] + (flat // 2 + 1) * 16)
                        buf = wiblk[k % 2]
                        for j in range(8):
                            ins2 = tensor.matmul(
                                PSA[:, roff + (j // 2) * 512 + (j % 2) * 256: roff + (j // 2) * 512 + (j % 2) * 256 + G],
                                buf[:, j * 128:(j + 1) * 128],
                                xnfm[:, k * G:(k + 1) * G],
                                start=(k == 0 and j % 2 == 0),
                                stop=(k == NDK - 1 and j % 2 == 1),
                            )
                        pe.inc(ins2)
                S("tensor", _)
                mm_done = pe.n

                if r < 2:
                    def _(vector, r=r, roff=roff, mm_done=mm_done):
                        W(vector, "vector", pe, mm_done)
                        for j in range(8):
                            jj = r * 8 + j
                            src = PSA[:, roff + (j // 2) * 512 + (j % 2) * 256: roff + (j // 2) * 512 + (j % 2) * 256 + G]
                            dst = xssm[:, jj * CONVW + (DCV - 1): jj * CONVW + (DCV - 1) + G]
                            ins = vector.tensor_copy(dst, src)
                        dv.inc(ins)
                    S("vector", _)
                    ev_marks.append((dv, dv.n))
                else:
                    def _(scalar, r=r, roff=roff, mm_done=mm_done):
                        W(scalar, "scalar", pe, mm_done)
                        for j in range(8):
                            jj = (r - 2) * 8 + j
                            src = PSA[:, roff + (j // 2) * 512 + (j % 2) * 256: roff + (j // 2) * 512 + (j % 2) * 256 + G]
                            ins = scalar.activation(szfm[:, jj * G:(jj + 1) * G], src, AF.Silu)
                        ac.inc(ins)
                    S("scalar", _)
                    ev_marks.append((ac, ac.n))
            prev_marks["wi"] = wi_pe0 + 4 * NDK
            wi_evac_dv = dv.n
            wi_evac_ac = ac.n

            def _(vector, first=first, wi_evac_dv=wi_evac_dv):
                W(vector, "vector", dv, wi_evac_dv)
                dst = bass.AP(xssm.tensor, xssm.offset,
                              [list(xssm.ap[0]), [CONVW, NDC], [1, DCV - 1]])
                if first:
                    dv.inc(vector.memset(dst, 0.0))
                else:
                    src = bass.AP(ctail.tensor, ctail.offset,
                                  [list(ctail.ap[0]), [DCV - 1, NDC], [1, DCV - 1]])
                    dv.inc(vector.tensor_copy(dst, src))
            S("vector", _)
            lead_done = dv.n

            def _(vector, lead_done=lead_done, wi_evac_ac=wi_evac_ac):
                W(vector, "vector", dv, lead_done)
                for k in range(DCV):
                    if k > 0:
                        W(vector, "vector", dv)
                    for dc in range(NDC):
                        src = xssm[:, dc * CONVW + k: dc * CONVW + k + G]
                        dst = xcsfm[:, dc * G:(dc + 1) * G]
                        if k == 0:
                            ins = vector.tensor_scalar(dst, src, convw[:, dc * DCV: dc * DCV + 1],
                                                       convb[:, dc:dc + 1], ALU.mult, ALU.add)
                        else:
                            ins = vector.scalar_tensor_tensor(dst, src,
                                                              convw[:, dc * DCV + k: dc * DCV + k + 1],
                                                              dst, ALU.mult, ALU.add)
                    dv.inc(ins)
            S("vector", _)
            conv_done = dv.n

            def _(vector, conv_done=conv_done):
                W(vector, "vector", dv, conv_done)
                dst = bass.AP(ctail.tensor, ctail.offset,
                              [list(ctail.ap[0]), [DCV - 1, NDC], [1, DCV - 1]])
                src = bass.AP(xssm.tensor, xssm.offset + G,
                              [list(xssm.ap[0]), [CONVW, NDC], [1, DCV - 1]])
                dv.inc(vector.tensor_copy(dst, src))
            S("vector", _)

            def _(scalar, conv_done=conv_done):
                W(scalar, "scalar", dv, conv_done)
                ac.inc(scalar.activation(xcsfm, xcsfm, AF.Silu))
            S("scalar", _)
            xcs_done = ac.n

            def _(tensor, xcs_done=xcs_done):
                W(tensor, "tensor", ac, xcs_done)
                for k in range(NDC):
                    tensor.matmul(psb(0, G, parts=1), wxd[:, k:k + 1],
                                  xcsfm[:, k * G:(k + 1) * G],
                                  start=(k == 0), stop=(k == NDC - 1))
                for k in range(NDC):
                    tensor.matmul(psb(1, G, parts=DS), wxb[:, k * DS:(k + 1) * DS],
                                  xcsfm[:, k * G:(k + 1) * G],
                                  start=(k == 0), stop=(k == NDC - 1))
                for k in range(NDC):
                    ins = tensor.matmul(psb(2, G, parts=DS), wxc[:, k * DS:(k + 1) * DS],
                                        xcsfm[:, k * G:(k + 1) * G],
                                        start=(k == 0), stop=(k == NDC - 1))
                pe.inc(ins)
            S("tensor", _)
            wx_done = pe.n

            def _(vector, wx_done=wx_done):
                W(vector, "vector", pe, wx_done)
                vector.tensor_copy(bfm, psb(1, G, parts=DS))
                vector.tensor_copy(cfm, psb(2, G, parts=DS))
                dv.inc(vector.tensor_copy(drel, psb(0, G, parts=1)))
            S("vector", _)
            draw_done = dv.n

            def _(scalar, draw_done=draw_done):
                W(scalar, "scalar", dv, draw_done)
                ac.inc(scalar.activation(dabs, drel, AF.Abs))
                W(scalar, "scalar", ac)
                ac.inc(scalar.activation(dexp, dabs, AF.Exp, scale=kcst[0:1, 2:3]))
                W(scalar, "scalar", ac)
                ac.inc(scalar.activation(dln, dexp, AF.Ln, bias=kcst[0:1, 1:2]))
                ac.inc(scalar.activation(dabs, drel, AF.Relu))
            S("scalar", _)

            def _(vector):
                W(vector, "vector", ac)
                dv.inc(vector.tensor_tensor(drow, dabs, dln, ALU.add))
            S("vector", _)
            drow_done = dv.n

            for q in range(GC):
                ci = ci0 + q
                qs = q * Q

                def _(vector, qs=qs, drow_done=drow_done):
                    W(vector, "vector", dv, drow_done)
                    dv.inc(vector.tensor_tensor_scan(crow, onesr[0:1, 0:Q],
                                                     drow[0:1, qs:qs + Q], 0.0,
                                                     ALU.mult, ALU.add))
                    W(vector, "vector", dv)
                    vector.tensor_scalar(argrow[0:1, 0:Q], crow,
                                         crow[0:1, Q - 1:Q], -1.0, ALU.subtract, ALU.mult)
                    vector.tensor_copy(argrow[0:1, Q:2 * Q], crow)
                    vector.tensor_copy(argrow[0:1, 2 * Q:2 * Q + 1], crow[0:1, Q - 1:Q])
                    dv.inc(vector.tensor_scalar(negc, crow, -1.0, None, ALU.mult))
                S("vector", _)
                rows_done = dv.n

                def _(tensor, rows_done=rows_done):
                    W(tensor, "tensor", dv, rows_done)
                    tensor.matmul(psb(4, 2 * Q + 1, parts=DS), arow, argrow,
                                  start=True, stop=True)
                    tensor.matmul(psb(3, 128), negc, onesr[0:1, 0:Q], start=True, stop=False)
                    ins = tensor.matmul(psb(3, 128), onesr[0:1, 0:Q], crow, start=False, stop=True)
                    pe.inc(ins)
                S("tensor", _)
                args_done = pe.n

                def _(scalar, args_done=args_done):
                    W(scalar, "scalar", pe, args_done)
                    ac.inc(scalar.activation(ekall, psb(4, 2 * Q + 1, parts=DS), AF.Exp))
                S("scalar", _)
                ek_done = ac.n

                def _(vector, args_done=args_done):
                    W(vector, "vector", pe, args_done)
                    dv.inc(vector.tensor_tensor(dmt, psb(3, 128), maskS, ALU.add))
                S("vector", _)
                dmt_done = dv.n

                estk_done = ac.n

                def _(scalar, qs=qs, drow_done=drow_done):
                    W(scalar, "scalar", dv, drow_done)
                    W(scalar, "scalar", pe)
                    s_cf.inc(scalar.dma_start(out=cflat, in_=cfm[0:DS, qs:qs + Q]), 16)
                S("scalar", _)
                cflat_done = s_cf.n

                def _(tensor, cflat_done=cflat_done, ek_done=ek_done):
                    W(tensor, "tensor", s_cf, cflat_done)
                    W(tensor, "tensor", ac, ek_done)
                    W(tensor, "tensor", dv)
                    for t4 in range(4):
                        ins = tensor.matmul(PSA[:, t4 * 512: t4 * 512 + 512],
                                            onesr[0:1, 0:128],
                                            cflat[0:1, t4 * 512:(t4 + 1) * 512],
                                            start=True, stop=True)
                    pe.inc(ins)
                S("tensor", _)
                cb_done = pe.n

                def _(vector, qs=qs, ek_done=ek_done):
                    W(vector, "vector", ac, ek_done)
                    vector.tensor_tensor(kbfm, bfm[0:DS, qs:qs + Q], ekall[0:DS, 0:Q], ALU.mult)
                    dv.inc(vector.tensor_tensor(qbfm, cfm[0:DS, qs:qs + Q],
                                                ekall[0:DS, Q:2 * Q], ALU.mult))
                S("vector", _)
                kb_done = dv.n

                def _(tensor, qs=qs, kb_done=kb_done):
                    W(tensor, "tensor", dv, kb_done)
                    TR(tensor, psb(6, DS), bfm[0:DS, qs:qs + Q])
                    TR(tensor, psb(7, DS), kbfm)
                    pe.inc(TR(tensor, psb(5, 1), drow[0:1, qs:qs + Q]))
                S("tensor", _)
                tr1_done = pe.n

                def _(vector, tr1_done=tr1_done):
                    W(vector, "vector", pe, tr1_done)
                    vector.tensor_copy(bttm, psb(6, DS))
                    vector.tensor_copy(kbtm, psb(7, DS))
                    dv.inc(vector.tensor_copy(dcolq, psb(5, 1)))
                S("vector", _)
                tr1_ev = dv.n

                pe1, dv1 = pe.n, dv.n

                def _(tensor, qs=qs, pe1=pe1, dv1=dv1, tr1_ev=tr1_ev):
                    W(tensor, "tensor", dv, tr1_ev)
                    for dc in range(NDC):
                        if dc >= 2:
                            W(tensor, "tensor", dv, dv1 + dc - 1)
                        pe.inc(TR(tensor, psb(6 + dc % 2, 128),
                                  xcsfm[:, dc * G + qs: dc * G + qs + Q]))
                S("tensor", _)

                def _(vector, pe1=pe1):
                    for dc in range(NDC):
                        W(vector, "vector", pe, pe1 + dc + 1)
                        dv.inc(vector.tensor_scalar(xcdtm[:, dc * 128:(dc + 1) * 128],
                                                    psb(6 + dc % 2, 128), dcolq,
                                                    None, ALU.mult))
                S("vector", _)
                xcd_done = dv.n

                for half in range(2):
                    def _(scalar, half=half, dmt_done=dmt_done):
                        W(scalar, "scalar", dv, dmt_done)
                        if half == 1:
                            W(scalar, "scalar", dv)
                        for n8 in range(8):
                            n = half * 8 + n8
                            ins = scalar.activation(estk[:, n8 * Q:(n8 + 1) * Q], dmt,
                                                    AF.Exp, scale=abc[:, n:n + 1])
                        ac.inc(ins)
                    S("scalar", _)
                    eh_done = ac.n

                    def _(vector, half=half, eh_done=eh_done, cb_done=cb_done, tr1_ev=tr1_ev):
                        W(vector, "vector", ac, eh_done)
                        W(vector, "vector", pe, cb_done)
                        W(vector, "vector", dv, tr1_ev)
                        W(vector, "vector", dv)
                        for n8 in range(8):
                            n = half * 8 + n8
                            ins = vector.scalar_tensor_tensor(
                                mtmp[:, n8 * Q:(n8 + 1) * Q], estk[:, n8 * Q:(n8 + 1) * Q],
                                bttm[:, n:n + 1], PSA[:, n * Q:(n + 1) * Q],
                                ALU.mult, ALU.mult)
                        dv.inc(ins)
                        h = 4
                        while h >= 1:
                            W(vector, "vector", dv)
                            for i2 in range(h):
                                if h == 1 and half == 0:
                                    ins = vector.tensor_tensor(mtt, mtmp[:, 0:Q],
                                                               mtmp[:, Q:2 * Q], ALU.add)
                                elif h == 1:
                                    ins = vector.tensor_tensor(mtmp[:, 0:Q], mtmp[:, 0:Q],
                                                               mtmp[:, Q:2 * Q], ALU.add)
                                else:
                                    ins = vector.tensor_tensor(
                                        mtmp[:, i2 * Q:(i2 + 1) * Q],
                                        mtmp[:, i2 * Q:(i2 + 1) * Q],
                                        mtmp[:, (i2 + h) * Q:(i2 + h + 1) * Q], ALU.add)
                            dv.inc(ins)
                            h //= 2
                    S("vector", _)

                def _(vector):
                    W(vector, "vector", dv)
                    dv.inc(vector.tensor_tensor(mtt, mtt, mtmp[:, 0:Q], ALU.add))
                S("vector", _)
                mt_done = dv.n

                pe2, dv2 = pe.n, dv.n

                def _(tensor, b=b, pe2=pe2, dv2=dv2, xcd_done=xcd_done, mt_done=mt_done):
                    W(tensor, "tensor", dv, max(xcd_done, mt_done))
                    for dc in range(NDC):
                        if dc >= 2:
                            W(tensor, "tensor", dv, dv2 + dc - 1)
                        bank = 4 + dc % 2
                        tensor.matmul(psb(bank, Q), xcdtm[:, dc * 128:(dc + 1) * 128],
                                      mtt, start=True, stop=False)
                        pe.inc(tensor.matmul(psb(bank, Q), hst[0:DS, dc * 128:(dc + 1) * 128],
                                             qbfm, start=False, stop=True))
                    for t4 in range(4):
                        ins = tensor.matmul(PSA[0:DS, t4 * 512:(t4 + 1) * 512],
                                            kbtm, xcdtm[:, t4 * 512:(t4 + 1) * 512],
                                            start=True, stop=True)
                    pe.inc(ins)
                S("tensor", _)
                ymm_all = pe.n

                def _(vector, qs=qs, pe2=pe2):
                    for dc in range(NDC):
                        W(vector, "vector", pe, pe2 + dc + 1)
                        bank = 4 + dc % 2
                        ins = vector.scalar_tensor_tensor(
                            y3fm[:, dc * G + qs: dc * G + qs + Q],
                            xcsfm[:, dc * G + qs: dc * G + qs + Q],
                            dcol[:, dc:dc + 1], psb(bank, Q), ALU.mult, ALU.add)
                        dv.inc(ins)
                S("vector", _)
                yev = dv.n

                def _(vector, qs=qs, yev=yev):
                    W(vector, "vector", dv, yev)
                    for dc in range(NDC):
                        ins = vector.tensor_tensor(
                            y3fm[:, dc * G + qs: dc * G + qs + Q],
                            y3fm[:, dc * G + qs: dc * G + qs + Q],
                            szfm[:, dc * G + qs: dc * G + qs + Q], ALU.mult)
                    dv.inc(ins)
                S("vector", _)

                def _(vector, ymm_all=ymm_all):
                    W(vector, "vector", pe, ymm_all)
                    dv.inc(vector.scalar_tensor_tensor(
                        hst[0:DS, 0:DI], hst[0:DS, 0:DI],
                        ekall[0:DS, 2 * Q:2 * Q + 1], PSA[0:DS, 0:DI],
                        ALU.mult, ALU.add))
                S("vector", _)
            y3_done = dv.n

            pe_wb = pe.n
            wo_base = [s_wo[0].n, s_wo[1].n]

            def _(sync, pe_wb=pe_wb):
                for k in range(NDC):
                    if k >= 2:
                        W(sync, "sync", pe, pe_wb + k - 1)
                    else:
                        W(sync, "sync", pe, prev_marks["wo"])
                    ins = sync.dma_start(out=woblk[k % 2], in_=wo_d[k * 128:(k + 1) * 128, :])
                    s_wo[k % 2].inc(ins, 16)
            S("sync", _)

            def _(tensor, wo_base=wo_base, y3_done=y3_done):
                W(tensor, "tensor", dv, y3_done)
                for k in range(NDC):
                    W(tensor, "tensor", s_wo[k % 2], wo_base[k % 2] + (k // 2 + 1) * 16)
                    buf = woblk[k % 2]
                    for j in range(8):
                        ins2 = tensor.matmul(
                            PSA[:, (j // 2) * 512 + (j % 2) * G: (j // 2) * 512 + (j % 2) * G + G],
                            buf[:, j * 128:(j + 1) * 128],
                            y3fm[:, k * G:(k + 1) * G],
                            start=(k == 0 and j % 2 == 0),
                            stop=(k == NDC - 1 and j % 2 == 1))
                    pe.inc(ins2)
            S("tensor", _)
            womm_done = pe.n
            prev_marks["wo"] = womm_done

            def _(vector, womm_done=womm_done):
                W(vector, "vector", pe, womm_done)
                for j in range(8):
                    src = PSA[:, (j // 2) * 512 + (j % 2) * G: (j // 2) * 512 + (j % 2) * G + G]
                    ins = vector.tensor_copy(xnfm[:, j * G:(j + 1) * G], src)
                dv.inc(ins)
            S("vector", _)
            attn_done = dv.n

            items2 = [(xnfm[:, j * G + i * 128: j * G + (i + 1) * 128], i, j)
                      for j in range(8) for i in range(GC)]
            pe3, dv3 = pe.n, dv.n

            def _(tensor, items2=items2, attn_done=attn_done, dv3=dv3):
                W(tensor, "tensor", dv, attn_done)
                for jj, (src, i, j) in enumerate(items2):
                    if jj >= 2:
                        W(tensor, "tensor", dv, dv3 + jj - 1)
                    pe.inc(TR(tensor, psb(6 + jj % 2, 128), src))
            S("tensor", _)

            def _(vector, items2=items2, pe3=pe3):
                W(vector, "vector", s_st1, prev_marks["x1st"])
                for jj, (_src, i, j) in enumerate(items2):
                    W(vector, "vector", pe, pe3 + jj + 1)
                    dv.inc(vector.tensor_tensor(x1t[i][:, j * 128:(j + 1) * 128],
                                                psb(6 + jj % 2, 128),
                                                x0t[i][:, j * 128:(j + 1) * 128], ALU.add))
            S("vector", _)
            x1_done = dv.n
            x1_done_mark[0] = x1_done

            def _(sync, t0=t0, x1_done=x1_done):
                W(sync, "sync", dv, x1_done)
                for i in range(GC):
                    s_st1.inc(sync.dma_start(out=sx1_d[t0 + i * 128: t0 + (i + 1) * 128, :],
                                             in_=x1t[i]), 16)
            S("sync", _)

            for i in range(GC):
                def _(scalar, i=i, x1_done=x1_done):
                    W(scalar, "scalar", dv, x1_done)
                    W(scalar, "scalar", dv)
                    ac.inc(scalar.activation(sqt, x1t[i], AF.Square))
                S("scalar", _)

                def _(vector, i=i, x1_done=x1_done):
                    W(vector, "vector", dv, x1_done)
                    W(vector, "vector", ac)
                    vector.reduce_sum(lns[:, 0:1], x1t[i], axis=AX)
                    dv.inc(vector.reduce_sum(lns[:, 1:2], sqt, axis=AX))
                    W(vector, "vector", dv)
                    vector.tensor_scalar(lnm[i], lns[:, 0:1], 1.0 / DM, None, ALU.mult)
                    dv.inc(vector.tensor_scalar(lns[:, 3:4], lns[:, 1:2], 1.0 / DM, None, ALU.mult))
                    W(vector, "vector", dv)
                    dv.inc(vector.tensor_tensor(lns[:, 2:3], lnm[i], lnm[i], ALU.mult))
                    W(vector, "vector", dv)
                    dv.inc(vector.tensor_tensor(lns[:, 4:5], lns[:, 3:4], lns[:, 2:3], ALU.subtract))
                S("vector", _)

                def _(scalar, i=i):
                    W(scalar, "scalar", dv)
                    ac.inc(scalar.activation(lns[:, 5:6], lns[:, 4:5], AF.Sqrt, bias=kcst[:, 0:1]))
                S("scalar", _)

                def _(vector, i=i):
                    W(vector, "vector", ac)
                    W(vector, "vector", s_st1, prev_marks["xlnst"])
                    dv.inc(vector.reciprocal(lni[i], lns[:, 5:6]))
                    W(vector, "vector", dv)
                    dv.inc(vector.tensor_scalar(xlnt[i], x1t[i], lnm[i], lni[i],
                                                ALU.subtract, ALU.mult))
                S("vector", _)
            xln_done = dv.n

            def _(sync, t0=t0, xln_done=xln_done):
                W(sync, "sync", dv, xln_done)
                for i in range(GC):
                    s_st1.inc(sync.dma_start(out=sxln_d[t0 + i * 128: t0 + (i + 1) * 128, :],
                                             in_=xlnt[i]), 16)
            S("sync", _)
            prev_marks["x1st"] = s_st1.n
            prev_marks["xlnst"] = s_st1.n

            items3 = [(xlnt[i][:, k * 128:(k + 1) * 128], xlnfm[:, k * G + i * 128: k * G + (i + 1) * 128])
                      for k in range(NDK) for i in range(GC)]
            pe4, dv4 = pe.n, dv.n

            def _(tensor, items3=items3, xln_done=xln_done, dv4=dv4):
                W(tensor, "tensor", dv, xln_done)
                for j, (src, _dst) in enumerate(items3):
                    if j >= 2:
                        W(tensor, "tensor", dv, dv4 + j - 1)
                    pe.inc(TR(tensor, psb(6 + j % 2, 128), src))
            S("tensor", _)

            def _(vector, items3=items3, pe4=pe4):
                for j, (_src, dst) in enumerate(items3):
                    W(vector, "vector", pe, pe4 + j + 1)
                    dv.inc(vector.tensor_copy(dst, psb(6 + j % 2, 128)))
            S("vector", _)
            xlnfm_done = dv.n

            for i in range(GC):
                ci = ci0 + i
                bank = 4 + i % 2

                def _(tensor, i=i, bank=bank, xlnfm_done=xlnfm_done):
                    W(tensor, "tensor", dv, xlnfm_done)
                    for k in range(NDK):
                        ins = tensor.matmul(psb(bank, E),
                                            xlnfm[:, k * G + i * 128: k * G + (i + 1) * 128],
                                            rwt[:, k * E:(k + 1) * E],
                                            start=(k == 0), stop=(k == NDK - 1))
                    pe.inc(ins)
                S("tensor", _)
                rt_done = pe.n

                def _(vector, ci=ci, bank=bank, rt_done=rt_done):
                    W(vector, "vector", pe, rt_done)
                    dv.inc(vector.reduce_max(lgm, psb(bank, E), axis=AX))
                    W(vector, "vector", dv)
                    vector.tensor_scalar(lgnm, lgm, -1.0, None, ALU.mult)
                    dv.inc(vector.tensor_tensor(lgu, psb(bank, E),
                                                lgm.to_broadcast([128, E]), ALU.is_ge))
                    W(vector, "vector", dv)
                    dv.inc(vector.tensor_copy(lgf, lgu))
                    W(vector, "vector", dv)
                    dv.inc(vector.tensor_tensor(lgw, lgf, rev8, ALU.mult))
                    W(vector, "vector", dv)
                    dv.inc(vector.reduce_max(lgm2, lgw, axis=AX))
                    W(vector, "vector", dv)
                    dv.inc(vector.tensor_scalar(ecol[:, ci:ci + 1], lgm2, -1.0, float(E),
                                                ALU.mult, ALU.add))
                S("vector", _)
                sm1 = dv.n

                def _(scalar, bank=bank, sm1=sm1):
                    W(scalar, "scalar", dv, sm1)
                    ac.inc(scalar.activation(lge, psb(bank, E), AF.Exp, bias=lgnm))
                S("scalar", _)
                sm2 = ac.n

                def _(vector, ci=ci, sm2=sm2):
                    W(vector, "vector", ac, sm2)
                    dv.inc(vector.reduce_sum(lgs, lge, axis=AX))
                    W(vector, "vector", dv)
                    dv.inc(vector.reciprocal(pcol[:, ci:ci + 1], lgs))
                S("vector", _)

            prev_marks["wi_ev"] = [(dv, dv.n), (ac, ac.n)]

            if (g % GPS) == GPS - 1 and g + 1 < NG:
                def _(vector):
                    W(vector, "vector", dv)
                    W(vector, "vector", pe)
                    vector.memset(hst[:], 0.0)
                    dv.inc(vector.memset(ctail[:], 0.0))
                S("vector", _)

        a_pe, a_dv, a_ac = pe.n, dv.n, ac.n
        a_st1 = s_st1.n

        eti_done = [0]
        def _(vector):
            W(vector, "vector", dv, a_dv)
            dv.inc(vector.memset(carry[:], 0.0))
        S("vector", _)

        for p2 in range(2):
            if p2 == 1:
                def _(tensor):
                    W(tensor, "tensor", dv)
                    pe.inc(TR(tensor, psb(6, E, parts=1), cnts[0:E, 0:1]))
                S("tensor", _)

                def _(vector):
                    W(vector, "vector", pe)
                    dv.inc(vector.tensor_copy(cntr, psb(6, E, parts=1)))
                S("vector", _)

                def _(tensor):
                    W(tensor, "tensor", dv)
                    pe.inc(tensor.matmul(psb(6, E, parts=NCORES), r1hot, cntr,
                                         start=True, stop=True))
                S("tensor", _)

                def _(vector):
                    W(vector, "vector", pe)
                    dv.inc(vector.tensor_copy(ccsb, psb(6, E, parts=NCORES)))
                S("vector", _)

                def _(gpsimd):
                    W(gpsimd, "gpsimd", dv)
                    s_gm.inc(gpsimd.dma_start(out=ccin_d.ap(), in_=ccsb), 16)
                    W(gpsimd, "gpsimd", s_gm)
                    cc.inc(gpsimd.collective_compute(
                        "AllReduce", ALU.add, replica_groups=[list(range(NCORES))],
                        ins=[ccin_d.ap().opt()], outs=[ccout_d.ap().opt()]))
                    W(gpsimd, "gpsimd", cc)
                    s_gm.inc(gpsimd.dma_start(out=ccall, in_=ccout_d.ap()), 16)
                S("gpsimd", _)

                def _(tensor):
                    W(tensor, "tensor", s_gm)
                    pe.inc(tensor.matmul(psb(6, E, parts=1), rpfx, ccall,
                                         start=True, stop=True))
                S("tensor", _)

                def _(vector):
                    W(vector, "vector", pe)
                    dv.inc(vector.tensor_copy(baser, psb(6, E, parts=1)))
                S("vector", _)

                def _(tensor):
                    W(tensor, "tensor", dv)
                    pe.inc(TR(tensor, psb(6, 1, parts=E), baser))
                S("tensor", _)

                def _(vector):
                    W(vector, "vector", pe)
                    dv.inc(vector.tensor_copy(basec, psb(6, 1, parts=E)))
                    W(vector, "vector", dv)
                    dv.inc(vector.tensor_scalar(pcq[:, 0:1], cnts, 1.0 / TT,
                                                -0.5 + 0.5 / TT, ALU.mult, ALU.add))
                    W(vector, "vector", dv)
                    dv.inc(vector.tensor_scalar(pcq[:, 0:1], cnts, TT - 1.0, 1.0 / TT,
                                                ALU.add, ALU.mult))
                    W(vector, "vector", dv)
                    dv.inc(vector.tensor_scalar(pcq[:, 1:2], pcq[:, 0:1],
                                                -0.5 + 1.0 / (2 * TT), None, ALU.add))
                    W(vector, "vector", dv)
                    dv.inc(vector.tensor_scalar(pcq[:, 2:3], pcq[:, 1:2], 12582912.0,
                                                None, ALU.add))
                    W(vector, "vector", dv)
                    dv.inc(vector.tensor_scalar(pcq[:, 3:4], pcq[:, 2:3], 12582912.0,
                                                None, ALU.subtract))
                    W(vector, "vector", dv)
                    dv.inc(vector.tensor_scalar(pcq[:, 0:1], pcq[:, 3:4], float(TT),
                                                None, ALU.mult))
                S("vector", _)
                pc_done = dv.n

                def _(tensor, pc_done=pc_done):
                    W(tensor, "tensor", dv, pc_done)
                    pe.inc(tensor.matmul(psb(6, 1, parts=E), tri8, pcq[:, 0:1],
                                         start=True, stop=True))
                S("tensor", _)

                def _(vector):
                    W(vector, "vector", pe)
                    dv.inc(vector.tensor_copy(poffm1, psb(6, 1, parts=E)))
                    W(vector, "vector", dv)
                    dv.inc(vector.tensor_tensor(cuminc, poffm1, pcq[:, 0:1], ALU.add))
                    W(vector, "vector", dv)
                    dv.inc(vector.tensor_scalar(poffm1, poffm1, -1.0, None, ALU.add))
                    dv.inc(vector.memset(carry[:], 0.0))
                S("vector", _)

                def _(vector):
                    W(vector, "vector", dv)
                    dv.inc(vector.tensor_tensor(etu, cuminc.to_broadcast([E, NT]),
                                                ktile, ALU.is_le))
                    W(vector, "vector", dv)
                    dv.inc(vector.tensor_copy(etf, etu))
                S("vector", _)

                def _(tensor):
                    W(tensor, "tensor", dv)
                    pe.inc(tensor.matmul(psb(6, NT, parts=1), onesc8, etf,
                                         start=True, stop=True))
                S("tensor", _)

                def _(vector):
                    W(vector, "vector", pe)
                    dv.inc(vector.tensor_copy(etr, psb(6, NT, parts=1)))
                    W(vector, "vector", dv)
                    dv.inc(vector.tensor_scalar(etfr, etr, float(E - 1), None, ALU.min))
                    W(vector, "vector", dv)
                    dv.inc(vector.tensor_copy(eti, etfr))
                S("vector", _)
                eti_done[0] = dv.n

            for tc in range(NTC):
                c0 = tc * TCH
                nqc = TCH // Q
                def _(tensor, tc=tc):
                    W(tensor, "tensor", dv)
                    pe.inc(TR(tensor, psb(7, 128, parts=nqc),
                              ecol[:, tc * nqc:(tc + 1) * nqc]))
                S("tensor", _)

                def _(vector):
                    W(vector, "vector", pe)
                    dv.inc(vector.tensor_copy(ef16[0:nqc, 0:128], psb(7, 128, parts=nqc)))
                S("vector", _)

                def _(gpsimd):
                    W(gpsimd, "gpsimd", dv)
                    s_gm.inc(gpsimd.dma_start(out=erow_c, in_=ef16[0:nqc, 0:128]), 16)
                S("gpsimd", _)
                ef_done = s_gm.n

                def _(tensor, ef_done=ef_done):
                    W(tensor, "tensor", s_gm, ef_done)
                    W(tensor, "tensor", dv)
                    for t4 in range(TCH // 512):
                        ins = tensor.matmul(PSA[0:E, t4 * 512:(t4 + 1) * 512],
                                            onesr[0:1, 0:E],
                                            erow_c[0:1, t4 * 512:(t4 + 1) * 512],
                                            start=True, stop=True)
                    pe.inc(ins)
                S("tensor", _)
                indmm = pe.n

                def _(vector, indmm=indmm):
                    W(vector, "vector", pe, indmm)
                    dv.inc(vector.tensor_tensor(tmq_c, PSA[0:E, 0:TCH],
                                                iota8c.to_broadcast([E, TCH]), ALU.is_equal))
                    W(vector, "vector", dv)
                    dv.inc(vector.tensor_copy(ind_c, tmq_c))
                    W(vector, "vector", dv)
                    dv.inc(vector.tensor_tensor_scan(po_c, ones8t[0:E, 0:TCH], ind_c,
                                                     carry[0:E, 0:1], ALU.mult, ALU.add))
                    W(vector, "vector", dv)
                    dv.inc(vector.tensor_copy(carry, po_c[0:E, TCH - 1:TCH]))
                S("vector", _)
                po_done = dv.n

                if p2 == 1:
                    def _(vector, po_done=po_done):
                        W(vector, "vector", dv, po_done)
                        dv.inc(vector.tensor_scalar(tmp_c, po_c, poffm1, None, ALU.add))
                        W(vector, "vector", dv)
                        dv.inc(vector.tensor_tensor(tmp_c, tmp_c, ind_c, ALU.mult))
                    S("vector", _)
                    srmm0 = dv.n

                    def _(tensor, srmm0=srmm0):
                        W(tensor, "tensor", dv, srmm0)
                        for t4 in range(TCH // 512):
                            ins = tensor.matmul(PSA[0:1, 2048 + t4 * 512: 2048 + (t4 + 1) * 512],
                                                onesc8, tmp_c[0:E, t4 * 512:(t4 + 1) * 512],
                                                start=True, stop=True)
                        pe.inc(ins)
                    S("tensor", _)

                    def _(vector):
                        W(vector, "vector", pe)
                        dv.inc(vector.tensor_copy(srow_c, PSA[0:1, 2048:2048 + TCH]))
                    S("vector", _)

                    def _(vector):
                        W(vector, "vector", dv)
                        dv.inc(vector.tensor_scalar(tmp_c, po_c, basec, None, ALU.add))
                        W(vector, "vector", dv)
                        dv.inc(vector.tensor_scalar(tmq_c, tmp_c, CCAP + 0.5, None, ALU.is_le))
                        W(vector, "vector", dv)
                        dv.inc(vector.tensor_copy(tmp_c, tmq_c))
                        W(vector, "vector", dv)
                        dv.inc(vector.tensor_tensor(tmp_c, tmp_c, ind_c, ALU.mult))
                    S("vector", _)
                    krmm0 = dv.n

                    def _(tensor, krmm0=krmm0):
                        W(tensor, "tensor", dv, krmm0)
                        for t4 in range(TCH // 512):
                            ins = tensor.matmul(PSA[0:1, 2048 + t4 * 512: 2048 + (t4 + 1) * 512],
                                                onesc8, tmp_c[0:E, t4 * 512:(t4 + 1) * 512],
                                                start=True, stop=True)
                        pe.inc(ins)
                    S("tensor", _)

                    def _(vector):
                        W(vector, "vector", pe)
                        dv.inc(vector.tensor_copy(krow_c, PSA[0:1, 2048:2048 + TCH]))
                    S("vector", _)
                    rows_ok = dv.n

                    def _(gpsimd, tc=tc, rows_ok=rows_ok):
                        W(gpsimd, "gpsimd", dv, rows_ok)
                        r0 = tc * (TCH // 128)
                        s_gm.inc(gpsimd.dma_start(out=srows[r0:r0 + TCH // 128, 0:128],
                                                   in_=srow_c), 16)
                        s_gm.inc(gpsimd.dma_start(out=krows[r0:r0 + TCH // 128, 0:128],
                                                   in_=krow_c), 16)
                    S("gpsimd", _)

            if p2 == 0:
                def _(vector):
                    W(vector, "vector", dv)
                    dv.inc(vector.tensor_copy(cnts, carry))
                S("vector", _)

        rows_all = s_gm.n

        def _(tensor, rows_all=rows_all):
            W(tensor, "tensor", s_gm, rows_all)
            pe.inc(TR(tensor, psb(6, NCH), srows[0:NROWB, 0:128]))
            pe.inc(TR(tensor, psb(7, NCH), krows[0:NROWB, 0:128]))
        S("tensor", _)

        def _(vector):
            W(vector, "vector", pe)
            vector.tensor_copy(scols, psb(6, NCH))
            dv.inc(vector.tensor_copy(kcols, psb(7, NCH)))
            W(vector, "vector", dv)
            vector.tensor_copy(scoli, scols)
            dv.inc(vector.tensor_tensor(scalec, kcols, pcol, ALU.mult))
        S("vector", _)
        cols_done = dv.n

        sg_scb = s_sc.n
        xl_base = [s_xl[0].n, s_xl[1].n]

        def _(sync, cols_done=cols_done, sg_scb=sg_scb):
            W(sync, "sync", dv, cols_done)
            W(sync, "sync", s_st1, a_st1)
            for c in range(NCH):
                if c >= 2:
                    W(sync, "sync", s_sc, sg_scb + ((c - 2) // 2 + 1) * 2 * 16)
                ins = sync.dma_start(out=xlt[c % 2], in_=sxln_d[c * 128:(c + 1) * 128, :])
                s_xl[c % 2].inc(ins, 16)
        S("sync", _)

        def _(gpsimd, cols_done=cols_done, xl_base=xl_base):
            W(gpsimd, "gpsimd", dv, cols_done)
            W(gpsimd, "gpsimd", s_zx, sxs_zeroed)
            for c in range(NCH):
                W(gpsimd, "gpsimd", s_xl[c % 2], xl_base[c % 2] + (c // 2 + 1) * 16)
                ins = gpsimd.indirect_dma_start(
                    out=sxs_d[0:NSLOT, :],
                    out_offset=bass.IndirectOffsetOnAxis(scoli[:, c:c + 1], 0),
                    in_=xlt[c % 2], in_offset=None)
                s_sc.inc(ins, 16)
        S("gpsimd", _)
        scat_done = s_sc.n

        for kt in range(NT):
            def _(sync, kt=kt, scat_done=scat_done):
                W(sync, "sync", s_sc, scat_done)
                W(sync, "sync", ac)
                W(sync, "sync", dv)
                for s2 in range(NSUB):
                    s_bw.inc(sync.dma_start(out=xlt[s2 % 2],
                                            in_=sxs_d[kt * TT + s2 * 128: kt * TT + (s2 + 1) * 128, :]), 16)
                s_bw.inc(dyndma(sync, b1sb, b1_b, eti[0:1, kt:kt + 1],
                                128 * NFK * 4, 0, (E - 1) * 128 * NFK * 4,
                                [[NFK * 4, 128], [1, NFK * 4]]), 16)
                s_bw.inc(dyndma(sync, b2sb, b2_b, eti[0:1, kt:kt + 1],
                                128 * NDK * 4, 0, (E - 1) * 128 * NDK * 4,
                                [[NDK * 4, 128], [1, NDK * 4]]), 16)
            S("sync", _)
            xs_ld = s_bw.n

            items4 = [(xlt[s2 % 2][:, k * 128:(k + 1) * 128],
                       xsfm[:, k * TT + s2 * 128: k * TT + (s2 + 1) * 128])
                      for s2 in range(NSUB) for k in range(NDK)]
            pe5, dv5 = pe.n, dv.n

            def _(tensor, items4=items4, xs_ld=xs_ld, dv5=dv5):
                W(tensor, "tensor", s_bw, xs_ld)
                W(tensor, "tensor", dv)
                for j, (src, _dst) in enumerate(items4):
                    if j >= 2:
                        W(tensor, "tensor", dv, dv5 + j - 1)
                    pe.inc(TR(tensor, psb(6 + j % 2, 128), src))
            S("tensor", _)

            def _(vector, items4=items4, pe5=pe5):
                for j, (_src, dst) in enumerate(items4):
                    W(vector, "vector", pe, pe5 + j + 1)
                    dv.inc(vector.tensor_copy(dst, psb(6 + j % 2, 128)))
            S("vector", _)
            xsfm_done = dv.n

            pe_fc0 = pe.n
            for fc in range(8):
                def _(sync, kt=kt, fc=fc, pe_fc0=pe_fc0):
                    if fc >= 2:
                        W(sync, "sync", pe, pe_fc0 + 2 * (fc - 2) + 1)
                    buf = w1blk[fc % 2]
                    RB = FF * 4
                    dst = bass.AP(buf.tensor, buf.offset,
                                  [list(buf.ap[0]), [512, NDK], [1, 512]])
                    ins = dyndma(sync, dst, w1_b, eti[0:1, kt:kt + 1],
                                 DM * RB, fc * 512 * 4, (E - 1) * DM * RB,
                                 [[RB, 128], [128 * RB, NDK], [1, 512 * 4]])
                    s_w1[fc % 2].inc(ins, 16)
                S("sync", _)
                w1_ld = s_w1[fc % 2].n

                def _(scalar, kt=kt, fc=fc, pe_fc0=pe_fc0):
                    W(scalar, "scalar", dv, eti_done[0])
                    if fc >= 2:
                        W(scalar, "scalar", pe, pe_fc0 + 2 * (fc - 2) + 2)
                    buf = w2blk[fc % 2]
                    RB2 = DM * 4
                    dst = bass.AP(buf.tensor, buf.offset,
                                  [list(buf.ap[0]), [DM, 4], [1, DM]])
                    ins = dyndma(scalar, dst, w2_b, eti[0:1, kt:kt + 1],
                                 FF * RB2, fc * 512 * RB2, (E - 1) * FF * RB2,
                                 [[RB2, 128], [128 * RB2, 4], [1, RB2]])
                    s_w2[fc % 2].inc(ins, 16)
                S("scalar", _)
                w2_ld = s_w2[fc % 2].n

                def _(tensor, kt=kt, fc=fc, w1_ld=w1_ld, xsfm_done=xsfm_done):
                    W(tensor, "tensor", s_w1[fc % 2], w1_ld)
                    W(tensor, "tensor", dv, xsfm_done)
                    if fc > 0:
                        W(tensor, "tensor", ac)
                    buf = w1blk[fc % 2]
                    for s2 in range(4):
                        bank = s2 // 2
                        off = bank * 512 + (s2 % 2) * TT
                        for kc in range(NDK):
                            ins = tensor.matmul(
                                PSA[:, off:off + TT],
                                buf[:, kc * 512 + s2 * 128: kc * 512 + (s2 + 1) * 128],
                                xsfm[:, kc * TT:(kc + 1) * TT],
                                start=(kc == 0 and s2 % 2 == 0),
                                stop=(kc == NDK - 1 and s2 % 2 == 1))
                    pe.inc(ins)
                S("tensor", _)
                hmm_done = pe.n

                gelu_ac0 = ac.n

                def _(scalar, fc=fc, hmm_done=hmm_done, xs_ld=xs_ld):
                    W(scalar, "scalar", pe, hmm_done)
                    W(scalar, "scalar", s_bw, xs_ld)
                    for s2 in range(4):
                        bank = s2 // 2
                        off = bank * 512 + (s2 % 2) * TT
                        jglob = fc * 4 + s2
                        ac.inc(scalar.activation(hsb[:, s2 * TT:(s2 + 1) * TT],
                                                 PSA[:, off:off + TT], AF.Gelu,
                                                 bias=b1sb[:, jglob:jglob + 1]))
                S("scalar", _)
                gelu_done = ac.n

                def _(tensor, kt=kt, fc=fc, gelu_ac0=gelu_ac0, w2_ld=w2_ld):
                    W(tensor, "tensor", s_w2[fc % 2], w2_ld)
                    buf = w2blk[fc % 2]
                    for s2 in range(4):
                        W(tensor, "tensor", ac, gelu_ac0 + s2 + 1)
                        for j in range(8):
                            bank = 2 + j // 2
                            off = bank * 512 + (j % 2) * TT
                            ins = tensor.matmul(
                                PSA[:, off:off + TT],
                                buf[:, s2 * DM + j * 128: s2 * DM + (j + 1) * 128],
                                hsb[:, s2 * TT:(s2 + 1) * TT],
                                start=(fc == 0 and s2 == 0 and j % 2 == 0),
                                stop=(fc == 7 and s2 == 3 and j % 2 == 1))
                    pe.inc(ins)
                S("tensor", _)
            eomm_done = pe.n

            def _(vector, kt=kt, eomm_done=eomm_done, xs_ld=xs_ld):
                W(vector, "vector", pe, eomm_done)
                W(vector, "vector", s_bw, xs_ld)
                for j in range(8):
                    bank = 2 + j // 2
                    off = bank * 512 + (j % 2) * TT
                    ins = vector.tensor_scalar(eofm[:, j * TT:(j + 1) * TT],
                                               PSA[:, off:off + TT],
                                               b2sb[:, j:j + 1], None, ALU.add)
                dv.inc(ins)
            S("vector", _)
            eo_done = dv.n

            items5 = [(eofm[:, j * TT + s2 * 128: j * TT + (s2 + 1) * 128], s2, j)
                      for s2 in range(NSUB) for j in range(NDK)]
            pe6, dv6 = pe.n, dv.n

            def _(tensor, items5=items5, eo_done=eo_done, dv6=dv6):
                W(tensor, "tensor", dv, eo_done)
                for jj, (src, s2, j) in enumerate(items5):
                    if jj >= 2:
                        W(tensor, "tensor", dv, dv6 + jj - 1)
                    pe.inc(TR(tensor, psb(6 + jj % 2, 128), src))
            S("tensor", _)

            def _(vector, items5=items5, pe6=pe6):
                W(vector, "vector", s_mo)
                for jj, (_src, s2, j) in enumerate(items5):
                    W(vector, "vector", pe, pe6 + jj + 1)
                    dv.inc(vector.tensor_copy(eotm[s2][:, j * 128:(j + 1) * 128],
                                              psb(6 + jj % 2, 128)))
            S("vector", _)
            eot_done = dv.n

            def _(scalar, kt=kt, eot_done=eot_done):
                W(scalar, "scalar", dv, eot_done)
                for s2 in range(NSUB):
                    s_mo.inc(scalar.dma_start(out=smoe_d[kt * TT + s2 * 128: kt * TT + (s2 + 1) * 128, :],
                                              in_=eotm[s2]), 16)
            S("scalar", _)
        moe_st = s_mo.n

        x1f_base = [s_x1f[0].n, s_x1f[1].n]
        mot_base = [s_mg[0].n, s_mg[1].n]
        out_dvb = dv.n
        st_base = sfin.n

        def _(sync, out_dvb=out_dvb):
            W(sync, "sync", s_st1, a_st1)
            for c in range(NCH):
                if c >= 2:
                    W(sync, "sync", dv, out_dvb + c - 1)
                ins = sync.dma_start(out=x1f[c % 2], in_=sx1_d[c * 128:(c + 1) * 128, :])
                s_x1f[c % 2].inc(ins, 16)
        S("sync", _)

        def _(gpsimd, moe_st=moe_st, out_dvb=out_dvb):
            W(gpsimd, "gpsimd", s_mo, moe_st)
            for c in range(NCH):
                if c >= 2:
                    W(gpsimd, "gpsimd", dv, out_dvb + c - 1)
                ins = gpsimd.indirect_dma_start(
                    out=mot[c % 2], out_offset=None,
                    in_=smoe_d[0:NSLOT, :],
                    in_offset=bass.IndirectOffsetOnAxis(scoli[:, c:c + 1], 0))
                s_mg[c % 2].inc(ins, 16)
        S("gpsimd", _)

        def _(vector, x1f_base=x1f_base, mot_base=mot_base, st_base=st_base):
            for c in range(NCH):
                W(vector, "vector", s_x1f[c % 2], x1f_base[c % 2] + (c // 2 + 1) * 16)
                W(vector, "vector", s_mg[c % 2], mot_base[c % 2] + (c // 2 + 1) * 16)
                if c >= 2:
                    W(vector, "vector", sfin, st_base + ((c - 2) // 2 + 1) * 2 * 16)
                dv.inc(vector.scalar_tensor_tensor(outt[c % 2], mot[c % 2],
                                                   scalec[:, c:c + 1], x1f[c % 2],
                                                   ALU.mult, ALU.add))
        S("vector", _)

        def _(scalar, out_dvb=out_dvb):
            for c in range(NCH):
                W(scalar, "scalar", dv, out_dvb + c + 1)
                sfin.inc(scalar.dma_start(out=out_d[c * 128:(c + 1) * 128, :],
                                          in_=outt[c % 2]), 16)
            W(scalar, "scalar", sfin, sfin.n)
        S("scalar", _)

    return nc


_NC_CACHE = {}


def _get_nc(key, cfg):
    if key not in _NC_CACHE:
        _NC_CACHE[key] = build(cfg)
    return _NC_CACHE[key]


def prep_inputs(core, cfg, L0_attn_x, L0_attn_W_in, L0_attn_conv_w, L0_attn_conv_b,
                L0_attn_W_x, L0_attn_A_log, L0_attn_D_param, L0_attn_W_out,
                L0_ffn_router_w, L0_ffn_w1, L0_ffn_b1, L0_ffn_w2, L0_ffn_b2):
    NB = cfg.NB
    f32 = np.float32
    x = np.ascontiguousarray(L0_attn_x[core * NB:(core + 1) * NB]).reshape(cfg.TOKS, DM)
    wx = np.asarray(L0_attn_W_x, f32)
    d = {
        "x0": np.asarray(x, f32),
        "w_in": np.ascontiguousarray(np.asarray(L0_attn_W_in, f32)),
        "conv_w": np.ascontiguousarray(
            np.asarray(L0_attn_conv_w, f32).reshape(NDC, 128, DCV).transpose(1, 0, 2).reshape(128, NDC * DCV)),
        "conv_b": np.ascontiguousarray(np.asarray(L0_attn_conv_b, f32).reshape(NDC, 128).T),
        "wx_d": np.ascontiguousarray(wx[:, 0].reshape(NDC, 128).T),
        "wx_b": np.ascontiguousarray(
            wx[:, 1:1 + DS].reshape(NDC, 128, DS).transpose(1, 0, 2).reshape(128, NDC * DS)),
        "wx_c": np.ascontiguousarray(
            wx[:, 1 + DS:1 + 2 * DS].reshape(NDC, 128, DS).transpose(1, 0, 2).reshape(128, NDC * DS)),
        "a_log": np.asarray(L0_attn_A_log, f32).reshape(1, DS),
        "d_param": np.ascontiguousarray(np.asarray(L0_attn_D_param, f32).reshape(NDC, 128).T),
        "w_out": np.ascontiguousarray(np.asarray(L0_attn_W_out, f32)),
        "rw_t": np.ascontiguousarray(
            np.asarray(L0_ffn_router_w, f32).T.reshape(NDK, 128, E).transpose(1, 0, 2).reshape(128, NDK * E)),
        "w1": np.ascontiguousarray(np.asarray(L0_ffn_w1, f32).reshape(E * DM, FF)),
        "w2": np.ascontiguousarray(np.asarray(L0_ffn_w2, f32).reshape(E * FF, DM)),
        "b1t": np.ascontiguousarray(
            np.asarray(L0_ffn_b1, f32).reshape(E, NFK, 128).transpose(0, 2, 1).reshape(E * 128, NFK)),
        "b2t": np.ascontiguousarray(
            np.asarray(L0_ffn_b2, f32).reshape(E, NDK, 128).transpose(0, 2, 1).reshape(E * 128, NDK)),
        "r1hot": np.eye(NCORES, dtype=f32)[core].reshape(1, NCORES),
        "rpfx": (np.arange(NCORES) < core).astype(f32).reshape(NCORES, 1),
    }
    return d


def kernel(**inputs):
    cfg = Cfg(NB=4, L=2048, G=256, TT=256, CCAP=12288)
    nc = _get_nc("prod", cfg)
    in_maps = [prep_inputs(c, cfg, **inputs) for c in range(NCORES)]
    res = run_bass_kernel_spmd(nc, in_maps, core_ids=list(range(NCORES)))
    outs = [np.asarray(r["out"]).reshape(cfg.NB, cfg.L, DM) for r in res.results]
    return np.concatenate(outs, axis=0)


# version 04
# speedup vs baseline: 1.0089x; 1.0089x over previous
import math

import numpy as np

import concourse.bass as bass
import concourse.mybir as mybir
from concourse.bass_utils import run_bass_kernel_spmd

F32 = mybir.dt.float32
BF16 = mybir.dt.bfloat16
I32 = mybir.dt.int32
U32 = mybir.dt.uint32
AF = mybir.ActivationFunctionType
ALU = mybir.AluOpType
AX = mybir.AxisListType.X

NCORES = 8
DM = 1024
DI = 2048
DS = 16
DCV = 4
E = 8
FF = 4096
Q = 128
NDK = DM // 128
NDC = DI // 128
NFK = FF // 128


class Cfg:
    def __init__(self, NB=4, L=2048, G=256, TT=256, CCAP=12288):
        self.NB, self.L, self.G, self.TT, self.CCAP = NB, L, G, TT, CCAP
        self.TOKS = NB * L
        self.GPS = L // G
        self.NG = NB * self.GPS
        self.GC = G // Q
        self.NCH = self.TOKS // Q
        self.NT = self.TOKS // TT + E
        self.NSLOT = self.NT * TT
        self.TCH = min(1024, self.TOKS)
        self.NTC = self.TOKS // self.TCH
        assert L % G == 0 and G % Q == 0 and TT % 128 == 0
        assert self.TOKS % self.TCH == 0 and self.TCH % Q == 0


class Sem:
    def __init__(self, nc, name):
        self.h = nc.alloc_semaphore(name)
        self.n = 0
        self.name = name

    def inc(self, inst, k=1):
        inst.then_inc(self.h, k)
        self.n += k
        return inst


class Pool:

    def __init__(self, nc, name, elems_per_part):
        self.base = nc.alloc_sbuf_tensor(name, [128, elems_per_part], F32).ap()
        self.off = 0
        self.cap = elems_per_part

    def take(self, nelem, dtype=F32, parts=128):
        o = self.off
        self.off += (nelem + 15) & ~15
        assert self.off <= self.cap, f"pool overflow {self.off} > {self.cap}"
        v = self.base[0:parts, o:o + nelem]
        if dtype is not F32:
            v = v.bitcast(dtype)
        return v

    def mark(self):
        return self.off

    def reset(self, m=0):
        self.off = m


def build(cfg: Cfg):
    nc = bass.Bass("TRN2", target_bir_lowering=False, debug=False, num_devices=NCORES)
    NB, L, G, TT, CCAP = cfg.NB, cfg.L, cfg.G, cfg.TT, cfg.CCAP
    TOKS, GPS, NG, GC, NCH = cfg.TOKS, cfg.GPS, cfg.NG, cfg.GC, cfg.NCH
    NT, NSLOT, TCH, NTC = cfg.NT, cfg.NSLOT, cfg.TCH, cfg.NTC
    CONVW = G + DCV - 1
    NSUB = TT // 128
    NROWB = TOKS // 128
    assert NROWB <= 128 and NT <= 128

    x0_d = nc.dram_tensor("x0", [TOKS, DM], F32, kind="ExternalInput").ap()
    wi_d = nc.dram_tensor("w_in", [DM, 2 * DI], F32, kind="ExternalInput").ap()
    cw_d = nc.dram_tensor("conv_w", [128, NDC * DCV], F32, kind="ExternalInput").ap()
    cb_d = nc.dram_tensor("conv_b", [128, NDC], F32, kind="ExternalInput").ap()
    wxd_d = nc.dram_tensor("wx_d", [128, NDC], F32, kind="ExternalInput").ap()
    wxb_d = nc.dram_tensor("wx_b", [128, NDC * DS], F32, kind="ExternalInput").ap()
    wxc_d = nc.dram_tensor("wx_c", [128, NDC * DS], F32, kind="ExternalInput").ap()
    alog_d = nc.dram_tensor("a_log", [1, DS], F32, kind="ExternalInput").ap()
    dp_d = nc.dram_tensor("d_param", [128, NDC], F32, kind="ExternalInput").ap()
    wo_d = nc.dram_tensor("w_out", [DI, DM], F32, kind="ExternalInput").ap()
    rwt_d = nc.dram_tensor("rw_t", [128, NDK * E], F32, kind="ExternalInput").ap()
    w1_d = nc.dram_tensor("w1", [E * DM, FF], F32, kind="ExternalInput").ap()
    w2_d = nc.dram_tensor("w2", [E * FF, DM], F32, kind="ExternalInput").ap()
    b1_d = nc.dram_tensor("b1t", [E * 128, NFK], F32, kind="ExternalInput").ap()
    b2_d = nc.dram_tensor("b2t", [E * 128, NDK], F32, kind="ExternalInput").ap()
    r1hot_d = nc.dram_tensor("r1hot", [1, NCORES], F32, kind="ExternalInput").ap()
    rpfx_d = nc.dram_tensor("rpfx", [NCORES, 1], F32, kind="ExternalInput").ap()

    out_d = nc.dram_tensor("out", [TOKS, DM], F32, kind="ExternalOutput").ap()

    sx1_d = nc.dram_tensor("s_x1", [TOKS, DM], F32, kind="Internal").ap()
    sxln_d = nc.dram_tensor("s_xln", [TOKS, DM], F32, kind="Internal").ap()
    sxs_d = nc.dram_tensor("s_xs", [NSLOT, DM], F32, kind="Internal").ap()
    smoe_d = nc.dram_tensor("s_moe", [NSLOT, DM], F32, kind="Internal").ap()
    w1h_d = nc.dram_tensor("w1h", [E * DM, FF], BF16, kind="Internal").ap()
    w1l_d = nc.dram_tensor("w1l", [E * DM, FF], BF16, kind="Internal").ap()
    w2h_d = nc.dram_tensor("w2h", [E * FF, DM], BF16, kind="Internal").ap()
    w2l_d = nc.dram_tensor("w2l", [E * FF, DM], BF16, kind="Internal").ap()
    winh_d = nc.dram_tensor("winh", [DM, 2 * DI], BF16, kind="Internal").ap()
    winl_d = nc.dram_tensor("winl", [DM, 2 * DI], BF16, kind="Internal").ap()
    woh_d = nc.dram_tensor("woh", [DI, DM], BF16, kind="Internal").ap()
    wol_d = nc.dram_tensor("wol", [DI, DM], BF16, kind="Internal").ap()
    ccin_d = nc.dram_tensor("cc_in", [NCORES, E], F32, kind="Internal")
    ccout_d = nc.dram_tensor("cc_out", [NCORES, E], F32, kind="Internal")

    ident_c = nc.inline_tensor(np.eye(128, dtype=np.float32), name="c_ident")
    mS = np.where(np.arange(128)[None, :] >= np.arange(128)[:, None], 0.0, 1e9).astype(np.float32)
    maskS_c = nc.inline_tensor(mS, name="c_maskS")
    iota8c_c = nc.inline_tensor(np.arange(E, dtype=np.float32).reshape(E, 1), name="c_iota8c")
    rev8_c = nc.inline_tensor(np.tile(np.arange(E, 0, -1).astype(np.float32), (128, 1)), name="c_rev8")
    tri8_c = nc.inline_tensor(np.triu(np.ones((E, E), np.float32), 1), name="c_tri8")
    ktile_c = nc.inline_tensor(np.tile((np.arange(NT) * TT).astype(np.float32), (E, 1)), name="c_ktile")

    def sb(name, shape, dtype=F32):
        return nc.alloc_sbuf_tensor(name, shape, dtype).ap()

    ident = sb("ident", [128, 128])
    maskS = sb("maskS", [128, 128])
    iota8c = sb("iota8c", [E, 1])
    rev8 = sb("rev8", [128, E])
    tri8 = sb("tri8", [E, E])
    ktile = sb("ktile", [E, NT])
    convw = sb("convw", [128, NDC * DCV])
    convb = sb("convb", [128, NDC])
    wxd = sb("wxd", [128, NDC])
    wxb = sb("wxb", [128, NDC * DS])
    wxc = sb("wxc", [128, NDC * DS])
    dcol = sb("dcol", [128, NDC])
    rwt = sb("rwt", [128, NDK * E])
    alog = sb("alog", [1, DS])
    arow = sb("arow", [1, DS])
    abc = sb("abc", [128, DS])
    hst = sb("hst", [DS, DI])
    ctail = sb("ctail", [128, NDC * (DCV - 1)])
    ecol = sb("ecol", [128, NCH])
    pcol = sb("pcol", [128, NCH])
    onesr = sb("onesr", [1, 512])
    onesc8 = sb("onesc8", [E, 1])
    ones8t = sb("ones8t", [E, TCH])
    wspi = sb("wspi", [128, 1024])
    wsph = sb("wsph", [128, 1024], BF16)
    wspl = sb("wspl", [128, 1024], BF16)
    gpad = sb("gpad", [1, 1])
    zt = sb("zt", [128, 1024])
    scols = sb("scols", [128, NCH])
    scoli = sb("scoli", [128, NCH], I32)
    kcols = sb("kcols", [128, NCH])
    scalec = sb("scalec", [128, NCH])
    cnts = sb("cnts", [E, 1])
    carry = sb("carry", [E, 1])
    basec = sb("basec", [E, 1])
    poffm1 = sb("poffm1", [E, 1])
    cuminc = sb("cuminc", [E, 1])
    kcst = sb("kcst", [128, 4])
    r1hot = sb("r1hot_s", [1, NCORES])
    rpfx = sb("rpfx_s", [NCORES, 1])

    pool = Pool(nc, "pool", (nc.sbuf_bytes_remaining - 512) // 4)

    x0t = [pool.take(DM) for _ in range(2)]
    xnt = [pool.take(DM) for _ in range(2)]
    sqt = pool.take(DM)
    lnm = [pool.take(1) for _ in range(2)]
    lni = [pool.take(1) for _ in range(2)]
    lns = pool.take(8)
    xnfm = pool.take(NDK * G)
    wiblk = [pool.take(1024) for _ in range(2)]
    xssm = pool.take(NDC * CONVW)
    szfm = pool.take(NDC * G)
    xcsfm = pool.take(NDC * G)
    xcdtm = pool.take(DI)
    drow = pool.take(G, parts=1)
    dabs = pool.take(G, parts=1)
    dexp = pool.take(G, parts=1)
    dln = pool.take(G, parts=1)
    drel = pool.take(G, parts=1)
    bfm = pool.take(G, parts=DS)
    cfm = pool.take(G, parts=DS)
    crow = pool.take(Q, parts=1)
    negc = pool.take(Q, parts=1)
    argrow = pool.take(2 * Q + 1, parts=1)
    ekall = pool.take(2 * Q + 1, parts=DS)
    dmt = pool.take(128)
    estk = pool.take((DS // 2) * Q)
    cflat = pool.take(DS * Q, parts=1)
    bttm = pool.take(DS)
    kbfm = pool.take(Q, parts=DS)
    qbfm = pool.take(Q, parts=DS)
    kbtm = pool.take(DS)
    dcolq = pool.take(1)
    mtmp = pool.take((DS // 2) * Q)
    mtt = pool.take(128)
    y3fm = pool.take(NDC * G)
    woblk = [pool.take(DM) for _ in range(2)]
    x1t = [pool.take(DM) for _ in range(2)]
    xlnt = [pool.take(DM) for _ in range(2)]
    xlnfm = pool.take(NDK * G)
    lgu = pool.take(E, U32)
    lgf = pool.take(E)
    lgw = pool.take(E)
    lgm = pool.take(1)
    lgm2 = pool.take(1)
    lgnm = pool.take(1)
    lge = pool.take(E)
    lgs = pool.take(1)
    a_mark = pool.mark()

    pool.reset(0)
    erow_c = pool.take(TCH, parts=1)
    ind_c = pool.take(TCH, parts=E)
    po_c = pool.take(TCH, parts=E)
    tmp_c = pool.take(TCH, parts=E)
    tmq_c = pool.take(TCH, dtype=U32, parts=E)
    ef16 = pool.take(128, parts=16)
    srow_c = pool.take(TCH, parts=1)
    krow_c = pool.take(TCH, parts=1)
    srows = pool.take(128, parts=NROWB)
    krows = pool.take(128, parts=NROWB)
    cntr = pool.take(E, parts=1)
    ccsb = pool.take(E, parts=NCORES)
    ccall = pool.take(E, parts=NCORES)
    baser = pool.take(E, parts=1)
    pcq = pool.take(4, parts=E)
    eti = sb("eti", [1, NT], I32)
    etu = pool.take(NT, dtype=U32, parts=E)
    etf = pool.take(NT, parts=E)
    etr = pool.take(NT, parts=1)
    etfr = pool.take(NT, parts=1)
    xlt = [pool.take(DM) for _ in range(2)]
    w1blk = [pool.take(NDK * 512) for _ in range(2)]
    w2blk = [pool.take(4 * DM) for _ in range(2)]
    xsfm = pool.take(NDK * TT)
    hsb = pool.take(4 * TT)
    hhl = pool.take(4 * TT)
    eofm = pool.take(NDK * TT)
    eotm = [pool.take(DM) for _ in range(NSUB)]
    b1sb = pool.take(NFK)
    b2sb = pool.take(NDK)
    mot = [pool.take(DM) for _ in range(2)]
    x1f = [pool.take(DM) for _ in range(2)]
    outt = [pool.take(DM) for _ in range(2)]
    b_mark = pool.mark()
    assert max(a_mark, b_mark) <= pool.cap

    PSA = nc.alloc_psum_tensor("PSA", [128, 4096], F32).ap()

    def psb(bank, nelem, parts=128):
        return PSA[0:parts, bank * 512: bank * 512 + nelem]

    def TR(tensor, out, in_):
        p = in_.shape[0]
        return tensor.transpose(out, in_, ident[0:p, 0:p])

    U8 = mybir.dt.uint8

    def dyndma(eng, dst, src_ap, idx_ap, scale_b, add_b, max_b, dims):
        with eng.register() as r:
            eng.reg_load(r, idx_ap)
            eng.reg_alu(r, r, scale_b, ALU.mult)
            if add_b:
                eng.reg_alu(r, r, add_b, ALU.add)
            rv = eng.snap(r, min_val=0, max_val=max_b)
            src = bass.AP(src_ap.tensor, rv, dims)
            ins = eng.dma_start(out=dst.bitcast(U8), in_=src)
            try:
                for reg in rv.val:
                    eng.free_register(reg)
            except TypeError:
                eng.free_register(rv.val)
        return ins

    w1h_b = w1h_d.bitcast(U8)
    w1l_b = w1l_d.bitcast(U8)
    w2h_b = w2h_d.bitcast(U8)
    w2l_b = w2l_d.bitcast(U8)
    b1_b = b1_d.bitcast(U8)
    b2_b = b2_d.bitcast(U8)

    s_ms = Sem(nc, "s_ms")
    s_ws2 = [Sem(nc, "s_wsA"), Sem(nc, "s_wsB")]
    s_wt2 = [Sem(nc, "s_wtA"), Sem(nc, "s_wtB")]
    s_gp = Sem(nc, "s_gp")
    s_sp = Sem(nc, "s_sp")
    s_gm = Sem(nc, "s_gm")
    s_x0 = Sem(nc, "s_x0")
    s_wi = [Sem(nc, "s_wiA"), Sem(nc, "s_wiB")]
    s_wo = [Sem(nc, "s_woA"), Sem(nc, "s_woB")]
    s_st1 = Sem(nc, "s_st1")
    s_cf = Sem(nc, "s_cf")
    s_zx = Sem(nc, "s_zx")
    s_xl = [Sem(nc, "s_xlA"), Sem(nc, "s_xlB")]
    s_sc = Sem(nc, "s_sc")
    s_bw = Sem(nc, "s_bw")
    s_w1 = [Sem(nc, "s_w1A"), Sem(nc, "s_w1B")]
    s_w2 = [Sem(nc, "s_w2A"), Sem(nc, "s_w2B")]
    s_mo = Sem(nc, "s_mo")
    s_x1f = [Sem(nc, "s_x1fA"), Sem(nc, "s_x1fB")]
    s_mg = [Sem(nc, "s_mgA"), Sem(nc, "s_mgB")]
    pe = Sem(nc, "pe")
    dv = Sem(nc, "dv")
    ac = Sem(nc, "ac")
    cc = Sem(nc, "cc")
    sfin = Sem(nc, "sfin")

    waited = {}

    def W(eng, en, sem, val=None):
        v = sem.n if val is None else val
        if v <= 0:
            return
        k = (en, sem.name)
        if waited.get(k, -1) < v:
            eng.wait_ge(sem.h, v)
            waited[k] = v

    with nc.Block() as block:
        RUN = {"sync": block.sync, "scalar": block.scalar, "vector": block.vector,
               "tensor": block.tensor, "gpsimd": block.gpsimd}

        def S(name, fn):
            RUN[name](fn)

        def _(sync):
            for dst, src in [(ident, ident_c.ap()), (maskS, maskS_c.ap()),
                             (iota8c, iota8c_c.ap()), (rev8, rev8_c.ap()),
                             (tri8, tri8_c.ap()), (ktile, ktile_c.ap()),
                             (convw, cw_d), (convb, cb_d), (wxd, wxd_d),
                             (wxb, wxb_d), (wxc, wxc_d), (dcol, dp_d),
                             (rwt, rwt_d), (alog, alog_d),
                             (r1hot, r1hot_d), (rpfx, rpfx_d)]:
                s_ms.inc(sync.dma_start(out=dst, in_=src), 16)
        S("sync", _)

        def _(vector):
            W(vector, "vector", s_ms)
            vector.memset(onesr[:], 1.0)
            vector.memset(kcst[:, 0:1], 1e-5)
            vector.memset(kcst[:, 1:2], 1.0)
            vector.memset(kcst[:, 2:3], -1.0)
            vector.memset(onesc8[:], 1.0)
            vector.memset(ones8t[:], 1.0)
            vector.memset(zt[:], 0.0)
            vector.memset(ctail[:], 0.0)
            dv.inc(vector.memset(hst[:], 0.0))
        S("vector", _)

        def _(scalar):
            W(scalar, "scalar", s_ms)
            ac.inc(scalar.activation(arow[:], alog[:], AF.Exp))
        S("scalar", _)

        def _(vector):
            W(vector, "vector", ac)
            dv.inc(vector.tensor_scalar_mul(arow[:], arow[:], -1.0))
        S("vector", _)

        def _(tensor):
            W(tensor, "tensor", dv)
            pe.inc(tensor.matmul(psb(6, DS), onesr[0:1, 0:128], arow[:], start=True, stop=True))
        S("tensor", _)

        def _(vector):
            W(vector, "vector", pe)
            dv.inc(vector.tensor_copy(abc[:], psb(6, DS)))
        S("vector", _)
        abc_done = dv.n

        sjobs = []
        for cb in range(2 * DI // 512):
            for rb in range(DM // 128):
                sjobs.append((wi_d, winh_d, winl_d, rb, cb))
        for rb in range(DI // 128):
            for cb in range(DM // 512):
                sjobs.append((wo_d, woh_d, wol_d, rb, cb))
        for rb in range(E * DM // 128):
            for cb in range(FF // 512):
                sjobs.append((w1_d, w1h_d, w1l_d, rb, cb))
        for rb in range(E * FF // 128):
            for cb in range(DM // 512):
                sjobs.append((w2_d, w2h_d, w2l_d, rb, cb))
        NWIC = 2 * DI // 512 * (DM // 128)
        NWOC = (DI // 128) * (DM // 512)
        NW1C = (E * DM // 128) * (FF // 512)
        miles = {cb * (DM // 128) + (DM // 128) - 1: cb + 1
                 for cb in range(2 * DI // 512)}
        miles[NWIC + NWOC - 1] = 9
        miles[NWIC + NWOC + NW1C - 1] = 10
        miles[len(sjobs) - 1] = 11

        def _(gpsimd):
            def ld(c):
                src = sjobs[c][0]
                rb, cb = sjobs[c][3], sjobs[c][4]
                i = c % 2
                s_ws2[i].inc(gpsimd.dma_start(
                    out=wspi[:, i * 512:(i + 1) * 512],
                    in_=src[rb * 128:(rb + 1) * 128, cb * 512:(cb + 1) * 512]), 16)
            ld(0)
            ld(1)
            for c in range(len(sjobs)):
                _src, hid, lod, rb, cb = sjobs[c]
                i = c % 2
                W(gpsimd, "gpsimd", s_ws2[i], 16 * (c // 2 + 1))
                if c >= 2:
                    W(gpsimd, "gpsimd", s_wt2[i], 32 * (c // 2))
                s_gp.inc(gpsimd.tensor_copy(wsph[:, i * 512:(i + 1) * 512],
                                            wspi[:, i * 512:(i + 1) * 512]))
                W(gpsimd, "gpsimd", s_gp, 2 * c + 1)
                s_gp.inc(gpsimd.tensor_tensor(wspl[:, i * 512:(i + 1) * 512],
                                              wspi[:, i * 512:(i + 1) * 512],
                                              wsph[:, i * 512:(i + 1) * 512],
                                              ALU.subtract))
                W(gpsimd, "gpsimd", s_gp, 2 * c + 2)
                s_wt2[i].inc(gpsimd.dma_start(
                    out=hid[rb * 128:(rb + 1) * 128, cb * 512:(cb + 1) * 512],
                    in_=wsph[:, i * 512:(i + 1) * 512]), 16)
                s_wt2[i].inc(gpsimd.dma_start(
                    out=lod[rb * 128:(rb + 1) * 128, cb * 512:(cb + 1) * 512],
                    in_=wspl[:, i * 512:(i + 1) * 512]), 16)
                if c + 2 < len(sjobs):
                    ld(c + 2)
                if c in miles:
                    W(gpsimd, "gpsimd", s_wt2[0], 32 * (c // 2 + 1))
                    W(gpsimd, "gpsimd", s_wt2[1], 32 * ((c + 1) // 2))
                    s_sp.inc(gpsimd.memset(gpad[0:1, 0:1], 0.0))
            W(gpsimd, "gpsimd", dv, abc_done)
            rp = (128 * 1024) // DM
            for i in range((NSLOT + rp - 1) // rp):
                r0, r1 = i * rp, min(NSLOT, (i + 1) * rp)
                s_zx.inc(gpsimd.dma_start(out=sxs_d[r0:r1, :],
                                           in_=zt[0:(r1 - r0) * DM // 1024, :]), 16)
        S("gpsimd", _)
        sxs_zeroed = s_zx.n

        x1_done_mark = [0]
        prev_marks = {"wi": 0, "wo": 0, "x1st": 0, "xlnst": 0}

        for g in range(NG):
            b = g // GPS
            first = (g % GPS) == 0
            t0 = g * G
            ci0 = g * GC

            def _(sync, t0=t0):
                W(sync, "sync", dv, x1_done_mark[0])
                for i in range(GC):
                    s_x0.inc(sync.dma_start(out=x0t[i],
                                            in_=x0_d[t0 + i * 128: t0 + (i + 1) * 128, :]), 16)
            S("sync", _)
            x0_ld = s_x0.n

            for i in range(GC):
                def _(scalar, i=i, x0_ld=x0_ld):
                    W(scalar, "scalar", s_x0, x0_ld)
                    W(scalar, "scalar", dv)
                    ac.inc(scalar.activation(sqt, x0t[i], AF.Square))
                S("scalar", _)

                def _(vector, i=i, x0_ld=x0_ld):
                    W(vector, "vector", s_x0, x0_ld)
                    W(vector, "vector", ac)
                    vector.reduce_sum(lns[:, 0:1], x0t[i], axis=AX)
                    dv.inc(vector.reduce_sum(lns[:, 1:2], sqt, axis=AX))
                    W(vector, "vector", dv)
                    vector.tensor_scalar(lnm[i], lns[:, 0:1], 1.0 / DM, None, ALU.mult)
                    dv.inc(vector.tensor_scalar(lns[:, 3:4], lns[:, 1:2], 1.0 / DM, None, ALU.mult))
                    W(vector, "vector", dv)
                    dv.inc(vector.tensor_tensor(lns[:, 2:3], lnm[i], lnm[i], ALU.mult))
                    W(vector, "vector", dv)
                    dv.inc(vector.tensor_tensor(lns[:, 4:5], lns[:, 3:4], lns[:, 2:3], ALU.subtract))
                S("vector", _)

                def _(scalar, i=i):
                    W(scalar, "scalar", dv)
                    ac.inc(scalar.activation(lns[:, 5:6], lns[:, 4:5], AF.Sqrt, bias=kcst[:, 0:1]))
                S("scalar", _)

                def _(vector, i=i):
                    W(vector, "vector", ac)
                    dv.inc(vector.reciprocal(lni[i], lns[:, 5:6]))
                    W(vector, "vector", dv)
                    dv.inc(vector.tensor_scalar(xnt[i], x0t[i], lnm[i], lni[i],
                                                ALU.subtract, ALU.mult))
                S("vector", _)
            xn_done = dv.n

            xnbf = xnfm.bitcast(BF16)
            items = [(xnt[i][:, k * 128:(k + 1) * 128], k * G + i * 128)
                     for k in range(NDK) for i in range(GC)]
            pe0, dv0, ac0 = pe.n, dv.n, ac.n

            def _(tensor, items=items, xn_done=xn_done, dv0=dv0):
                W(tensor, "tensor", dv, xn_done)
                for j, (src, _off) in enumerate(items):
                    if j >= 2:
                        W(tensor, "tensor", dv, dv0 + j - 1)
                    pe.inc(TR(tensor, psb(6 + j % 2, 128), src))
            S("tensor", _)

            def _(scalar, items=items, pe0=pe0):
                for j, (_src, off) in enumerate(items):
                    W(scalar, "scalar", pe, pe0 + j + 1)
                    ac.inc(scalar.activation(xnbf[:, off:off + 128],
                                             psb(6 + j % 2, 128), AF.Copy))
            S("scalar", _)

            def _(vector, items=items, ac0=ac0):
                for j, (_src, off) in enumerate(items):
                    W(vector, "vector", ac, ac0 + j + 1)
                    dv.inc(vector.tensor_tensor(
                        xnbf[:, NDK * G + off: NDK * G + off + 128],
                        psb(6 + j % 2, 128), xnbf[:, off:off + 128], ALU.subtract))
            S("vector", _)
            xnfm_done = dv.n

            wi_pe0 = pe.n
            wi_base = [s_wi[0].n, s_wi[1].n]
            ev_marks = list(prev_marks.get("wi_ev", [None, None]))
            for r in range(4):
                roff = (r % 2) * 2048

                def _(sync, r=r, wi_pe0=wi_pe0):
                    for k in range(NDK):
                        flat = r * NDK + k
                        if flat >= 2:
                            W(sync, "sync", pe, wi_pe0 + flat - 1)
                        else:
                            W(sync, "sync", pe, prev_marks["wi"])
                        W(sync, "sync", s_sp, 2 * r + 2)
                        bf = wiblk[k % 2].bitcast(BF16)
                        ins = sync.dma_start(out=bf[:, 0:1024],
                                             in_=winh_d[k * 128:(k + 1) * 128,
                                                        r * 1024:(r + 1) * 1024])
                        s_wi[k % 2].inc(ins, 16)
                        ins = sync.dma_start(out=bf[:, 1024:2048],
                                             in_=winl_d[k * 128:(k + 1) * 128,
                                                        r * 1024:(r + 1) * 1024])
                        s_wi[k % 2].inc(ins, 16)
                S("sync", _)

                def _(tensor, r=r, roff=roff, wi_base=wi_base, xnfm_done=xnfm_done,
                      em=ev_marks[r]):
                    W(tensor, "tensor", dv, xnfm_done)
                    if em is not None:
                        W(tensor, "tensor", em[0], em[1])
                    xnbf = xnfm.bitcast(BF16)
                    for k in range(NDK):
                        flat = r * NDK + k
                        W(tensor, "tensor", s_wi[k % 2],
                          wi_base[k % 2] + (flat // 2 + 1) * 32)
                        bf = wiblk[k % 2].bitcast(BF16)
                        xh = xnbf[:, k * G:(k + 1) * G]
                        xl = xnbf[:, NDK * G + k * G: NDK * G + (k + 1) * G]
                        for j in range(8):
                            dst = PSA[:, roff + (j // 2) * 512 + (j % 2) * 256: roff + (j // 2) * 512 + (j % 2) * 256 + G]
                            wh = bf[:, j * 128:(j + 1) * 128]
                            wl = bf[:, 1024 + j * 128: 1024 + (j + 1) * 128]
                            tensor.matmul(dst, wh, xh,
                                          start=(k == 0 and j % 2 == 0), stop=False)
                            tensor.matmul(dst, wh, xl, start=False, stop=False)
                            ins2 = tensor.matmul(
                                dst, wl, xh, start=False,
                                stop=(k == NDK - 1 and j % 2 == 1))
                        pe.inc(ins2)
                S("tensor", _)
                mm_done = pe.n

                if r < 2:
                    def _(vector, r=r, roff=roff, mm_done=mm_done):
                        W(vector, "vector", pe, mm_done)
                        for j in range(8):
                            jj = r * 8 + j
                            src = PSA[:, roff + (j // 2) * 512 + (j % 2) * 256: roff + (j // 2) * 512 + (j % 2) * 256 + G]
                            dst = xssm[:, jj * CONVW + (DCV - 1): jj * CONVW + (DCV - 1) + G]
                            ins = vector.tensor_copy(dst, src)
                        dv.inc(ins)
                    S("vector", _)
                    ev_marks.append((dv, dv.n))
                else:
                    def _(scalar, r=r, roff=roff, mm_done=mm_done):
                        W(scalar, "scalar", pe, mm_done)
                        for j in range(8):
                            jj = (r - 2) * 8 + j
                            src = PSA[:, roff + (j // 2) * 512 + (j % 2) * 256: roff + (j // 2) * 512 + (j % 2) * 256 + G]
                            ins = scalar.activation(szfm[:, jj * G:(jj + 1) * G], src, AF.Silu)
                        ac.inc(ins)
                    S("scalar", _)
                    ev_marks.append((ac, ac.n))
            prev_marks["wi"] = wi_pe0 + 4 * NDK
            wi_evac_dv = dv.n
            wi_evac_ac = ac.n

            def _(vector, first=first, wi_evac_dv=wi_evac_dv):
                W(vector, "vector", dv, wi_evac_dv)
                dst = bass.AP(xssm.tensor, xssm.offset,
                              [list(xssm.ap[0]), [CONVW, NDC], [1, DCV - 1]])
                if first:
                    dv.inc(vector.memset(dst, 0.0))
                else:
                    src = bass.AP(ctail.tensor, ctail.offset,
                                  [list(ctail.ap[0]), [DCV - 1, NDC], [1, DCV - 1]])
                    dv.inc(vector.tensor_copy(dst, src))
            S("vector", _)
            lead_done = dv.n

            def _(vector, lead_done=lead_done, wi_evac_ac=wi_evac_ac):
                W(vector, "vector", dv, lead_done)
                for k in range(DCV):
                    if k > 0:
                        W(vector, "vector", dv)
                    for dc in range(NDC):
                        src = xssm[:, dc * CONVW + k: dc * CONVW + k + G]
                        dst = xcsfm[:, dc * G:(dc + 1) * G]
                        if k == 0:
                            ins = vector.tensor_scalar(dst, src, convw[:, dc * DCV: dc * DCV + 1],
                                                       convb[:, dc:dc + 1], ALU.mult, ALU.add)
                        else:
                            ins = vector.scalar_tensor_tensor(dst, src,
                                                              convw[:, dc * DCV + k: dc * DCV + k + 1],
                                                              dst, ALU.mult, ALU.add)
                    dv.inc(ins)
            S("vector", _)
            conv_done = dv.n

            def _(vector, conv_done=conv_done):
                W(vector, "vector", dv, conv_done)
                dst = bass.AP(ctail.tensor, ctail.offset,
                              [list(ctail.ap[0]), [DCV - 1, NDC], [1, DCV - 1]])
                src = bass.AP(xssm.tensor, xssm.offset + G,
                              [list(xssm.ap[0]), [CONVW, NDC], [1, DCV - 1]])
                dv.inc(vector.tensor_copy(dst, src))
            S("vector", _)

            def _(scalar, conv_done=conv_done):
                W(scalar, "scalar", dv, conv_done)
                ac.inc(scalar.activation(xcsfm, xcsfm, AF.Silu))
            S("scalar", _)
            xcs_done = ac.n

            def _(tensor, xcs_done=xcs_done):
                W(tensor, "tensor", ac, xcs_done)
                for k in range(NDC):
                    tensor.matmul(psb(0, G, parts=1), wxd[:, k:k + 1],
                                  xcsfm[:, k * G:(k + 1) * G],
                                  start=(k == 0), stop=(k == NDC - 1))
                for k in range(NDC):
                    tensor.matmul(psb(1, G, parts=DS), wxb[:, k * DS:(k + 1) * DS],
                                  xcsfm[:, k * G:(k + 1) * G],
                                  start=(k == 0), stop=(k == NDC - 1))
                for k in range(NDC):
                    ins = tensor.matmul(psb(2, G, parts=DS), wxc[:, k * DS:(k + 1) * DS],
                                        xcsfm[:, k * G:(k + 1) * G],
                                        start=(k == 0), stop=(k == NDC - 1))
                pe.inc(ins)
            S("tensor", _)
            wx_done = pe.n

            def _(vector, wx_done=wx_done):
                W(vector, "vector", pe, wx_done)
                vector.tensor_copy(bfm, psb(1, G, parts=DS))
                vector.tensor_copy(cfm, psb(2, G, parts=DS))
                dv.inc(vector.tensor_copy(drel, psb(0, G, parts=1)))
            S("vector", _)
            draw_done = dv.n

            def _(scalar, draw_done=draw_done):
                W(scalar, "scalar", dv, draw_done)
                ac.inc(scalar.activation(dabs, drel, AF.Abs))
                W(scalar, "scalar", ac)
                ac.inc(scalar.activation(dexp, dabs, AF.Exp, scale=kcst[0:1, 2:3]))
                W(scalar, "scalar", ac)
                ac.inc(scalar.activation(dln, dexp, AF.Ln, bias=kcst[0:1, 1:2]))
                ac.inc(scalar.activation(dabs, drel, AF.Relu))
            S("scalar", _)

            def _(vector):
                W(vector, "vector", ac)
                dv.inc(vector.tensor_tensor(drow, dabs, dln, ALU.add))
            S("vector", _)
            drow_done = dv.n

            for q in range(GC):
                ci = ci0 + q
                qs = q * Q

                def _(vector, qs=qs, drow_done=drow_done):
                    W(vector, "vector", dv, drow_done)
                    dv.inc(vector.tensor_tensor_scan(crow, onesr[0:1, 0:Q],
                                                     drow[0:1, qs:qs + Q], 0.0,
                                                     ALU.mult, ALU.add))
                    W(vector, "vector", dv)
                    vector.tensor_scalar(argrow[0:1, 0:Q], crow,
                                         crow[0:1, Q - 1:Q], -1.0, ALU.subtract, ALU.mult)
                    vector.tensor_copy(argrow[0:1, Q:2 * Q], crow)
                    vector.tensor_copy(argrow[0:1, 2 * Q:2 * Q + 1], crow[0:1, Q - 1:Q])
                    dv.inc(vector.tensor_scalar(negc, crow, -1.0, None, ALU.mult))
                S("vector", _)
                rows_done = dv.n

                def _(tensor, rows_done=rows_done):
                    W(tensor, "tensor", dv, rows_done)
                    tensor.matmul(psb(4, 2 * Q + 1, parts=DS), arow, argrow,
                                  start=True, stop=True)
                    tensor.matmul(psb(3, 128), negc, onesr[0:1, 0:Q], start=True, stop=False)
                    ins = tensor.matmul(psb(3, 128), onesr[0:1, 0:Q], crow, start=False, stop=True)
                    pe.inc(ins)
                S("tensor", _)
                args_done = pe.n

                def _(scalar, args_done=args_done):
                    W(scalar, "scalar", pe, args_done)
                    ac.inc(scalar.activation(ekall, psb(4, 2 * Q + 1, parts=DS), AF.Exp))
                S("scalar", _)
                ek_done = ac.n

                def _(vector, args_done=args_done):
                    W(vector, "vector", pe, args_done)
                    dv.inc(vector.tensor_tensor(dmt, psb(3, 128), maskS, ALU.add))
                S("vector", _)
                dmt_done = dv.n

                estk_done = ac.n

                def _(scalar, qs=qs, drow_done=drow_done):
                    W(scalar, "scalar", dv, drow_done)
                    W(scalar, "scalar", pe)
                    s_cf.inc(scalar.dma_start(out=cflat, in_=cfm[0:DS, qs:qs + Q]), 16)
                S("scalar", _)
                cflat_done = s_cf.n

                def _(tensor, cflat_done=cflat_done, ek_done=ek_done):
                    W(tensor, "tensor", s_cf, cflat_done)
                    W(tensor, "tensor", ac, ek_done)
                    W(tensor, "tensor", dv)
                    for t4 in range(4):
                        ins = tensor.matmul(PSA[:, t4 * 512: t4 * 512 + 512],
                                            onesr[0:1, 0:128],
                                            cflat[0:1, t4 * 512:(t4 + 1) * 512],
                                            start=True, stop=True)
                    pe.inc(ins)
                S("tensor", _)
                cb_done = pe.n

                def _(vector, qs=qs, ek_done=ek_done):
                    W(vector, "vector", ac, ek_done)
                    vector.tensor_tensor(kbfm, bfm[0:DS, qs:qs + Q], ekall[0:DS, 0:Q], ALU.mult)
                    dv.inc(vector.tensor_tensor(qbfm, cfm[0:DS, qs:qs + Q],
                                                ekall[0:DS, Q:2 * Q], ALU.mult))
                S("vector", _)
                kb_done = dv.n

                def _(tensor, qs=qs, kb_done=kb_done):
                    W(tensor, "tensor", dv, kb_done)
                    TR(tensor, psb(6, DS), bfm[0:DS, qs:qs + Q])
                    TR(tensor, psb(7, DS), kbfm)
                    pe.inc(TR(tensor, psb(5, 1), drow[0:1, qs:qs + Q]))
                S("tensor", _)
                tr1_done = pe.n

                def _(vector, tr1_done=tr1_done):
                    W(vector, "vector", pe, tr1_done)
                    vector.tensor_copy(bttm, psb(6, DS))
                    vector.tensor_copy(kbtm, psb(7, DS))
                    dv.inc(vector.tensor_copy(dcolq, psb(5, 1)))
                S("vector", _)
                tr1_ev = dv.n

                pe1, dv1 = pe.n, dv.n

                def _(tensor, qs=qs, pe1=pe1, dv1=dv1, tr1_ev=tr1_ev):
                    W(tensor, "tensor", dv, tr1_ev)
                    for dc in range(NDC):
                        if dc >= 2:
                            W(tensor, "tensor", dv, dv1 + dc - 1)
                        pe.inc(TR(tensor, psb(6 + dc % 2, 128),
                                  xcsfm[:, dc * G + qs: dc * G + qs + Q]))
                S("tensor", _)

                def _(vector, pe1=pe1):
                    for dc in range(NDC):
                        W(vector, "vector", pe, pe1 + dc + 1)
                        dv.inc(vector.tensor_scalar(xcdtm[:, dc * 128:(dc + 1) * 128],
                                                    psb(6 + dc % 2, 128), dcolq,
                                                    None, ALU.mult))
                S("vector", _)
                xcd_done = dv.n

                for half in range(2):
                    def _(scalar, half=half, dmt_done=dmt_done):
                        W(scalar, "scalar", dv, dmt_done)
                        if half == 1:
                            W(scalar, "scalar", dv)
                        for n8 in range(8):
                            n = half * 8 + n8
                            ins = scalar.activation(estk[:, n8 * Q:(n8 + 1) * Q], dmt,
                                                    AF.Exp, scale=abc[:, n:n + 1])
                        ac.inc(ins)
                    S("scalar", _)
                    eh_done = ac.n

                    def _(vector, half=half, eh_done=eh_done, cb_done=cb_done, tr1_ev=tr1_ev):
                        W(vector, "vector", ac, eh_done)
                        W(vector, "vector", pe, cb_done)
                        W(vector, "vector", dv, tr1_ev)
                        W(vector, "vector", dv)
                        for n8 in range(8):
                            n = half * 8 + n8
                            ins = vector.scalar_tensor_tensor(
                                mtmp[:, n8 * Q:(n8 + 1) * Q], estk[:, n8 * Q:(n8 + 1) * Q],
                                bttm[:, n:n + 1], PSA[:, n * Q:(n + 1) * Q],
                                ALU.mult, ALU.mult)
                        dv.inc(ins)
                        h = 4
                        while h >= 1:
                            W(vector, "vector", dv)
                            for i2 in range(h):
                                if h == 1 and half == 0:
                                    ins = vector.tensor_tensor(mtt, mtmp[:, 0:Q],
                                                               mtmp[:, Q:2 * Q], ALU.add)
                                elif h == 1:
                                    ins = vector.tensor_tensor(mtmp[:, 0:Q], mtmp[:, 0:Q],
                                                               mtmp[:, Q:2 * Q], ALU.add)
                                else:
                                    ins = vector.tensor_tensor(
                                        mtmp[:, i2 * Q:(i2 + 1) * Q],
                                        mtmp[:, i2 * Q:(i2 + 1) * Q],
                                        mtmp[:, (i2 + h) * Q:(i2 + h + 1) * Q], ALU.add)
                            dv.inc(ins)
                            h //= 2
                    S("vector", _)

                def _(vector):
                    W(vector, "vector", dv)
                    dv.inc(vector.tensor_tensor(mtt, mtt, mtmp[:, 0:Q], ALU.add))
                S("vector", _)
                mt_done = dv.n

                pe2, dv2 = pe.n, dv.n

                def _(tensor, pe2=pe2, dv2=dv2, xcd_done=xcd_done, mt_done=mt_done):
                    W(tensor, "tensor", dv, max(xcd_done, mt_done))
                    for t4 in range(4):
                        ins = tensor.matmul(PSA[0:DS, t4 * 512:(t4 + 1) * 512],
                                            kbtm, xcdtm[:, t4 * 512:(t4 + 1) * 512],
                                            start=True, stop=True)
                    pe.inc(ins)
                    for dc in range(NDC):
                        if dc >= 2:
                            W(tensor, "tensor", dv, dv2 + dc - 1)
                        bank = 4 + dc % 2
                        tensor.matmul(psb(bank, Q), xcdtm[:, dc * 128:(dc + 1) * 128],
                                      mtt, start=True, stop=False)
                        pe.inc(tensor.matmul(psb(bank, Q), hst[0:DS, dc * 128:(dc + 1) * 128],
                                             qbfm, start=False, stop=True))
                S("tensor", _)
                ymm_all = pe.n

                def _(vector, qs=qs, pe2=pe2):
                    for dc in range(NDC):
                        W(vector, "vector", pe, pe2 + dc + 2)
                        bank = 4 + dc % 2
                        ins = vector.scalar_tensor_tensor(
                            xcdtm[:, dc * 128:(dc + 1) * 128],
                            xcsfm[:, dc * G + qs: dc * G + qs + Q],
                            dcol[:, dc:dc + 1], psb(bank, Q), ALU.mult, ALU.add)
                        dv.inc(ins)
                S("vector", _)
                st1 = dv.n

                def _(vector, ymm_all=ymm_all):
                    W(vector, "vector", pe, ymm_all)
                    dv.inc(vector.scalar_tensor_tensor(
                        hst[0:DS, 0:DI], hst[0:DS, 0:DI],
                        ekall[0:DS, 2 * Q:2 * Q + 1], PSA[0:DS, 0:DI],
                        ALU.mult, ALU.add))
                S("vector", _)

                y3bf = y3fm.bitcast(BF16)
                def _(vector, qs=qs, st1=st1):
                    W(vector, "vector", dv, st1)
                    for dc in range(NDC):
                        ins = vector.tensor_tensor(
                            xcdtm[:, dc * 128:(dc + 1) * 128],
                            xcdtm[:, dc * 128:(dc + 1) * 128],
                            szfm[:, dc * G + qs: dc * G + qs + Q], ALU.mult)
                    dv.inc(ins)
                S("vector", _)
                st2 = dv.n

                def _(scalar, qs=qs, st2=st2, y3bf=y3bf):
                    W(scalar, "scalar", dv, st2)
                    for dc in range(NDC):
                        ins = scalar.activation(y3bf[:, dc * G + qs: dc * G + qs + Q],
                                                xcdtm[:, dc * 128:(dc + 1) * 128],
                                                AF.Copy)
                    ac.inc(ins)
                S("scalar", _)
                st3 = ac.n

                def _(vector, qs=qs, st3=st3, y3bf=y3bf):
                    W(vector, "vector", ac, st3)
                    for dc in range(NDC):
                        ins = vector.tensor_tensor(
                            y3bf[:, NDC * G + dc * G + qs: NDC * G + dc * G + qs + Q],
                            xcdtm[:, dc * 128:(dc + 1) * 128],
                            y3bf[:, dc * G + qs: dc * G + qs + Q], ALU.subtract)
                    dv.inc(ins)
                S("vector", _)
            y3_done = dv.n

            pe_wb = pe.n
            wo_base = [s_wo[0].n, s_wo[1].n]

            def _(sync, pe_wb=pe_wb):
                for k in range(NDC):
                    if k >= 2:
                        W(sync, "sync", pe, pe_wb + k - 1)
                    else:
                        W(sync, "sync", pe, prev_marks["wo"])
                    W(sync, "sync", s_sp, 9)
                    bf = woblk[k % 2].bitcast(BF16)
                    s_wo[k % 2].inc(sync.dma_start(
                        out=bf[:, 0:1024], in_=woh_d[k * 128:(k + 1) * 128, :]), 16)
                    s_wo[k % 2].inc(sync.dma_start(
                        out=bf[:, 1024:2048], in_=wol_d[k * 128:(k + 1) * 128, :]), 16)
            S("sync", _)

            def _(tensor, wo_base=wo_base, y3_done=y3_done):
                W(tensor, "tensor", dv, y3_done)
                y3bf = y3fm.bitcast(BF16)
                for k in range(NDC):
                    W(tensor, "tensor", s_wo[k % 2], wo_base[k % 2] + (k // 2 + 1) * 32)
                    bf = woblk[k % 2].bitcast(BF16)
                    yh = y3bf[:, k * G:(k + 1) * G]
                    yl = y3bf[:, NDC * G + k * G: NDC * G + (k + 1) * G]
                    for j in range(8):
                        dst = PSA[:, (j // 2) * 512 + (j % 2) * G: (j // 2) * 512 + (j % 2) * G + G]
                        wh = bf[:, j * 128:(j + 1) * 128]
                        wl = bf[:, 1024 + j * 128: 1024 + (j + 1) * 128]
                        tensor.matmul(dst, wh, yh,
                                      start=(k == 0 and j % 2 == 0), stop=False)
                        tensor.matmul(dst, wh, yl, start=False, stop=False)
                        ins2 = tensor.matmul(dst, wl, yh, start=False,
                                             stop=(k == NDC - 1 and j % 2 == 1))
                    pe.inc(ins2)
            S("tensor", _)
            womm_done = pe.n
            prev_marks["wo"] = womm_done

            def _(vector, womm_done=womm_done):
                W(vector, "vector", pe, womm_done)
                for j in range(8):
                    src = PSA[:, (j // 2) * 512 + (j % 2) * G: (j // 2) * 512 + (j % 2) * G + G]
                    ins = vector.tensor_copy(xnfm[:, j * G:(j + 1) * G], src)
                dv.inc(ins)
            S("vector", _)
            attn_done = dv.n

            items2 = [(xnfm[:, j * G + i * 128: j * G + (i + 1) * 128], i, j)
                      for j in range(8) for i in range(GC)]
            pe3, dv3 = pe.n, dv.n

            def _(tensor, items2=items2, attn_done=attn_done, dv3=dv3):
                W(tensor, "tensor", dv, attn_done)
                for jj, (src, i, j) in enumerate(items2):
                    if jj >= 2:
                        W(tensor, "tensor", dv, dv3 + jj - 1)
                    pe.inc(TR(tensor, psb(6 + jj % 2, 128), src))
            S("tensor", _)

            def _(vector, items2=items2, pe3=pe3):
                W(vector, "vector", s_st1, prev_marks["x1st"])
                for jj, (_src, i, j) in enumerate(items2):
                    W(vector, "vector", pe, pe3 + jj + 1)
                    dv.inc(vector.tensor_tensor(x1t[i][:, j * 128:(j + 1) * 128],
                                                psb(6 + jj % 2, 128),
                                                x0t[i][:, j * 128:(j + 1) * 128], ALU.add))
            S("vector", _)
            x1_done = dv.n
            x1_done_mark[0] = x1_done

            def _(sync, t0=t0, x1_done=x1_done):
                W(sync, "sync", dv, x1_done)
                for i in range(GC):
                    s_st1.inc(sync.dma_start(out=sx1_d[t0 + i * 128: t0 + (i + 1) * 128, :],
                                             in_=x1t[i]), 16)
            S("sync", _)

            for i in range(GC):
                def _(scalar, i=i, x1_done=x1_done):
                    W(scalar, "scalar", dv, x1_done)
                    W(scalar, "scalar", dv)
                    ac.inc(scalar.activation(sqt, x1t[i], AF.Square))
                S("scalar", _)

                def _(vector, i=i, x1_done=x1_done):
                    W(vector, "vector", dv, x1_done)
                    W(vector, "vector", ac)
                    vector.reduce_sum(lns[:, 0:1], x1t[i], axis=AX)
                    dv.inc(vector.reduce_sum(lns[:, 1:2], sqt, axis=AX))
                    W(vector, "vector", dv)
                    vector.tensor_scalar(lnm[i], lns[:, 0:1], 1.0 / DM, None, ALU.mult)
                    dv.inc(vector.tensor_scalar(lns[:, 3:4], lns[:, 1:2], 1.0 / DM, None, ALU.mult))
                    W(vector, "vector", dv)
                    dv.inc(vector.tensor_tensor(lns[:, 2:3], lnm[i], lnm[i], ALU.mult))
                    W(vector, "vector", dv)
                    dv.inc(vector.tensor_tensor(lns[:, 4:5], lns[:, 3:4], lns[:, 2:3], ALU.subtract))
                S("vector", _)

                def _(scalar, i=i):
                    W(scalar, "scalar", dv)
                    ac.inc(scalar.activation(lns[:, 5:6], lns[:, 4:5], AF.Sqrt, bias=kcst[:, 0:1]))
                S("scalar", _)

                def _(vector, i=i):
                    W(vector, "vector", ac)
                    W(vector, "vector", s_st1, prev_marks["xlnst"])
                    dv.inc(vector.reciprocal(lni[i], lns[:, 5:6]))
                    W(vector, "vector", dv)
                    dv.inc(vector.tensor_scalar(xlnt[i], x1t[i], lnm[i], lni[i],
                                                ALU.subtract, ALU.mult))
                S("vector", _)
            xln_done = dv.n

            def _(sync, t0=t0, xln_done=xln_done):
                W(sync, "sync", dv, xln_done)
                for i in range(GC):
                    s_st1.inc(sync.dma_start(out=sxln_d[t0 + i * 128: t0 + (i + 1) * 128, :],
                                             in_=xlnt[i]), 16)
            S("sync", _)
            prev_marks["x1st"] = s_st1.n
            prev_marks["xlnst"] = s_st1.n

            items3 = [(xlnt[i][:, k * 128:(k + 1) * 128], xlnfm[:, k * G + i * 128: k * G + (i + 1) * 128])
                      for k in range(NDK) for i in range(GC)]
            pe4, dv4 = pe.n, dv.n

            def _(tensor, items3=items3, xln_done=xln_done, dv4=dv4):
                W(tensor, "tensor", dv, xln_done)
                for j, (src, _dst) in enumerate(items3):
                    if j >= 2:
                        W(tensor, "tensor", dv, dv4 + j - 1)
                    pe.inc(TR(tensor, psb(6 + j % 2, 128), src))
            S("tensor", _)

            def _(vector, items3=items3, pe4=pe4):
                for j, (_src, dst) in enumerate(items3):
                    W(vector, "vector", pe, pe4 + j + 1)
                    dv.inc(vector.tensor_copy(dst, psb(6 + j % 2, 128)))
            S("vector", _)
            xlnfm_done = dv.n

            for i in range(GC):
                ci = ci0 + i
                bank = 4 + i % 2

                def _(tensor, i=i, bank=bank, xlnfm_done=xlnfm_done):
                    W(tensor, "tensor", dv, xlnfm_done)
                    for k in range(NDK):
                        ins = tensor.matmul(psb(bank, E),
                                            xlnfm[:, k * G + i * 128: k * G + (i + 1) * 128],
                                            rwt[:, k * E:(k + 1) * E],
                                            start=(k == 0), stop=(k == NDK - 1))
                    pe.inc(ins)
                S("tensor", _)
                rt_done = pe.n

                def _(vector, ci=ci, bank=bank, rt_done=rt_done):
                    W(vector, "vector", pe, rt_done)
                    dv.inc(vector.reduce_max(lgm, psb(bank, E), axis=AX))
                    W(vector, "vector", dv)
                    vector.tensor_scalar(lgnm, lgm, -1.0, None, ALU.mult)
                    dv.inc(vector.tensor_tensor(lgu, psb(bank, E),
                                                lgm.to_broadcast([128, E]), ALU.is_ge))
                    W(vector, "vector", dv)
                    dv.inc(vector.tensor_copy(lgf, lgu))
                    W(vector, "vector", dv)
                    dv.inc(vector.tensor_tensor(lgw, lgf, rev8, ALU.mult))
                    W(vector, "vector", dv)
                    dv.inc(vector.reduce_max(lgm2, lgw, axis=AX))
                    W(vector, "vector", dv)
                    dv.inc(vector.tensor_scalar(ecol[:, ci:ci + 1], lgm2, -1.0, float(E),
                                                ALU.mult, ALU.add))
                S("vector", _)
                sm1 = dv.n

                def _(scalar, bank=bank, sm1=sm1):
                    W(scalar, "scalar", dv, sm1)
                    ac.inc(scalar.activation(lge, psb(bank, E), AF.Exp, bias=lgnm))
                S("scalar", _)
                sm2 = ac.n

                def _(vector, ci=ci, sm2=sm2):
                    W(vector, "vector", ac, sm2)
                    dv.inc(vector.reduce_sum(lgs, lge, axis=AX))
                    W(vector, "vector", dv)
                    dv.inc(vector.reciprocal(pcol[:, ci:ci + 1], lgs))
                S("vector", _)

            prev_marks["wi_ev"] = [(dv, dv.n), (ac, ac.n)]

            if (g % GPS) == GPS - 1 and g + 1 < NG:
                def _(vector):
                    W(vector, "vector", dv)
                    W(vector, "vector", pe)
                    vector.memset(hst[:], 0.0)
                    dv.inc(vector.memset(ctail[:], 0.0))
                S("vector", _)

        a_pe, a_dv, a_ac = pe.n, dv.n, ac.n
        a_st1 = s_st1.n

        eti_done = [0]
        def _(vector):
            W(vector, "vector", dv, a_dv)
            dv.inc(vector.memset(carry[:], 0.0))
        S("vector", _)

        for p2 in range(2):
            if p2 == 1:
                def _(tensor):
                    W(tensor, "tensor", dv)
                    pe.inc(TR(tensor, psb(6, E, parts=1), cnts[0:E, 0:1]))
                S("tensor", _)

                def _(vector):
                    W(vector, "vector", pe)
                    dv.inc(vector.tensor_copy(cntr, psb(6, E, parts=1)))
                S("vector", _)

                def _(tensor):
                    W(tensor, "tensor", dv)
                    pe.inc(tensor.matmul(psb(6, E, parts=NCORES), r1hot, cntr,
                                         start=True, stop=True))
                S("tensor", _)

                def _(vector):
                    W(vector, "vector", pe)
                    dv.inc(vector.tensor_copy(ccsb, psb(6, E, parts=NCORES)))
                S("vector", _)

                def _(gpsimd):
                    W(gpsimd, "gpsimd", dv)
                    s_gm.inc(gpsimd.dma_start(out=ccin_d.ap(), in_=ccsb), 16)
                    W(gpsimd, "gpsimd", s_gm)
                    cc.inc(gpsimd.collective_compute(
                        "AllReduce", ALU.add, replica_groups=[list(range(NCORES))],
                        ins=[ccin_d.ap().opt()], outs=[ccout_d.ap().opt()]))
                    W(gpsimd, "gpsimd", cc)
                    s_gm.inc(gpsimd.dma_start(out=ccall, in_=ccout_d.ap()), 16)
                S("gpsimd", _)

                def _(tensor):
                    W(tensor, "tensor", s_gm)
                    pe.inc(tensor.matmul(psb(6, E, parts=1), rpfx, ccall,
                                         start=True, stop=True))
                S("tensor", _)

                def _(vector):
                    W(vector, "vector", pe)
                    dv.inc(vector.tensor_copy(baser, psb(6, E, parts=1)))
                S("vector", _)

                def _(tensor):
                    W(tensor, "tensor", dv)
                    pe.inc(TR(tensor, psb(6, 1, parts=E), baser))
                S("tensor", _)

                def _(vector):
                    W(vector, "vector", pe)
                    dv.inc(vector.tensor_copy(basec, psb(6, 1, parts=E)))
                    W(vector, "vector", dv)
                    dv.inc(vector.tensor_scalar(pcq[:, 0:1], cnts, 1.0 / TT,
                                                -0.5 + 0.5 / TT, ALU.mult, ALU.add))
                    W(vector, "vector", dv)
                    dv.inc(vector.tensor_scalar(pcq[:, 0:1], cnts, TT - 1.0, 1.0 / TT,
                                                ALU.add, ALU.mult))
                    W(vector, "vector", dv)
                    dv.inc(vector.tensor_scalar(pcq[:, 1:2], pcq[:, 0:1],
                                                -0.5 + 1.0 / (2 * TT), None, ALU.add))
                    W(vector, "vector", dv)
                    dv.inc(vector.tensor_scalar(pcq[:, 2:3], pcq[:, 1:2], 12582912.0,
                                                None, ALU.add))
                    W(vector, "vector", dv)
                    dv.inc(vector.tensor_scalar(pcq[:, 3:4], pcq[:, 2:3], 12582912.0,
                                                None, ALU.subtract))
                    W(vector, "vector", dv)
                    dv.inc(vector.tensor_scalar(pcq[:, 0:1], pcq[:, 3:4], float(TT),
                                                None, ALU.mult))
                S("vector", _)
                pc_done = dv.n

                def _(tensor, pc_done=pc_done):
                    W(tensor, "tensor", dv, pc_done)
                    pe.inc(tensor.matmul(psb(6, 1, parts=E), tri8, pcq[:, 0:1],
                                         start=True, stop=True))
                S("tensor", _)

                def _(vector):
                    W(vector, "vector", pe)
                    dv.inc(vector.tensor_copy(poffm1, psb(6, 1, parts=E)))
                    W(vector, "vector", dv)
                    dv.inc(vector.tensor_tensor(cuminc, poffm1, pcq[:, 0:1], ALU.add))
                    W(vector, "vector", dv)
                    dv.inc(vector.tensor_scalar(poffm1, poffm1, -1.0, None, ALU.add))
                    dv.inc(vector.memset(carry[:], 0.0))
                S("vector", _)

                def _(vector):
                    W(vector, "vector", dv)
                    dv.inc(vector.tensor_tensor(etu, cuminc.to_broadcast([E, NT]),
                                                ktile, ALU.is_le))
                    W(vector, "vector", dv)
                    dv.inc(vector.tensor_copy(etf, etu))
                S("vector", _)

                def _(tensor):
                    W(tensor, "tensor", dv)
                    pe.inc(tensor.matmul(psb(6, NT, parts=1), onesc8, etf,
                                         start=True, stop=True))
                S("tensor", _)

                def _(vector):
                    W(vector, "vector", pe)
                    dv.inc(vector.tensor_copy(etr, psb(6, NT, parts=1)))
                    W(vector, "vector", dv)
                    dv.inc(vector.tensor_scalar(etfr, etr, float(E - 1), None, ALU.min))
                    W(vector, "vector", dv)
                    dv.inc(vector.tensor_copy(eti, etfr))
                S("vector", _)
                eti_done[0] = dv.n

            for tc in range(NTC):
                c0 = tc * TCH
                nqc = TCH // Q
                def _(tensor, tc=tc):
                    W(tensor, "tensor", dv)
                    pe.inc(TR(tensor, psb(7, 128, parts=nqc),
                              ecol[:, tc * nqc:(tc + 1) * nqc]))
                S("tensor", _)

                def _(vector):
                    W(vector, "vector", pe)
                    dv.inc(vector.tensor_copy(ef16[0:nqc, 0:128], psb(7, 128, parts=nqc)))
                S("vector", _)

                def _(gpsimd):
                    W(gpsimd, "gpsimd", dv)
                    s_gm.inc(gpsimd.dma_start(out=erow_c, in_=ef16[0:nqc, 0:128]), 16)
                S("gpsimd", _)
                ef_done = s_gm.n

                def _(tensor, ef_done=ef_done):
                    W(tensor, "tensor", s_gm, ef_done)
                    W(tensor, "tensor", dv)
                    for t4 in range(TCH // 512):
                        ins = tensor.matmul(PSA[0:E, t4 * 512:(t4 + 1) * 512],
                                            onesr[0:1, 0:E],
                                            erow_c[0:1, t4 * 512:(t4 + 1) * 512],
                                            start=True, stop=True)
                    pe.inc(ins)
                S("tensor", _)
                indmm = pe.n

                def _(vector, indmm=indmm):
                    W(vector, "vector", pe, indmm)
                    dv.inc(vector.tensor_tensor(tmq_c, PSA[0:E, 0:TCH],
                                                iota8c.to_broadcast([E, TCH]), ALU.is_equal))
                    W(vector, "vector", dv)
                    dv.inc(vector.tensor_copy(ind_c, tmq_c))
                    W(vector, "vector", dv)
                    dv.inc(vector.tensor_tensor_scan(po_c, ones8t[0:E, 0:TCH], ind_c,
                                                     carry[0:E, 0:1], ALU.mult, ALU.add))
                    W(vector, "vector", dv)
                    dv.inc(vector.tensor_copy(carry, po_c[0:E, TCH - 1:TCH]))
                S("vector", _)
                po_done = dv.n

                if p2 == 1:
                    def _(vector, po_done=po_done):
                        W(vector, "vector", dv, po_done)
                        dv.inc(vector.tensor_scalar(tmp_c, po_c, poffm1, None, ALU.add))
                        W(vector, "vector", dv)
                        dv.inc(vector.tensor_tensor(tmp_c, tmp_c, ind_c, ALU.mult))
                    S("vector", _)
                    srmm0 = dv.n

                    def _(tensor, srmm0=srmm0):
                        W(tensor, "tensor", dv, srmm0)
                        for t4 in range(TCH // 512):
                            ins = tensor.matmul(PSA[0:1, 2048 + t4 * 512: 2048 + (t4 + 1) * 512],
                                                onesc8, tmp_c[0:E, t4 * 512:(t4 + 1) * 512],
                                                start=True, stop=True)
                        pe.inc(ins)
                    S("tensor", _)

                    def _(vector):
                        W(vector, "vector", pe)
                        dv.inc(vector.tensor_copy(srow_c, PSA[0:1, 2048:2048 + TCH]))
                    S("vector", _)

                    def _(vector):
                        W(vector, "vector", dv)
                        dv.inc(vector.tensor_scalar(tmp_c, po_c, basec, None, ALU.add))
                        W(vector, "vector", dv)
                        dv.inc(vector.tensor_scalar(tmq_c, tmp_c, CCAP + 0.5, None, ALU.is_le))
                        W(vector, "vector", dv)
                        dv.inc(vector.tensor_copy(tmp_c, tmq_c))
                        W(vector, "vector", dv)
                        dv.inc(vector.tensor_tensor(tmp_c, tmp_c, ind_c, ALU.mult))
                    S("vector", _)
                    krmm0 = dv.n

                    def _(tensor, krmm0=krmm0):
                        W(tensor, "tensor", dv, krmm0)
                        for t4 in range(TCH // 512):
                            ins = tensor.matmul(PSA[0:1, 2048 + t4 * 512: 2048 + (t4 + 1) * 512],
                                                onesc8, tmp_c[0:E, t4 * 512:(t4 + 1) * 512],
                                                start=True, stop=True)
                        pe.inc(ins)
                    S("tensor", _)

                    def _(vector):
                        W(vector, "vector", pe)
                        dv.inc(vector.tensor_copy(krow_c, PSA[0:1, 2048:2048 + TCH]))
                    S("vector", _)
                    rows_ok = dv.n

                    def _(gpsimd, tc=tc, rows_ok=rows_ok):
                        W(gpsimd, "gpsimd", dv, rows_ok)
                        r0 = tc * (TCH // 128)
                        s_gm.inc(gpsimd.dma_start(out=srows[r0:r0 + TCH // 128, 0:128],
                                                   in_=srow_c), 16)
                        s_gm.inc(gpsimd.dma_start(out=krows[r0:r0 + TCH // 128, 0:128],
                                                   in_=krow_c), 16)
                    S("gpsimd", _)

            if p2 == 0:
                def _(vector):
                    W(vector, "vector", dv)
                    dv.inc(vector.tensor_copy(cnts, carry))
                S("vector", _)

        rows_all = s_gm.n

        def _(tensor, rows_all=rows_all):
            W(tensor, "tensor", s_gm, rows_all)
            pe.inc(TR(tensor, psb(6, NCH), srows[0:NROWB, 0:128]))
            pe.inc(TR(tensor, psb(7, NCH), krows[0:NROWB, 0:128]))
        S("tensor", _)

        def _(vector):
            W(vector, "vector", pe)
            vector.tensor_copy(scols, psb(6, NCH))
            dv.inc(vector.tensor_copy(kcols, psb(7, NCH)))
            W(vector, "vector", dv)
            vector.tensor_copy(scoli, scols)
            dv.inc(vector.tensor_tensor(scalec, kcols, pcol, ALU.mult))
        S("vector", _)
        cols_done = dv.n

        sg_scb = s_sc.n
        xl_base = [s_xl[0].n, s_xl[1].n]

        def _(sync, cols_done=cols_done, sg_scb=sg_scb):
            W(sync, "sync", dv, cols_done)
            W(sync, "sync", s_st1, a_st1)
            for c in range(NCH):
                if c >= 2:
                    W(sync, "sync", s_sc, sg_scb + ((c - 2) // 2 + 1) * 2 * 16)
                ins = sync.dma_start(out=xlt[c % 2], in_=sxln_d[c * 128:(c + 1) * 128, :])
                s_xl[c % 2].inc(ins, 16)
        S("sync", _)

        def _(gpsimd, cols_done=cols_done, xl_base=xl_base):
            W(gpsimd, "gpsimd", dv, cols_done)
            W(gpsimd, "gpsimd", s_zx, sxs_zeroed)
            for c in range(NCH):
                W(gpsimd, "gpsimd", s_xl[c % 2], xl_base[c % 2] + (c // 2 + 1) * 16)
                ins = gpsimd.indirect_dma_start(
                    out=sxs_d[0:NSLOT, :],
                    out_offset=bass.IndirectOffsetOnAxis(scoli[:, c:c + 1], 0),
                    in_=xlt[c % 2], in_offset=None)
                s_sc.inc(ins, 16)
        S("gpsimd", _)
        scat_done = s_sc.n

        for kt in range(NT):
            def _(sync, kt=kt, scat_done=scat_done):
                W(sync, "sync", s_sc, scat_done)
                W(sync, "sync", ac)
                W(sync, "sync", dv)
                for s2 in range(NSUB):
                    s_bw.inc(sync.dma_start(out=xlt[s2 % 2],
                                            in_=sxs_d[kt * TT + s2 * 128: kt * TT + (s2 + 1) * 128, :]), 16)
                s_bw.inc(dyndma(sync, b1sb, b1_b, eti[0:1, kt:kt + 1],
                                128 * NFK * 4, 0, (E - 1) * 128 * NFK * 4,
                                [[NFK * 4, 128], [1, NFK * 4]]), 16)
                s_bw.inc(dyndma(sync, b2sb, b2_b, eti[0:1, kt:kt + 1],
                                128 * NDK * 4, 0, (E - 1) * 128 * NDK * 4,
                                [[NDK * 4, 128], [1, NDK * 4]]), 16)
            S("sync", _)
            xs_ld = s_bw.n

            xsbf = xsfm.bitcast(BF16)
            items4 = [(xlt[s2 % 2][:, k * 128:(k + 1) * 128],
                       k * TT + s2 * 128)
                      for s2 in range(NSUB) for k in range(NDK)]
            pe5, dv5, ac5 = pe.n, dv.n, ac.n

            def _(tensor, items4=items4, xs_ld=xs_ld, dv5=dv5):
                W(tensor, "tensor", s_bw, xs_ld)
                W(tensor, "tensor", dv)
                for j, (src, _off) in enumerate(items4):
                    if j >= 2:
                        W(tensor, "tensor", dv, dv5 + j - 1)
                    pe.inc(TR(tensor, psb(6 + j % 2, 128), src))
            S("tensor", _)

            def _(scalar, items4=items4, pe5=pe5, xsbf=xsbf):
                for j, (_src, off) in enumerate(items4):
                    W(scalar, "scalar", pe, pe5 + j + 1)
                    ac.inc(scalar.activation(xsbf[:, off:off + 128],
                                             psb(6 + j % 2, 128), AF.Copy))
            S("scalar", _)

            def _(vector, items4=items4, ac5=ac5, xsbf=xsbf):
                for j, (_src, off) in enumerate(items4):
                    W(vector, "vector", ac, ac5 + j + 1)
                    dv.inc(vector.tensor_tensor(
                        xsbf[:, NDK * TT + off: NDK * TT + off + 128],
                        psb(6 + j % 2, 128), xsbf[:, off:off + 128], ALU.subtract))
            S("vector", _)
            xsfm_done = dv.n

            hbf = hhl.bitcast(BF16)
            pe_fc0 = pe.n
            for fc in range(8):
                def _(sync, kt=kt, fc=fc, pe_fc0=pe_fc0):
                    if fc >= 2:
                        W(sync, "sync", pe, pe_fc0 + 2 * (fc - 2) + 1)
                    W(sync, "sync", s_sp, 10)
                    bfv = w1blk[fc % 2].bitcast(BF16)
                    RB = FF * 2
                    for half, src in ((0, w1h_b), (1, w1l_b)):
                        dst = bass.AP(bfv.tensor, bfv.offset + half * (NDK * 512),
                                      [list(bfv.ap[0]), [512, NDK], [1, 512]])
                        ins = dyndma(sync, dst, src, eti[0:1, kt:kt + 1],
                                     DM * RB, fc * 512 * 2, (E - 1) * DM * RB,
                                     [[RB, 128], [128 * RB, NDK], [1, 512 * 2]])
                        s_w1[fc % 2].inc(ins, 16)
                S("sync", _)
                w1_ld = s_w1[fc % 2].n

                def _(scalar, kt=kt, fc=fc, pe_fc0=pe_fc0):
                    W(scalar, "scalar", dv, eti_done[0])
                    W(scalar, "scalar", s_sp, 11)
                    if fc >= 2:
                        W(scalar, "scalar", pe, pe_fc0 + 2 * (fc - 2) + 2)
                    bfv = w2blk[fc % 2].bitcast(BF16)
                    RB2 = DM * 2
                    for half, src in ((0, w2h_b), (1, w2l_b)):
                        dst = bass.AP(bfv.tensor, bfv.offset + half * (4 * DM),
                                      [list(bfv.ap[0]), [DM, 4], [1, DM]])
                        ins = dyndma(scalar, dst, src, eti[0:1, kt:kt + 1],
                                     FF * RB2, fc * 512 * RB2, (E - 1) * FF * RB2,
                                     [[RB2, 128], [128 * RB2, 4], [1, RB2]])
                        s_w2[fc % 2].inc(ins, 16)
                S("scalar", _)
                w2_ld = s_w2[fc % 2].n

                def _(tensor, kt=kt, fc=fc, w1_ld=w1_ld, xsfm_done=xsfm_done):
                    W(tensor, "tensor", s_w1[fc % 2], w1_ld)
                    W(tensor, "tensor", dv, xsfm_done)
                    if fc > 0:
                        W(tensor, "tensor", ac)
                    bfv = w1blk[fc % 2].bitcast(BF16)
                    xsbf = xsfm.bitcast(BF16)
                    for s2 in range(4):
                        bank = s2 // 2
                        off = bank * 512 + (s2 % 2) * TT
                        for kc in range(NDK):
                            dst = PSA[:, off:off + TT]
                            wh = bfv[:, kc * 512 + s2 * 128: kc * 512 + (s2 + 1) * 128]
                            wl = bfv[:, NDK * 512 + kc * 512 + s2 * 128:
                                     NDK * 512 + kc * 512 + (s2 + 1) * 128]
                            xh = xsbf[:, kc * TT:(kc + 1) * TT]
                            xl = xsbf[:, NDK * TT + kc * TT: NDK * TT + (kc + 1) * TT]
                            tensor.matmul(dst, wh, xh,
                                          start=(kc == 0 and s2 % 2 == 0), stop=False)
                            tensor.matmul(dst, wh, xl, start=False, stop=False)
                            ins = tensor.matmul(
                                dst, wl, xh, start=False,
                                stop=(kc == NDK - 1 and s2 % 2 == 1))
                    pe.inc(ins)
                S("tensor", _)
                hmm_done = pe.n

                gelu_ac0 = ac.n

                def _(scalar, fc=fc, hmm_done=hmm_done, xs_ld=xs_ld, hbf=hbf):
                    W(scalar, "scalar", pe, hmm_done)
                    W(scalar, "scalar", s_bw, xs_ld)
                    for s2 in range(4):
                        bank = s2 // 2
                        off = bank * 512 + (s2 % 2) * TT
                        jglob = fc * 4 + s2
                        scalar.activation(hsb[:, s2 * TT:(s2 + 1) * TT],
                                          PSA[:, off:off + TT], AF.Gelu,
                                          bias=b1sb[:, jglob:jglob + 1])
                        ac.inc(scalar.activation(hbf[:, s2 * TT:(s2 + 1) * TT],
                                                 PSA[:, off:off + TT], AF.Gelu,
                                                 bias=b1sb[:, jglob:jglob + 1]))
                S("scalar", _)
                gelu_done = ac.n

                dvh0 = dv.n

                def _(vector, fc=fc, gelu_ac0=gelu_ac0, hbf=hbf):
                    for s2 in range(4):
                        W(vector, "vector", ac, gelu_ac0 + s2 + 1)
                        dv.inc(vector.tensor_tensor(
                            hbf[:, 4 * TT + s2 * TT: 4 * TT + (s2 + 1) * TT],
                            hsb[:, s2 * TT:(s2 + 1) * TT],
                            hbf[:, s2 * TT:(s2 + 1) * TT], ALU.subtract))
                S("vector", _)

                def _(tensor, kt=kt, fc=fc, gelu_ac0=gelu_ac0, dvh0=dvh0, w2_ld=w2_ld,
                      hbf=hbf):
                    W(tensor, "tensor", s_w2[fc % 2], w2_ld)
                    bfv = w2blk[fc % 2].bitcast(BF16)
                    for s2 in range(4):
                        W(tensor, "tensor", ac, gelu_ac0 + s2 + 1)
                        W(tensor, "tensor", dv, dvh0 + s2 + 1)
                        hh = hbf[:, s2 * TT:(s2 + 1) * TT]
                        hl = hbf[:, 4 * TT + s2 * TT: 4 * TT + (s2 + 1) * TT]
                        for j in range(8):
                            bank = 2 + j // 2
                            off = bank * 512 + (j % 2) * TT
                            dst = PSA[:, off:off + TT]
                            wh = bfv[:, s2 * DM + j * 128: s2 * DM + (j + 1) * 128]
                            wl = bfv[:, 4 * DM + s2 * DM + j * 128:
                                     4 * DM + s2 * DM + (j + 1) * 128]
                            tensor.matmul(dst, wh, hh,
                                          start=(fc == 0 and s2 == 0 and j % 2 == 0),
                                          stop=False)
                            tensor.matmul(dst, wh, hl, start=False, stop=False)
                            ins = tensor.matmul(
                                dst, wl, hh, start=False,
                                stop=(fc == 7 and s2 == 3 and j % 2 == 1))
                    pe.inc(ins)
                S("tensor", _)
            eomm_done = pe.n

            def _(vector, kt=kt, eomm_done=eomm_done, xs_ld=xs_ld):
                W(vector, "vector", pe, eomm_done)
                W(vector, "vector", s_bw, xs_ld)
                for j in range(8):
                    bank = 2 + j // 2
                    off = bank * 512 + (j % 2) * TT
                    ins = vector.tensor_scalar(eofm[:, j * TT:(j + 1) * TT],
                                               PSA[:, off:off + TT],
                                               b2sb[:, j:j + 1], None, ALU.add)
                dv.inc(ins)
            S("vector", _)
            eo_done = dv.n

            items5 = [(eofm[:, j * TT + s2 * 128: j * TT + (s2 + 1) * 128], s2, j)
                      for s2 in range(NSUB) for j in range(NDK)]
            pe6, dv6 = pe.n, dv.n

            def _(tensor, items5=items5, eo_done=eo_done, dv6=dv6):
                W(tensor, "tensor", dv, eo_done)
                for jj, (src, s2, j) in enumerate(items5):
                    if jj >= 2:
                        W(tensor, "tensor", dv, dv6 + jj - 1)
                    pe.inc(TR(tensor, psb(6 + jj % 2, 128), src))
            S("tensor", _)

            def _(vector, items5=items5, pe6=pe6):
                W(vector, "vector", s_mo)
                for jj, (_src, s2, j) in enumerate(items5):
                    W(vector, "vector", pe, pe6 + jj + 1)
                    dv.inc(vector.tensor_copy(eotm[s2][:, j * 128:(j + 1) * 128],
                                              psb(6 + jj % 2, 128)))
            S("vector", _)
            eot_done = dv.n

            def _(scalar, kt=kt, eot_done=eot_done):
                W(scalar, "scalar", dv, eot_done)
                for s2 in range(NSUB):
                    s_mo.inc(scalar.dma_start(out=smoe_d[kt * TT + s2 * 128: kt * TT + (s2 + 1) * 128, :],
                                              in_=eotm[s2]), 16)
            S("scalar", _)
        moe_st = s_mo.n

        x1f_base = [s_x1f[0].n, s_x1f[1].n]
        mot_base = [s_mg[0].n, s_mg[1].n]
        out_dvb = dv.n
        st_base = sfin.n

        def _(sync, out_dvb=out_dvb):
            W(sync, "sync", s_st1, a_st1)
            for c in range(NCH):
                if c >= 2:
                    W(sync, "sync", dv, out_dvb + c - 1)
                ins = sync.dma_start(out=x1f[c % 2], in_=sx1_d[c * 128:(c + 1) * 128, :])
                s_x1f[c % 2].inc(ins, 16)
        S("sync", _)

        def _(gpsimd, moe_st=moe_st, out_dvb=out_dvb):
            W(gpsimd, "gpsimd", s_mo, moe_st)
            for c in range(NCH):
                if c >= 2:
                    W(gpsimd, "gpsimd", dv, out_dvb + c - 1)
                ins = gpsimd.indirect_dma_start(
                    out=mot[c % 2], out_offset=None,
                    in_=smoe_d[0:NSLOT, :],
                    in_offset=bass.IndirectOffsetOnAxis(scoli[:, c:c + 1], 0))
                s_mg[c % 2].inc(ins, 16)
        S("gpsimd", _)

        def _(vector, x1f_base=x1f_base, mot_base=mot_base, st_base=st_base):
            for c in range(NCH):
                W(vector, "vector", s_x1f[c % 2], x1f_base[c % 2] + (c // 2 + 1) * 16)
                W(vector, "vector", s_mg[c % 2], mot_base[c % 2] + (c // 2 + 1) * 16)
                if c >= 2:
                    W(vector, "vector", sfin, st_base + ((c - 2) // 2 + 1) * 2 * 16)
                dv.inc(vector.scalar_tensor_tensor(outt[c % 2], mot[c % 2],
                                                   scalec[:, c:c + 1], x1f[c % 2],
                                                   ALU.mult, ALU.add))
        S("vector", _)

        def _(scalar, out_dvb=out_dvb):
            for c in range(NCH):
                W(scalar, "scalar", dv, out_dvb + c + 1)
                sfin.inc(scalar.dma_start(out=out_d[c * 128:(c + 1) * 128, :],
                                          in_=outt[c % 2]), 16)
            W(scalar, "scalar", sfin, sfin.n)
        S("scalar", _)

    return nc


_NC_CACHE = {}


def _get_nc(key, cfg):
    if key not in _NC_CACHE:
        _NC_CACHE[key] = build(cfg)
    return _NC_CACHE[key]


def prep_inputs(core, cfg, L0_attn_x, L0_attn_W_in, L0_attn_conv_w, L0_attn_conv_b,
                L0_attn_W_x, L0_attn_A_log, L0_attn_D_param, L0_attn_W_out,
                L0_ffn_router_w, L0_ffn_w1, L0_ffn_b1, L0_ffn_w2, L0_ffn_b2):
    NB = cfg.NB
    f32 = np.float32
    x = np.ascontiguousarray(L0_attn_x[core * NB:(core + 1) * NB]).reshape(cfg.TOKS, DM)
    wx = np.asarray(L0_attn_W_x, f32)
    d = {
        "x0": np.asarray(x, f32),
        "w_in": np.ascontiguousarray(np.asarray(L0_attn_W_in, f32)),
        "conv_w": np.ascontiguousarray(
            np.asarray(L0_attn_conv_w, f32).reshape(NDC, 128, DCV).transpose(1, 0, 2).reshape(128, NDC * DCV)),
        "conv_b": np.ascontiguousarray(np.asarray(L0_attn_conv_b, f32).reshape(NDC, 128).T),
        "wx_d": np.ascontiguousarray(wx[:, 0].reshape(NDC, 128).T),
        "wx_b": np.ascontiguousarray(
            wx[:, 1:1 + DS].reshape(NDC, 128, DS).transpose(1, 0, 2).reshape(128, NDC * DS)),
        "wx_c": np.ascontiguousarray(
            wx[:, 1 + DS:1 + 2 * DS].reshape(NDC, 128, DS).transpose(1, 0, 2).reshape(128, NDC * DS)),
        "a_log": np.asarray(L0_attn_A_log, f32).reshape(1, DS),
        "d_param": np.ascontiguousarray(np.asarray(L0_attn_D_param, f32).reshape(NDC, 128).T),
        "w_out": np.ascontiguousarray(np.asarray(L0_attn_W_out, f32)),
        "rw_t": np.ascontiguousarray(
            np.asarray(L0_ffn_router_w, f32).T.reshape(NDK, 128, E).transpose(1, 0, 2).reshape(128, NDK * E)),
        "w1": np.ascontiguousarray(np.asarray(L0_ffn_w1, f32).reshape(E * DM, FF)),
        "w2": np.ascontiguousarray(np.asarray(L0_ffn_w2, f32).reshape(E * FF, DM)),
        "b1t": np.ascontiguousarray(
            np.asarray(L0_ffn_b1, f32).reshape(E, NFK, 128).transpose(0, 2, 1).reshape(E * 128, NFK)),
        "b2t": np.ascontiguousarray(
            np.asarray(L0_ffn_b2, f32).reshape(E, NDK, 128).transpose(0, 2, 1).reshape(E * 128, NDK)),
        "r1hot": np.eye(NCORES, dtype=f32)[core].reshape(1, NCORES),
        "rpfx": (np.arange(NCORES) < core).astype(f32).reshape(NCORES, 1),
    }
    return d


def kernel(**inputs):
    cfg = Cfg(NB=4, L=2048, G=256, TT=256, CCAP=12288)
    nc = _get_nc("prod", cfg)
    in_maps = [prep_inputs(c, cfg, **inputs) for c in range(NCORES)]
    res = run_bass_kernel_spmd(nc, in_maps, core_ids=list(range(NCORES)))
    outs = [np.asarray(r["out"]).reshape(cfg.NB, cfg.L, DM) for r in res.results]
    return np.concatenate(outs, axis=0)


# version 05
# speedup vs baseline: 1.0804x; 1.0708x over previous
import math

import numpy as np

import concourse.bass as bass
import concourse.mybir as mybir
from concourse.bass_utils import run_bass_kernel_spmd

F32 = mybir.dt.float32
BF16 = mybir.dt.bfloat16
I32 = mybir.dt.int32
U32 = mybir.dt.uint32
AF = mybir.ActivationFunctionType
ALU = mybir.AluOpType
AX = mybir.AxisListType.X

NCORES = 8
DM = 1024
DI = 2048
DS = 16
DCV = 4
E = 8
FF = 4096
Q = 128
NDK = DM // 128
NDC = DI // 128
NFK = FF // 128


class Cfg:
    def __init__(self, NB=4, L=2048, G=256, TT=256, CCAP=12288):
        self.NB, self.L, self.G, self.TT, self.CCAP = NB, L, G, TT, CCAP
        self.TOKS = NB * L
        self.GPS = L // G
        self.NG = NB * self.GPS
        self.GC = G // Q
        self.NCH = self.TOKS // Q
        self.NT = self.TOKS // TT + E
        self.NSLOT = self.NT * TT
        self.TCH = min(1024, self.TOKS)
        self.NTC = self.TOKS // self.TCH
        assert L % G == 0 and G % Q == 0 and TT % 128 == 0
        assert self.TOKS % self.TCH == 0 and self.TCH % Q == 0


class Sem:
    def __init__(self, nc, name):
        self.h = nc.alloc_semaphore(name)
        self.n = 0
        self.name = name

    def inc(self, inst, k=1):
        inst.then_inc(self.h, k)
        self.n += k
        return inst


class Pool:

    def __init__(self, nc, name, elems_per_part):
        self.base = nc.alloc_sbuf_tensor(name, [128, elems_per_part], F32).ap()
        self.off = 0
        self.cap = elems_per_part

    def take(self, nelem, dtype=F32, parts=128):
        o = self.off
        self.off += (nelem + 15) & ~15
        assert self.off <= self.cap, f"pool overflow {self.off} > {self.cap}"
        v = self.base[0:parts, o:o + nelem]
        if dtype is not F32:
            v = v.bitcast(dtype)
        return v

    def mark(self):
        return self.off

    def reset(self, m=0):
        self.off = m


def build(cfg: Cfg):
    nc = bass.Bass("TRN2", target_bir_lowering=False, debug=False, num_devices=NCORES)
    NB, L, G, TT, CCAP = cfg.NB, cfg.L, cfg.G, cfg.TT, cfg.CCAP
    TOKS, GPS, NG, GC, NCH = cfg.TOKS, cfg.GPS, cfg.NG, cfg.GC, cfg.NCH
    NT, NSLOT, TCH, NTC = cfg.NT, cfg.NSLOT, cfg.TCH, cfg.NTC
    CONVW = G + DCV - 1
    NSUB = TT // 128
    NROWB = TOKS // 128
    assert NROWB <= 128 and NT <= 128

    x0_d = nc.dram_tensor("x0", [TOKS, DM], F32, kind="ExternalInput").ap()
    wi_d = nc.dram_tensor("w_in", [DM, 2 * DI], F32, kind="ExternalInput").ap()
    cw_d = nc.dram_tensor("conv_w", [128, NDC * DCV], F32, kind="ExternalInput").ap()
    cb_d = nc.dram_tensor("conv_b", [128, NDC], F32, kind="ExternalInput").ap()
    wxd_d = nc.dram_tensor("wx_d", [128, NDC], F32, kind="ExternalInput").ap()
    wxb_d = nc.dram_tensor("wx_b", [128, NDC * DS], F32, kind="ExternalInput").ap()
    wxc_d = nc.dram_tensor("wx_c", [128, NDC * DS], F32, kind="ExternalInput").ap()
    alog_d = nc.dram_tensor("a_log", [1, DS], F32, kind="ExternalInput").ap()
    dp_d = nc.dram_tensor("d_param", [128, NDC], F32, kind="ExternalInput").ap()
    wo_d = nc.dram_tensor("w_out", [DI, DM], F32, kind="ExternalInput").ap()
    rwt_d = nc.dram_tensor("rw_t", [128, NDK * E], F32, kind="ExternalInput").ap()
    w1_d = nc.dram_tensor("w1", [E * DM, FF], F32, kind="ExternalInput").ap()
    w2_d = nc.dram_tensor("w2", [E * FF, DM], F32, kind="ExternalInput").ap()
    b1_d = nc.dram_tensor("b1t", [E * 128, NFK], F32, kind="ExternalInput").ap()
    b2_d = nc.dram_tensor("b2t", [E * 128, NDK], F32, kind="ExternalInput").ap()
    r1hot_d = nc.dram_tensor("r1hot", [1, NCORES], F32, kind="ExternalInput").ap()
    rpfx_d = nc.dram_tensor("rpfx", [NCORES, 1], F32, kind="ExternalInput").ap()

    out_d = nc.dram_tensor("out", [TOKS, DM], F32, kind="ExternalOutput").ap()

    sx1_d = nc.dram_tensor("s_x1", [TOKS, DM], F32, kind="Internal").ap()
    sxln_d = nc.dram_tensor("s_xln", [TOKS, DM], F32, kind="Internal").ap()
    sxs_d = nc.dram_tensor("s_xs", [NSLOT, DM], F32, kind="Internal").ap()
    smoe_d = nc.dram_tensor("s_moe", [NSLOT, DM], F32, kind="Internal").ap()
    winhl_d = nc.dram_tensor("winhl", [DM, 4 * DI], BF16, kind="Internal").ap()
    wohl_d = nc.dram_tensor("wohl", [DI, 2 * DM], BF16, kind="Internal").ap()
    w1hl_d = nc.dram_tensor("w1hl", [E * DM, 2 * FF], BF16, kind="Internal").ap()
    w2hl_d = nc.dram_tensor("w2hl", [E * FF, 2 * DM], BF16, kind="Internal").ap()
    ccin_d = nc.dram_tensor("cc_in", [NCORES, E], F32, kind="Internal")
    ccout_d = nc.dram_tensor("cc_out", [NCORES, E], F32, kind="Internal")

    ident_c = nc.inline_tensor(np.eye(128, dtype=np.float32), name="c_ident")
    mS = np.where(np.arange(128)[None, :] >= np.arange(128)[:, None], 0.0, 1e9).astype(np.float32)
    maskS_c = nc.inline_tensor(mS, name="c_maskS")
    iota8c_c = nc.inline_tensor(np.arange(E, dtype=np.float32).reshape(E, 1), name="c_iota8c")
    rev8_c = nc.inline_tensor(np.tile(np.arange(E, 0, -1).astype(np.float32), (128, 1)), name="c_rev8")
    tri8_c = nc.inline_tensor(np.triu(np.ones((E, E), np.float32), 1), name="c_tri8")
    ktile_c = nc.inline_tensor(np.tile((np.arange(NT) * TT).astype(np.float32), (E, 1)), name="c_ktile")

    def sb(name, shape, dtype=F32):
        return nc.alloc_sbuf_tensor(name, shape, dtype).ap()

    ident = sb("ident", [128, 128])
    maskS = sb("maskS", [128, 128])
    iota8c = sb("iota8c", [E, 1])
    rev8 = sb("rev8", [128, E])
    tri8 = sb("tri8", [E, E])
    ktile = sb("ktile", [E, NT])
    convw = sb("convw", [128, NDC * DCV])
    convb = sb("convb", [128, NDC])
    wxd = sb("wxd", [128, NDC])
    wxb = sb("wxb", [128, NDC * DS])
    wxc = sb("wxc", [128, NDC * DS])
    dcol = sb("dcol", [128, NDC])
    rwt = sb("rwt", [128, NDK * E])
    alog = sb("alog", [1, DS])
    arow = sb("arow", [1, DS])
    abc = sb("abc", [128, DS])
    hst = sb("hst", [DS, DI])
    ctail = sb("ctail", [128, NDC * (DCV - 1)])
    ecol = sb("ecol", [128, NCH])
    pcol = sb("pcol", [128, NCH])
    onesr = sb("onesr", [1, 512])
    onesc8 = sb("onesc8", [E, 1])
    ones8t = sb("ones8t", [E, TCH])
    wspi = sb("wspi", [128, 1024])
    wsph = sb("wsph", [128, 1024], BF16)
    wspl = sb("wspl", [128, 1024], BF16)
    gpad = sb("gpad", [1, 1])
    zt = sb("zt", [128, 1024])
    scols = sb("scols", [128, NCH])
    scoli = sb("scoli", [128, NCH], I32)
    kcols = sb("kcols", [128, NCH])
    scalec = sb("scalec", [128, NCH])
    cnts = sb("cnts", [E, 1])
    carry = sb("carry", [E, 1])
    basec = sb("basec", [E, 1])
    poffm1 = sb("poffm1", [E, 1])
    cuminc = sb("cuminc", [E, 1])
    kcst = sb("kcst", [128, 4])
    r1hot = sb("r1hot_s", [1, NCORES])
    rpfx = sb("rpfx_s", [NCORES, 1])

    pool = Pool(nc, "pool", (nc.sbuf_bytes_remaining - 512) // 4)

    x0t = [pool.take(DM) for _ in range(2)]
    xnt = [pool.take(DM) for _ in range(2)]
    sqt = pool.take(DM)
    lnm = [pool.take(1) for _ in range(2)]
    lni = [pool.take(1) for _ in range(2)]
    lns = pool.take(8)
    xnfm = pool.take(NDK * G)
    wiblk = [pool.take(1024) for _ in range(2)]
    xssm = pool.take(NDC * CONVW)
    szfm = pool.take(NDC * G)
    xcsfm = pool.take(NDC * G)
    xcdtm = pool.take(DI)
    drow = pool.take(G, parts=1)
    dabs = pool.take(G, parts=1)
    dexp = pool.take(G, parts=1)
    dln = pool.take(G, parts=1)
    drel = pool.take(G, parts=1)
    bfm = pool.take(G, parts=DS)
    cfm = pool.take(G, parts=DS)
    crow = pool.take(Q, parts=1)
    negc = pool.take(Q, parts=1)
    argrow = pool.take(2 * Q + 1, parts=1)
    ekall = pool.take(2 * Q + 1, parts=DS)
    dmt = pool.take(128)
    estk = pool.take((DS // 2) * Q)
    cflat = pool.take(DS * Q, parts=1)
    bttm = pool.take(DS)
    kbfm = pool.take(Q, parts=DS)
    qbfm = pool.take(Q, parts=DS)
    kbtm = pool.take(DS)
    dcolq = pool.take(1)
    mtmp = pool.take((DS // 2) * Q)
    mtt = pool.take(128)
    y3fm = pool.take(NDC * G)
    woblk = [pool.take(DM) for _ in range(2)]
    x1t = [pool.take(DM) for _ in range(2)]
    xlnt = [pool.take(DM) for _ in range(2)]
    xlnfm = pool.take(NDK * G)
    lgu = pool.take(E, U32)
    lgf = pool.take(E)
    lgw = pool.take(E)
    lgm = pool.take(1)
    lgm2 = pool.take(1)
    lgnm = pool.take(1)
    lge = pool.take(E)
    lgs = pool.take(1)
    a_mark = pool.mark()

    pool.reset(0)
    erow_c = pool.take(TCH, parts=1)
    ind_c = pool.take(TCH, parts=E)
    po_c = pool.take(TCH, parts=E)
    tmp_c = pool.take(TCH, parts=E)
    tmq_c = pool.take(TCH, dtype=U32, parts=E)
    ef16 = pool.take(128, parts=16)
    srow_c = pool.take(TCH, parts=1)
    krow_c = pool.take(TCH, parts=1)
    srows = pool.take(128, parts=NROWB)
    krows = pool.take(128, parts=NROWB)
    cntr = pool.take(E, parts=1)
    ccsb = pool.take(E, parts=NCORES)
    ccall = pool.take(E, parts=NCORES)
    baser = pool.take(E, parts=1)
    pcq = pool.take(4, parts=E)
    eti = sb("eti", [1, NT], I32)
    etu = pool.take(NT, dtype=U32, parts=E)
    etf = pool.take(NT, parts=E)
    etr = pool.take(NT, parts=1)
    etfr = pool.take(NT, parts=1)
    xlt = [pool.take(DM) for _ in range(2)]
    w1blk = [pool.take(NDK * 512) for _ in range(2)]
    w2blk = [pool.take(4 * DM) for _ in range(2)]
    xsfm = pool.take(NDK * TT)
    hsb = pool.take(4 * TT)
    hhl = pool.take(4 * TT)
    eofm = pool.take(NDK * TT)
    eotm = [pool.take(DM) for _ in range(NSUB)]
    b1sb = pool.take(NFK)
    b2sb = pool.take(NDK)
    mot = [pool.take(DM) for _ in range(2)]
    x1f = [pool.take(DM) for _ in range(2)]
    outt = [pool.take(DM) for _ in range(2)]
    b_mark = pool.mark()
    assert max(a_mark, b_mark) <= pool.cap

    PSA = nc.alloc_psum_tensor("PSA", [128, 4096], F32).ap()

    def psb(bank, nelem, parts=128):
        return PSA[0:parts, bank * 512: bank * 512 + nelem]

    def TR(tensor, out, in_):
        p = in_.shape[0]
        return tensor.transpose(out, in_, ident[0:p, 0:p])

    U8 = mybir.dt.uint8

    def dyndma(eng, dst, src_ap, idx_ap, scale_b, add_b, max_b, dims):
        with eng.register() as r:
            eng.reg_load(r, idx_ap)
            eng.reg_alu(r, r, scale_b, ALU.mult)
            if add_b:
                eng.reg_alu(r, r, add_b, ALU.add)
            rv = eng.snap(r, min_val=0, max_val=max_b)
            src = bass.AP(src_ap.tensor, rv, dims)
            ins = eng.dma_start(out=dst.bitcast(U8), in_=src)
            try:
                for reg in rv.val:
                    eng.free_register(reg)
            except TypeError:
                eng.free_register(rv.val)
        return ins

    w1hl_b = w1hl_d.bitcast(U8)
    w2hl_b = w2hl_d.bitcast(U8)
    b1_b = b1_d.bitcast(U8)
    b2_b = b2_d.bitcast(U8)

    s_ms = Sem(nc, "s_ms")
    s_ws2 = [Sem(nc, "s_wsA"), Sem(nc, "s_wsB")]
    s_wt2 = [Sem(nc, "s_wtA"), Sem(nc, "s_wtB")]
    s_gp = Sem(nc, "s_gp")
    s_sp = Sem(nc, "s_sp")
    s_gm = Sem(nc, "s_gm")
    s_x0 = Sem(nc, "s_x0")
    s_wi = [Sem(nc, "s_wiA"), Sem(nc, "s_wiB")]
    s_wo = [Sem(nc, "s_woA"), Sem(nc, "s_woB")]
    s_st1 = Sem(nc, "s_st1")
    s_cf = Sem(nc, "s_cf")
    s_zx = Sem(nc, "s_zx")
    s_xl = [Sem(nc, "s_xlA"), Sem(nc, "s_xlB")]
    s_sc = Sem(nc, "s_sc")
    s_bw = Sem(nc, "s_bw")
    s_w1 = [Sem(nc, "s_w1A"), Sem(nc, "s_w1B")]
    s_w2 = [Sem(nc, "s_w2A"), Sem(nc, "s_w2B")]
    s_mo = Sem(nc, "s_mo")
    s_x1f = [Sem(nc, "s_x1fA"), Sem(nc, "s_x1fB")]
    s_mg = [Sem(nc, "s_mgA"), Sem(nc, "s_mgB")]
    pe = Sem(nc, "pe")
    dv = Sem(nc, "dv")
    ac = Sem(nc, "ac")
    cc = Sem(nc, "cc")
    sfin = Sem(nc, "sfin")

    waited = {}

    def W(eng, en, sem, val=None):
        v = sem.n if val is None else val
        if v <= 0:
            return
        k = (en, sem.name)
        if waited.get(k, -1) < v:
            eng.wait_ge(sem.h, v)
            waited[k] = v

    with nc.Block() as block:
        RUN = {"sync": block.sync, "scalar": block.scalar, "vector": block.vector,
               "tensor": block.tensor, "gpsimd": block.gpsimd}

        def S(name, fn):
            RUN[name](fn)

        def _(sync):
            for dst, src in [(ident, ident_c.ap()), (maskS, maskS_c.ap()),
                             (iota8c, iota8c_c.ap()), (rev8, rev8_c.ap()),
                             (tri8, tri8_c.ap()), (ktile, ktile_c.ap()),
                             (convw, cw_d), (convb, cb_d), (wxd, wxd_d),
                             (wxb, wxb_d), (wxc, wxc_d), (dcol, dp_d),
                             (rwt, rwt_d), (alog, alog_d),
                             (r1hot, r1hot_d), (rpfx, rpfx_d)]:
                s_ms.inc(sync.dma_start(out=dst, in_=src), 16)
        S("sync", _)

        def _(vector):
            W(vector, "vector", s_ms)
            vector.memset(onesr[:], 1.0)
            vector.memset(kcst[:, 0:1], 1e-5)
            vector.memset(kcst[:, 1:2], 1.0)
            vector.memset(kcst[:, 2:3], -1.0)
            vector.memset(onesc8[:], 1.0)
            vector.memset(ones8t[:], 1.0)
            vector.memset(zt[:], 0.0)
            vector.memset(ctail[:], 0.0)
            dv.inc(vector.memset(hst[:], 0.0))
        S("vector", _)

        def _(scalar):
            W(scalar, "scalar", s_ms)
            ac.inc(scalar.activation(arow[:], alog[:], AF.Exp))
        S("scalar", _)

        def _(vector):
            W(vector, "vector", ac)
            dv.inc(vector.tensor_scalar_mul(arow[:], arow[:], -1.0))
        S("vector", _)

        def _(tensor):
            W(tensor, "tensor", dv)
            pe.inc(tensor.matmul(psb(6, DS), onesr[0:1, 0:128], arow[:], start=True, stop=True))
        S("tensor", _)

        def _(vector):
            W(vector, "vector", pe)
            dv.inc(vector.tensor_copy(abc[:], psb(6, DS)))
        S("vector", _)
        abc_done = dv.n

        sjobs = []
        for cb in range(2 * DI // 512):
            for rb in range(DM // 128):
                r = cb // 2
                hc = r * 2048 + (cb % 2) * 512
                sjobs.append((wi_d, winhl_d, rb, cb, hc, hc + 1024))
        for rb in range(DI // 128):
            for cb in range(DM // 512):
                sjobs.append((wo_d, wohl_d, rb, cb, cb * 512, DM + cb * 512))
        for rb in range(E * DM // 128):
            for cb in range(FF // 512):
                sjobs.append((w1_d, w1hl_d, rb, cb, cb * 1024, cb * 1024 + 512))
        for rb in range(E * FF // 128):
            for cb in range(DM // 512):
                sjobs.append((w2_d, w2hl_d, rb, cb, cb * 512, DM + cb * 512))
        NWIC = 2 * DI // 512 * (DM // 128)
        NWOC = (DI // 128) * (DM // 512)
        NW1C = (E * DM // 128) * (FF // 512)
        miles = {cb * (DM // 128) + (DM // 128) - 1: cb + 1
                 for cb in range(2 * DI // 512)}
        miles[NWIC + NWOC - 1] = 9
        miles[NWIC + NWOC + NW1C - 1] = 10
        miles[len(sjobs) - 1] = 11

        def _(gpsimd):
            def ld(c):
                src, _dst, rb, cb = sjobs[c][:4]
                i = c % 2
                s_ws2[i].inc(gpsimd.dma_start(
                    out=wspi[:, i * 512:(i + 1) * 512],
                    in_=src[rb * 128:(rb + 1) * 128, cb * 512:(cb + 1) * 512]), 16)
            ld(0)
            ld(1)
            for c in range(len(sjobs)):
                _src, dst, rb, cb, hc, lc = sjobs[c]
                i = c % 2
                W(gpsimd, "gpsimd", s_ws2[i], 16 * (c // 2 + 1))
                if c >= 2:
                    W(gpsimd, "gpsimd", s_wt2[i], 32 * (c // 2))
                s_gp.inc(gpsimd.tensor_copy(wsph[:, i * 512:(i + 1) * 512],
                                            wspi[:, i * 512:(i + 1) * 512]))
                W(gpsimd, "gpsimd", s_gp, 2 * c + 1)
                s_gp.inc(gpsimd.tensor_tensor(wspl[:, i * 512:(i + 1) * 512],
                                              wspi[:, i * 512:(i + 1) * 512],
                                              wsph[:, i * 512:(i + 1) * 512],
                                              ALU.subtract))
                W(gpsimd, "gpsimd", s_gp, 2 * c + 2)
                s_wt2[i].inc(gpsimd.dma_start(
                    out=dst[rb * 128:(rb + 1) * 128, hc:hc + 512],
                    in_=wsph[:, i * 512:(i + 1) * 512]), 16)
                s_wt2[i].inc(gpsimd.dma_start(
                    out=dst[rb * 128:(rb + 1) * 128, lc:lc + 512],
                    in_=wspl[:, i * 512:(i + 1) * 512]), 16)
                if c + 2 < len(sjobs):
                    ld(c + 2)
                if c in miles:
                    W(gpsimd, "gpsimd", s_wt2[0], 32 * (c // 2 + 1))
                    W(gpsimd, "gpsimd", s_wt2[1], 32 * ((c + 1) // 2))
                    s_sp.inc(gpsimd.memset(gpad[0:1, 0:1], 0.0))
            W(gpsimd, "gpsimd", dv, abc_done)
            rp = (128 * 1024) // DM
            for i in range((NSLOT + rp - 1) // rp):
                r0, r1 = i * rp, min(NSLOT, (i + 1) * rp)
                s_zx.inc(gpsimd.dma_start(out=sxs_d[r0:r1, :],
                                           in_=zt[0:(r1 - r0) * DM // 1024, :]), 16)
        S("gpsimd", _)
        sxs_zeroed = s_zx.n

        x1_done_mark = [0]
        prev_marks = {"wi": 0, "wo": 0, "x1st": 0, "xlnst": 0}

        for g in range(NG):
            b = g // GPS
            first = (g % GPS) == 0
            t0 = g * G
            ci0 = g * GC

            def _(sync, t0=t0):
                W(sync, "sync", dv, x1_done_mark[0])
                for i in range(GC):
                    s_x0.inc(sync.dma_start(out=x0t[i],
                                            in_=x0_d[t0 + i * 128: t0 + (i + 1) * 128, :]), 16)
            S("sync", _)
            x0_ld = s_x0.n

            for i in range(GC):
                def _(scalar, i=i, x0_ld=x0_ld):
                    W(scalar, "scalar", s_x0, x0_ld)
                    W(scalar, "scalar", dv)
                    ac.inc(scalar.activation(sqt, x0t[i], AF.Square))
                S("scalar", _)

                def _(vector, i=i, x0_ld=x0_ld):
                    W(vector, "vector", s_x0, x0_ld)
                    W(vector, "vector", ac)
                    vector.reduce_sum(lns[:, 0:1], x0t[i], axis=AX)
                    dv.inc(vector.reduce_sum(lns[:, 1:2], sqt, axis=AX))
                    W(vector, "vector", dv)
                    vector.tensor_scalar(lnm[i], lns[:, 0:1], 1.0 / DM, None, ALU.mult)
                    dv.inc(vector.tensor_scalar(lns[:, 3:4], lns[:, 1:2], 1.0 / DM, None, ALU.mult))
                    W(vector, "vector", dv)
                    dv.inc(vector.tensor_tensor(lns[:, 2:3], lnm[i], lnm[i], ALU.mult))
                    W(vector, "vector", dv)
                    dv.inc(vector.tensor_tensor(lns[:, 4:5], lns[:, 3:4], lns[:, 2:3], ALU.subtract))
                S("vector", _)

                def _(scalar, i=i):
                    W(scalar, "scalar", dv)
                    ac.inc(scalar.activation(lns[:, 5:6], lns[:, 4:5], AF.Sqrt, bias=kcst[:, 0:1]))
                S("scalar", _)

                def _(vector, i=i):
                    W(vector, "vector", ac)
                    dv.inc(vector.reciprocal(lni[i], lns[:, 5:6]))
                    W(vector, "vector", dv)
                    dv.inc(vector.tensor_scalar(xnt[i], x0t[i], lnm[i], lni[i],
                                                ALU.subtract, ALU.mult))
                S("vector", _)
            xn_done = dv.n

            SLX = (4, 5, 6, 7)
            xnbf = xnfm.bitcast(BF16)
            items = [(xnt[i][:, k * 128:(k + 1) * 128], k * G + i * 128)
                     for k in range(NDK) for i in range(GC)]
            pe0, dv0, ac0 = pe.n, dv.n, ac.n

            def _(tensor, items=items, xn_done=xn_done, dv0=dv0,
                  gac=prev_marks.get("gac", 0)):
                W(tensor, "tensor", dv, xn_done)
                W(tensor, "tensor", ac, gac)
                for j, (src, _off) in enumerate(items):
                    if j >= 4:
                        W(tensor, "tensor", dv, dv0 + j - 3)
                    pe.inc(TR(tensor, psb(SLX[j % 4], 128), src))
            S("tensor", _)

            def _(scalar, items=items, pe0=pe0):
                for j, (_src, off) in enumerate(items):
                    W(scalar, "scalar", pe, pe0 + j + 1)
                    ac.inc(scalar.activation(xnbf[:, off:off + 128],
                                             psb(SLX[j % 4], 128), AF.Copy))
            S("scalar", _)

            def _(vector, items=items, ac0=ac0):
                for j, (_src, off) in enumerate(items):
                    W(vector, "vector", ac, ac0 + j + 1)
                    dv.inc(vector.tensor_tensor(
                        xnbf[:, NDK * G + off: NDK * G + off + 128],
                        psb(SLX[j % 4], 128), xnbf[:, off:off + 128], ALU.subtract))
            S("vector", _)
            xnfm_done = dv.n

            wi_pe0 = pe.n
            wi_base = [s_wi[0].n, s_wi[1].n]
            ev_marks = list(prev_marks.get("wi_ev", [None, None]))
            for r in range(4):
                roff = (r % 2) * 2048

                def _(sync, r=r, wi_pe0=wi_pe0):
                    for k in range(NDK):
                        flat = r * NDK + k
                        if flat >= 2:
                            W(sync, "sync", pe, wi_pe0 + flat - 1)
                        else:
                            W(sync, "sync", pe, prev_marks["wi"])
                        W(sync, "sync", s_sp, 2 * r + 2)
                        bf = wiblk[k % 2].bitcast(BF16)
                        ins = sync.dma_start(out=bf[:, 0:2048],
                                             in_=winhl_d[k * 128:(k + 1) * 128,
                                                         r * 2048:(r + 1) * 2048])
                        s_wi[k % 2].inc(ins, 16)
                S("sync", _)

                def _(tensor, r=r, roff=roff, wi_base=wi_base, xnfm_done=xnfm_done,
                      em=ev_marks[r]):
                    W(tensor, "tensor", dv, xnfm_done)
                    if em is not None:
                        W(tensor, "tensor", em[0], em[1])
                    xnbf = xnfm.bitcast(BF16)
                    for k in range(NDK):
                        flat = r * NDK + k
                        W(tensor, "tensor", s_wi[k % 2],
                          wi_base[k % 2] + (flat // 2 + 1) * 16)
                        bf = wiblk[k % 2].bitcast(BF16)
                        xh = xnbf[:, k * G:(k + 1) * G]
                        xl = xnbf[:, NDK * G + k * G: NDK * G + (k + 1) * G]
                        for j in range(8):
                            dst = PSA[:, roff + (j // 2) * 512 + (j % 2) * 256: roff + (j // 2) * 512 + (j % 2) * 256 + G]
                            wh = bf[:, j * 128:(j + 1) * 128]
                            wl = bf[:, 1024 + j * 128: 1024 + (j + 1) * 128]
                            tensor.matmul(dst, wh, xh,
                                          start=(k == 0 and j % 2 == 0), stop=False)
                            tensor.matmul(dst, wh, xl, start=False, stop=False)
                            ins2 = tensor.matmul(
                                dst, wl, xh, start=False,
                                stop=(k == NDK - 1 and j % 2 == 1))
                        pe.inc(ins2)
                S("tensor", _)
                mm_done = pe.n

                if r < 2:
                    def _(vector, r=r, roff=roff, mm_done=mm_done):
                        W(vector, "vector", pe, mm_done)
                        for j in range(8):
                            jj = r * 8 + j
                            src = PSA[:, roff + (j // 2) * 512 + (j % 2) * 256: roff + (j // 2) * 512 + (j % 2) * 256 + G]
                            dst = xssm[:, jj * CONVW + (DCV - 1): jj * CONVW + (DCV - 1) + G]
                            ins = vector.tensor_copy(dst, src)
                        dv.inc(ins)
                    S("vector", _)
                    ev_marks.append((dv, dv.n))
                else:
                    def _(scalar, r=r, roff=roff, mm_done=mm_done):
                        W(scalar, "scalar", pe, mm_done)
                        for j in range(8):
                            jj = (r - 2) * 8 + j
                            src = PSA[:, roff + (j // 2) * 512 + (j % 2) * 256: roff + (j // 2) * 512 + (j % 2) * 256 + G]
                            ins = scalar.activation(szfm[:, jj * G:(jj + 1) * G], src, AF.Silu)
                        ac.inc(ins)
                    S("scalar", _)
                    ev_marks.append((ac, ac.n))
            prev_marks["wi"] = wi_pe0 + 4 * NDK
            wi_evac_dv = dv.n
            wi_evac_ac = ac.n

            def _(vector, first=first, wi_evac_dv=wi_evac_dv):
                W(vector, "vector", dv, wi_evac_dv)
                dst = bass.AP(xssm.tensor, xssm.offset,
                              [list(xssm.ap[0]), [CONVW, NDC], [1, DCV - 1]])
                if first:
                    dv.inc(vector.memset(dst, 0.0))
                else:
                    src = bass.AP(ctail.tensor, ctail.offset,
                                  [list(ctail.ap[0]), [DCV - 1, NDC], [1, DCV - 1]])
                    dv.inc(vector.tensor_copy(dst, src))
            S("vector", _)
            lead_done = dv.n

            def _(vector, lead_done=lead_done, wi_evac_ac=wi_evac_ac):
                W(vector, "vector", dv, lead_done)
                for k in range(DCV):
                    if k > 0:
                        W(vector, "vector", dv)
                    for dc in range(NDC):
                        src = xssm[:, dc * CONVW + k: dc * CONVW + k + G]
                        dst = xcsfm[:, dc * G:(dc + 1) * G]
                        if k == 0:
                            ins = vector.tensor_scalar(dst, src, convw[:, dc * DCV: dc * DCV + 1],
                                                       convb[:, dc:dc + 1], ALU.mult, ALU.add)
                        else:
                            ins = vector.scalar_tensor_tensor(dst, src,
                                                              convw[:, dc * DCV + k: dc * DCV + k + 1],
                                                              dst, ALU.mult, ALU.add)
                    dv.inc(ins)
            S("vector", _)
            conv_done = dv.n

            def _(vector, conv_done=conv_done):
                W(vector, "vector", dv, conv_done)
                dst = bass.AP(ctail.tensor, ctail.offset,
                              [list(ctail.ap[0]), [DCV - 1, NDC], [1, DCV - 1]])
                src = bass.AP(xssm.tensor, xssm.offset + G,
                              [list(xssm.ap[0]), [CONVW, NDC], [1, DCV - 1]])
                dv.inc(vector.tensor_copy(dst, src))
            S("vector", _)

            def _(scalar, conv_done=conv_done):
                W(scalar, "scalar", dv, conv_done)
                ac.inc(scalar.activation(xcsfm, xcsfm, AF.Silu))
            S("scalar", _)
            xcs_done = ac.n

            def _(tensor, xcs_done=xcs_done):
                W(tensor, "tensor", ac, xcs_done)
                for k in range(NDC):
                    tensor.matmul(psb(0, G, parts=1), wxd[:, k:k + 1],
                                  xcsfm[:, k * G:(k + 1) * G],
                                  start=(k == 0), stop=(k == NDC - 1))
                for k in range(NDC):
                    tensor.matmul(psb(1, G, parts=DS), wxb[:, k * DS:(k + 1) * DS],
                                  xcsfm[:, k * G:(k + 1) * G],
                                  start=(k == 0), stop=(k == NDC - 1))
                for k in range(NDC):
                    ins = tensor.matmul(psb(2, G, parts=DS), wxc[:, k * DS:(k + 1) * DS],
                                        xcsfm[:, k * G:(k + 1) * G],
                                        start=(k == 0), stop=(k == NDC - 1))
                pe.inc(ins)
            S("tensor", _)
            wx_done = pe.n

            def _(vector, wx_done=wx_done):
                W(vector, "vector", pe, wx_done)
                vector.tensor_copy(bfm, psb(1, G, parts=DS))
                vector.tensor_copy(cfm, psb(2, G, parts=DS))
                dv.inc(vector.tensor_copy(drel, psb(0, G, parts=1)))
            S("vector", _)
            draw_done = dv.n

            def _(scalar, draw_done=draw_done):
                W(scalar, "scalar", dv, draw_done)
                ac.inc(scalar.activation(dabs, drel, AF.Abs))
                W(scalar, "scalar", ac)
                ac.inc(scalar.activation(dexp, dabs, AF.Exp, scale=kcst[0:1, 2:3]))
                W(scalar, "scalar", ac)
                ac.inc(scalar.activation(dln, dexp, AF.Ln, bias=kcst[0:1, 1:2]))
                ac.inc(scalar.activation(dabs, drel, AF.Relu))
            S("scalar", _)

            def _(vector):
                W(vector, "vector", ac)
                dv.inc(vector.tensor_tensor(drow, dabs, dln, ALU.add))
            S("vector", _)
            drow_done = dv.n

            for q in range(GC):
                ci = ci0 + q
                qs = q * Q

                def _(vector, qs=qs, drow_done=drow_done):
                    W(vector, "vector", dv, drow_done)
                    dv.inc(vector.tensor_tensor_scan(crow, onesr[0:1, 0:Q],
                                                     drow[0:1, qs:qs + Q], 0.0,
                                                     ALU.mult, ALU.add))
                    W(vector, "vector", dv)
                    vector.tensor_scalar(argrow[0:1, 0:Q], crow,
                                         crow[0:1, Q - 1:Q], -1.0, ALU.subtract, ALU.mult)
                    vector.tensor_copy(argrow[0:1, Q:2 * Q], crow)
                    vector.tensor_copy(argrow[0:1, 2 * Q:2 * Q + 1], crow[0:1, Q - 1:Q])
                    dv.inc(vector.tensor_scalar(negc, crow, -1.0, None, ALU.mult))
                S("vector", _)
                rows_done = dv.n

                def _(tensor, rows_done=rows_done):
                    W(tensor, "tensor", dv, rows_done)
                    tensor.matmul(psb(4, 2 * Q + 1, parts=DS), arow, argrow,
                                  start=True, stop=True)
                    tensor.matmul(psb(3, 128), negc, onesr[0:1, 0:Q], start=True, stop=False)
                    ins = tensor.matmul(psb(3, 128), onesr[0:1, 0:Q], crow, start=False, stop=True)
                    pe.inc(ins)
                S("tensor", _)
                args_done = pe.n

                def _(scalar, args_done=args_done):
                    W(scalar, "scalar", pe, args_done)
                    ac.inc(scalar.activation(ekall, psb(4, 2 * Q + 1, parts=DS), AF.Exp))
                S("scalar", _)
                ek_done = ac.n

                def _(vector, args_done=args_done):
                    W(vector, "vector", pe, args_done)
                    dv.inc(vector.tensor_tensor(dmt, psb(3, 128), maskS, ALU.add))
                S("vector", _)
                dmt_done = dv.n

                estk_done = ac.n

                def _(scalar, qs=qs, drow_done=drow_done):
                    W(scalar, "scalar", dv, drow_done)
                    W(scalar, "scalar", pe)
                    s_cf.inc(scalar.dma_start(out=cflat, in_=cfm[0:DS, qs:qs + Q]), 16)
                S("scalar", _)
                cflat_done = s_cf.n

                def _(tensor, cflat_done=cflat_done, ek_done=ek_done):
                    W(tensor, "tensor", s_cf, cflat_done)
                    W(tensor, "tensor", ac, ek_done)
                    W(tensor, "tensor", dv)
                    for t4 in range(4):
                        ins = tensor.matmul(PSA[:, t4 * 512: t4 * 512 + 512],
                                            onesr[0:1, 0:128],
                                            cflat[0:1, t4 * 512:(t4 + 1) * 512],
                                            start=True, stop=True)
                    pe.inc(ins)
                S("tensor", _)
                cb_done = pe.n

                def _(vector, qs=qs, ek_done=ek_done):
                    W(vector, "vector", ac, ek_done)
                    vector.tensor_tensor(kbfm, bfm[0:DS, qs:qs + Q], ekall[0:DS, 0:Q], ALU.mult)
                    dv.inc(vector.tensor_tensor(qbfm, cfm[0:DS, qs:qs + Q],
                                                ekall[0:DS, Q:2 * Q], ALU.mult))
                S("vector", _)
                kb_done = dv.n

                def _(tensor, qs=qs, kb_done=kb_done):
                    W(tensor, "tensor", dv, kb_done)
                    TR(tensor, psb(6, DS), bfm[0:DS, qs:qs + Q])
                    TR(tensor, psb(7, DS), kbfm)
                    pe.inc(TR(tensor, psb(5, 1), drow[0:1, qs:qs + Q]))
                S("tensor", _)
                tr1_done = pe.n

                def _(vector, tr1_done=tr1_done):
                    W(vector, "vector", pe, tr1_done)
                    vector.tensor_copy(bttm, psb(6, DS))
                    vector.tensor_copy(kbtm, psb(7, DS))
                    dv.inc(vector.tensor_copy(dcolq, psb(5, 1)))
                S("vector", _)
                tr1_ev = dv.n

                pe1, dv1 = pe.n, dv.n

                def _(tensor, qs=qs, pe1=pe1, dv1=dv1, tr1_ev=tr1_ev):
                    W(tensor, "tensor", dv, tr1_ev)
                    for dc in range(NDC):
                        if dc >= 2:
                            W(tensor, "tensor", dv, dv1 + dc - 1)
                        pe.inc(TR(tensor, psb(6 + dc % 2, 128),
                                  xcsfm[:, dc * G + qs: dc * G + qs + Q]))
                S("tensor", _)

                def _(vector, pe1=pe1):
                    for dc in range(NDC):
                        W(vector, "vector", pe, pe1 + dc + 1)
                        dv.inc(vector.tensor_scalar(xcdtm[:, dc * 128:(dc + 1) * 128],
                                                    psb(6 + dc % 2, 128), dcolq,
                                                    None, ALU.mult))
                S("vector", _)
                xcd_done = dv.n

                for half in range(2):
                    def _(scalar, half=half, dmt_done=dmt_done):
                        W(scalar, "scalar", dv, dmt_done)
                        if half == 1:
                            W(scalar, "scalar", dv)
                        for n8 in range(8):
                            n = half * 8 + n8
                            ins = scalar.activation(estk[:, n8 * Q:(n8 + 1) * Q], dmt,
                                                    AF.Exp, scale=abc[:, n:n + 1])
                        ac.inc(ins)
                    S("scalar", _)
                    eh_done = ac.n

                    def _(vector, half=half, eh_done=eh_done, cb_done=cb_done, tr1_ev=tr1_ev):
                        W(vector, "vector", ac, eh_done)
                        W(vector, "vector", pe, cb_done)
                        W(vector, "vector", dv, tr1_ev)
                        W(vector, "vector", dv)
                        for n8 in range(8):
                            n = half * 8 + n8
                            ins = vector.scalar_tensor_tensor(
                                mtmp[:, n8 * Q:(n8 + 1) * Q], estk[:, n8 * Q:(n8 + 1) * Q],
                                bttm[:, n:n + 1], PSA[:, n * Q:(n + 1) * Q],
                                ALU.mult, ALU.mult)
                        dv.inc(ins)
                        h = 4
                        while h >= 1:
                            W(vector, "vector", dv)
                            for i2 in range(h):
                                if h == 1 and half == 0:
                                    ins = vector.tensor_tensor(mtt, mtmp[:, 0:Q],
                                                               mtmp[:, Q:2 * Q], ALU.add)
                                elif h == 1:
                                    ins = vector.tensor_tensor(mtmp[:, 0:Q], mtmp[:, 0:Q],
                                                               mtmp[:, Q:2 * Q], ALU.add)
                                else:
                                    ins = vector.tensor_tensor(
                                        mtmp[:, i2 * Q:(i2 + 1) * Q],
                                        mtmp[:, i2 * Q:(i2 + 1) * Q],
                                        mtmp[:, (i2 + h) * Q:(i2 + h + 1) * Q], ALU.add)
                            dv.inc(ins)
                            h //= 2
                    S("vector", _)

                def _(vector):
                    W(vector, "vector", dv)
                    dv.inc(vector.tensor_tensor(mtt, mtt, mtmp[:, 0:Q], ALU.add))
                S("vector", _)
                mt_done = dv.n

                pe2, dv2 = pe.n, dv.n

                def _(tensor, pe2=pe2, dv2=dv2, xcd_done=xcd_done, mt_done=mt_done):
                    W(tensor, "tensor", dv, max(xcd_done, mt_done))
                    for t4 in range(4):
                        ins = tensor.matmul(PSA[0:DS, t4 * 512:(t4 + 1) * 512],
                                            kbtm, xcdtm[:, t4 * 512:(t4 + 1) * 512],
                                            start=True, stop=True)
                    pe.inc(ins)
                    for dc in range(NDC):
                        if dc >= 2:
                            W(tensor, "tensor", dv, dv2 + dc - 1)
                        bank = 4 + dc % 2
                        tensor.matmul(psb(bank, Q), xcdtm[:, dc * 128:(dc + 1) * 128],
                                      mtt, start=True, stop=False)
                        pe.inc(tensor.matmul(psb(bank, Q), hst[0:DS, dc * 128:(dc + 1) * 128],
                                             qbfm, start=False, stop=True))
                S("tensor", _)
                ymm_all = pe.n

                def _(vector, qs=qs, pe2=pe2):
                    for dc in range(NDC):
                        W(vector, "vector", pe, pe2 + dc + 2)
                        bank = 4 + dc % 2
                        ins = vector.scalar_tensor_tensor(
                            xcdtm[:, dc * 128:(dc + 1) * 128],
                            xcsfm[:, dc * G + qs: dc * G + qs + Q],
                            dcol[:, dc:dc + 1], psb(bank, Q), ALU.mult, ALU.add)
                        dv.inc(ins)
                S("vector", _)
                st1 = dv.n

                def _(vector, ymm_all=ymm_all):
                    W(vector, "vector", pe, ymm_all)
                    dv.inc(vector.scalar_tensor_tensor(
                        hst[0:DS, 0:DI], hst[0:DS, 0:DI],
                        ekall[0:DS, 2 * Q:2 * Q + 1], PSA[0:DS, 0:DI],
                        ALU.mult, ALU.add))
                S("vector", _)

                y3bf = y3fm.bitcast(BF16)
                def _(vector, qs=qs, st1=st1):
                    W(vector, "vector", dv, st1)
                    for dc in range(NDC):
                        ins = vector.tensor_tensor(
                            xcdtm[:, dc * 128:(dc + 1) * 128],
                            xcdtm[:, dc * 128:(dc + 1) * 128],
                            szfm[:, dc * G + qs: dc * G + qs + Q], ALU.mult)
                    dv.inc(ins)
                S("vector", _)
                st2 = dv.n

                def _(scalar, qs=qs, st2=st2, y3bf=y3bf):
                    W(scalar, "scalar", dv, st2)
                    for dc in range(NDC):
                        ins = scalar.activation(y3bf[:, dc * G + qs: dc * G + qs + Q],
                                                xcdtm[:, dc * 128:(dc + 1) * 128],
                                                AF.Copy)
                    ac.inc(ins)
                S("scalar", _)
                st3 = ac.n

                def _(vector, qs=qs, st3=st3, y3bf=y3bf):
                    W(vector, "vector", ac, st3)
                    for dc in range(NDC):
                        ins = vector.tensor_tensor(
                            y3bf[:, NDC * G + dc * G + qs: NDC * G + dc * G + qs + Q],
                            xcdtm[:, dc * 128:(dc + 1) * 128],
                            y3bf[:, dc * G + qs: dc * G + qs + Q], ALU.subtract)
                    dv.inc(ins)
                S("vector", _)
            y3_done = dv.n

            pe_wb = pe.n
            wo_base = [s_wo[0].n, s_wo[1].n]

            def _(sync, pe_wb=pe_wb):
                for k in range(NDC):
                    if k >= 2:
                        W(sync, "sync", pe, pe_wb + k - 1)
                    else:
                        W(sync, "sync", pe, prev_marks["wo"])
                    W(sync, "sync", s_sp, 9)
                    bf = woblk[k % 2].bitcast(BF16)
                    s_wo[k % 2].inc(sync.dma_start(
                        out=bf[:, 0:2048], in_=wohl_d[k * 128:(k + 1) * 128, :]), 16)
            S("sync", _)

            def _(tensor, wo_base=wo_base, y3_done=y3_done):
                W(tensor, "tensor", dv, y3_done)
                y3bf = y3fm.bitcast(BF16)
                for k in range(NDC):
                    W(tensor, "tensor", s_wo[k % 2], wo_base[k % 2] + (k // 2 + 1) * 16)
                    bf = woblk[k % 2].bitcast(BF16)
                    yh = y3bf[:, k * G:(k + 1) * G]
                    yl = y3bf[:, NDC * G + k * G: NDC * G + (k + 1) * G]
                    for j in range(8):
                        dst = PSA[:, (j // 2) * 512 + (j % 2) * G: (j // 2) * 512 + (j % 2) * G + G]
                        wh = bf[:, j * 128:(j + 1) * 128]
                        wl = bf[:, 1024 + j * 128: 1024 + (j + 1) * 128]
                        tensor.matmul(dst, wh, yh,
                                      start=(k == 0 and j % 2 == 0), stop=False)
                        tensor.matmul(dst, wh, yl, start=False, stop=False)
                        ins2 = tensor.matmul(dst, wl, yh, start=False,
                                             stop=(k == NDC - 1 and j % 2 == 1))
                    pe.inc(ins2)
            S("tensor", _)
            womm_done = pe.n
            prev_marks["wo"] = womm_done

            def _(vector, womm_done=womm_done):
                W(vector, "vector", pe, womm_done)
                for j in range(8):
                    src = PSA[:, (j // 2) * 512 + (j % 2) * G: (j // 2) * 512 + (j % 2) * G + G]
                    ins = vector.tensor_copy(xnfm[:, j * G:(j + 1) * G], src)
                dv.inc(ins)
            S("vector", _)
            attn_done = dv.n

            SLA = (0, 1, 6, 7)
            items2 = [(xnfm[:, j * G + i * 128: j * G + (i + 1) * 128], i, j)
                      for j in range(8) for i in range(GC)]
            pe3, dv3 = pe.n, dv.n

            def _(tensor, items2=items2, attn_done=attn_done, dv3=dv3):
                W(tensor, "tensor", dv, attn_done)
                for jj, (src, i, j) in enumerate(items2):
                    if jj >= 4:
                        W(tensor, "tensor", dv, dv3 + jj - 3)
                    pe.inc(TR(tensor, psb(SLA[jj % 4], 128), src))
            S("tensor", _)

            def _(vector, items2=items2, pe3=pe3):
                W(vector, "vector", s_st1, prev_marks["x1st"])
                for jj, (_src, i, j) in enumerate(items2):
                    W(vector, "vector", pe, pe3 + jj + 1)
                    dv.inc(vector.tensor_tensor(x1t[i][:, j * 128:(j + 1) * 128],
                                                psb(SLA[jj % 4], 128),
                                                x0t[i][:, j * 128:(j + 1) * 128], ALU.add))
            S("vector", _)
            x1_done = dv.n
            x1_done_mark[0] = x1_done

            def _(sync, t0=t0, x1_done=x1_done):
                W(sync, "sync", dv, x1_done)
                for i in range(GC):
                    s_st1.inc(sync.dma_start(out=sx1_d[t0 + i * 128: t0 + (i + 1) * 128, :],
                                             in_=x1t[i]), 16)
            S("sync", _)

            for i in range(GC):
                def _(scalar, i=i, x1_done=x1_done):
                    W(scalar, "scalar", dv, x1_done)
                    W(scalar, "scalar", dv)
                    ac.inc(scalar.activation(sqt, x1t[i], AF.Square))
                S("scalar", _)

                def _(vector, i=i, x1_done=x1_done):
                    W(vector, "vector", dv, x1_done)
                    W(vector, "vector", ac)
                    vector.reduce_sum(lns[:, 0:1], x1t[i], axis=AX)
                    dv.inc(vector.reduce_sum(lns[:, 1:2], sqt, axis=AX))
                    W(vector, "vector", dv)
                    vector.tensor_scalar(lnm[i], lns[:, 0:1], 1.0 / DM, None, ALU.mult)
                    dv.inc(vector.tensor_scalar(lns[:, 3:4], lns[:, 1:2], 1.0 / DM, None, ALU.mult))
                    W(vector, "vector", dv)
                    dv.inc(vector.tensor_tensor(lns[:, 2:3], lnm[i], lnm[i], ALU.mult))
                    W(vector, "vector", dv)
                    dv.inc(vector.tensor_tensor(lns[:, 4:5], lns[:, 3:4], lns[:, 2:3], ALU.subtract))
                S("vector", _)

                def _(scalar, i=i):
                    W(scalar, "scalar", dv)
                    ac.inc(scalar.activation(lns[:, 5:6], lns[:, 4:5], AF.Sqrt, bias=kcst[:, 0:1]))
                S("scalar", _)

                def _(vector, i=i):
                    W(vector, "vector", ac)
                    W(vector, "vector", s_st1, prev_marks["xlnst"])
                    dv.inc(vector.reciprocal(lni[i], lns[:, 5:6]))
                    W(vector, "vector", dv)
                    dv.inc(vector.tensor_scalar(xlnt[i], x1t[i], lnm[i], lni[i],
                                                ALU.subtract, ALU.mult))
                S("vector", _)
            xln_done = dv.n

            def _(sync, t0=t0, xln_done=xln_done):
                W(sync, "sync", dv, xln_done)
                for i in range(GC):
                    s_st1.inc(sync.dma_start(out=sxln_d[t0 + i * 128: t0 + (i + 1) * 128, :],
                                             in_=xlnt[i]), 16)
            S("sync", _)
            prev_marks["x1st"] = s_st1.n
            prev_marks["xlnst"] = s_st1.n

            SLN = (4, 5, 6, 7)
            items3 = [(xlnt[i][:, k * 128:(k + 1) * 128], xlnfm[:, k * G + i * 128: k * G + (i + 1) * 128])
                      for k in range(NDK) for i in range(GC)]
            pe4, dv4 = pe.n, dv.n

            def _(tensor, items3=items3, xln_done=xln_done, dv4=dv4):
                W(tensor, "tensor", dv, xln_done)
                for j, (src, _dst) in enumerate(items3):
                    if j >= 4:
                        W(tensor, "tensor", dv, dv4 + j - 3)
                    pe.inc(TR(tensor, psb(SLN[j % 4], 128), src))
            S("tensor", _)

            def _(vector, items3=items3, pe4=pe4):
                for j, (_src, dst) in enumerate(items3):
                    W(vector, "vector", pe, pe4 + j + 1)
                    dv.inc(vector.tensor_copy(dst, psb(SLN[j % 4], 128)))
            S("vector", _)
            xlnfm_done = dv.n

            for i in range(GC):
                ci = ci0 + i
                bank = 4 + i % 2

                def _(tensor, i=i, bank=bank, xlnfm_done=xlnfm_done):
                    W(tensor, "tensor", dv, xlnfm_done)
                    for k in range(NDK):
                        ins = tensor.matmul(psb(bank, E),
                                            xlnfm[:, k * G + i * 128: k * G + (i + 1) * 128],
                                            rwt[:, k * E:(k + 1) * E],
                                            start=(k == 0), stop=(k == NDK - 1))
                    pe.inc(ins)
                S("tensor", _)
                rt_done = pe.n

                def _(vector, ci=ci, bank=bank, rt_done=rt_done):
                    W(vector, "vector", pe, rt_done)
                    dv.inc(vector.reduce_max(lgm, psb(bank, E), axis=AX))
                    W(vector, "vector", dv)
                    vector.tensor_scalar(lgnm, lgm, -1.0, None, ALU.mult)
                    dv.inc(vector.tensor_tensor(lgu, psb(bank, E),
                                                lgm.to_broadcast([128, E]), ALU.is_ge))
                    W(vector, "vector", dv)
                    dv.inc(vector.tensor_copy(lgf, lgu))
                    W(vector, "vector", dv)
                    dv.inc(vector.tensor_tensor(lgw, lgf, rev8, ALU.mult))
                    W(vector, "vector", dv)
                    dv.inc(vector.reduce_max(lgm2, lgw, axis=AX))
                    W(vector, "vector", dv)
                    dv.inc(vector.tensor_scalar(ecol[:, ci:ci + 1], lgm2, -1.0, float(E),
                                                ALU.mult, ALU.add))
                S("vector", _)
                sm1 = dv.n

                def _(scalar, bank=bank, sm1=sm1):
                    W(scalar, "scalar", dv, sm1)
                    ac.inc(scalar.activation(lge, psb(bank, E), AF.Exp, bias=lgnm))
                S("scalar", _)
                sm2 = ac.n

                def _(vector, ci=ci, sm2=sm2):
                    W(vector, "vector", ac, sm2)
                    dv.inc(vector.reduce_sum(lgs, lge, axis=AX))
                    W(vector, "vector", dv)
                    dv.inc(vector.reciprocal(pcol[:, ci:ci + 1], lgs))
                S("vector", _)

            prev_marks["wi_ev"] = [(dv, dv.n), (ac, ac.n)]
            prev_marks["gac"] = ac.n

            if (g % GPS) == GPS - 1 and g + 1 < NG:
                def _(vector):
                    W(vector, "vector", dv)
                    W(vector, "vector", pe)
                    vector.memset(hst[:], 0.0)
                    dv.inc(vector.memset(ctail[:], 0.0))
                S("vector", _)

        a_pe, a_dv, a_ac = pe.n, dv.n, ac.n
        a_st1 = s_st1.n

        eti_done = [0]
        def _(vector):
            W(vector, "vector", dv, a_dv)
            dv.inc(vector.memset(carry[:], 0.0))
        S("vector", _)

        for p2 in range(2):
            if p2 == 1:
                def _(tensor):
                    W(tensor, "tensor", dv)
                    pe.inc(TR(tensor, psb(6, E, parts=1), cnts[0:E, 0:1]))
                S("tensor", _)

                def _(vector):
                    W(vector, "vector", pe)
                    dv.inc(vector.tensor_copy(cntr, psb(6, E, parts=1)))
                S("vector", _)

                def _(tensor):
                    W(tensor, "tensor", dv)
                    pe.inc(tensor.matmul(psb(6, E, parts=NCORES), r1hot, cntr,
                                         start=True, stop=True))
                S("tensor", _)

                def _(vector):
                    W(vector, "vector", pe)
                    dv.inc(vector.tensor_copy(ccsb, psb(6, E, parts=NCORES)))
                S("vector", _)

                def _(gpsimd):
                    W(gpsimd, "gpsimd", dv)
                    s_gm.inc(gpsimd.dma_start(out=ccin_d.ap(), in_=ccsb), 16)
                    W(gpsimd, "gpsimd", s_gm)
                    cc.inc(gpsimd.collective_compute(
                        "AllReduce", ALU.add, replica_groups=[list(range(NCORES))],
                        ins=[ccin_d.ap().opt()], outs=[ccout_d.ap().opt()]))
                    W(gpsimd, "gpsimd", cc)
                    s_gm.inc(gpsimd.dma_start(out=ccall, in_=ccout_d.ap()), 16)
                S("gpsimd", _)

                def _(tensor):
                    W(tensor, "tensor", s_gm)
                    pe.inc(tensor.matmul(psb(6, E, parts=1), rpfx, ccall,
                                         start=True, stop=True))
                S("tensor", _)

                def _(vector):
                    W(vector, "vector", pe)
                    dv.inc(vector.tensor_copy(baser, psb(6, E, parts=1)))
                S("vector", _)

                def _(tensor):
                    W(tensor, "tensor", dv)
                    pe.inc(TR(tensor, psb(6, 1, parts=E), baser))
                S("tensor", _)

                def _(vector):
                    W(vector, "vector", pe)
                    dv.inc(vector.tensor_copy(basec, psb(6, 1, parts=E)))
                    W(vector, "vector", dv)
                    dv.inc(vector.tensor_scalar(pcq[:, 0:1], cnts, 1.0 / TT,
                                                -0.5 + 0.5 / TT, ALU.mult, ALU.add))
                    W(vector, "vector", dv)
                    dv.inc(vector.tensor_scalar(pcq[:, 0:1], cnts, TT - 1.0, 1.0 / TT,
                                                ALU.add, ALU.mult))
                    W(vector, "vector", dv)
                    dv.inc(vector.tensor_scalar(pcq[:, 1:2], pcq[:, 0:1],
                                                -0.5 + 1.0 / (2 * TT), None, ALU.add))
                    W(vector, "vector", dv)
                    dv.inc(vector.tensor_scalar(pcq[:, 2:3], pcq[:, 1:2], 12582912.0,
                                                None, ALU.add))
                    W(vector, "vector", dv)
                    dv.inc(vector.tensor_scalar(pcq[:, 3:4], pcq[:, 2:3], 12582912.0,
                                                None, ALU.subtract))
                    W(vector, "vector", dv)
                    dv.inc(vector.tensor_scalar(pcq[:, 0:1], pcq[:, 3:4], float(TT),
                                                None, ALU.mult))
                S("vector", _)
                pc_done = dv.n

                def _(tensor, pc_done=pc_done):
                    W(tensor, "tensor", dv, pc_done)
                    pe.inc(tensor.matmul(psb(6, 1, parts=E), tri8, pcq[:, 0:1],
                                         start=True, stop=True))
                S("tensor", _)

                def _(vector):
                    W(vector, "vector", pe)
                    dv.inc(vector.tensor_copy(poffm1, psb(6, 1, parts=E)))
                    W(vector, "vector", dv)
                    dv.inc(vector.tensor_tensor(cuminc, poffm1, pcq[:, 0:1], ALU.add))
                    W(vector, "vector", dv)
                    dv.inc(vector.tensor_scalar(poffm1, poffm1, -1.0, None, ALU.add))
                    dv.inc(vector.memset(carry[:], 0.0))
                S("vector", _)

                def _(vector):
                    W(vector, "vector", dv)
                    dv.inc(vector.tensor_tensor(etu, cuminc.to_broadcast([E, NT]),
                                                ktile, ALU.is_le))
                    W(vector, "vector", dv)
                    dv.inc(vector.tensor_copy(etf, etu))
                S("vector", _)

                def _(tensor):
                    W(tensor, "tensor", dv)
                    pe.inc(tensor.matmul(psb(6, NT, parts=1), onesc8, etf,
                                         start=True, stop=True))
                S("tensor", _)

                def _(vector):
                    W(vector, "vector", pe)
                    dv.inc(vector.tensor_copy(etr, psb(6, NT, parts=1)))
                    W(vector, "vector", dv)
                    dv.inc(vector.tensor_scalar(etfr, etr, float(E - 1), None, ALU.min))
                    W(vector, "vector", dv)
                    dv.inc(vector.tensor_copy(eti, etfr))
                S("vector", _)
                eti_done[0] = dv.n

            for tc in range(NTC):
                c0 = tc * TCH
                nqc = TCH // Q
                def _(tensor, tc=tc):
                    W(tensor, "tensor", dv)
                    pe.inc(TR(tensor, psb(7, 128, parts=nqc),
                              ecol[:, tc * nqc:(tc + 1) * nqc]))
                S("tensor", _)

                def _(vector):
                    W(vector, "vector", pe)
                    dv.inc(vector.tensor_copy(ef16[0:nqc, 0:128], psb(7, 128, parts=nqc)))
                S("vector", _)

                def _(gpsimd):
                    W(gpsimd, "gpsimd", dv)
                    s_gm.inc(gpsimd.dma_start(out=erow_c, in_=ef16[0:nqc, 0:128]), 16)
                S("gpsimd", _)
                ef_done = s_gm.n

                def _(tensor, ef_done=ef_done):
                    W(tensor, "tensor", s_gm, ef_done)
                    W(tensor, "tensor", dv)
                    for t4 in range(TCH // 512):
                        ins = tensor.matmul(PSA[0:E, t4 * 512:(t4 + 1) * 512],
                                            onesr[0:1, 0:E],
                                            erow_c[0:1, t4 * 512:(t4 + 1) * 512],
                                            start=True, stop=True)
                    pe.inc(ins)
                S("tensor", _)
                indmm = pe.n

                def _(vector, indmm=indmm):
                    W(vector, "vector", pe, indmm)
                    dv.inc(vector.tensor_tensor(tmq_c, PSA[0:E, 0:TCH],
                                                iota8c.to_broadcast([E, TCH]), ALU.is_equal))
                    W(vector, "vector", dv)
                    dv.inc(vector.tensor_copy(ind_c, tmq_c))
                    W(vector, "vector", dv)
                    dv.inc(vector.tensor_tensor_scan(po_c, ones8t[0:E, 0:TCH], ind_c,
                                                     carry[0:E, 0:1], ALU.mult, ALU.add))
                    W(vector, "vector", dv)
                    dv.inc(vector.tensor_copy(carry, po_c[0:E, TCH - 1:TCH]))
                S("vector", _)
                po_done = dv.n

                if p2 == 1:
                    def _(vector, po_done=po_done):
                        W(vector, "vector", dv, po_done)
                        dv.inc(vector.tensor_scalar(tmp_c, po_c, poffm1, None, ALU.add))
                        W(vector, "vector", dv)
                        dv.inc(vector.tensor_tensor(tmp_c, tmp_c, ind_c, ALU.mult))
                    S("vector", _)
                    srmm0 = dv.n

                    def _(tensor, srmm0=srmm0):
                        W(tensor, "tensor", dv, srmm0)
                        for t4 in range(TCH // 512):
                            ins = tensor.matmul(PSA[0:1, 2048 + t4 * 512: 2048 + (t4 + 1) * 512],
                                                onesc8, tmp_c[0:E, t4 * 512:(t4 + 1) * 512],
                                                start=True, stop=True)
                        pe.inc(ins)
                    S("tensor", _)

                    def _(vector):
                        W(vector, "vector", pe)
                        dv.inc(vector.tensor_copy(srow_c, PSA[0:1, 2048:2048 + TCH]))
                    S("vector", _)

                    def _(vector):
                        W(vector, "vector", dv)
                        dv.inc(vector.tensor_scalar(tmp_c, po_c, basec, None, ALU.add))
                        W(vector, "vector", dv)
                        dv.inc(vector.tensor_scalar(tmq_c, tmp_c, CCAP + 0.5, None, ALU.is_le))
                        W(vector, "vector", dv)
                        dv.inc(vector.tensor_copy(tmp_c, tmq_c))
                        W(vector, "vector", dv)
                        dv.inc(vector.tensor_tensor(tmp_c, tmp_c, ind_c, ALU.mult))
                    S("vector", _)
                    krmm0 = dv.n

                    def _(tensor, krmm0=krmm0):
                        W(tensor, "tensor", dv, krmm0)
                        for t4 in range(TCH // 512):
                            ins = tensor.matmul(PSA[0:1, 2048 + t4 * 512: 2048 + (t4 + 1) * 512],
                                                onesc8, tmp_c[0:E, t4 * 512:(t4 + 1) * 512],
                                                start=True, stop=True)
                        pe.inc(ins)
                    S("tensor", _)

                    def _(vector):
                        W(vector, "vector", pe)
                        dv.inc(vector.tensor_copy(krow_c, PSA[0:1, 2048:2048 + TCH]))
                    S("vector", _)
                    rows_ok = dv.n

                    def _(gpsimd, tc=tc, rows_ok=rows_ok):
                        W(gpsimd, "gpsimd", dv, rows_ok)
                        r0 = tc * (TCH // 128)
                        s_gm.inc(gpsimd.dma_start(out=srows[r0:r0 + TCH // 128, 0:128],
                                                   in_=srow_c), 16)
                        s_gm.inc(gpsimd.dma_start(out=krows[r0:r0 + TCH // 128, 0:128],
                                                   in_=krow_c), 16)
                    S("gpsimd", _)

            if p2 == 0:
                def _(vector):
                    W(vector, "vector", dv)
                    dv.inc(vector.tensor_copy(cnts, carry))
                S("vector", _)

        rows_all = s_gm.n

        def _(tensor, rows_all=rows_all):
            W(tensor, "tensor", s_gm, rows_all)
            pe.inc(TR(tensor, psb(6, NCH), srows[0:NROWB, 0:128]))
            pe.inc(TR(tensor, psb(7, NCH), krows[0:NROWB, 0:128]))
        S("tensor", _)

        def _(vector):
            W(vector, "vector", pe)
            vector.tensor_copy(scols, psb(6, NCH))
            dv.inc(vector.tensor_copy(kcols, psb(7, NCH)))
            W(vector, "vector", dv)
            vector.tensor_copy(scoli, scols)
            dv.inc(vector.tensor_tensor(scalec, kcols, pcol, ALU.mult))
        S("vector", _)
        cols_done = dv.n

        sg_scb = s_sc.n
        xl_base = [s_xl[0].n, s_xl[1].n]

        def _(sync, cols_done=cols_done, sg_scb=sg_scb):
            W(sync, "sync", dv, cols_done)
            W(sync, "sync", s_st1, a_st1)
            for c in range(NCH):
                if c >= 2:
                    W(sync, "sync", s_sc, sg_scb + ((c - 2) // 2 + 1) * 2 * 16)
                ins = sync.dma_start(out=xlt[c % 2], in_=sxln_d[c * 128:(c + 1) * 128, :])
                s_xl[c % 2].inc(ins, 16)
        S("sync", _)

        def _(gpsimd, cols_done=cols_done, xl_base=xl_base):
            W(gpsimd, "gpsimd", dv, cols_done)
            W(gpsimd, "gpsimd", s_zx, sxs_zeroed)
            for c in range(NCH):
                W(gpsimd, "gpsimd", s_xl[c % 2], xl_base[c % 2] + (c // 2 + 1) * 16)
                ins = gpsimd.indirect_dma_start(
                    out=sxs_d[0:NSLOT, :],
                    out_offset=bass.IndirectOffsetOnAxis(scoli[:, c:c + 1], 0),
                    in_=xlt[c % 2], in_offset=None)
                s_sc.inc(ins, 16)
        S("gpsimd", _)
        scat_done = s_sc.n

        for kt in range(NT):
            def _(sync, kt=kt, scat_done=scat_done):
                W(sync, "sync", s_sc, scat_done)
                W(sync, "sync", ac)
                W(sync, "sync", dv)
                for s2 in range(NSUB):
                    s_bw.inc(sync.dma_start(out=xlt[s2 % 2],
                                            in_=sxs_d[kt * TT + s2 * 128: kt * TT + (s2 + 1) * 128, :]), 16)
                s_bw.inc(dyndma(sync, b1sb, b1_b, eti[0:1, kt:kt + 1],
                                128 * NFK * 4, 0, (E - 1) * 128 * NFK * 4,
                                [[NFK * 4, 128], [1, NFK * 4]]), 16)
                s_bw.inc(dyndma(sync, b2sb, b2_b, eti[0:1, kt:kt + 1],
                                128 * NDK * 4, 0, (E - 1) * 128 * NDK * 4,
                                [[NDK * 4, 128], [1, NDK * 4]]), 16)
            S("sync", _)
            xs_ld = s_bw.n

            SLS = (2, 3, 6, 7)
            xsbf = xsfm.bitcast(BF16)
            items4 = [(xlt[s2 % 2][:, k * 128:(k + 1) * 128],
                       k * TT + s2 * 128)
                      for s2 in range(NSUB) for k in range(NDK)]
            pe5, dv5, ac5 = pe.n, dv.n, ac.n

            def _(tensor, items4=items4, xs_ld=xs_ld, dv5=dv5):
                W(tensor, "tensor", s_bw, xs_ld)
                W(tensor, "tensor", dv)
                for j, (src, _off) in enumerate(items4):
                    if j >= 4:
                        W(tensor, "tensor", dv, dv5 + j - 3)
                    pe.inc(TR(tensor, psb(SLS[j % 4], 128), src))
            S("tensor", _)

            def _(scalar, items4=items4, pe5=pe5, xsbf=xsbf):
                for j, (_src, off) in enumerate(items4):
                    W(scalar, "scalar", pe, pe5 + j + 1)
                    ac.inc(scalar.activation(xsbf[:, off:off + 128],
                                             psb(SLS[j % 4], 128), AF.Copy))
            S("scalar", _)

            def _(vector, items4=items4, ac5=ac5, xsbf=xsbf):
                for j, (_src, off) in enumerate(items4):
                    W(vector, "vector", ac, ac5 + j + 1)
                    dv.inc(vector.tensor_tensor(
                        xsbf[:, NDK * TT + off: NDK * TT + off + 128],
                        psb(SLS[j % 4], 128), xsbf[:, off:off + 128], ALU.subtract))
            S("vector", _)
            xsfm_done = dv.n

            hbf = hhl.bitcast(BF16)
            pe_fc0 = pe.n
            for fc in range(8):
                def _(sync, kt=kt, fc=fc, pe_fc0=pe_fc0):
                    if fc >= 2:
                        W(sync, "sync", pe, pe_fc0 + 2 * (fc - 2) + 1)
                    W(sync, "sync", s_sp, 10)
                    bfv = w1blk[fc % 2].bitcast(BF16)
                    RB = 2 * FF * 2
                    dst = bass.AP(bfv.tensor, bfv.offset,
                                  [list(bfv.ap[0]), [1024, NDK], [1, 1024]])
                    ins = dyndma(sync, dst, w1hl_b, eti[0:1, kt:kt + 1],
                                 DM * RB, fc * 1024 * 2, (E - 1) * DM * RB,
                                 [[RB, 128], [128 * RB, NDK], [1, 1024 * 2]])
                    s_w1[fc % 2].inc(ins, 16)
                S("sync", _)
                w1_ld = s_w1[fc % 2].n

                def _(scalar, kt=kt, fc=fc, pe_fc0=pe_fc0):
                    W(scalar, "scalar", dv, eti_done[0])
                    W(scalar, "scalar", s_sp, 11)
                    if fc >= 2:
                        W(scalar, "scalar", pe, pe_fc0 + 2 * (fc - 2) + 2)
                    bfv = w2blk[fc % 2].bitcast(BF16)
                    RB2 = 2 * DM * 2
                    dst = bass.AP(bfv.tensor, bfv.offset,
                                  [list(bfv.ap[0]), [2048, 4], [1, 2048]])
                    ins = dyndma(scalar, dst, w2hl_b, eti[0:1, kt:kt + 1],
                                 FF * RB2, fc * 512 * RB2, (E - 1) * FF * RB2,
                                 [[RB2, 128], [128 * RB2, 4], [1, RB2]])
                    s_w2[fc % 2].inc(ins, 16)
                S("scalar", _)
                w2_ld = s_w2[fc % 2].n

                def _(tensor, kt=kt, fc=fc, w1_ld=w1_ld, xsfm_done=xsfm_done):
                    W(tensor, "tensor", s_w1[fc % 2], w1_ld)
                    W(tensor, "tensor", dv, xsfm_done)
                    if fc > 0:
                        W(tensor, "tensor", ac)
                    bfv = w1blk[fc % 2].bitcast(BF16)
                    xsbf = xsfm.bitcast(BF16)
                    for s2 in range(4):
                        bank = s2 // 2
                        off = bank * 512 + (s2 % 2) * TT
                        for kc in range(NDK):
                            dst = PSA[:, off:off + TT]
                            wh = bfv[:, kc * 1024 + s2 * 128: kc * 1024 + (s2 + 1) * 128]
                            wl = bfv[:, kc * 1024 + 512 + s2 * 128:
                                     kc * 1024 + 512 + (s2 + 1) * 128]
                            xh = xsbf[:, kc * TT:(kc + 1) * TT]
                            xl = xsbf[:, NDK * TT + kc * TT: NDK * TT + (kc + 1) * TT]
                            tensor.matmul(dst, wh, xh,
                                          start=(kc == 0 and s2 % 2 == 0), stop=False)
                            tensor.matmul(dst, wh, xl, start=False, stop=False)
                            ins = tensor.matmul(
                                dst, wl, xh, start=False,
                                stop=(kc == NDK - 1 and s2 % 2 == 1))
                    pe.inc(ins)
                S("tensor", _)
                hmm_done = pe.n

                gelu_ac0 = ac.n

                def _(scalar, fc=fc, hmm_done=hmm_done, xs_ld=xs_ld, hbf=hbf):
                    W(scalar, "scalar", pe, hmm_done)
                    W(scalar, "scalar", s_bw, xs_ld)
                    for s2 in range(4):
                        bank = s2 // 2
                        off = bank * 512 + (s2 % 2) * TT
                        jglob = fc * 4 + s2
                        scalar.activation(hsb[:, s2 * TT:(s2 + 1) * TT],
                                          PSA[:, off:off + TT], AF.Gelu,
                                          bias=b1sb[:, jglob:jglob + 1])
                        ac.inc(scalar.activation(hbf[:, s2 * TT:(s2 + 1) * TT],
                                                 PSA[:, off:off + TT], AF.Gelu,
                                                 bias=b1sb[:, jglob:jglob + 1]))
                S("scalar", _)
                gelu_done = ac.n

                dvh0 = dv.n

                def _(vector, fc=fc, gelu_ac0=gelu_ac0, hbf=hbf):
                    for s2 in range(4):
                        W(vector, "vector", ac, gelu_ac0 + s2 + 1)
                        dv.inc(vector.tensor_tensor(
                            hbf[:, 4 * TT + s2 * TT: 4 * TT + (s2 + 1) * TT],
                            hsb[:, s2 * TT:(s2 + 1) * TT],
                            hbf[:, s2 * TT:(s2 + 1) * TT], ALU.subtract))
                S("vector", _)

                def _(tensor, kt=kt, fc=fc, gelu_ac0=gelu_ac0, dvh0=dvh0, w2_ld=w2_ld,
                      hbf=hbf):
                    W(tensor, "tensor", s_w2[fc % 2], w2_ld)
                    bfv = w2blk[fc % 2].bitcast(BF16)
                    for s2 in range(4):
                        W(tensor, "tensor", ac, gelu_ac0 + s2 + 1)
                        W(tensor, "tensor", dv, dvh0 + s2 + 1)
                        hh = hbf[:, s2 * TT:(s2 + 1) * TT]
                        hl = hbf[:, 4 * TT + s2 * TT: 4 * TT + (s2 + 1) * TT]
                        for j in range(8):
                            bank = 2 + j // 2
                            off = bank * 512 + (j % 2) * TT
                            dst = PSA[:, off:off + TT]
                            wh = bfv[:, s2 * 2048 + j * 128: s2 * 2048 + (j + 1) * 128]
                            wl = bfv[:, s2 * 2048 + 1024 + j * 128:
                                     s2 * 2048 + 1024 + (j + 1) * 128]
                            tensor.matmul(dst, wh, hh,
                                          start=(fc == 0 and s2 == 0 and j % 2 == 0),
                                          stop=False)
                            tensor.matmul(dst, wh, hl, start=False, stop=False)
                            ins = tensor.matmul(
                                dst, wl, hh, start=False,
                                stop=(fc == 7 and s2 == 3 and j % 2 == 1))
                    pe.inc(ins)
                S("tensor", _)
            eomm_done = pe.n

            def _(vector, kt=kt, eomm_done=eomm_done, xs_ld=xs_ld):
                W(vector, "vector", pe, eomm_done)
                W(vector, "vector", s_bw, xs_ld)
                for j in range(8):
                    bank = 2 + j // 2
                    off = bank * 512 + (j % 2) * TT
                    ins = vector.tensor_scalar(eofm[:, j * TT:(j + 1) * TT],
                                               PSA[:, off:off + TT],
                                               b2sb[:, j:j + 1], None, ALU.add)
                dv.inc(ins)
            S("vector", _)
            eo_done = dv.n

            SLE = (0, 1, 6, 7)
            items5 = [(eofm[:, j * TT + s2 * 128: j * TT + (s2 + 1) * 128], s2, j)
                      for s2 in range(NSUB) for j in range(NDK)]
            pe6, dv6 = pe.n, dv.n

            def _(tensor, items5=items5, eo_done=eo_done, dv6=dv6):
                W(tensor, "tensor", dv, eo_done)
                W(tensor, "tensor", ac)
                for jj, (src, s2, j) in enumerate(items5):
                    if jj >= 4:
                        W(tensor, "tensor", dv, dv6 + jj - 3)
                    pe.inc(TR(tensor, psb(SLE[jj % 4], 128), src))
            S("tensor", _)

            def _(vector, items5=items5, pe6=pe6):
                W(vector, "vector", s_mo)
                for jj, (_src, s2, j) in enumerate(items5):
                    W(vector, "vector", pe, pe6 + jj + 1)
                    dv.inc(vector.tensor_copy(eotm[s2][:, j * 128:(j + 1) * 128],
                                              psb(SLE[jj % 4], 128)))
            S("vector", _)
            eot_done = dv.n

            def _(scalar, kt=kt, eot_done=eot_done):
                W(scalar, "scalar", dv, eot_done)
                for s2 in range(NSUB):
                    s_mo.inc(scalar.dma_start(out=smoe_d[kt * TT + s2 * 128: kt * TT + (s2 + 1) * 128, :],
                                              in_=eotm[s2]), 16)
            S("scalar", _)
        moe_st = s_mo.n

        x1f_base = [s_x1f[0].n, s_x1f[1].n]
        mot_base = [s_mg[0].n, s_mg[1].n]
        out_dvb = dv.n
        st_base = sfin.n

        def _(sync, out_dvb=out_dvb):
            W(sync, "sync", s_st1, a_st1)
            for c in range(NCH):
                if c >= 2:
                    W(sync, "sync", dv, out_dvb + c - 1)
                ins = sync.dma_start(out=x1f[c % 2], in_=sx1_d[c * 128:(c + 1) * 128, :])
                s_x1f[c % 2].inc(ins, 16)
        S("sync", _)

        def _(gpsimd, moe_st=moe_st, out_dvb=out_dvb):
            W(gpsimd, "gpsimd", s_mo, moe_st)
            for c in range(NCH):
                if c >= 2:
                    W(gpsimd, "gpsimd", dv, out_dvb + c - 1)
                ins = gpsimd.indirect_dma_start(
                    out=mot[c % 2], out_offset=None,
                    in_=smoe_d[0:NSLOT, :],
                    in_offset=bass.IndirectOffsetOnAxis(scoli[:, c:c + 1], 0))
                s_mg[c % 2].inc(ins, 16)
        S("gpsimd", _)

        def _(vector, x1f_base=x1f_base, mot_base=mot_base, st_base=st_base):
            for c in range(NCH):
                W(vector, "vector", s_x1f[c % 2], x1f_base[c % 2] + (c // 2 + 1) * 16)
                W(vector, "vector", s_mg[c % 2], mot_base[c % 2] + (c // 2 + 1) * 16)
                if c >= 2:
                    W(vector, "vector", sfin, st_base + ((c - 2) // 2 + 1) * 2 * 16)
                dv.inc(vector.scalar_tensor_tensor(outt[c % 2], mot[c % 2],
                                                   scalec[:, c:c + 1], x1f[c % 2],
                                                   ALU.mult, ALU.add))
        S("vector", _)

        def _(scalar, out_dvb=out_dvb):
            for c in range(NCH):
                W(scalar, "scalar", dv, out_dvb + c + 1)
                sfin.inc(scalar.dma_start(out=out_d[c * 128:(c + 1) * 128, :],
                                          in_=outt[c % 2]), 16)
            W(scalar, "scalar", sfin, sfin.n)
        S("scalar", _)

    return nc


_NC_CACHE = {}


def _get_nc(key, cfg):
    if key not in _NC_CACHE:
        _NC_CACHE[key] = build(cfg)
    return _NC_CACHE[key]


def prep_inputs(core, cfg, L0_attn_x, L0_attn_W_in, L0_attn_conv_w, L0_attn_conv_b,
                L0_attn_W_x, L0_attn_A_log, L0_attn_D_param, L0_attn_W_out,
                L0_ffn_router_w, L0_ffn_w1, L0_ffn_b1, L0_ffn_w2, L0_ffn_b2):
    NB = cfg.NB
    f32 = np.float32
    x = np.ascontiguousarray(L0_attn_x[core * NB:(core + 1) * NB]).reshape(cfg.TOKS, DM)
    wx = np.asarray(L0_attn_W_x, f32)
    d = {
        "x0": np.asarray(x, f32),
        "w_in": np.ascontiguousarray(np.asarray(L0_attn_W_in, f32)),
        "conv_w": np.ascontiguousarray(
            np.asarray(L0_attn_conv_w, f32).reshape(NDC, 128, DCV).transpose(1, 0, 2).reshape(128, NDC * DCV)),
        "conv_b": np.ascontiguousarray(np.asarray(L0_attn_conv_b, f32).reshape(NDC, 128).T),
        "wx_d": np.ascontiguousarray(wx[:, 0].reshape(NDC, 128).T),
        "wx_b": np.ascontiguousarray(
            wx[:, 1:1 + DS].reshape(NDC, 128, DS).transpose(1, 0, 2).reshape(128, NDC * DS)),
        "wx_c": np.ascontiguousarray(
            wx[:, 1 + DS:1 + 2 * DS].reshape(NDC, 128, DS).transpose(1, 0, 2).reshape(128, NDC * DS)),
        "a_log": np.asarray(L0_attn_A_log, f32).reshape(1, DS),
        "d_param": np.ascontiguousarray(np.asarray(L0_attn_D_param, f32).reshape(NDC, 128).T),
        "w_out": np.ascontiguousarray(np.asarray(L0_attn_W_out, f32)),
        "rw_t": np.ascontiguousarray(
            np.asarray(L0_ffn_router_w, f32).T.reshape(NDK, 128, E).transpose(1, 0, 2).reshape(128, NDK * E)),
        "w1": np.ascontiguousarray(np.asarray(L0_ffn_w1, f32).reshape(E * DM, FF)),
        "w2": np.ascontiguousarray(np.asarray(L0_ffn_w2, f32).reshape(E * FF, DM)),
        "b1t": np.ascontiguousarray(
            np.asarray(L0_ffn_b1, f32).reshape(E, NFK, 128).transpose(0, 2, 1).reshape(E * 128, NFK)),
        "b2t": np.ascontiguousarray(
            np.asarray(L0_ffn_b2, f32).reshape(E, NDK, 128).transpose(0, 2, 1).reshape(E * 128, NDK)),
        "r1hot": np.eye(NCORES, dtype=f32)[core].reshape(1, NCORES),
        "rpfx": (np.arange(NCORES) < core).astype(f32).reshape(NCORES, 1),
    }
    return d


def kernel(**inputs):
    cfg = Cfg(NB=4, L=2048, G=256, TT=256, CCAP=12288)
    nc = _get_nc("prod", cfg)
    in_maps = [prep_inputs(c, cfg, **inputs) for c in range(NCORES)]
    res = run_bass_kernel_spmd(nc, in_maps, core_ids=list(range(NCORES)))
    outs = [np.asarray(r["out"]).reshape(cfg.NB, cfg.L, DM) for r in res.results]
    return np.concatenate(outs, axis=0)


# version 07 (final)
# speedup vs baseline: 1.2048x; 1.1152x over previous
import math

import numpy as np

import concourse.bass as bass
import concourse.mybir as mybir
from concourse.bass_utils import run_bass_kernel_spmd

F32 = mybir.dt.float32
BF16 = mybir.dt.bfloat16
I32 = mybir.dt.int32
U32 = mybir.dt.uint32
AF = mybir.ActivationFunctionType
ALU = mybir.AluOpType
AX = mybir.AxisListType.X

NCORES = 8
DM = 1024
DI = 2048
DS = 16
DCV = 4
E = 8
FF = 4096
Q = 128
NDK = DM // 128
NDC = DI // 128
NFK = FF // 128


class Cfg:
    def __init__(self, NB=4, L=2048, G=256, TT=256, CCAP=12288):
        self.NB, self.L, self.G, self.TT, self.CCAP = NB, L, G, TT, CCAP
        self.TOKS = NB * L
        self.GPS = L // G
        self.NG = NB * self.GPS
        self.GC = G // Q
        self.NCH = self.TOKS // Q
        self.NT = self.TOKS // TT + E
        self.NSLOT = self.NT * TT
        self.TCH = min(1024, self.TOKS)
        self.NTC = self.TOKS // self.TCH
        assert L % G == 0 and G % Q == 0 and TT % 128 == 0
        assert self.TOKS % self.TCH == 0 and self.TCH % Q == 0


class Sem:
    def __init__(self, nc, name):
        self.h = nc.alloc_semaphore(name)
        self.n = 0
        self.name = name

    def inc(self, inst, k=1):
        inst.then_inc(self.h, k)
        self.n += k
        return inst


class Pool:

    def __init__(self, nc, name, elems_per_part):
        self.base = nc.alloc_sbuf_tensor(name, [128, elems_per_part], F32).ap()
        self.off = 0
        self.cap = elems_per_part

    def take(self, nelem, dtype=F32, parts=128):
        o = self.off
        self.off += (nelem + 15) & ~15
        assert self.off <= self.cap, f"pool overflow {self.off} > {self.cap}"
        v = self.base[0:parts, o:o + nelem]
        if dtype is not F32:
            v = v.bitcast(dtype)
        return v

    def mark(self):
        return self.off

    def reset(self, m=0):
        self.off = m


def build(cfg: Cfg):
    nc = bass.Bass("TRN2", target_bir_lowering=False, debug=False, num_devices=NCORES)
    NB, L, G, TT, CCAP = cfg.NB, cfg.L, cfg.G, cfg.TT, cfg.CCAP
    TOKS, GPS, NG, GC, NCH = cfg.TOKS, cfg.GPS, cfg.NG, cfg.GC, cfg.NCH
    NT, NSLOT, TCH, NTC = cfg.NT, cfg.NSLOT, cfg.TCH, cfg.NTC
    CONVW = G + DCV - 1
    NSUB = TT // 128
    NROWB = TOKS // 128
    assert NROWB <= 128 and NT <= 128

    x0_d = nc.dram_tensor("x0", [TOKS, DM], F32, kind="ExternalInput").ap()
    wi_d = nc.dram_tensor("w_in", [DM, 2 * DI], F32, kind="ExternalInput").ap()
    cw_d = nc.dram_tensor("conv_w", [128, NDC * DCV], F32, kind="ExternalInput").ap()
    cb_d = nc.dram_tensor("conv_b", [128, NDC], F32, kind="ExternalInput").ap()
    wxd_d = nc.dram_tensor("wx_d", [128, NDC], F32, kind="ExternalInput").ap()
    wxb_d = nc.dram_tensor("wx_b", [128, NDC * DS], F32, kind="ExternalInput").ap()
    wxc_d = nc.dram_tensor("wx_c", [128, NDC * DS], F32, kind="ExternalInput").ap()
    alog_d = nc.dram_tensor("a_log", [1, DS], F32, kind="ExternalInput").ap()
    dp_d = nc.dram_tensor("d_param", [128, NDC], F32, kind="ExternalInput").ap()
    wo_d = nc.dram_tensor("w_out", [DI, DM], F32, kind="ExternalInput").ap()
    rwt_d = nc.dram_tensor("rw_t", [128, NDK * E], F32, kind="ExternalInput").ap()
    w1_d = nc.dram_tensor("w1", [E * DM, FF], F32, kind="ExternalInput").ap()
    w2_d = nc.dram_tensor("w2", [E * FF, DM], F32, kind="ExternalInput").ap()
    b1_d = nc.dram_tensor("b1t", [E * 128, NFK], F32, kind="ExternalInput").ap()
    b2_d = nc.dram_tensor("b2t", [E * 128, NDK], F32, kind="ExternalInput").ap()
    r1hot_d = nc.dram_tensor("r1hot", [1, NCORES], F32, kind="ExternalInput").ap()
    rpfx_d = nc.dram_tensor("rpfx", [NCORES, 1], F32, kind="ExternalInput").ap()

    out_d = nc.dram_tensor("out", [TOKS, DM], F32, kind="ExternalOutput").ap()

    sx1_d = nc.dram_tensor("s_x1", [TOKS, DM], F32, kind="Internal").ap()
    sxln_d = nc.dram_tensor("s_xln", [TOKS, DM], F32, kind="Internal").ap()
    sxs_d = nc.dram_tensor("s_xs", [NSLOT, DM], F32, kind="Internal").ap()
    smoe_d = nc.dram_tensor("s_moe", [NSLOT, DM], F32, kind="Internal").ap()
    winhl_d = nc.dram_tensor("winhl", [DM, 4 * DI], BF16, kind="Internal").ap()
    wohl_d = nc.dram_tensor("wohl", [DI, 2 * DM], BF16, kind="Internal").ap()
    w1hl_d = nc.dram_tensor("w1hl", [E * DM, 2 * FF], BF16, kind="Internal").ap()
    w2hl_d = nc.dram_tensor("w2hl", [E * FF, 2 * DM], BF16, kind="Internal").ap()
    ccin_d = nc.dram_tensor("cc_in", [NCORES, E], F32, kind="Internal")
    ccout_d = nc.dram_tensor("cc_out", [NCORES, E], F32, kind="Internal")

    ident_c = nc.inline_tensor(np.eye(128, dtype=np.float32), name="c_ident")
    mS = np.where(np.arange(128)[None, :] >= np.arange(128)[:, None], 0.0, 1e9).astype(np.float32)
    maskS_c = nc.inline_tensor(mS, name="c_maskS")
    iota8c_c = nc.inline_tensor(np.arange(E, dtype=np.float32).reshape(E, 1), name="c_iota8c")
    rev8_c = nc.inline_tensor(np.tile(np.arange(E, 0, -1).astype(np.float32), (128, 1)), name="c_rev8")
    tri8_c = nc.inline_tensor(np.triu(np.ones((E, E), np.float32), 1), name="c_tri8")
    ktile_c = nc.inline_tensor(np.tile((np.arange(NT) * TT).astype(np.float32), (E, 1)), name="c_ktile")

    def sb(name, shape, dtype=F32):
        return nc.alloc_sbuf_tensor(name, shape, dtype).ap()

    ident = sb("ident", [128, 128])
    maskS = sb("maskS", [128, 128])
    iota8c = sb("iota8c", [E, 1])
    rev8 = sb("rev8", [128, E])
    tri8 = sb("tri8", [E, E])
    ktile = sb("ktile", [E, NT])
    convw = sb("convw", [128, NDC * DCV])
    convb = sb("convb", [128, NDC])
    wxd = sb("wxd", [128, NDC])
    wxb = sb("wxb", [128, NDC * DS])
    wxc = sb("wxc", [128, NDC * DS])
    dcol = sb("dcol", [128, NDC])
    rwt = sb("rwt", [128, NDK * E])
    alog = sb("alog", [1, DS])
    arow = sb("arow", [1, DS])
    abc = sb("abc", [128, DS])
    hst = sb("hst", [48, DI])
    ctail = sb("ctail", [128, NDC * (DCV - 1)])
    ecol = sb("ecol", [128, NCH])
    pcol = sb("pcol", [128, NCH])
    onesr = sb("onesr", [1, 512])
    onesc8 = sb("onesc8", [E, 1])
    ones8t = sb("ones8t", [E, TCH])
    wspi = sb("wspi", [128, 1024])
    wsph = sb("wsph", [128, 1024], BF16)
    wspl = sb("wspl", [128, 1024], BF16)
    gpad = sb("gpad", [1, 1])
    zt = sb("zt", [128, 512])
    scols = sb("scols", [128, NCH])
    scoli = sb("scoli", [128, NCH], I32)
    kcols = sb("kcols", [128, NCH])
    scalec = sb("scalec", [128, NCH])
    cnts = sb("cnts", [E, 1])
    carry = sb("carry", [E, 1])
    basec = sb("basec", [E, 1])
    poffm1 = sb("poffm1", [E, 1])
    cuminc = sb("cuminc", [E, 1])
    kcst = sb("kcst", [128, 4])
    r1hot = sb("r1hot_s", [1, NCORES])
    rpfx = sb("rpfx_s", [NCORES, 1])

    pool = Pool(nc, "pool", (nc.sbuf_bytes_remaining - 512) // 4)

    x0t = [pool.take(DM) for _ in range(2)]
    xnt = [pool.take(DM) for _ in range(2)]
    sqt = pool.take(DM)
    lnm = [pool.take(1) for _ in range(2)]
    lni = [pool.take(1) for _ in range(2)]
    lns = pool.take(8)
    xnfm = pool.take(NDK * G)
    wiblk = [pool.take(1024) for _ in range(3)]
    xssm = pool.take(NDC * CONVW)
    szfm = pool.take(NDC * G)
    xcsfm = pool.take(NDC * G)
    xcdtm = pool.take(DI)
    dpk = pool.take(G)
    dabs = dpk[0:1, :]
    drel = dpk[32:33, :]
    drow = pool.take(G, parts=1)
    dexp = pool.take(G, parts=1)
    dln = pool.take(G, parts=1)
    bfm = pool.take(G, parts=DS)
    cfm = pool.take(G, parts=DS)
    crow = pool.take(Q, parts=1)
    negc = pool.take(Q, parts=1)
    argrow = pool.take(2 * Q + 1, parts=1)
    ekall = pool.take(2 * Q + 1, parts=DS)
    dmt = pool.take(128)
    estk = pool.take((DS // 2) * Q)
    cflat = pool.take(DS * Q, parts=1)
    bttm = pool.take(DS)
    kbfm = pool.take(Q, parts=DS)
    qbfm = pool.take(Q, parts=48)
    kbtm = pool.take(DS)
    dcolq = pool.take(1)
    mtmp = pool.take((DS // 2) * Q)
    mtt = pool.take(128)
    y3fm = pool.take(NDC * G)
    woblk = [pool.take(DM) for _ in range(3)]
    x1t = [pool.take(DM) for _ in range(2)]
    xlnt = [pool.take(DM) for _ in range(2)]
    xlnfm = pool.take(NDK * G)
    lgu = pool.take(E, U32)
    lgf = pool.take(E)
    lgw = pool.take(E)
    lgm = pool.take(1)
    lgm2 = pool.take(1)
    lgnm = pool.take(1)
    lge = pool.take(E)
    lgs = pool.take(1)
    a_mark = pool.mark()

    pool.reset(0)
    erow_c = pool.take(TCH, parts=1)
    ind_c = pool.take(TCH, parts=E)
    po_c = pool.take(TCH, parts=E)
    tmp_c = pool.take(TCH, parts=E)
    tmq_c = pool.take(TCH, dtype=U32, parts=E)
    ef16 = pool.take(128, parts=16)
    srow_c = pool.take(TCH, parts=1)
    krow_c = pool.take(TCH, parts=1)
    srows = pool.take(128, parts=NROWB)
    krows = pool.take(128, parts=NROWB)
    cntr = pool.take(E, parts=1)
    ccsb = pool.take(E, parts=NCORES)
    ccall = pool.take(E, parts=NCORES)
    baser = pool.take(E, parts=1)
    pcq = pool.take(4, parts=E)
    eti = sb("eti", [1, NT], I32)
    etu = pool.take(NT, dtype=U32, parts=E)
    etf = pool.take(NT, parts=E)
    etr = pool.take(NT, parts=1)
    etfr = pool.take(NT, parts=1)
    xlt = [pool.take(DM) for _ in range(2)]
    w1blk = [pool.take(NDK * 512) for _ in range(2)]
    w2blk = [pool.take(4 * DM) for _ in range(2)]
    xsfm = pool.take(NDK * TT)
    hsb = pool.take(4 * TT)
    hhl = pool.take(4 * TT)
    eofm = pool.take(NDK * TT)
    eotm = [pool.take(DM) for _ in range(NSUB)]
    b1sb = pool.take(NFK)
    b2sb = pool.take(NDK)
    mot = [pool.take(DM) for _ in range(2)]
    x1f = [pool.take(DM) for _ in range(2)]
    outt = [pool.take(DM) for _ in range(2)]
    b_mark = pool.mark()
    assert max(a_mark, b_mark) <= pool.cap

    PSA = nc.alloc_psum_tensor("PSA", [128, 4096], F32).ap()

    def psb(bank, nelem, parts=128):
        return PSA[0:parts, bank * 512: bank * 512 + nelem]

    def TR(tensor, out, in_):
        p = in_.shape[0]
        return tensor.transpose(out, in_, ident[0:p, 0:p])

    U8 = mybir.dt.uint8

    def dyndma(eng, dst, src_ap, idx_ap, scale_b, add_b, max_b, dims):
        with eng.register() as r:
            eng.reg_load(r, idx_ap)
            eng.reg_alu(r, r, scale_b, ALU.mult)
            if add_b:
                eng.reg_alu(r, r, add_b, ALU.add)
            rv = eng.snap(r, min_val=0, max_val=max_b)
            src = bass.AP(src_ap.tensor, rv, dims)
            ins = eng.dma_start(out=dst.bitcast(U8), in_=src)
            try:
                for reg in rv.val:
                    eng.free_register(reg)
            except TypeError:
                eng.free_register(rv.val)
        return ins

    w1hl_b = w1hl_d.bitcast(U8)
    w2hl_b = w2hl_d.bitcast(U8)
    b1_b = b1_d.bitcast(U8)
    b2_b = b2_d.bitcast(U8)

    s_ms = Sem(nc, "s_ms")
    s_ws2 = [Sem(nc, "s_wsA"), Sem(nc, "s_wsB")]
    s_wt2 = [Sem(nc, "s_wtA"), Sem(nc, "s_wtB")]
    s_gp = Sem(nc, "s_gp")
    s_sp = Sem(nc, "s_sp")
    s_gm = Sem(nc, "s_gm")
    s_x0 = Sem(nc, "s_x0")
    s_wi = [Sem(nc, f"s_wi{i}") for i in range(3)]
    s_wo = [Sem(nc, f"s_wo{i}") for i in range(3)]
    s_st1 = Sem(nc, "s_st1")
    s_cf = Sem(nc, "s_cf")
    s_zx = Sem(nc, "s_zx")
    s_xl = [Sem(nc, "s_xlA"), Sem(nc, "s_xlB")]
    s_sc = Sem(nc, "s_sc")
    s_bw = Sem(nc, "s_bw")
    s_w1 = [Sem(nc, "s_w1A"), Sem(nc, "s_w1B")]
    s_w2 = [Sem(nc, "s_w2A"), Sem(nc, "s_w2B")]
    s_mo = Sem(nc, "s_mo")
    s_x1f = [Sem(nc, "s_x1fA"), Sem(nc, "s_x1fB")]
    s_mg = [Sem(nc, "s_mgA"), Sem(nc, "s_mgB")]
    pe = Sem(nc, "pe")
    dv = Sem(nc, "dv")
    ac = Sem(nc, "ac")
    cc = Sem(nc, "cc")
    sfin = Sem(nc, "sfin")

    waited = {}

    def W(eng, en, sem, val=None):
        v = sem.n if val is None else val
        if v <= 0:
            return
        k = (en, sem.name)
        if waited.get(k, -1) < v:
            eng.wait_ge(sem.h, v)
            waited[k] = v

    with nc.Block() as block:
        RUN = {"sync": block.sync, "scalar": block.scalar, "vector": block.vector,
               "tensor": block.tensor, "gpsimd": block.gpsimd}

        def S(name, fn):
            RUN[name](fn)

        def _(sync):
            for dst, src in [(ident, ident_c.ap()), (maskS, maskS_c.ap()),
                             (iota8c, iota8c_c.ap()), (rev8, rev8_c.ap()),
                             (tri8, tri8_c.ap()), (ktile, ktile_c.ap()),
                             (convw, cw_d), (convb, cb_d), (wxd, wxd_d),
                             (wxb, wxb_d), (wxc, wxc_d), (dcol, dp_d),
                             (rwt, rwt_d), (alog, alog_d),
                             (r1hot, r1hot_d), (rpfx, rpfx_d)]:
                s_ms.inc(sync.dma_start(out=dst, in_=src), 16)
        S("sync", _)

        def _(vector):
            W(vector, "vector", s_ms)
            vector.memset(onesr[:], 1.0)
            vector.memset(kcst[:, 0:1], 1e-5)
            vector.memset(kcst[:, 1:2], 1.0)
            vector.memset(kcst[:, 2:3], -1.0)
            vector.memset(onesc8[:], 1.0)
            vector.memset(ones8t[:], 1.0)
            vector.memset(zt[:], 0.0)
            vector.memset(ctail[:], 0.0)
            dv.inc(vector.memset(hst[:], 0.0))
        S("vector", _)

        def _(scalar):
            W(scalar, "scalar", s_ms)
            ac.inc(scalar.activation(arow[:], alog[:], AF.Exp))
        S("scalar", _)

        def _(vector):
            W(vector, "vector", ac)
            dv.inc(vector.tensor_scalar_mul(arow[:], arow[:], -1.0))
        S("vector", _)

        def _(tensor):
            W(tensor, "tensor", dv)
            pe.inc(tensor.matmul(psb(6, DS), onesr[0:1, 0:128], arow[:], start=True, stop=True))
        S("tensor", _)

        def _(vector):
            W(vector, "vector", pe)
            dv.inc(vector.tensor_copy(abc[:], psb(6, DS)))
        S("vector", _)
        abc_done = dv.n

        sjobs = []
        for cb in range(2 * DI // 512):
            for rb in range(DM // 128):
                r = cb // 2
                hc = r * 2048 + (cb % 2) * 512
                sjobs.append((wi_d, winhl_d, rb, cb, hc, hc + 1024))
        for rb in range(DI // 128):
            for cb in range(DM // 512):
                sjobs.append((wo_d, wohl_d, rb, cb, cb * 512, DM + cb * 512))
        for rb in range(E * DM // 128):
            for cb in range(FF // 512):
                sjobs.append((w1_d, w1hl_d, rb, cb, cb * 1024, cb * 1024 + 512))
        for rb in range(E * FF // 128):
            for cb in range(DM // 512):
                sjobs.append((w2_d, w2hl_d, rb, cb, cb * 512, DM + cb * 512))
        NWIC = 2 * DI // 512 * (DM // 128)
        NWOC = (DI // 128) * (DM // 512)
        NW1C = (E * DM // 128) * (FF // 512)
        miles = {cb * (DM // 128) + (DM // 128) - 1: cb + 1
                 for cb in range(2 * DI // 512)}
        miles[NWIC + NWOC - 1] = 9
        miles[NWIC + NWOC + NW1C - 1] = 10
        miles[len(sjobs) - 1] = 11

        def _(gpsimd):
            def ld(c):
                src, _dst, rb, cb = sjobs[c][:4]
                i = c % 2
                s_ws2[i].inc(gpsimd.dma_start(
                    out=wspi[:, i * 512:(i + 1) * 512],
                    in_=src[rb * 128:(rb + 1) * 128, cb * 512:(cb + 1) * 512]), 16)
            ld(0)
            ld(1)
            for c in range(len(sjobs)):
                _src, dst, rb, cb, hc, lc = sjobs[c]
                i = c % 2
                W(gpsimd, "gpsimd", s_ws2[i], 16 * (c // 2 + 1))
                if c >= 2:
                    W(gpsimd, "gpsimd", s_wt2[i], 32 * (c // 2))
                s_gp.inc(gpsimd.tensor_copy(wsph[:, i * 512:(i + 1) * 512],
                                            wspi[:, i * 512:(i + 1) * 512]))
                W(gpsimd, "gpsimd", s_gp, 2 * c + 1)
                s_gp.inc(gpsimd.tensor_tensor(wspl[:, i * 512:(i + 1) * 512],
                                              wspi[:, i * 512:(i + 1) * 512],
                                              wsph[:, i * 512:(i + 1) * 512],
                                              ALU.subtract))
                W(gpsimd, "gpsimd", s_gp, 2 * c + 2)
                s_wt2[i].inc(gpsimd.dma_start(
                    out=dst[rb * 128:(rb + 1) * 128, hc:hc + 512],
                    in_=wsph[:, i * 512:(i + 1) * 512]), 16)
                s_wt2[i].inc(gpsimd.dma_start(
                    out=dst[rb * 128:(rb + 1) * 128, lc:lc + 512],
                    in_=wspl[:, i * 512:(i + 1) * 512]), 16)
                if c + 2 < len(sjobs):
                    ld(c + 2)
                if c in miles:
                    W(gpsimd, "gpsimd", s_wt2[0], 32 * (c // 2 + 1))
                    W(gpsimd, "gpsimd", s_wt2[1], 32 * ((c + 1) // 2))
                    s_sp.inc(gpsimd.memset(gpad[0:1, 0:1], 0.0))
            W(gpsimd, "gpsimd", dv, abc_done)
            rp = (128 * 512) // DM
            for i in range((NSLOT + rp - 1) // rp):
                r0, r1 = i * rp, min(NSLOT, (i + 1) * rp)
                s_zx.inc(gpsimd.dma_start(out=sxs_d[r0:r1, :],
                                           in_=zt[0:(r1 - r0) * DM // 512, :]), 16)
        S("gpsimd", _)
        sxs_zeroed = s_zx.n

        x1_done_mark = [0]
        prev_marks = {"wi": 0, "wo": 0, "x1st": 0, "xlnst": 0}
        scan_marks = {"hupd": 0, "mtstt": 0}

        for g in range(NG):
            b = g // GPS
            first = (g % GPS) == 0
            t0 = g * G
            ci0 = g * GC

            def _(scalar, t0=t0):
                W(scalar, "scalar", dv, x1_done_mark[0])
                for i in range(GC):
                    s_x0.inc(scalar.dma_start(out=x0t[i],
                                              in_=x0_d[t0 + i * 128: t0 + (i + 1) * 128, :]), 16)
            S("scalar", _)
            x0_ld = s_x0.n

            for i in range(GC):
                def _(scalar, i=i, x0_ld=x0_ld):
                    W(scalar, "scalar", s_x0, x0_ld)
                    W(scalar, "scalar", dv)
                    ac.inc(scalar.activation(sqt, x0t[i], AF.Square))
                S("scalar", _)

                def _(vector, i=i, x0_ld=x0_ld):
                    W(vector, "vector", s_x0, x0_ld)
                    W(vector, "vector", ac)
                    vector.reduce_sum(lns[:, 0:1], x0t[i], axis=AX)
                    dv.inc(vector.reduce_sum(lns[:, 1:2], sqt, axis=AX))
                    W(vector, "vector", dv)
                    vector.tensor_scalar(lnm[i], lns[:, 0:1], 1.0 / DM, None, ALU.mult)
                    dv.inc(vector.tensor_scalar(lns[:, 3:4], lns[:, 1:2], 1.0 / DM, None, ALU.mult))
                    W(vector, "vector", dv)
                    dv.inc(vector.tensor_tensor(lns[:, 2:3], lnm[i], lnm[i], ALU.mult))
                    W(vector, "vector", dv)
                    dv.inc(vector.tensor_tensor(lns[:, 4:5], lns[:, 3:4], lns[:, 2:3], ALU.subtract))
                S("vector", _)

                def _(scalar, i=i):
                    W(scalar, "scalar", dv)
                    ac.inc(scalar.activation(lns[:, 5:6], lns[:, 4:5], AF.Sqrt, bias=kcst[:, 0:1]))
                S("scalar", _)

                def _(vector, i=i):
                    W(vector, "vector", ac)
                    dv.inc(vector.reciprocal(lni[i], lns[:, 5:6]))
                    W(vector, "vector", dv)
                    dv.inc(vector.tensor_scalar(xnt[i], x0t[i], lnm[i], lni[i],
                                                ALU.subtract, ALU.mult))
                S("vector", _)
            xn_done = dv.n

            SLX = (4, 5, 6, 7)
            xnbf = xnfm.bitcast(BF16)
            items = [(xnt[i][:, k * 128:(k + 1) * 128], k * G + i * 128)
                     for k in range(NDK) for i in range(GC)]
            pe0, dv0, ac0 = pe.n, dv.n, ac.n

            def _(tensor, items=items, xn_done=xn_done, dv0=dv0,
                  gac=prev_marks.get("gac", 0)):
                W(tensor, "tensor", dv, xn_done)
                W(tensor, "tensor", ac, gac)
                for j, (src, _off) in enumerate(items):
                    if j >= 4:
                        W(tensor, "tensor", dv, dv0 + j - 3)
                    pe.inc(TR(tensor, psb(SLX[j % 4], 128), src))
            S("tensor", _)

            def _(scalar, items=items, pe0=pe0):
                for j, (_src, off) in enumerate(items):
                    W(scalar, "scalar", pe, pe0 + j + 1)
                    ac.inc(scalar.activation(xnbf[:, off:off + 128],
                                             psb(SLX[j % 4], 128), AF.Copy))
            S("scalar", _)

            def _(vector, items=items, ac0=ac0):
                for j, (_src, off) in enumerate(items):
                    W(vector, "vector", ac, ac0 + j + 1)
                    dv.inc(vector.tensor_tensor(
                        xnbf[:, NDK * G + off: NDK * G + off + 128],
                        psb(SLX[j % 4], 128), xnbf[:, off:off + 128], ALU.subtract))
            S("vector", _)
            xnfm_done = dv.n

            wi_pe0 = pe.n
            wi_base = [s.n for s in s_wi]
            ev_marks = list(prev_marks.get("wi_ev", [None, None]))
            for r in range(4):
                roff = (r % 2) * 2048

                def _(sync, r=r, wi_pe0=wi_pe0):
                    for k in range(NDK):
                        flat = r * NDK + k
                        if flat >= 3:
                            W(sync, "sync", pe, wi_pe0 + flat - 2)
                        else:
                            W(sync, "sync", pe, prev_marks["wi"])
                        W(sync, "sync", s_sp, 2 * r + 2)
                        bf = wiblk[flat % 3].bitcast(BF16)
                        ins = sync.dma_start(out=bf[:, 0:2048],
                                             in_=winhl_d[k * 128:(k + 1) * 128,
                                                         r * 2048:(r + 1) * 2048])
                        s_wi[flat % 3].inc(ins, 16)
                S("sync", _)

                def _(tensor, r=r, roff=roff, wi_base=wi_base, xnfm_done=xnfm_done,
                      em=ev_marks[r]):
                    W(tensor, "tensor", dv, xnfm_done)
                    if em is not None:
                        W(tensor, "tensor", em[0], em[1])
                    xnbf = xnfm.bitcast(BF16)
                    for k in range(NDK):
                        flat = r * NDK + k
                        W(tensor, "tensor", s_wi[flat % 3],
                          wi_base[flat % 3] + (flat // 3 + 1) * 16)
                        bf = wiblk[flat % 3].bitcast(BF16)
                        xh = xnbf[:, k * G:(k + 1) * G]
                        xl = xnbf[:, NDK * G + k * G: NDK * G + (k + 1) * G]
                        for j in range(8):
                            dst = PSA[:, roff + (j // 2) * 512 + (j % 2) * 256: roff + (j // 2) * 512 + (j % 2) * 256 + G]
                            wh = bf[:, j * 128:(j + 1) * 128]
                            wl = bf[:, 1024 + j * 128: 1024 + (j + 1) * 128]
                            tensor.matmul(dst, wh, xh,
                                          start=(k == 0 and j % 2 == 0), stop=False)
                            tensor.matmul(dst, wh, xl, start=False, stop=False)
                            ins2 = tensor.matmul(
                                dst, wl, xh, start=False,
                                stop=(k == NDK - 1 and j % 2 == 1))
                        pe.inc(ins2)
                S("tensor", _)
                mm_done = pe.n

                if r < 2:
                    def _(vector, r=r, roff=roff, mm_done=mm_done):
                        W(vector, "vector", pe, mm_done)
                        for j in range(8):
                            jj = r * 8 + j
                            src = PSA[:, roff + (j // 2) * 512 + (j % 2) * 256: roff + (j // 2) * 512 + (j % 2) * 256 + G]
                            dst = xssm[:, jj * CONVW + (DCV - 1): jj * CONVW + (DCV - 1) + G]
                            ins = vector.tensor_copy(dst, src)
                        dv.inc(ins)
                    S("vector", _)
                    ev_marks.append((dv, dv.n))
                else:
                    def _(scalar, r=r, roff=roff, mm_done=mm_done):
                        W(scalar, "scalar", pe, mm_done)
                        for j in range(8):
                            jj = (r - 2) * 8 + j
                            src = PSA[:, roff + (j // 2) * 512 + (j % 2) * 256: roff + (j // 2) * 512 + (j % 2) * 256 + G]
                            ins = scalar.activation(szfm[:, jj * G:(jj + 1) * G], src, AF.Silu)
                        ac.inc(ins)
                    S("scalar", _)
                    ev_marks.append((ac, ac.n))
            prev_marks["wi"] = wi_pe0 + 4 * NDK
            wi_evac_dv = dv.n
            wi_evac_ac = ac.n

            def _(vector, first=first, wi_evac_dv=wi_evac_dv):
                W(vector, "vector", dv, wi_evac_dv)
                dst = bass.AP(xssm.tensor, xssm.offset,
                              [list(xssm.ap[0]), [CONVW, NDC], [1, DCV - 1]])
                if first:
                    dv.inc(vector.memset(dst, 0.0))
                else:
                    src = bass.AP(ctail.tensor, ctail.offset,
                                  [list(ctail.ap[0]), [DCV - 1, NDC], [1, DCV - 1]])
                    dv.inc(vector.tensor_copy(dst, src))
            S("vector", _)
            lead_done = dv.n

            def _(vector, lead_done=lead_done, wi_evac_ac=wi_evac_ac):
                W(vector, "vector", dv, lead_done)
                for k in range(DCV):
                    if k > 0:
                        W(vector, "vector", dv)
                    for dc in range(NDC):
                        src = xssm[:, dc * CONVW + k: dc * CONVW + k + G]
                        dst = xcsfm[:, dc * G:(dc + 1) * G]
                        if k == 0:
                            ins = vector.tensor_scalar(dst, src, convw[:, dc * DCV: dc * DCV + 1],
                                                       convb[:, dc:dc + 1], ALU.mult, ALU.add)
                        else:
                            ins = vector.scalar_tensor_tensor(dst, src,
                                                              convw[:, dc * DCV + k: dc * DCV + k + 1],
                                                              dst, ALU.mult, ALU.add)
                    dv.inc(ins)
            S("vector", _)
            conv_done = dv.n

            def _(vector, conv_done=conv_done):
                W(vector, "vector", dv, conv_done)
                dst = bass.AP(ctail.tensor, ctail.offset,
                              [list(ctail.ap[0]), [DCV - 1, NDC], [1, DCV - 1]])
                src = bass.AP(xssm.tensor, xssm.offset + G,
                              [list(xssm.ap[0]), [CONVW, NDC], [1, DCV - 1]])
                dv.inc(vector.tensor_copy(dst, src))
            S("vector", _)

            def _(scalar, conv_done=conv_done):
                W(scalar, "scalar", dv, conv_done)
                ac.inc(scalar.activation(xcsfm, xcsfm, AF.Silu))
            S("scalar", _)
            xcs_done = ac.n

            def _(tensor, xcs_done=xcs_done):
                W(tensor, "tensor", ac, xcs_done)
                for k in range(NDC):
                    tensor.matmul(psb(0, G, parts=1), wxd[:, k:k + 1],
                                  xcsfm[:, k * G:(k + 1) * G],
                                  start=(k == 0), stop=(k == NDC - 1))
                for k in range(NDC):
                    tensor.matmul(psb(1, G, parts=DS), wxb[:, k * DS:(k + 1) * DS],
                                  xcsfm[:, k * G:(k + 1) * G],
                                  start=(k == 0), stop=(k == NDC - 1))
                for k in range(NDC):
                    ins = tensor.matmul(psb(2, G, parts=DS), wxc[:, k * DS:(k + 1) * DS],
                                        xcsfm[:, k * G:(k + 1) * G],
                                        start=(k == 0), stop=(k == NDC - 1))
                pe.inc(ins)
            S("tensor", _)
            wx_done = pe.n

            def _(vector, wx_done=wx_done):
                W(vector, "vector", pe, wx_done)
                vector.tensor_copy(bfm, psb(1, G, parts=DS))
                vector.tensor_copy(cfm, psb(2, G, parts=DS))
                dv.inc(vector.tensor_copy(drel, psb(0, G, parts=1)))
            S("vector", _)
            draw_done = dv.n

            def _(scalar, draw_done=draw_done):
                W(scalar, "scalar", dv, draw_done)
                ac.inc(scalar.activation(dabs, drel, AF.Abs))
                W(scalar, "scalar", ac)
                ac.inc(scalar.activation(dexp, dabs, AF.Exp, scale=kcst[0:1, 2:3]))
                W(scalar, "scalar", ac)
                ac.inc(scalar.activation(dln, dexp, AF.Ln, bias=kcst[0:1, 1:2]))
                ac.inc(scalar.activation(dabs, drel, AF.Relu))
            S("scalar", _)

            def _(vector):
                W(vector, "vector", ac)
                dv.inc(vector.tensor_tensor(drow, dabs, dln, ALU.add))
            S("vector", _)
            drow_done = dv.n

            for q in range(GC):
                ci = ci0 + q
                qs = q * Q

                def _(vector, qs=qs, drow_done=drow_done):
                    W(vector, "vector", dv, drow_done)
                    dv.inc(vector.tensor_tensor_scan(crow, onesr[0:1, 0:Q],
                                                     drow[0:1, qs:qs + Q], 0.0,
                                                     ALU.mult, ALU.add))
                    W(vector, "vector", dv)
                    vector.tensor_scalar(argrow[0:1, 0:Q], crow,
                                         crow[0:1, Q - 1:Q], -1.0, ALU.subtract, ALU.mult)
                    vector.tensor_copy(argrow[0:1, Q:2 * Q], crow)
                    vector.tensor_copy(argrow[0:1, 2 * Q:2 * Q + 1], crow[0:1, Q - 1:Q])
                    dv.inc(vector.tensor_scalar(negc, crow, -1.0, None, ALU.mult))
                S("vector", _)
                rows_done = dv.n

                def _(tensor, rows_done=rows_done):
                    W(tensor, "tensor", dv, rows_done)
                    tensor.matmul(psb(4, 2 * Q + 1, parts=DS), arow, argrow,
                                  start=True, stop=True)
                    tensor.matmul(psb(3, 128), negc, onesr[0:1, 0:Q], start=True, stop=False)
                    ins = tensor.matmul(psb(3, 128), onesr[0:1, 0:Q], crow, start=False, stop=True)
                    pe.inc(ins)
                S("tensor", _)
                args_done = pe.n

                def _(scalar, args_done=args_done):
                    W(scalar, "scalar", pe, args_done)
                    ac.inc(scalar.activation(ekall, psb(4, 2 * Q + 1, parts=DS), AF.Exp))
                S("scalar", _)
                ek_done = ac.n

                def _(vector, args_done=args_done):
                    W(vector, "vector", pe, args_done)
                    dv.inc(vector.tensor_tensor(dmt, psb(3, 128), maskS, ALU.add))
                S("vector", _)
                dmt_done = dv.n

                estk_done = ac.n

                def _(scalar, qs=qs, drow_done=drow_done):
                    W(scalar, "scalar", dv, drow_done)
                    W(scalar, "scalar", pe)
                    s_cf.inc(scalar.dma_start(out=cflat, in_=cfm[0:DS, qs:qs + Q]), 16)
                S("scalar", _)
                cflat_done = s_cf.n

                def _(tensor, cflat_done=cflat_done, ek_done=ek_done,
                      cb_free=max(draw_done, dmt_done,
                                  scan_marks["hupd"], scan_marks["mtstt"])):
                    W(tensor, "tensor", s_cf, cflat_done)
                    W(tensor, "tensor", ac, ek_done)
                    W(tensor, "tensor", dv, cb_free)
                    for t4 in range(4):
                        ins = tensor.matmul(PSA[:, t4 * 512: t4 * 512 + 512],
                                            onesr[0:1, 0:128],
                                            cflat[0:1, t4 * 512:(t4 + 1) * 512],
                                            start=True, stop=True)
                    pe.inc(ins)
                S("tensor", _)
                cb_done = pe.n

                def _(vector, qs=qs, ek_done=ek_done):
                    W(vector, "vector", ac, ek_done)
                    vector.tensor_tensor(kbfm, bfm[0:DS, qs:qs + Q], ekall[0:DS, 0:Q], ALU.mult)
                    dv.inc(vector.tensor_tensor(qbfm[0:DS, :], cfm[0:DS, qs:qs + Q],
                                                ekall[0:DS, Q:2 * Q], ALU.mult))
                S("vector", _)
                kb_done = dv.n

                def _(tensor, qs=qs, kb_done=kb_done):
                    W(tensor, "tensor", dv, kb_done)
                    TR(tensor, psb(6, DS), bfm[0:DS, qs:qs + Q])
                    TR(tensor, psb(7, DS), kbfm)
                    pe.inc(TR(tensor, psb(5, 1), drow[0:1, qs:qs + Q]))
                S("tensor", _)
                tr1_done = pe.n

                def _(vector, tr1_done=tr1_done):
                    W(vector, "vector", pe, tr1_done)
                    vector.tensor_copy(bttm, psb(6, DS))
                    vector.tensor_copy(kbtm, psb(7, DS))
                    dv.inc(vector.tensor_copy(dcolq, psb(5, 1)))
                S("vector", _)
                tr1_ev = dv.n

                SLC = (4, 5, 6, 7)
                pe1, dv1 = pe.n, dv.n

                def _(tensor, qs=qs, pe1=pe1, dv1=dv1, tr1_ev=tr1_ev, ek_done=ek_done):
                    W(tensor, "tensor", dv, tr1_ev)
                    W(tensor, "tensor", ac, ek_done)
                    for dc in range(NDC):
                        if dc >= 4:
                            W(tensor, "tensor", dv, dv1 + dc - 3)
                        pe.inc(TR(tensor, psb(SLC[dc % 4], 128),
                                  xcsfm[:, dc * G + qs: dc * G + qs + Q]))
                S("tensor", _)

                def _(vector, pe1=pe1):
                    for dc in range(NDC):
                        W(vector, "vector", pe, pe1 + dc + 1)
                        dv.inc(vector.tensor_scalar(xcdtm[:, dc * 128:(dc + 1) * 128],
                                                    psb(SLC[dc % 4], 128), dcolq,
                                                    None, ALU.mult))
                S("vector", _)
                xcd_done = dv.n

                mt_stt1 = 0
                for half in range(2):
                    if half == 1:
                        mt_stt1 = dv.n + 1
                    def _(scalar, half=half, dmt_done=dmt_done):
                        W(scalar, "scalar", dv, dmt_done)
                        if half == 1:
                            W(scalar, "scalar", dv)
                        for n8 in range(8):
                            n = half * 8 + n8
                            ins = scalar.activation(estk[:, n8 * Q:(n8 + 1) * Q], dmt,
                                                    AF.Exp, scale=abc[:, n:n + 1])
                        ac.inc(ins)
                    S("scalar", _)
                    eh_done = ac.n

                    def _(vector, half=half, eh_done=eh_done, cb_done=cb_done, tr1_ev=tr1_ev):
                        W(vector, "vector", ac, eh_done)
                        W(vector, "vector", pe, cb_done)
                        W(vector, "vector", dv, tr1_ev)
                        W(vector, "vector", dv)
                        for n8 in range(8):
                            n = half * 8 + n8
                            ins = vector.scalar_tensor_tensor(
                                mtmp[:, n8 * Q:(n8 + 1) * Q], estk[:, n8 * Q:(n8 + 1) * Q],
                                bttm[:, n:n + 1], PSA[:, n * Q:(n + 1) * Q],
                                ALU.mult, ALU.mult)
                        dv.inc(ins)
                        h = 4
                        while h >= 1:
                            W(vector, "vector", dv)
                            for i2 in range(h):
                                if h == 1 and half == 0:
                                    ins = vector.tensor_tensor(mtt, mtmp[:, 0:Q],
                                                               mtmp[:, Q:2 * Q], ALU.add)
                                elif h == 1:
                                    ins = vector.tensor_tensor(mtmp[:, 0:Q], mtmp[:, 0:Q],
                                                               mtmp[:, Q:2 * Q], ALU.add)
                                else:
                                    ins = vector.tensor_tensor(
                                        mtmp[:, i2 * Q:(i2 + 1) * Q],
                                        mtmp[:, i2 * Q:(i2 + 1) * Q],
                                        mtmp[:, (i2 + h) * Q:(i2 + h + 1) * Q], ALU.add)
                            dv.inc(ins)
                            h //= 2
                    S("vector", _)

                def _(vector):
                    W(vector, "vector", dv)
                    dv.inc(vector.tensor_tensor(mtt, mtt, mtmp[:, 0:Q], ALU.add))
                S("vector", _)
                mt_done = dv.n

                pe2, dv2 = pe.n, dv.n

                def _(tensor, pe2=pe2, dv2=dv2, xcd_done=xcd_done, mt_done=mt_done):
                    W(tensor, "tensor", dv, max(xcd_done, mt_done))
                    for t4 in range(4):
                        ins = tensor.matmul(PSA[0:DS, t4 * 512:(t4 + 1) * 512],
                                            kbtm, xcdtm[:, t4 * 512:(t4 + 1) * 512],
                                            start=True, stop=True)
                    pe.inc(ins)
                    for dc in range(NDC):
                        if dc >= 2:
                            W(tensor, "tensor", dv, dv2 + dc - 1)
                        bank = 4 + dc % 2
                        tensor.matmul(psb(bank, Q), xcdtm[:, dc * 128:(dc + 1) * 128],
                                      mtt, start=True, stop=False)
                        pe.inc(tensor.matmul(psb(bank, Q),
                                             hst[0:DS, dc * 128:(dc + 1) * 128],
                                             qbfm[0:DS, :], start=False, stop=True))
                S("tensor", _)
                ymm_all = pe.n

                def _(vector, qs=qs, pe2=pe2):
                    for dc in range(NDC):
                        W(vector, "vector", pe, pe2 + dc + 2)
                        bank = 4 + dc % 2
                        ins = vector.scalar_tensor_tensor(
                            xcdtm[:, dc * 128:(dc + 1) * 128],
                            xcsfm[:, dc * G + qs: dc * G + qs + Q],
                            dcol[:, dc:dc + 1], psb(bank, Q), ALU.mult, ALU.add)
                        dv.inc(ins)
                S("vector", _)
                st1 = dv.n

                def _(vector, ymm_all=ymm_all):
                    W(vector, "vector", pe, ymm_all)
                    dv.inc(vector.scalar_tensor_tensor(
                        hst[0:DS, 0:DI], hst[0:DS, 0:DI],
                        ekall[0:DS, 2 * Q:2 * Q + 1], PSA[0:DS, 0:DI],
                        ALU.mult, ALU.add))
                S("vector", _)
                scan_marks["hupd"] = dv.n
                scan_marks["mtstt"] = mt_stt1

                y3bf = y3fm.bitcast(BF16)
                def _(vector, qs=qs, st1=st1):
                    W(vector, "vector", dv, st1)
                    for dc in range(NDC):
                        ins = vector.tensor_tensor(
                            xcdtm[:, dc * 128:(dc + 1) * 128],
                            xcdtm[:, dc * 128:(dc + 1) * 128],
                            szfm[:, dc * G + qs: dc * G + qs + Q], ALU.mult)
                    dv.inc(ins)
                S("vector", _)
                st2 = dv.n

                def _(scalar, qs=qs, st2=st2, y3bf=y3bf):
                    W(scalar, "scalar", dv, st2)
                    for dc in range(NDC):
                        ins = scalar.activation(y3bf[:, dc * G + qs: dc * G + qs + Q],
                                                xcdtm[:, dc * 128:(dc + 1) * 128],
                                                AF.Copy)
                    ac.inc(ins)
                S("scalar", _)
                st3 = ac.n

                def _(vector, qs=qs, st3=st3, y3bf=y3bf):
                    W(vector, "vector", ac, st3)
                    for dc in range(NDC):
                        ins = vector.tensor_tensor(
                            y3bf[:, NDC * G + dc * G + qs: NDC * G + dc * G + qs + Q],
                            xcdtm[:, dc * 128:(dc + 1) * 128],
                            y3bf[:, dc * G + qs: dc * G + qs + Q], ALU.subtract)
                    dv.inc(ins)
                S("vector", _)
            y3_done = dv.n

            pe_wb = pe.n
            wo_base = [s.n for s in s_wo]

            def _(sync, pe_wb=pe_wb):
                for k in range(NDC):
                    if k >= 3:
                        W(sync, "sync", pe, pe_wb + k - 2)
                    else:
                        W(sync, "sync", pe, prev_marks["wo"])
                    W(sync, "sync", s_sp, 9)
                    bf = woblk[k % 3].bitcast(BF16)
                    s_wo[k % 3].inc(sync.dma_start(
                        out=bf[:, 0:2048], in_=wohl_d[k * 128:(k + 1) * 128, :]), 16)
            S("sync", _)

            def _(tensor, wo_base=wo_base, y3_done=y3_done):
                W(tensor, "tensor", dv, y3_done)
                y3bf = y3fm.bitcast(BF16)
                for k in range(NDC):
                    W(tensor, "tensor", s_wo[k % 3], wo_base[k % 3] + (k // 3 + 1) * 16)
                    bf = woblk[k % 3].bitcast(BF16)
                    yh = y3bf[:, k * G:(k + 1) * G]
                    yl = y3bf[:, NDC * G + k * G: NDC * G + (k + 1) * G]
                    for j in range(8):
                        dst = PSA[:, (j // 2) * 512 + (j % 2) * G: (j // 2) * 512 + (j % 2) * G + G]
                        wh = bf[:, j * 128:(j + 1) * 128]
                        wl = bf[:, 1024 + j * 128: 1024 + (j + 1) * 128]
                        tensor.matmul(dst, wh, yh,
                                      start=(k == 0 and j % 2 == 0), stop=False)
                        tensor.matmul(dst, wh, yl, start=False, stop=False)
                        ins2 = tensor.matmul(dst, wl, yh, start=False,
                                             stop=(k == NDC - 1 and j % 2 == 1))
                    pe.inc(ins2)
            S("tensor", _)
            womm_done = pe.n
            prev_marks["wo"] = womm_done

            def _(vector, womm_done=womm_done):
                W(vector, "vector", pe, womm_done)
                for j in range(8):
                    src = PSA[:, (j // 2) * 512 + (j % 2) * G: (j // 2) * 512 + (j % 2) * G + G]
                    ins = vector.tensor_copy(xnfm[:, j * G:(j + 1) * G], src)
                dv.inc(ins)
            S("vector", _)
            attn_done = dv.n

            SLA = (0, 1, 6, 7)
            items2 = [(xnfm[:, j * G + i * 128: j * G + (i + 1) * 128], i, j)
                      for j in range(8) for i in range(GC)]
            pe3, dv3 = pe.n, dv.n

            def _(tensor, items2=items2, attn_done=attn_done, dv3=dv3):
                W(tensor, "tensor", dv, attn_done)
                for jj, (src, i, j) in enumerate(items2):
                    if jj >= 4:
                        W(tensor, "tensor", dv, dv3 + jj - 3)
                    pe.inc(TR(tensor, psb(SLA[jj % 4], 128), src))
            S("tensor", _)

            def _(vector, items2=items2, pe3=pe3):
                W(vector, "vector", s_st1, prev_marks["x1st"])
                for jj, (_src, i, j) in enumerate(items2):
                    W(vector, "vector", pe, pe3 + jj + 1)
                    dv.inc(vector.tensor_tensor(x1t[i][:, j * 128:(j + 1) * 128],
                                                psb(SLA[jj % 4], 128),
                                                x0t[i][:, j * 128:(j + 1) * 128], ALU.add))
            S("vector", _)
            x1_done = dv.n
            x1_done_mark[0] = x1_done

            def _(scalar, t0=t0, x1_done=x1_done):
                W(scalar, "scalar", dv, x1_done)
                for i in range(GC):
                    s_st1.inc(scalar.dma_start(out=sx1_d[t0 + i * 128: t0 + (i + 1) * 128, :],
                                               in_=x1t[i]), 16)
            S("scalar", _)

            for i in range(GC):
                def _(scalar, i=i, x1_done=x1_done):
                    W(scalar, "scalar", dv, x1_done)
                    W(scalar, "scalar", dv)
                    ac.inc(scalar.activation(sqt, x1t[i], AF.Square))
                S("scalar", _)

                def _(vector, i=i, x1_done=x1_done):
                    W(vector, "vector", dv, x1_done)
                    W(vector, "vector", ac)
                    vector.reduce_sum(lns[:, 0:1], x1t[i], axis=AX)
                    dv.inc(vector.reduce_sum(lns[:, 1:2], sqt, axis=AX))
                    W(vector, "vector", dv)
                    vector.tensor_scalar(lnm[i], lns[:, 0:1], 1.0 / DM, None, ALU.mult)
                    dv.inc(vector.tensor_scalar(lns[:, 3:4], lns[:, 1:2], 1.0 / DM, None, ALU.mult))
                    W(vector, "vector", dv)
                    dv.inc(vector.tensor_tensor(lns[:, 2:3], lnm[i], lnm[i], ALU.mult))
                    W(vector, "vector", dv)
                    dv.inc(vector.tensor_tensor(lns[:, 4:5], lns[:, 3:4], lns[:, 2:3], ALU.subtract))
                S("vector", _)

                def _(scalar, i=i):
                    W(scalar, "scalar", dv)
                    ac.inc(scalar.activation(lns[:, 5:6], lns[:, 4:5], AF.Sqrt, bias=kcst[:, 0:1]))
                S("scalar", _)

                def _(vector, i=i):
                    W(vector, "vector", ac)
                    W(vector, "vector", s_st1, prev_marks["xlnst"])
                    dv.inc(vector.reciprocal(lni[i], lns[:, 5:6]))
                    W(vector, "vector", dv)
                    dv.inc(vector.tensor_scalar(xlnt[i], x1t[i], lnm[i], lni[i],
                                                ALU.subtract, ALU.mult))
                S("vector", _)
            xln_done = dv.n

            def _(scalar, t0=t0, xln_done=xln_done):
                W(scalar, "scalar", dv, xln_done)
                for i in range(GC):
                    s_st1.inc(scalar.dma_start(out=sxln_d[t0 + i * 128: t0 + (i + 1) * 128, :],
                                               in_=xlnt[i]), 16)
            S("scalar", _)
            prev_marks["x1st"] = s_st1.n
            prev_marks["xlnst"] = s_st1.n

            SLN = (4, 5, 6, 7)
            items3 = [(xlnt[i][:, k * 128:(k + 1) * 128], xlnfm[:, k * G + i * 128: k * G + (i + 1) * 128])
                      for k in range(NDK) for i in range(GC)]
            pe4, dv4 = pe.n, dv.n

            def _(tensor, items3=items3, xln_done=xln_done, dv4=dv4):
                W(tensor, "tensor", dv, xln_done)
                for j, (src, _dst) in enumerate(items3):
                    if j >= 4:
                        W(tensor, "tensor", dv, dv4 + j - 3)
                    pe.inc(TR(tensor, psb(SLN[j % 4], 128), src))
            S("tensor", _)

            def _(vector, items3=items3, pe4=pe4):
                for j, (_src, dst) in enumerate(items3):
                    W(vector, "vector", pe, pe4 + j + 1)
                    dv.inc(vector.tensor_copy(dst, psb(SLN[j % 4], 128)))
            S("vector", _)
            xlnfm_done = dv.n

            for i in range(GC):
                ci = ci0 + i
                bank = 4 + i % 2

                def _(tensor, i=i, bank=bank, xlnfm_done=xlnfm_done):
                    W(tensor, "tensor", dv, xlnfm_done)
                    for k in range(NDK):
                        ins = tensor.matmul(psb(bank, E),
                                            xlnfm[:, k * G + i * 128: k * G + (i + 1) * 128],
                                            rwt[:, k * E:(k + 1) * E],
                                            start=(k == 0), stop=(k == NDK - 1))
                    pe.inc(ins)
                S("tensor", _)
                rt_done = pe.n

                def _(vector, ci=ci, bank=bank, rt_done=rt_done):
                    W(vector, "vector", pe, rt_done)
                    dv.inc(vector.reduce_max(lgm, psb(bank, E), axis=AX))
                    W(vector, "vector", dv)
                    vector.tensor_scalar(lgnm, lgm, -1.0, None, ALU.mult)
                    dv.inc(vector.tensor_tensor(lgu, psb(bank, E),
                                                lgm.to_broadcast([128, E]), ALU.is_ge))
                    W(vector, "vector", dv)
                    dv.inc(vector.tensor_copy(lgf, lgu))
                    W(vector, "vector", dv)
                    dv.inc(vector.tensor_tensor(lgw, lgf, rev8, ALU.mult))
                    W(vector, "vector", dv)
                    dv.inc(vector.reduce_max(lgm2, lgw, axis=AX))
                    W(vector, "vector", dv)
                    dv.inc(vector.tensor_scalar(ecol[:, ci:ci + 1], lgm2, -1.0, float(E),
                                                ALU.mult, ALU.add))
                S("vector", _)
                sm1 = dv.n

                def _(scalar, bank=bank, sm1=sm1):
                    W(scalar, "scalar", dv, sm1)
                    ac.inc(scalar.activation(lge, psb(bank, E), AF.Exp, bias=lgnm))
                S("scalar", _)
                sm2 = ac.n

                def _(vector, ci=ci, sm2=sm2):
                    W(vector, "vector", ac, sm2)
                    dv.inc(vector.reduce_sum(lgs, lge, axis=AX))
                    W(vector, "vector", dv)
                    dv.inc(vector.reciprocal(pcol[:, ci:ci + 1], lgs))
                S("vector", _)

            prev_marks["wi_ev"] = [(dv, dv.n), (ac, ac.n)]
            prev_marks["gac"] = ac.n

            if (g % GPS) == GPS - 1 and g + 1 < NG:
                def _(vector):
                    W(vector, "vector", dv)
                    W(vector, "vector", pe)
                    vector.memset(hst[:], 0.0)
                    dv.inc(vector.memset(ctail[:], 0.0))
                S("vector", _)

        a_pe, a_dv, a_ac = pe.n, dv.n, ac.n
        a_st1 = s_st1.n

        eti_done = [0]
        def _(vector):
            W(vector, "vector", dv, a_dv)
            dv.inc(vector.memset(carry[:], 0.0))
        S("vector", _)

        for p2 in range(2):
            if p2 == 1:
                def _(tensor):
                    W(tensor, "tensor", dv)
                    pe.inc(TR(tensor, psb(6, E, parts=1), cnts[0:E, 0:1]))
                S("tensor", _)

                def _(vector):
                    W(vector, "vector", pe)
                    dv.inc(vector.tensor_copy(cntr, psb(6, E, parts=1)))
                S("vector", _)

                def _(tensor):
                    W(tensor, "tensor", dv)
                    pe.inc(tensor.matmul(psb(6, E, parts=NCORES), r1hot, cntr,
                                         start=True, stop=True))
                S("tensor", _)

                def _(vector):
                    W(vector, "vector", pe)
                    dv.inc(vector.tensor_copy(ccsb, psb(6, E, parts=NCORES)))
                S("vector", _)

                def _(gpsimd):
                    W(gpsimd, "gpsimd", dv)
                    s_gm.inc(gpsimd.dma_start(out=ccin_d.ap(), in_=ccsb), 16)
                    W(gpsimd, "gpsimd", s_gm)
                    cc.inc(gpsimd.collective_compute(
                        "AllReduce", ALU.add, replica_groups=[list(range(NCORES))],
                        ins=[ccin_d.ap().opt()], outs=[ccout_d.ap().opt()]))
                    W(gpsimd, "gpsimd", cc)
                    s_gm.inc(gpsimd.dma_start(out=ccall, in_=ccout_d.ap()), 16)
                S("gpsimd", _)

                def _(tensor):
                    W(tensor, "tensor", s_gm)
                    pe.inc(tensor.matmul(psb(6, E, parts=1), rpfx, ccall,
                                         start=True, stop=True))
                S("tensor", _)

                def _(vector):
                    W(vector, "vector", pe)
                    dv.inc(vector.tensor_copy(baser, psb(6, E, parts=1)))
                S("vector", _)

                def _(tensor):
                    W(tensor, "tensor", dv)
                    pe.inc(TR(tensor, psb(6, 1, parts=E), baser))
                S("tensor", _)

                def _(vector):
                    W(vector, "vector", pe)
                    dv.inc(vector.tensor_copy(basec, psb(6, 1, parts=E)))
                    W(vector, "vector", dv)
                    dv.inc(vector.tensor_scalar(pcq[:, 0:1], cnts, 1.0 / TT,
                                                -0.5 + 0.5 / TT, ALU.mult, ALU.add))
                    W(vector, "vector", dv)
                    dv.inc(vector.tensor_scalar(pcq[:, 0:1], cnts, TT - 1.0, 1.0 / TT,
                                                ALU.add, ALU.mult))
                    W(vector, "vector", dv)
                    dv.inc(vector.tensor_scalar(pcq[:, 1:2], pcq[:, 0:1],
                                                -0.5 + 1.0 / (2 * TT), None, ALU.add))
                    W(vector, "vector", dv)
                    dv.inc(vector.tensor_scalar(pcq[:, 2:3], pcq[:, 1:2], 12582912.0,
                                                None, ALU.add))
                    W(vector, "vector", dv)
                    dv.inc(vector.tensor_scalar(pcq[:, 3:4], pcq[:, 2:3], 12582912.0,
                                                None, ALU.subtract))
                    W(vector, "vector", dv)
                    dv.inc(vector.tensor_scalar(pcq[:, 0:1], pcq[:, 3:4], float(TT),
                                                None, ALU.mult))
                S("vector", _)
                pc_done = dv.n

                def _(tensor, pc_done=pc_done):
                    W(tensor, "tensor", dv, pc_done)
                    pe.inc(tensor.matmul(psb(6, 1, parts=E), tri8, pcq[:, 0:1],
                                         start=True, stop=True))
                S("tensor", _)

                def _(vector):
                    W(vector, "vector", pe)
                    dv.inc(vector.tensor_copy(poffm1, psb(6, 1, parts=E)))
                    W(vector, "vector", dv)
                    dv.inc(vector.tensor_tensor(cuminc, poffm1, pcq[:, 0:1], ALU.add))
                    W(vector, "vector", dv)
                    dv.inc(vector.tensor_scalar(poffm1, poffm1, -1.0, None, ALU.add))
                    dv.inc(vector.memset(carry[:], 0.0))
                S("vector", _)

                def _(vector):
                    W(vector, "vector", dv)
                    dv.inc(vector.tensor_tensor(etu, cuminc.to_broadcast([E, NT]),
                                                ktile, ALU.is_le))
                    W(vector, "vector", dv)
                    dv.inc(vector.tensor_copy(etf, etu))
                S("vector", _)

                def _(tensor):
                    W(tensor, "tensor", dv)
                    pe.inc(tensor.matmul(psb(6, NT, parts=1), onesc8, etf,
                                         start=True, stop=True))
                S("tensor", _)

                def _(vector):
                    W(vector, "vector", pe)
                    dv.inc(vector.tensor_copy(etr, psb(6, NT, parts=1)))
                    W(vector, "vector", dv)
                    dv.inc(vector.tensor_scalar(etfr, etr, float(E - 1), None, ALU.min))
                    W(vector, "vector", dv)
                    dv.inc(vector.tensor_copy(eti, etfr))
                S("vector", _)
                eti_done[0] = dv.n

            for tc in range(NTC):
                c0 = tc * TCH
                nqc = TCH // Q
                def _(tensor, tc=tc):
                    W(tensor, "tensor", dv)
                    pe.inc(TR(tensor, psb(7, 128, parts=nqc),
                              ecol[:, tc * nqc:(tc + 1) * nqc]))
                S("tensor", _)

                def _(vector):
                    W(vector, "vector", pe)
                    dv.inc(vector.tensor_copy(ef16[0:nqc, 0:128], psb(7, 128, parts=nqc)))
                S("vector", _)

                def _(gpsimd):
                    W(gpsimd, "gpsimd", dv)
                    s_gm.inc(gpsimd.dma_start(out=erow_c, in_=ef16[0:nqc, 0:128]), 16)
                S("gpsimd", _)
                ef_done = s_gm.n

                def _(tensor, ef_done=ef_done):
                    W(tensor, "tensor", s_gm, ef_done)
                    W(tensor, "tensor", dv)
                    for t4 in range(TCH // 512):
                        ins = tensor.matmul(PSA[0:E, t4 * 512:(t4 + 1) * 512],
                                            onesr[0:1, 0:E],
                                            erow_c[0:1, t4 * 512:(t4 + 1) * 512],
                                            start=True, stop=True)
                    pe.inc(ins)
                S("tensor", _)
                indmm = pe.n

                def _(vector, indmm=indmm):
                    W(vector, "vector", pe, indmm)
                    dv.inc(vector.tensor_tensor(tmq_c, PSA[0:E, 0:TCH],
                                                iota8c.to_broadcast([E, TCH]), ALU.is_equal))
                    W(vector, "vector", dv)
                    dv.inc(vector.tensor_copy(ind_c, tmq_c))
                    W(vector, "vector", dv)
                    dv.inc(vector.tensor_tensor_scan(po_c, ones8t[0:E, 0:TCH], ind_c,
                                                     carry[0:E, 0:1], ALU.mult, ALU.add))
                    W(vector, "vector", dv)
                    dv.inc(vector.tensor_copy(carry, po_c[0:E, TCH - 1:TCH]))
                S("vector", _)
                po_done = dv.n

                if p2 == 1:
                    def _(vector, po_done=po_done):
                        W(vector, "vector", dv, po_done)
                        dv.inc(vector.tensor_scalar(tmp_c, po_c, poffm1, None, ALU.add))
                        W(vector, "vector", dv)
                        dv.inc(vector.tensor_tensor(tmp_c, tmp_c, ind_c, ALU.mult))
                    S("vector", _)
                    srmm0 = dv.n

                    def _(tensor, srmm0=srmm0):
                        W(tensor, "tensor", dv, srmm0)
                        for t4 in range(TCH // 512):
                            ins = tensor.matmul(PSA[0:1, 2048 + t4 * 512: 2048 + (t4 + 1) * 512],
                                                onesc8, tmp_c[0:E, t4 * 512:(t4 + 1) * 512],
                                                start=True, stop=True)
                        pe.inc(ins)
                    S("tensor", _)

                    def _(vector):
                        W(vector, "vector", pe)
                        dv.inc(vector.tensor_copy(srow_c, PSA[0:1, 2048:2048 + TCH]))
                    S("vector", _)

                    def _(vector):
                        W(vector, "vector", dv)
                        dv.inc(vector.tensor_scalar(tmp_c, po_c, basec, None, ALU.add))
                        W(vector, "vector", dv)
                        dv.inc(vector.tensor_scalar(tmq_c, tmp_c, CCAP + 0.5, None, ALU.is_le))
                        W(vector, "vector", dv)
                        dv.inc(vector.tensor_copy(tmp_c, tmq_c))
                        W(vector, "vector", dv)
                        dv.inc(vector.tensor_tensor(tmp_c, tmp_c, ind_c, ALU.mult))
                    S("vector", _)
                    krmm0 = dv.n

                    def _(tensor, krmm0=krmm0):
                        W(tensor, "tensor", dv, krmm0)
                        for t4 in range(TCH // 512):
                            ins = tensor.matmul(PSA[0:1, 2048 + t4 * 512: 2048 + (t4 + 1) * 512],
                                                onesc8, tmp_c[0:E, t4 * 512:(t4 + 1) * 512],
                                                start=True, stop=True)
                        pe.inc(ins)
                    S("tensor", _)

                    def _(vector):
                        W(vector, "vector", pe)
                        dv.inc(vector.tensor_copy(krow_c, PSA[0:1, 2048:2048 + TCH]))
                    S("vector", _)
                    rows_ok = dv.n

                    def _(gpsimd, tc=tc, rows_ok=rows_ok):
                        W(gpsimd, "gpsimd", dv, rows_ok)
                        r0 = tc * (TCH // 128)
                        s_gm.inc(gpsimd.dma_start(out=srows[r0:r0 + TCH // 128, 0:128],
                                                   in_=srow_c), 16)
                        s_gm.inc(gpsimd.dma_start(out=krows[r0:r0 + TCH // 128, 0:128],
                                                   in_=krow_c), 16)
                    S("gpsimd", _)

            if p2 == 0:
                def _(vector):
                    W(vector, "vector", dv)
                    dv.inc(vector.tensor_copy(cnts, carry))
                S("vector", _)

        rows_all = s_gm.n

        def _(tensor, rows_all=rows_all):
            W(tensor, "tensor", s_gm, rows_all)
            pe.inc(TR(tensor, psb(6, NCH), srows[0:NROWB, 0:128]))
            pe.inc(TR(tensor, psb(7, NCH), krows[0:NROWB, 0:128]))
        S("tensor", _)

        def _(vector):
            W(vector, "vector", pe)
            vector.tensor_copy(scols, psb(6, NCH))
            dv.inc(vector.tensor_copy(kcols, psb(7, NCH)))
            W(vector, "vector", dv)
            vector.tensor_copy(scoli, scols)
            dv.inc(vector.tensor_tensor(scalec, kcols, pcol, ALU.mult))
        S("vector", _)
        cols_done = dv.n

        sg_scb = s_sc.n
        xl_base = [s_xl[0].n, s_xl[1].n]

        def _(sync, cols_done=cols_done, sg_scb=sg_scb):
            W(sync, "sync", dv, cols_done)
            W(sync, "sync", s_st1, a_st1)
            for c in range(NCH):
                if c >= 2:
                    W(sync, "sync", s_sc, sg_scb + ((c - 2) // 2 + 1) * 2 * 16)
                ins = sync.dma_start(out=xlt[c % 2], in_=sxln_d[c * 128:(c + 1) * 128, :])
                s_xl[c % 2].inc(ins, 16)
        S("sync", _)

        def _(gpsimd, cols_done=cols_done, xl_base=xl_base):
            W(gpsimd, "gpsimd", dv, cols_done)
            W(gpsimd, "gpsimd", s_zx, sxs_zeroed)
            for c in range(NCH):
                W(gpsimd, "gpsimd", s_xl[c % 2], xl_base[c % 2] + (c // 2 + 1) * 16)
                ins = gpsimd.indirect_dma_start(
                    out=sxs_d[0:NSLOT, :],
                    out_offset=bass.IndirectOffsetOnAxis(scoli[:, c:c + 1], 0),
                    in_=xlt[c % 2], in_offset=None)
                s_sc.inc(ins, 16)
        S("gpsimd", _)
        scat_done = s_sc.n

        for kt in range(NT):
            def _(sync, kt=kt, scat_done=scat_done):
                W(sync, "sync", s_sc, scat_done)
                W(sync, "sync", ac)
                W(sync, "sync", dv)
                for s2 in range(NSUB):
                    s_bw.inc(sync.dma_start(out=xlt[s2 % 2],
                                            in_=sxs_d[kt * TT + s2 * 128: kt * TT + (s2 + 1) * 128, :]), 16)
                s_bw.inc(dyndma(sync, b1sb, b1_b, eti[0:1, kt:kt + 1],
                                128 * NFK * 4, 0, (E - 1) * 128 * NFK * 4,
                                [[NFK * 4, 128], [1, NFK * 4]]), 16)
                s_bw.inc(dyndma(sync, b2sb, b2_b, eti[0:1, kt:kt + 1],
                                128 * NDK * 4, 0, (E - 1) * 128 * NDK * 4,
                                [[NDK * 4, 128], [1, NDK * 4]]), 16)
            S("sync", _)
            xs_ld = s_bw.n

            SLS = (2, 3, 6, 7)
            xsbf = xsfm.bitcast(BF16)
            items4 = [(xlt[s2 % 2][:, k * 128:(k + 1) * 128],
                       k * TT + s2 * 128)
                      for s2 in range(NSUB) for k in range(NDK)]
            pe5, dv5, ac5 = pe.n, dv.n, ac.n

            def _(tensor, items4=items4, xs_ld=xs_ld, dv5=dv5):
                W(tensor, "tensor", s_bw, xs_ld)
                W(tensor, "tensor", dv)
                for j, (src, _off) in enumerate(items4):
                    if j >= 4:
                        W(tensor, "tensor", dv, dv5 + j - 3)
                    pe.inc(TR(tensor, psb(SLS[j % 4], 128), src))
            S("tensor", _)

            def _(scalar, items4=items4, pe5=pe5, xsbf=xsbf):
                for j, (_src, off) in enumerate(items4):
                    W(scalar, "scalar", pe, pe5 + j + 1)
                    ac.inc(scalar.activation(xsbf[:, off:off + 128],
                                             psb(SLS[j % 4], 128), AF.Copy))
            S("scalar", _)

            def _(vector, items4=items4, ac5=ac5, xsbf=xsbf):
                for j, (_src, off) in enumerate(items4):
                    W(vector, "vector", ac, ac5 + j + 1)
                    dv.inc(vector.tensor_tensor(
                        xsbf[:, NDK * TT + off: NDK * TT + off + 128],
                        psb(SLS[j % 4], 128), xsbf[:, off:off + 128], ALU.subtract))
            S("vector", _)
            xsfm_done = dv.n

            hbf = hhl.bitcast(BF16)
            pe_fc0 = pe.n
            gelu_marks = {}
            for fc in range(8):
                hb = 0 if fc % 2 == 0 else 3072

                def _(sync, kt=kt, fc=fc, pe_fc0=pe_fc0):
                    if fc >= 2:
                        W(sync, "sync", pe, pe_fc0 + 3 * (fc - 2) + 2)
                    W(sync, "sync", s_sp, 10)
                    bfv = w1blk[fc % 2].bitcast(BF16)
                    RB = 2 * FF * 2
                    dst = bass.AP(bfv.tensor, bfv.offset,
                                  [list(bfv.ap[0]), [1024, NDK], [1, 1024]])
                    ins = dyndma(sync, dst, w1hl_b, eti[0:1, kt:kt + 1],
                                 DM * RB, fc * 1024 * 2, (E - 1) * DM * RB,
                                 [[RB, 128], [128 * RB, NDK], [1, 1024 * 2]])
                    s_w1[fc % 2].inc(ins, 16)
                S("sync", _)
                w1_ld = s_w1[fc % 2].n

                def _(scalar, kt=kt, fc=fc, pe_fc0=pe_fc0):
                    W(scalar, "scalar", dv, eti_done[0])
                    W(scalar, "scalar", s_sp, 11)
                    if fc >= 2:
                        W(scalar, "scalar", pe, pe_fc0 + 3 * (fc - 2) + 3)
                    bfv = w2blk[fc % 2].bitcast(BF16)
                    RB2 = 2 * DM * 2
                    dst = bass.AP(bfv.tensor, bfv.offset,
                                  [list(bfv.ap[0]), [2048, 4], [1, 2048]])
                    ins = dyndma(scalar, dst, w2hl_b, eti[0:1, kt:kt + 1],
                                 FF * RB2, fc * 512 * RB2, (E - 1) * FF * RB2,
                                 [[RB2, 128], [128 * RB2, 4], [1, RB2]])
                    s_w2[fc % 2].inc(ins, 16)
                S("scalar", _)
                w2_ld = s_w2[fc % 2].n

                def _(tensor, kt=kt, fc=fc, w1_ld=w1_ld, xsfm_done=xsfm_done, hb=hb,
                      gm2=gelu_marks.get(fc - 2, 0)):
                    W(tensor, "tensor", s_w1[fc % 2], w1_ld)
                    W(tensor, "tensor", dv, xsfm_done)
                    if fc >= 2:
                        W(tensor, "tensor", ac, gm2)
                    bfv = w1blk[fc % 2].bitcast(BF16)
                    xsbf = xsfm.bitcast(BF16)
                    for s2 in range(4):
                        bank = s2 // 2
                        off = hb + bank * 512 + (s2 % 2) * TT
                        for kc in range(NDK):
                            dst = PSA[:, off:off + TT]
                            wh = bfv[:, kc * 1024 + s2 * 128: kc * 1024 + (s2 + 1) * 128]
                            wl = bfv[:, kc * 1024 + 512 + s2 * 128:
                                     kc * 1024 + 512 + (s2 + 1) * 128]
                            xh = xsbf[:, kc * TT:(kc + 1) * TT]
                            xl = xsbf[:, NDK * TT + kc * TT: NDK * TT + (kc + 1) * TT]
                            tensor.matmul(dst, wh, xh,
                                          start=(kc == 0 and s2 % 2 == 0), stop=False)
                            tensor.matmul(dst, wh, xl, start=False, stop=False)
                            ins = tensor.matmul(
                                dst, wl, xh, start=False,
                                stop=(kc == NDK - 1 and s2 % 2 == 1))
                        if s2 % 2 == 1:
                            pe.inc(ins)
                S("tensor", _)
                hmm_done = pe.n

                gelu_ac0 = ac.n

                def _(scalar, fc=fc, hmm_done=hmm_done, xs_ld=xs_ld, hbf=hbf, hb=hb):
                    W(scalar, "scalar", s_bw, xs_ld)
                    for s2 in range(4):
                        W(scalar, "scalar", pe, hmm_done - 1 + s2 // 2)
                        bank = s2 // 2
                        off = hb + bank * 512 + (s2 % 2) * TT
                        jglob = fc * 4 + s2
                        scalar.activation(hsb[:, s2 * TT:(s2 + 1) * TT],
                                          PSA[:, off:off + TT], AF.Gelu,
                                          bias=b1sb[:, jglob:jglob + 1])
                        ac.inc(scalar.activation(hbf[:, s2 * TT:(s2 + 1) * TT],
                                                 PSA[:, off:off + TT], AF.Gelu,
                                                 bias=b1sb[:, jglob:jglob + 1]))
                S("scalar", _)
                gelu_done = ac.n
                gelu_marks[fc] = gelu_done

                dvh0 = dv.n

                def _(vector, fc=fc, gelu_ac0=gelu_ac0, hbf=hbf):
                    for s2 in range(4):
                        W(vector, "vector", ac, gelu_ac0 + s2 + 1)
                        dv.inc(vector.tensor_tensor(
                            hbf[:, 4 * TT + s2 * TT: 4 * TT + (s2 + 1) * TT],
                            hsb[:, s2 * TT:(s2 + 1) * TT],
                            hbf[:, s2 * TT:(s2 + 1) * TT], ALU.subtract))
                S("vector", _)

                def _(tensor, kt=kt, fc=fc, gelu_ac0=gelu_ac0, dvh0=dvh0, w2_ld=w2_ld,
                      hbf=hbf):
                    W(tensor, "tensor", s_w2[fc % 2], w2_ld)
                    bfv = w2blk[fc % 2].bitcast(BF16)
                    for s2 in range(4):
                        W(tensor, "tensor", ac, gelu_ac0 + s2 + 1)
                        W(tensor, "tensor", dv, dvh0 + s2 + 1)
                        hh = hbf[:, s2 * TT:(s2 + 1) * TT]
                        hl = hbf[:, 4 * TT + s2 * TT: 4 * TT + (s2 + 1) * TT]
                        for j in range(8):
                            bank = 2 + j // 2
                            off = bank * 512 + (j % 2) * TT
                            dst = PSA[:, off:off + TT]
                            wh = bfv[:, s2 * 2048 + j * 128: s2 * 2048 + (j + 1) * 128]
                            wl = bfv[:, s2 * 2048 + 1024 + j * 128:
                                     s2 * 2048 + 1024 + (j + 1) * 128]
                            tensor.matmul(dst, wh, hh,
                                          start=(fc == 0 and s2 == 0 and j % 2 == 0),
                                          stop=False)
                            tensor.matmul(dst, wh, hl, start=False, stop=False)
                            ins = tensor.matmul(
                                dst, wl, hh, start=False,
                                stop=(fc == 7 and s2 == 3 and j % 2 == 1))
                    pe.inc(ins)
                S("tensor", _)
            eomm_done = pe.n

            def _(vector, kt=kt, eomm_done=eomm_done, xs_ld=xs_ld):
                W(vector, "vector", pe, eomm_done)
                W(vector, "vector", s_bw, xs_ld)
                for j in range(8):
                    bank = 2 + j // 2
                    off = bank * 512 + (j % 2) * TT
                    ins = vector.tensor_scalar(eofm[:, j * TT:(j + 1) * TT],
                                               PSA[:, off:off + TT],
                                               b2sb[:, j:j + 1], None, ALU.add)
                dv.inc(ins)
            S("vector", _)
            eo_done = dv.n

            SLE = (0, 1, 6, 7)
            items5 = [(eofm[:, j * TT + s2 * 128: j * TT + (s2 + 1) * 128], s2, j)
                      for s2 in range(NSUB) for j in range(NDK)]
            pe6, dv6 = pe.n, dv.n

            def _(tensor, items5=items5, eo_done=eo_done, dv6=dv6):
                W(tensor, "tensor", dv, eo_done)
                W(tensor, "tensor", ac)
                for jj, (src, s2, j) in enumerate(items5):
                    if jj >= 4:
                        W(tensor, "tensor", dv, dv6 + jj - 3)
                    pe.inc(TR(tensor, psb(SLE[jj % 4], 128), src))
            S("tensor", _)

            def _(vector, items5=items5, pe6=pe6):
                W(vector, "vector", s_mo)
                for jj, (_src, s2, j) in enumerate(items5):
                    W(vector, "vector", pe, pe6 + jj + 1)
                    dv.inc(vector.tensor_copy(eotm[s2][:, j * 128:(j + 1) * 128],
                                              psb(SLE[jj % 4], 128)))
            S("vector", _)
            eot_done = dv.n

            def _(scalar, kt=kt, eot_done=eot_done):
                W(scalar, "scalar", dv, eot_done)
                for s2 in range(NSUB):
                    s_mo.inc(scalar.dma_start(out=smoe_d[kt * TT + s2 * 128: kt * TT + (s2 + 1) * 128, :],
                                              in_=eotm[s2]), 16)
            S("scalar", _)
        moe_st = s_mo.n

        x1f_base = [s_x1f[0].n, s_x1f[1].n]
        mot_base = [s_mg[0].n, s_mg[1].n]
        out_dvb = dv.n
        st_base = sfin.n

        def _(sync, out_dvb=out_dvb):
            W(sync, "sync", s_st1, a_st1)
            for c in range(NCH):
                if c >= 2:
                    W(sync, "sync", dv, out_dvb + c - 1)
                ins = sync.dma_start(out=x1f[c % 2], in_=sx1_d[c * 128:(c + 1) * 128, :])
                s_x1f[c % 2].inc(ins, 16)
        S("sync", _)

        def _(gpsimd, moe_st=moe_st, out_dvb=out_dvb):
            W(gpsimd, "gpsimd", s_mo, moe_st)
            for c in range(NCH):
                if c >= 2:
                    W(gpsimd, "gpsimd", dv, out_dvb + c - 1)
                ins = gpsimd.indirect_dma_start(
                    out=mot[c % 2], out_offset=None,
                    in_=smoe_d[0:NSLOT, :],
                    in_offset=bass.IndirectOffsetOnAxis(scoli[:, c:c + 1], 0))
                s_mg[c % 2].inc(ins, 16)
        S("gpsimd", _)

        def _(vector, x1f_base=x1f_base, mot_base=mot_base, st_base=st_base):
            for c in range(NCH):
                W(vector, "vector", s_x1f[c % 2], x1f_base[c % 2] + (c // 2 + 1) * 16)
                W(vector, "vector", s_mg[c % 2], mot_base[c % 2] + (c // 2 + 1) * 16)
                if c >= 2:
                    W(vector, "vector", sfin, st_base + ((c - 2) // 2 + 1) * 2 * 16)
                dv.inc(vector.scalar_tensor_tensor(outt[c % 2], mot[c % 2],
                                                   scalec[:, c:c + 1], x1f[c % 2],
                                                   ALU.mult, ALU.add))
        S("vector", _)

        def _(scalar, out_dvb=out_dvb):
            for c in range(NCH):
                W(scalar, "scalar", dv, out_dvb + c + 1)
                sfin.inc(scalar.dma_start(out=out_d[c * 128:(c + 1) * 128, :],
                                          in_=outt[c % 2]), 16)
            W(scalar, "scalar", sfin, sfin.n)
        S("scalar", _)

    return nc


_NC_CACHE = {}


def _get_nc(key, cfg):
    if key not in _NC_CACHE:
        _NC_CACHE[key] = build(cfg)
    return _NC_CACHE[key]


def prep_inputs(core, cfg, L0_attn_x, L0_attn_W_in, L0_attn_conv_w, L0_attn_conv_b,
                L0_attn_W_x, L0_attn_A_log, L0_attn_D_param, L0_attn_W_out,
                L0_ffn_router_w, L0_ffn_w1, L0_ffn_b1, L0_ffn_w2, L0_ffn_b2):
    NB = cfg.NB
    f32 = np.float32
    x = np.ascontiguousarray(L0_attn_x[core * NB:(core + 1) * NB]).reshape(cfg.TOKS, DM)
    wx = np.asarray(L0_attn_W_x, f32)
    d = {
        "x0": np.asarray(x, f32),
        "w_in": np.ascontiguousarray(np.asarray(L0_attn_W_in, f32)),
        "conv_w": np.ascontiguousarray(
            np.asarray(L0_attn_conv_w, f32).reshape(NDC, 128, DCV).transpose(1, 0, 2).reshape(128, NDC * DCV)),
        "conv_b": np.ascontiguousarray(np.asarray(L0_attn_conv_b, f32).reshape(NDC, 128).T),
        "wx_d": np.ascontiguousarray(wx[:, 0].reshape(NDC, 128).T),
        "wx_b": np.ascontiguousarray(
            wx[:, 1:1 + DS].reshape(NDC, 128, DS).transpose(1, 0, 2).reshape(128, NDC * DS)),
        "wx_c": np.ascontiguousarray(
            wx[:, 1 + DS:1 + 2 * DS].reshape(NDC, 128, DS).transpose(1, 0, 2).reshape(128, NDC * DS)),
        "a_log": np.asarray(L0_attn_A_log, f32).reshape(1, DS),
        "d_param": np.ascontiguousarray(np.asarray(L0_attn_D_param, f32).reshape(NDC, 128).T),
        "w_out": np.ascontiguousarray(np.asarray(L0_attn_W_out, f32)),
        "rw_t": np.ascontiguousarray(
            np.asarray(L0_ffn_router_w, f32).T.reshape(NDK, 128, E).transpose(1, 0, 2).reshape(128, NDK * E)),
        "w1": np.ascontiguousarray(np.asarray(L0_ffn_w1, f32).reshape(E * DM, FF)),
        "w2": np.ascontiguousarray(np.asarray(L0_ffn_w2, f32).reshape(E * FF, DM)),
        "b1t": np.ascontiguousarray(
            np.asarray(L0_ffn_b1, f32).reshape(E, NFK, 128).transpose(0, 2, 1).reshape(E * 128, NFK)),
        "b2t": np.ascontiguousarray(
            np.asarray(L0_ffn_b2, f32).reshape(E, NDK, 128).transpose(0, 2, 1).reshape(E * 128, NDK)),
        "r1hot": np.eye(NCORES, dtype=f32)[core].reshape(1, NCORES),
        "rpfx": (np.arange(NCORES) < core).astype(f32).reshape(NCORES, 1),
    }
    return d


def kernel(**inputs):
    cfg = Cfg(NB=4, L=2048, G=256, TT=256, CCAP=12288)
    nc = _get_nc("prod", cfg)
    in_maps = [prep_inputs(c, cfg, **inputs) for c in range(NCORES)]
    res = run_bass_kernel_spmd(nc, in_maps, core_ids=list(range(NCORES)))
    outs = [np.asarray(r["out"]).reshape(cfg.NB, cfg.L, DM) for r in res.results]
    return np.concatenate(outs, axis=0)


# version 08
# speedup vs baseline: 1.2143x; 1.0079x over previous
import math

import numpy as np

import concourse.bass as bass
import concourse.mybir as mybir
from concourse.bass_utils import run_bass_kernel_spmd

F32 = mybir.dt.float32
BF16 = mybir.dt.bfloat16
I32 = mybir.dt.int32
U32 = mybir.dt.uint32
AF = mybir.ActivationFunctionType
ALU = mybir.AluOpType
AX = mybir.AxisListType.X

NCORES = 8
DM = 1024
DI = 2048
DS = 16
DCV = 4
E = 8
FF = 4096
Q = 128
NDK = DM // 128
NDC = DI // 128
NFK = FF // 128


class Cfg:
    def __init__(self, NB=4, L=2048, G=256, TT=256, CCAP=12288):
        self.NB, self.L, self.G, self.TT, self.CCAP = NB, L, G, TT, CCAP
        self.TOKS = NB * L
        self.GPS = L // G
        self.NG = NB * self.GPS
        self.GC = G // Q
        self.NCH = self.TOKS // Q
        self.NT = self.TOKS // TT + E
        self.NSLOT = self.NT * TT
        self.TCH = min(1024, self.TOKS)
        self.NTC = self.TOKS // self.TCH
        assert L % G == 0 and G % Q == 0 and TT % 128 == 0
        assert self.TOKS % self.TCH == 0 and self.TCH % Q == 0


class Sem:
    def __init__(self, nc, name):
        self.h = nc.alloc_semaphore(name)
        self.n = 0
        self.name = name

    def inc(self, inst, k=1):
        inst.then_inc(self.h, k)
        self.n += k
        return inst


class Pool:

    def __init__(self, nc, name, elems_per_part):
        self.base = nc.alloc_sbuf_tensor(name, [128, elems_per_part], F32).ap()
        self.off = 0
        self.cap = elems_per_part

    def take(self, nelem, dtype=F32, parts=128):
        o = self.off
        self.off += (nelem + 15) & ~15
        assert self.off <= self.cap, f"pool overflow {self.off} > {self.cap}"
        v = self.base[0:parts, o:o + nelem]
        if dtype is not F32:
            v = v.bitcast(dtype)
        return v

    def mark(self):
        return self.off

    def reset(self, m=0):
        self.off = m


def build(cfg: Cfg):
    nc = bass.Bass("TRN2", target_bir_lowering=False, debug=False, num_devices=NCORES)
    NB, L, G, TT, CCAP = cfg.NB, cfg.L, cfg.G, cfg.TT, cfg.CCAP
    TOKS, GPS, NG, GC, NCH = cfg.TOKS, cfg.GPS, cfg.NG, cfg.GC, cfg.NCH
    NT, NSLOT, TCH, NTC = cfg.NT, cfg.NSLOT, cfg.TCH, cfg.NTC
    CONVW = G + DCV - 1
    NSUB = TT // 128
    NROWB = TOKS // 128
    assert NROWB <= 128 and NT <= 128

    x0_d = nc.dram_tensor("x0", [TOKS, DM], F32, kind="ExternalInput").ap()
    wi_d = nc.dram_tensor("w_in", [DM, 2 * DI], F32, kind="ExternalInput").ap()
    cw_d = nc.dram_tensor("conv_w", [128, NDC * DCV], F32, kind="ExternalInput").ap()
    cb_d = nc.dram_tensor("conv_b", [128, NDC], F32, kind="ExternalInput").ap()
    wxd_d = nc.dram_tensor("wx_d", [128, NDC], F32, kind="ExternalInput").ap()
    wxb_d = nc.dram_tensor("wx_b", [128, NDC * DS], F32, kind="ExternalInput").ap()
    wxc_d = nc.dram_tensor("wx_c", [128, NDC * DS], F32, kind="ExternalInput").ap()
    alog_d = nc.dram_tensor("a_log", [1, DS], F32, kind="ExternalInput").ap()
    dp_d = nc.dram_tensor("d_param", [128, NDC], F32, kind="ExternalInput").ap()
    wo_d = nc.dram_tensor("w_out", [DI, DM], F32, kind="ExternalInput").ap()
    rwt_d = nc.dram_tensor("rw_t", [128, NDK * E], F32, kind="ExternalInput").ap()
    w1_d = nc.dram_tensor("w1", [E * DM, FF], F32, kind="ExternalInput").ap()
    w2_d = nc.dram_tensor("w2", [E * FF, DM], F32, kind="ExternalInput").ap()
    b1_d = nc.dram_tensor("b1t", [E * 128, NFK], F32, kind="ExternalInput").ap()
    b2_d = nc.dram_tensor("b2t", [E * 128, NDK], F32, kind="ExternalInput").ap()
    r1hot_d = nc.dram_tensor("r1hot", [1, NCORES], F32, kind="ExternalInput").ap()
    rpfx_d = nc.dram_tensor("rpfx", [NCORES, 1], F32, kind="ExternalInput").ap()

    out_d = nc.dram_tensor("out", [TOKS, DM], F32, kind="ExternalOutput").ap()

    sx1_d = nc.dram_tensor("s_x1", [TOKS, DM], F32, kind="Internal").ap()
    sxln_d = nc.dram_tensor("s_xln", [TOKS, DM], F32, kind="Internal").ap()
    sxs_d = nc.dram_tensor("s_xs", [NSLOT, DM], F32, kind="Internal").ap()
    smoe_d = nc.dram_tensor("s_moe", [NSLOT, DM], F32, kind="Internal").ap()
    winhl_d = nc.dram_tensor("winhl", [DM, 4 * DI], BF16, kind="Internal").ap()
    wohl_d = nc.dram_tensor("wohl", [DI, 2 * DM], BF16, kind="Internal").ap()
    w1hl_d = nc.dram_tensor("w1hl", [E * DM, 2 * FF], BF16, kind="Internal").ap()
    w2hl_d = nc.dram_tensor("w2hl", [E * FF, 2 * DM], BF16, kind="Internal").ap()
    ccin_d = nc.dram_tensor("cc_in", [NCORES, E], F32, kind="Internal")
    ccout_d = nc.dram_tensor("cc_out", [NCORES, E], F32, kind="Internal")

    ident_c = nc.inline_tensor(np.eye(128, dtype=np.float32), name="c_ident")
    mS = np.where(np.arange(128)[None, :] >= np.arange(128)[:, None], 0.0, 1e9).astype(np.float32)
    maskS_c = nc.inline_tensor(mS, name="c_maskS")
    iota8c_c = nc.inline_tensor(np.arange(E, dtype=np.float32).reshape(E, 1), name="c_iota8c")
    rev8_c = nc.inline_tensor(np.tile(np.arange(E, 0, -1).astype(np.float32), (128, 1)), name="c_rev8")
    tri8_c = nc.inline_tensor(np.triu(np.ones((E, E), np.float32), 1), name="c_tri8")
    ktile_c = nc.inline_tensor(np.tile((np.arange(NT) * TT).astype(np.float32), (E, 1)), name="c_ktile")

    def sb(name, shape, dtype=F32):
        return nc.alloc_sbuf_tensor(name, shape, dtype).ap()

    ident = sb("ident", [128, 128])
    maskS = sb("maskS", [128, 128])
    iota8c = sb("iota8c", [E, 1])
    rev8 = sb("rev8", [128, E])
    tri8 = sb("tri8", [E, E])
    ktile = sb("ktile", [E, NT])
    convw = sb("convw", [128, NDC * DCV])
    convb = sb("convb", [128, NDC])
    wxd = sb("wxd", [128, NDC])
    wxb = sb("wxb", [128, NDC * DS])
    wxc = sb("wxc", [128, NDC * DS])
    dcol = sb("dcol", [128, NDC])
    rwt = sb("rwt", [128, NDK * E])
    alog = sb("alog", [1, DS])
    arow = sb("arow", [1, DS])
    abc = sb("abc", [128, DS])
    hst = sb("hst", [48, DI])
    ctail = sb("ctail", [128, NDC * (DCV - 1)])
    ecol = sb("ecol", [128, NCH])
    pcol = sb("pcol", [128, NCH])
    onesr = sb("onesr", [1, 512])
    onesc8 = sb("onesc8", [E, 1])
    ones8t = sb("ones8t", [E, TCH])
    wspi = sb("wspi", [128, 1024])
    wsph = sb("wsph", [128, 1024], BF16)
    wspl = sb("wspl", [128, 1024], BF16)
    gpad = sb("gpad", [1, 1])
    zt = sb("zt", [128, 512])
    scols = sb("scols", [128, NCH])
    scoli = sb("scoli", [128, NCH], I32)
    kcols = sb("kcols", [128, NCH])
    scalec = sb("scalec", [128, NCH])
    cnts = sb("cnts", [E, 1])
    carry = sb("carry", [E, 1])
    basec = sb("basec", [E, 1])
    poffm1 = sb("poffm1", [E, 1])
    cuminc = sb("cuminc", [E, 1])
    kcst = sb("kcst", [128, 4])
    r1hot = sb("r1hot_s", [1, NCORES])
    rpfx = sb("rpfx_s", [NCORES, 1])

    pool = Pool(nc, "pool", (nc.sbuf_bytes_remaining - 512) // 4)

    x0t = [pool.take(DM) for _ in range(2)]
    xnt = [pool.take(DM) for _ in range(2)]
    sqt = pool.take(DM)
    lnm = [pool.take(1) for _ in range(2)]
    lni = [pool.take(1) for _ in range(2)]
    lns = pool.take(8)
    xnfm = pool.take(NDK * G)
    wiblk = [pool.take(1024) for _ in range(3)]
    xssm = pool.take(NDC * CONVW)
    szfm = pool.take(NDC * G)
    xcsfm = pool.take(NDC * G)
    xcdtm = pool.take(DI)
    dpk = pool.take(G)
    dabs = dpk[0:1, :]
    drel = dpk[32:33, :]
    drow = pool.take(G, parts=1)
    dexp = pool.take(G, parts=1)
    dln = pool.take(G, parts=1)
    bfm = pool.take(G, parts=DS)
    cfm = pool.take(G, parts=DS)
    crow = pool.take(Q, parts=1)
    negc = pool.take(Q, parts=1)
    argrow = pool.take(2 * Q + 1, parts=1)
    ekall = pool.take(2 * Q + 1, parts=DS)
    dmt = pool.take(128)
    estk = pool.take((DS // 2) * Q)
    cflat = pool.take(DS * Q, parts=1)
    bttm = pool.take(DS)
    kbfm = pool.take(Q, parts=DS)
    qbfm = pool.take(Q, parts=48)
    kbtm = pool.take(DS)
    dcolq = pool.take(1)
    mtmp = pool.take((DS // 2) * Q)
    mtt = pool.take(128)
    y3fm = pool.take(NDC * G)
    woblk = [pool.take(DM) for _ in range(3)]
    x1t = [pool.take(DM) for _ in range(2)]
    xlnt = [pool.take(DM) for _ in range(2)]
    xlnfm = pool.take(NDK * G)
    lgu = pool.take(E, U32)
    lgf = pool.take(E)
    lgw = pool.take(E)
    lgm = pool.take(1)
    lgm2 = pool.take(1)
    lgnm = pool.take(1)
    lge = pool.take(E)
    lgs = pool.take(1)
    a_mark = pool.mark()

    pool.reset(0)
    erow_c = pool.take(TCH, parts=1)
    ind_c = pool.take(TCH, parts=E)
    po_c = pool.take(TCH, parts=E)
    tmp_c = pool.take(TCH, parts=E)
    tmq_c = pool.take(TCH, dtype=U32, parts=E)
    ef16 = pool.take(128, parts=16)
    srow_c = pool.take(TCH, parts=1)
    krow_c = pool.take(TCH, parts=1)
    srows = pool.take(128, parts=NROWB)
    krows = pool.take(128, parts=NROWB)
    cntr = pool.take(E, parts=1)
    ccsb = pool.take(E, parts=NCORES)
    ccall = pool.take(E, parts=NCORES)
    baser = pool.take(E, parts=1)
    pcq = pool.take(4, parts=E)
    eti = sb("eti", [1, NT], I32)
    etu = pool.take(NT, dtype=U32, parts=E)
    etf = pool.take(NT, parts=E)
    etr = pool.take(NT, parts=1)
    etfr = pool.take(NT, parts=1)
    xlt = [pool.take(DM) for _ in range(2)]
    w1blk = [pool.take(NDK * 512) for _ in range(2)]
    w2blk = [pool.take(4 * DM) for _ in range(2)]
    xsfm = pool.take(NDK * TT)
    hsb = pool.take(4 * TT)
    hhl = pool.take(4 * TT)
    eofm = pool.take(NDK * TT)
    eotm = [pool.take(DM) for _ in range(NSUB)]
    b1sb = pool.take(NFK)
    b2sb = pool.take(NDK)
    mot = [pool.take(DM) for _ in range(2)]
    x1f = [pool.take(DM) for _ in range(2)]
    outt = [pool.take(DM) for _ in range(2)]
    b_mark = pool.mark()
    assert max(a_mark, b_mark) <= pool.cap

    PSA = nc.alloc_psum_tensor("PSA", [128, 4096], F32).ap()

    def psb(bank, nelem, parts=128):
        return PSA[0:parts, bank * 512: bank * 512 + nelem]

    def TR(tensor, out, in_):
        p = in_.shape[0]
        return tensor.transpose(out, in_, ident[0:p, 0:p])

    U8 = mybir.dt.uint8

    def dyndma(eng, dst, src_ap, idx_ap, scale_b, add_b, max_b, dims):
        with eng.register() as r:
            eng.reg_load(r, idx_ap)
            eng.reg_alu(r, r, scale_b, ALU.mult)
            if add_b:
                eng.reg_alu(r, r, add_b, ALU.add)
            rv = eng.snap(r, min_val=0, max_val=max_b)
            src = bass.AP(src_ap.tensor, rv, dims)
            ins = eng.dma_start(out=dst.bitcast(U8), in_=src)
            try:
                for reg in rv.val:
                    eng.free_register(reg)
            except TypeError:
                eng.free_register(rv.val)
        return ins

    w1hl_b = w1hl_d.bitcast(U8)
    w2hl_b = w2hl_d.bitcast(U8)
    b1_b = b1_d.bitcast(U8)
    b2_b = b2_d.bitcast(U8)

    s_ms = Sem(nc, "s_ms")
    s_ws2 = [Sem(nc, "s_wsA"), Sem(nc, "s_wsB")]
    s_wt2 = [Sem(nc, "s_wtA"), Sem(nc, "s_wtB")]
    s_gp = Sem(nc, "s_gp")
    s_sp = Sem(nc, "s_sp")
    s_gm = Sem(nc, "s_gm")
    s_x0 = Sem(nc, "s_x0")
    s_wi = [Sem(nc, f"s_wi{i}") for i in range(3)]
    s_wo = [Sem(nc, f"s_wo{i}") for i in range(3)]
    s_st1 = Sem(nc, "s_st1")
    s_cf = Sem(nc, "s_cf")
    s_zx = Sem(nc, "s_zx")
    s_xl = [Sem(nc, "s_xlA"), Sem(nc, "s_xlB")]
    s_sc = Sem(nc, "s_sc")
    s_bw = Sem(nc, "s_bw")
    s_w1 = [Sem(nc, "s_w1A"), Sem(nc, "s_w1B")]
    s_w2 = [Sem(nc, "s_w2A"), Sem(nc, "s_w2B")]
    s_mo = Sem(nc, "s_mo")
    s_x1f = [Sem(nc, "s_x1fA"), Sem(nc, "s_x1fB")]
    s_mg = [Sem(nc, "s_mgA"), Sem(nc, "s_mgB")]
    pe = Sem(nc, "pe")
    dv = Sem(nc, "dv")
    ac = Sem(nc, "ac")
    cc = Sem(nc, "cc")
    sfin = Sem(nc, "sfin")

    waited = {}

    def W(eng, en, sem, val=None):
        v = sem.n if val is None else val
        if v <= 0:
            return
        k = (en, sem.name)
        if waited.get(k, -1) < v:
            eng.wait_ge(sem.h, v)
            waited[k] = v

    with nc.Block() as block:
        RUN = {"sync": block.sync, "scalar": block.scalar, "vector": block.vector,
               "tensor": block.tensor, "gpsimd": block.gpsimd}

        def S(name, fn):
            RUN[name](fn)

        def _(sync):
            for dst, src in [(ident, ident_c.ap()), (maskS, maskS_c.ap()),
                             (iota8c, iota8c_c.ap()), (rev8, rev8_c.ap()),
                             (tri8, tri8_c.ap()), (ktile, ktile_c.ap()),
                             (convw, cw_d), (convb, cb_d), (wxd, wxd_d),
                             (wxb, wxb_d), (wxc, wxc_d), (dcol, dp_d),
                             (rwt, rwt_d), (alog, alog_d),
                             (r1hot, r1hot_d), (rpfx, rpfx_d)]:
                s_ms.inc(sync.dma_start(out=dst, in_=src), 16)
        S("sync", _)

        def _(vector):
            W(vector, "vector", s_ms)
            vector.memset(onesr[:], 1.0)
            vector.memset(kcst[:, 0:1], 1e-5)
            vector.memset(kcst[:, 1:2], 1.0)
            vector.memset(kcst[:, 2:3], -1.0)
            vector.memset(onesc8[:], 1.0)
            vector.memset(ones8t[:], 1.0)
            vector.memset(zt[:], 0.0)
            vector.memset(ctail[:], 0.0)
            dv.inc(vector.memset(hst[:], 0.0))
        S("vector", _)

        def _(scalar):
            W(scalar, "scalar", s_ms)
            ac.inc(scalar.activation(arow[:], alog[:], AF.Exp))
        S("scalar", _)

        def _(vector):
            W(vector, "vector", ac)
            dv.inc(vector.tensor_scalar_mul(arow[:], arow[:], -1.0))
        S("vector", _)

        def _(tensor):
            W(tensor, "tensor", dv)
            pe.inc(tensor.matmul(psb(6, DS), onesr[0:1, 0:128], arow[:], start=True, stop=True))
        S("tensor", _)

        def _(vector):
            W(vector, "vector", pe)
            dv.inc(vector.tensor_copy(abc[:], psb(6, DS)))
        S("vector", _)
        abc_done = dv.n

        sjobs = []
        for cb in range(2 * DI // 512):
            for rb in range(DM // 128):
                r = cb // 2
                hc = r * 2048 + (cb % 2) * 512
                sjobs.append((wi_d, winhl_d, rb, cb, hc, hc + 1024))
        for rb in range(DI // 128):
            for cb in range(DM // 512):
                sjobs.append((wo_d, wohl_d, rb, cb, cb * 512, DM + cb * 512))
        for rb in range(E * DM // 128):
            for cb in range(FF // 512):
                sjobs.append((w1_d, w1hl_d, rb, cb, cb * 1024, cb * 1024 + 512))
        for rb in range(E * FF // 128):
            for cb in range(DM // 512):
                sjobs.append((w2_d, w2hl_d, rb, cb, cb * 512, DM + cb * 512))
        NWIC = 2 * DI // 512 * (DM // 128)
        NWOC = (DI // 128) * (DM // 512)
        NW1C = (E * DM // 128) * (FF // 512)
        miles = {cb * (DM // 128) + (DM // 128) - 1: cb + 1
                 for cb in range(2 * DI // 512)}
        miles[NWIC + NWOC - 1] = 9
        miles[NWIC + NWOC + NW1C - 1] = 10
        miles[len(sjobs) - 1] = 11

        def _(gpsimd):
            def ld(c):
                src, _dst, rb, cb = sjobs[c][:4]
                i = c % 2
                s_ws2[i].inc(gpsimd.dma_start(
                    out=wspi[:, i * 512:(i + 1) * 512],
                    in_=src[rb * 128:(rb + 1) * 128, cb * 512:(cb + 1) * 512]), 16)
            ld(0)
            ld(1)
            for c in range(len(sjobs)):
                _src, dst, rb, cb, hc, lc = sjobs[c]
                i = c % 2
                W(gpsimd, "gpsimd", s_ws2[i], 16 * (c // 2 + 1))
                if c >= 2:
                    W(gpsimd, "gpsimd", s_wt2[i], 32 * (c // 2))
                s_gp.inc(gpsimd.tensor_copy(wsph[:, i * 512:(i + 1) * 512],
                                            wspi[:, i * 512:(i + 1) * 512]))
                W(gpsimd, "gpsimd", s_gp, 2 * c + 1)
                s_gp.inc(gpsimd.tensor_tensor(wspl[:, i * 512:(i + 1) * 512],
                                              wspi[:, i * 512:(i + 1) * 512],
                                              wsph[:, i * 512:(i + 1) * 512],
                                              ALU.subtract))
                W(gpsimd, "gpsimd", s_gp, 2 * c + 2)
                s_wt2[i].inc(gpsimd.dma_start(
                    out=dst[rb * 128:(rb + 1) * 128, hc:hc + 512],
                    in_=wsph[:, i * 512:(i + 1) * 512]), 16)
                s_wt2[i].inc(gpsimd.dma_start(
                    out=dst[rb * 128:(rb + 1) * 128, lc:lc + 512],
                    in_=wspl[:, i * 512:(i + 1) * 512]), 16)
                if c + 2 < len(sjobs):
                    ld(c + 2)
                if c in miles:
                    W(gpsimd, "gpsimd", s_wt2[0], 32 * (c // 2 + 1))
                    W(gpsimd, "gpsimd", s_wt2[1], 32 * ((c + 1) // 2))
                    s_sp.inc(gpsimd.memset(gpad[0:1, 0:1], 0.0))
            W(gpsimd, "gpsimd", dv, abc_done)
            rp = (128 * 512) // DM
            for i in range((NSLOT + rp - 1) // rp):
                r0, r1 = i * rp, min(NSLOT, (i + 1) * rp)
                s_zx.inc(gpsimd.dma_start(out=sxs_d[r0:r1, :],
                                           in_=zt[0:(r1 - r0) * DM // 512, :]), 16)
        S("gpsimd", _)
        sxs_zeroed = s_zx.n

        x1_done_mark = [0]
        prev_marks = {"wi": 0, "wo": 0, "x1st": 0, "xlnst": 0}
        scan_marks = {"hupd": 0, "mtstt": 0}

        for g in range(NG):
            b = g // GPS
            first = (g % GPS) == 0
            t0 = g * G
            ci0 = g * GC

            def _(scalar, t0=t0):
                W(scalar, "scalar", dv, x1_done_mark[0])
                for i in range(GC):
                    s_x0.inc(scalar.dma_start(out=x0t[i],
                                              in_=x0_d[t0 + i * 128: t0 + (i + 1) * 128, :]), 16)
            S("scalar", _)
            x0_ld = s_x0.n

            for i in range(GC):
                def _(scalar, i=i, x0_ld=x0_ld):
                    W(scalar, "scalar", s_x0, x0_ld)
                    W(scalar, "scalar", dv)
                    ac.inc(scalar.activation(sqt, x0t[i], AF.Square))
                S("scalar", _)

                def _(vector, i=i, x0_ld=x0_ld):
                    W(vector, "vector", s_x0, x0_ld)
                    W(vector, "vector", ac)
                    vector.reduce_sum(lns[:, 0:1], x0t[i], axis=AX)
                    dv.inc(vector.reduce_sum(lns[:, 1:2], sqt, axis=AX))
                    W(vector, "vector", dv)
                    vector.tensor_scalar(lnm[i], lns[:, 0:1], 1.0 / DM, None, ALU.mult)
                    dv.inc(vector.tensor_scalar(lns[:, 3:4], lns[:, 1:2], 1.0 / DM, None, ALU.mult))
                    W(vector, "vector", dv)
                    dv.inc(vector.tensor_tensor(lns[:, 2:3], lnm[i], lnm[i], ALU.mult))
                    W(vector, "vector", dv)
                    dv.inc(vector.tensor_tensor(lns[:, 4:5], lns[:, 3:4], lns[:, 2:3], ALU.subtract))
                S("vector", _)

                def _(scalar, i=i):
                    W(scalar, "scalar", dv)
                    ac.inc(scalar.activation(lns[:, 5:6], lns[:, 4:5], AF.Sqrt, bias=kcst[:, 0:1]))
                S("scalar", _)

                def _(vector, i=i):
                    W(vector, "vector", ac)
                    dv.inc(vector.reciprocal(lni[i], lns[:, 5:6]))
                    W(vector, "vector", dv)
                    dv.inc(vector.tensor_scalar(xnt[i], x0t[i], lnm[i], lni[i],
                                                ALU.subtract, ALU.mult))
                S("vector", _)
            xn_done = dv.n

            SLX = (4, 5, 6, 7)
            xnbf = xnfm.bitcast(BF16)
            items = [(xnt[i][:, k * 128:(k + 1) * 128], k * G + i * 128)
                     for k in range(NDK) for i in range(GC)]
            pe0, dv0, ac0 = pe.n, dv.n, ac.n

            def _(tensor, items=items, xn_done=xn_done, dv0=dv0,
                  gac=prev_marks.get("gac", 0)):
                W(tensor, "tensor", dv, xn_done)
                W(tensor, "tensor", ac, gac)
                for j, (src, _off) in enumerate(items):
                    if j >= 4:
                        W(tensor, "tensor", dv, dv0 + j - 3)
                    pe.inc(TR(tensor, psb(SLX[j % 4], 128), src))
            S("tensor", _)

            def _(scalar, items=items, pe0=pe0):
                for j, (_src, off) in enumerate(items):
                    W(scalar, "scalar", pe, pe0 + j + 1)
                    ac.inc(scalar.activation(xnbf[:, off:off + 128],
                                             psb(SLX[j % 4], 128), AF.Copy))
            S("scalar", _)

            def _(vector, items=items, ac0=ac0):
                for j, (_src, off) in enumerate(items):
                    W(vector, "vector", ac, ac0 + j + 1)
                    dv.inc(vector.tensor_tensor(
                        xnbf[:, NDK * G + off: NDK * G + off + 128],
                        psb(SLX[j % 4], 128), xnbf[:, off:off + 128], ALU.subtract))
            S("vector", _)
            xnfm_done = dv.n

            wi_pe0 = pe.n
            wi_base = [s.n for s in s_wi]
            ev_marks = list(prev_marks.get("wi_ev", [None, None]))
            for r in range(4):
                roff = (r % 2) * 2048

                def _(sync, r=r, wi_pe0=wi_pe0):
                    for k in range(NDK):
                        flat = r * NDK + k
                        if flat >= 3:
                            W(sync, "sync", pe, wi_pe0 + flat - 2)
                        else:
                            W(sync, "sync", pe, prev_marks["wi"])
                        W(sync, "sync", s_sp, 2 * r + 2)
                        bf = wiblk[flat % 3].bitcast(BF16)
                        ins = sync.dma_start(out=bf[:, 0:2048],
                                             in_=winhl_d[k * 128:(k + 1) * 128,
                                                         r * 2048:(r + 1) * 2048])
                        s_wi[flat % 3].inc(ins, 16)
                S("sync", _)

                def _(tensor, r=r, roff=roff, wi_base=wi_base, xnfm_done=xnfm_done,
                      em=ev_marks[r]):
                    W(tensor, "tensor", dv, xnfm_done)
                    if em is not None:
                        W(tensor, "tensor", em[0], em[1])
                    xnbf = xnfm.bitcast(BF16)
                    for k in range(NDK):
                        flat = r * NDK + k
                        W(tensor, "tensor", s_wi[flat % 3],
                          wi_base[flat % 3] + (flat // 3 + 1) * 16)
                        bf = wiblk[flat % 3].bitcast(BF16)
                        xh = xnbf[:, k * G:(k + 1) * G]
                        xl = xnbf[:, NDK * G + k * G: NDK * G + (k + 1) * G]
                        for j in range(8):
                            dst = PSA[:, roff + (j // 2) * 512 + (j % 2) * 256: roff + (j // 2) * 512 + (j % 2) * 256 + G]
                            wh = bf[:, j * 128:(j + 1) * 128]
                            wl = bf[:, 1024 + j * 128: 1024 + (j + 1) * 128]
                            tensor.matmul(dst, wh, xh,
                                          start=(k == 0 and j % 2 == 0), stop=False)
                            tensor.matmul(dst, wh, xl, start=False, stop=False)
                            ins2 = tensor.matmul(
                                dst, wl, xh, start=False,
                                stop=(k == NDK - 1 and j % 2 == 1))
                        pe.inc(ins2)
                S("tensor", _)
                mm_done = pe.n

                if r < 2:
                    def _(vector, r=r, roff=roff, mm_done=mm_done):
                        W(vector, "vector", pe, mm_done)
                        for j in range(8):
                            jj = r * 8 + j
                            src = PSA[:, roff + (j // 2) * 512 + (j % 2) * 256: roff + (j // 2) * 512 + (j % 2) * 256 + G]
                            dst = xssm[:, jj * CONVW + (DCV - 1): jj * CONVW + (DCV - 1) + G]
                            ins = vector.tensor_copy(dst, src)
                        dv.inc(ins)
                    S("vector", _)
                    ev_marks.append((dv, dv.n))
                else:
                    def _(scalar, r=r, roff=roff, mm_done=mm_done):
                        W(scalar, "scalar", pe, mm_done)
                        for j in range(8):
                            jj = (r - 2) * 8 + j
                            src = PSA[:, roff + (j // 2) * 512 + (j % 2) * 256: roff + (j // 2) * 512 + (j % 2) * 256 + G]
                            ins = scalar.activation(szfm[:, jj * G:(jj + 1) * G], src, AF.Silu)
                        ac.inc(ins)
                    S("scalar", _)
                    ev_marks.append((ac, ac.n))
            prev_marks["wi"] = wi_pe0 + 4 * NDK
            wi_evac_dv = dv.n
            wi_evac_ac = ac.n

            def _(vector, first=first, wi_evac_dv=wi_evac_dv):
                W(vector, "vector", dv, wi_evac_dv)
                dst = bass.AP(xssm.tensor, xssm.offset,
                              [list(xssm.ap[0]), [CONVW, NDC], [1, DCV - 1]])
                if first:
                    dv.inc(vector.memset(dst, 0.0))
                else:
                    src = bass.AP(ctail.tensor, ctail.offset,
                                  [list(ctail.ap[0]), [DCV - 1, NDC], [1, DCV - 1]])
                    dv.inc(vector.tensor_copy(dst, src))
            S("vector", _)
            lead_done = dv.n

            def _(vector, lead_done=lead_done, wi_evac_ac=wi_evac_ac):
                W(vector, "vector", dv, lead_done)
                for k in range(DCV):
                    if k > 0:
                        W(vector, "vector", dv)
                    for dc in range(NDC):
                        src = xssm[:, dc * CONVW + k: dc * CONVW + k + G]
                        dst = xcsfm[:, dc * G:(dc + 1) * G]
                        if k == 0:
                            ins = vector.tensor_scalar(dst, src, convw[:, dc * DCV: dc * DCV + 1],
                                                       convb[:, dc:dc + 1], ALU.mult, ALU.add)
                        else:
                            ins = vector.scalar_tensor_tensor(dst, src,
                                                              convw[:, dc * DCV + k: dc * DCV + k + 1],
                                                              dst, ALU.mult, ALU.add)
                    dv.inc(ins)
            S("vector", _)
            conv_done = dv.n

            def _(vector, conv_done=conv_done):
                W(vector, "vector", dv, conv_done)
                dst = bass.AP(ctail.tensor, ctail.offset,
                              [list(ctail.ap[0]), [DCV - 1, NDC], [1, DCV - 1]])
                src = bass.AP(xssm.tensor, xssm.offset + G,
                              [list(xssm.ap[0]), [CONVW, NDC], [1, DCV - 1]])
                dv.inc(vector.tensor_copy(dst, src))
            S("vector", _)

            def _(scalar, conv_done=conv_done):
                W(scalar, "scalar", dv, conv_done)
                ac.inc(scalar.activation(xcsfm, xcsfm, AF.Silu))
            S("scalar", _)
            xcs_done = ac.n

            def _(tensor, xcs_done=xcs_done):
                W(tensor, "tensor", ac, xcs_done)
                for k in range(NDC):
                    tensor.matmul(psb(0, G, parts=1), wxd[:, k:k + 1],
                                  xcsfm[:, k * G:(k + 1) * G],
                                  start=(k == 0), stop=(k == NDC - 1))
                for k in range(NDC):
                    tensor.matmul(psb(1, G, parts=DS), wxb[:, k * DS:(k + 1) * DS],
                                  xcsfm[:, k * G:(k + 1) * G],
                                  start=(k == 0), stop=(k == NDC - 1))
                for k in range(NDC):
                    ins = tensor.matmul(psb(2, G, parts=DS), wxc[:, k * DS:(k + 1) * DS],
                                        xcsfm[:, k * G:(k + 1) * G],
                                        start=(k == 0), stop=(k == NDC - 1))
                pe.inc(ins)
            S("tensor", _)
            wx_done = pe.n

            def _(vector, wx_done=wx_done):
                W(vector, "vector", pe, wx_done)
                vector.tensor_copy(bfm, psb(1, G, parts=DS))
                vector.tensor_copy(cfm, psb(2, G, parts=DS))
                dv.inc(vector.tensor_copy(drel, psb(0, G, parts=1)))
            S("vector", _)
            draw_done = dv.n

            def _(scalar, draw_done=draw_done):
                W(scalar, "scalar", dv, draw_done)
                ac.inc(scalar.activation(dabs, drel, AF.Abs))
                W(scalar, "scalar", ac)
                ac.inc(scalar.activation(dexp, dabs, AF.Exp, scale=kcst[0:1, 2:3]))
                W(scalar, "scalar", ac)
                ac.inc(scalar.activation(dln, dexp, AF.Ln, bias=kcst[0:1, 1:2]))
                ac.inc(scalar.activation(dabs, drel, AF.Relu))
            S("scalar", _)

            def _(vector):
                W(vector, "vector", ac)
                dv.inc(vector.tensor_tensor(drow, dabs, dln, ALU.add))
            S("vector", _)
            drow_done = dv.n

            for q in range(GC):
                ci = ci0 + q
                qs = q * Q

                def _(vector, qs=qs, drow_done=drow_done):
                    W(vector, "vector", dv, drow_done)
                    dv.inc(vector.tensor_tensor_scan(crow, onesr[0:1, 0:Q],
                                                     drow[0:1, qs:qs + Q], 0.0,
                                                     ALU.mult, ALU.add))
                    W(vector, "vector", dv)
                    vector.tensor_scalar(argrow[0:1, 0:Q], crow,
                                         crow[0:1, Q - 1:Q], -1.0, ALU.subtract, ALU.mult)
                    vector.tensor_copy(argrow[0:1, Q:2 * Q], crow)
                    vector.tensor_copy(argrow[0:1, 2 * Q:2 * Q + 1], crow[0:1, Q - 1:Q])
                    dv.inc(vector.tensor_scalar(negc, crow, -1.0, None, ALU.mult))
                S("vector", _)
                rows_done = dv.n

                def _(tensor, rows_done=rows_done):
                    W(tensor, "tensor", dv, rows_done)
                    tensor.matmul(psb(4, 2 * Q + 1, parts=DS), arow, argrow,
                                  start=True, stop=True)
                    tensor.matmul(psb(3, 128), negc, onesr[0:1, 0:Q], start=True, stop=False)
                    ins = tensor.matmul(psb(3, 128), onesr[0:1, 0:Q], crow, start=False, stop=True)
                    pe.inc(ins)
                S("tensor", _)
                args_done = pe.n

                def _(scalar, args_done=args_done):
                    W(scalar, "scalar", pe, args_done)
                    ac.inc(scalar.activation(ekall, psb(4, 2 * Q + 1, parts=DS), AF.Exp))
                S("scalar", _)
                ek_done = ac.n

                def _(vector, args_done=args_done):
                    W(vector, "vector", pe, args_done)
                    dv.inc(vector.tensor_tensor(dmt, psb(3, 128), maskS, ALU.add))
                S("vector", _)
                dmt_done = dv.n

                estk_done = ac.n

                def _(scalar, qs=qs, drow_done=drow_done):
                    W(scalar, "scalar", dv, drow_done)
                    W(scalar, "scalar", pe)
                    s_cf.inc(scalar.dma_start(out=cflat, in_=cfm[0:DS, qs:qs + Q]), 16)
                S("scalar", _)
                cflat_done = s_cf.n

                def _(tensor, cflat_done=cflat_done, ek_done=ek_done,
                      cb_free=max(draw_done, dmt_done,
                                  scan_marks["hupd"], scan_marks["mtstt"])):
                    W(tensor, "tensor", s_cf, cflat_done)
                    W(tensor, "tensor", ac, ek_done)
                    W(tensor, "tensor", dv, cb_free)
                    for t4 in range(4):
                        ins = tensor.matmul(PSA[:, t4 * 512: t4 * 512 + 512],
                                            onesr[0:1, 0:128],
                                            cflat[0:1, t4 * 512:(t4 + 1) * 512],
                                            start=True, stop=True)
                    pe.inc(ins)
                S("tensor", _)
                cb_done = pe.n

                def _(vector, qs=qs, ek_done=ek_done):
                    W(vector, "vector", ac, ek_done)
                    vector.tensor_tensor(kbfm, bfm[0:DS, qs:qs + Q], ekall[0:DS, 0:Q], ALU.mult)
                    dv.inc(vector.tensor_tensor(qbfm[0:DS, :], cfm[0:DS, qs:qs + Q],
                                                ekall[0:DS, Q:2 * Q], ALU.mult))
                S("vector", _)
                kb_done = dv.n

                def _(tensor, qs=qs, kb_done=kb_done):
                    W(tensor, "tensor", dv, kb_done)
                    TR(tensor, psb(6, DS), bfm[0:DS, qs:qs + Q])
                    TR(tensor, psb(7, DS), kbfm)
                    pe.inc(TR(tensor, psb(5, 1), drow[0:1, qs:qs + Q]))
                S("tensor", _)
                tr1_done = pe.n

                def _(vector, tr1_done=tr1_done):
                    W(vector, "vector", pe, tr1_done)
                    vector.tensor_copy(bttm, psb(6, DS))
                    vector.tensor_copy(kbtm, psb(7, DS))
                    dv.inc(vector.tensor_copy(dcolq, psb(5, 1)))
                S("vector", _)
                tr1_ev = dv.n

                SLC = (4, 5, 6, 7)
                pe1, dv1 = pe.n, dv.n

                def _(tensor, qs=qs, pe1=pe1, dv1=dv1, tr1_ev=tr1_ev, ek_done=ek_done):
                    W(tensor, "tensor", dv, tr1_ev)
                    W(tensor, "tensor", ac, ek_done)
                    for dc in range(NDC):
                        if dc >= 4:
                            W(tensor, "tensor", dv, dv1 + dc - 3)
                        pe.inc(TR(tensor, psb(SLC[dc % 4], 128),
                                  xcsfm[:, dc * G + qs: dc * G + qs + Q]))
                S("tensor", _)

                def _(vector, pe1=pe1):
                    for dc in range(NDC):
                        W(vector, "vector", pe, pe1 + dc + 1)
                        dv.inc(vector.tensor_scalar(xcdtm[:, dc * 128:(dc + 1) * 128],
                                                    psb(SLC[dc % 4], 128), dcolq,
                                                    None, ALU.mult))
                S("vector", _)
                xcd_done = dv.n

                mt_stt1 = 0
                for half in range(2):
                    if half == 1:
                        mt_stt1 = dv.n + 1
                    def _(scalar, half=half, dmt_done=dmt_done):
                        W(scalar, "scalar", dv, dmt_done)
                        if half == 1:
                            W(scalar, "scalar", dv)
                        for n8 in range(8):
                            n = half * 8 + n8
                            ins = scalar.activation(estk[:, n8 * Q:(n8 + 1) * Q], dmt,
                                                    AF.Exp, scale=abc[:, n:n + 1])
                        ac.inc(ins)
                    S("scalar", _)
                    eh_done = ac.n

                    def _(vector, half=half, eh_done=eh_done, cb_done=cb_done, tr1_ev=tr1_ev):
                        W(vector, "vector", ac, eh_done)
                        W(vector, "vector", pe, cb_done)
                        W(vector, "vector", dv, tr1_ev)
                        W(vector, "vector", dv)
                        for n8 in range(8):
                            n = half * 8 + n8
                            ins = vector.scalar_tensor_tensor(
                                mtmp[:, n8 * Q:(n8 + 1) * Q], estk[:, n8 * Q:(n8 + 1) * Q],
                                bttm[:, n:n + 1], PSA[:, n * Q:(n + 1) * Q],
                                ALU.mult, ALU.mult)
                        dv.inc(ins)
                        h = 4
                        while h >= 1:
                            W(vector, "vector", dv)
                            for i2 in range(h):
                                if h == 1 and half == 0:
                                    ins = vector.tensor_tensor(mtt, mtmp[:, 0:Q],
                                                               mtmp[:, Q:2 * Q], ALU.add)
                                elif h == 1:
                                    ins = vector.tensor_tensor(mtmp[:, 0:Q], mtmp[:, 0:Q],
                                                               mtmp[:, Q:2 * Q], ALU.add)
                                else:
                                    ins = vector.tensor_tensor(
                                        mtmp[:, i2 * Q:(i2 + 1) * Q],
                                        mtmp[:, i2 * Q:(i2 + 1) * Q],
                                        mtmp[:, (i2 + h) * Q:(i2 + h + 1) * Q], ALU.add)
                            dv.inc(ins)
                            h //= 2
                    S("vector", _)

                def _(vector):
                    W(vector, "vector", dv)
                    dv.inc(vector.tensor_tensor(mtt, mtt, mtmp[:, 0:Q], ALU.add))
                S("vector", _)
                mt_done = dv.n

                pe2, dv2 = pe.n, dv.n

                def _(tensor, pe2=pe2, dv2=dv2, xcd_done=xcd_done, mt_done=mt_done):
                    W(tensor, "tensor", dv, max(xcd_done, mt_done))
                    for t4 in range(4):
                        ins = tensor.matmul(PSA[0:DS, t4 * 512:(t4 + 1) * 512],
                                            kbtm, xcdtm[:, t4 * 512:(t4 + 1) * 512],
                                            start=True, stop=True)
                    pe.inc(ins)
                    for dc in range(NDC):
                        if dc >= 2:
                            W(tensor, "tensor", dv, dv2 + dc - 1)
                        bank = 4 + dc % 2
                        tensor.matmul(psb(bank, Q), xcdtm[:, dc * 128:(dc + 1) * 128],
                                      mtt, start=True, stop=False)
                        pe.inc(tensor.matmul(psb(bank, Q),
                                             hst[0:DS, dc * 128:(dc + 1) * 128],
                                             qbfm[0:DS, :], start=False, stop=True))
                S("tensor", _)
                ymm_all = pe.n

                def _(vector, qs=qs, pe2=pe2):
                    for dc in range(NDC):
                        W(vector, "vector", pe, pe2 + dc + 2)
                        bank = 4 + dc % 2
                        ins = vector.scalar_tensor_tensor(
                            xcdtm[:, dc * 128:(dc + 1) * 128],
                            xcsfm[:, dc * G + qs: dc * G + qs + Q],
                            dcol[:, dc:dc + 1], psb(bank, Q), ALU.mult, ALU.add)
                        dv.inc(ins)
                S("vector", _)
                st1 = dv.n

                def _(vector, ymm_all=ymm_all):
                    W(vector, "vector", pe, ymm_all)
                    dv.inc(vector.scalar_tensor_tensor(
                        hst[0:DS, 0:DI], hst[0:DS, 0:DI],
                        ekall[0:DS, 2 * Q:2 * Q + 1], PSA[0:DS, 0:DI],
                        ALU.mult, ALU.add))
                S("vector", _)
                scan_marks["hupd"] = dv.n
                scan_marks["mtstt"] = mt_stt1

                y3bf = y3fm.bitcast(BF16)
                def _(vector, qs=qs, st1=st1):
                    W(vector, "vector", dv, st1)
                    for dc in range(NDC):
                        ins = vector.tensor_tensor(
                            xcdtm[:, dc * 128:(dc + 1) * 128],
                            xcdtm[:, dc * 128:(dc + 1) * 128],
                            szfm[:, dc * G + qs: dc * G + qs + Q], ALU.mult)
                    dv.inc(ins)
                S("vector", _)
                st2 = dv.n

                def _(scalar, qs=qs, st2=st2, y3bf=y3bf):
                    W(scalar, "scalar", dv, st2)
                    for dc in range(NDC):
                        ins = scalar.activation(y3bf[:, dc * G + qs: dc * G + qs + Q],
                                                xcdtm[:, dc * 128:(dc + 1) * 128],
                                                AF.Copy)
                    ac.inc(ins)
                S("scalar", _)
                st3 = ac.n

                def _(vector, qs=qs, st3=st3, y3bf=y3bf):
                    W(vector, "vector", ac, st3)
                    for dc in range(NDC):
                        ins = vector.tensor_tensor(
                            y3bf[:, NDC * G + dc * G + qs: NDC * G + dc * G + qs + Q],
                            xcdtm[:, dc * 128:(dc + 1) * 128],
                            y3bf[:, dc * G + qs: dc * G + qs + Q], ALU.subtract)
                    dv.inc(ins)
                S("vector", _)
            y3_done = dv.n

            pe_wb = pe.n
            wo_base = [s.n for s in s_wo]

            def _(scalar, pe_wb=pe_wb):
                for k in range(NDC):
                    if k >= 3:
                        W(scalar, "scalar", pe, pe_wb + k - 2)
                    else:
                        W(scalar, "scalar", pe, prev_marks["wo"])
                    W(scalar, "scalar", s_sp, 9)
                    bf = woblk[k % 3].bitcast(BF16)
                    s_wo[k % 3].inc(scalar.dma_start(
                        out=bf[:, 0:2048], in_=wohl_d[k * 128:(k + 1) * 128, :]), 16)
            S("scalar", _)

            def _(tensor, wo_base=wo_base, y3_done=y3_done):
                W(tensor, "tensor", dv, y3_done)
                y3bf = y3fm.bitcast(BF16)
                for k in range(NDC):
                    W(tensor, "tensor", s_wo[k % 3], wo_base[k % 3] + (k // 3 + 1) * 16)
                    bf = woblk[k % 3].bitcast(BF16)
                    yh = y3bf[:, k * G:(k + 1) * G]
                    yl = y3bf[:, NDC * G + k * G: NDC * G + (k + 1) * G]
                    for j in range(8):
                        dst = PSA[:, (j // 2) * 512 + (j % 2) * G: (j // 2) * 512 + (j % 2) * G + G]
                        wh = bf[:, j * 128:(j + 1) * 128]
                        wl = bf[:, 1024 + j * 128: 1024 + (j + 1) * 128]
                        tensor.matmul(dst, wh, yh,
                                      start=(k == 0 and j % 2 == 0), stop=False)
                        tensor.matmul(dst, wh, yl, start=False, stop=False)
                        ins2 = tensor.matmul(dst, wl, yh, start=False,
                                             stop=(k == NDC - 1 and j % 2 == 1))
                    pe.inc(ins2)
            S("tensor", _)
            womm_done = pe.n
            prev_marks["wo"] = womm_done

            def _(vector, womm_done=womm_done):
                W(vector, "vector", pe, womm_done)
                for j in range(8):
                    src = PSA[:, (j // 2) * 512 + (j % 2) * G: (j // 2) * 512 + (j % 2) * G + G]
                    ins = vector.tensor_copy(xnfm[:, j * G:(j + 1) * G], src)
                dv.inc(ins)
            S("vector", _)
            attn_done = dv.n

            SLA = (0, 1, 6, 7)
            items2 = [(xnfm[:, j * G + i * 128: j * G + (i + 1) * 128], i, j)
                      for j in range(8) for i in range(GC)]
            pe3, dv3 = pe.n, dv.n

            def _(tensor, items2=items2, attn_done=attn_done, dv3=dv3):
                W(tensor, "tensor", dv, attn_done)
                for jj, (src, i, j) in enumerate(items2):
                    if jj >= 4:
                        W(tensor, "tensor", dv, dv3 + jj - 3)
                    pe.inc(TR(tensor, psb(SLA[jj % 4], 128), src))
            S("tensor", _)

            def _(vector, items2=items2, pe3=pe3):
                W(vector, "vector", s_st1, prev_marks["x1st"])
                for jj, (_src, i, j) in enumerate(items2):
                    W(vector, "vector", pe, pe3 + jj + 1)
                    dv.inc(vector.tensor_tensor(x1t[i][:, j * 128:(j + 1) * 128],
                                                psb(SLA[jj % 4], 128),
                                                x0t[i][:, j * 128:(j + 1) * 128], ALU.add))
            S("vector", _)
            x1_done = dv.n
            x1_done_mark[0] = x1_done

            def _(scalar, t0=t0, x1_done=x1_done):
                W(scalar, "scalar", dv, x1_done)
                for i in range(GC):
                    s_st1.inc(scalar.dma_start(out=sx1_d[t0 + i * 128: t0 + (i + 1) * 128, :],
                                               in_=x1t[i]), 16)
            S("scalar", _)

            for i in range(GC):
                def _(scalar, i=i, x1_done=x1_done):
                    W(scalar, "scalar", dv, x1_done)
                    W(scalar, "scalar", dv)
                    ac.inc(scalar.activation(sqt, x1t[i], AF.Square))
                S("scalar", _)

                def _(vector, i=i, x1_done=x1_done):
                    W(vector, "vector", dv, x1_done)
                    W(vector, "vector", ac)
                    vector.reduce_sum(lns[:, 0:1], x1t[i], axis=AX)
                    dv.inc(vector.reduce_sum(lns[:, 1:2], sqt, axis=AX))
                    W(vector, "vector", dv)
                    vector.tensor_scalar(lnm[i], lns[:, 0:1], 1.0 / DM, None, ALU.mult)
                    dv.inc(vector.tensor_scalar(lns[:, 3:4], lns[:, 1:2], 1.0 / DM, None, ALU.mult))
                    W(vector, "vector", dv)
                    dv.inc(vector.tensor_tensor(lns[:, 2:3], lnm[i], lnm[i], ALU.mult))
                    W(vector, "vector", dv)
                    dv.inc(vector.tensor_tensor(lns[:, 4:5], lns[:, 3:4], lns[:, 2:3], ALU.subtract))
                S("vector", _)

                def _(scalar, i=i):
                    W(scalar, "scalar", dv)
                    ac.inc(scalar.activation(lns[:, 5:6], lns[:, 4:5], AF.Sqrt, bias=kcst[:, 0:1]))
                S("scalar", _)

                def _(vector, i=i):
                    W(vector, "vector", ac)
                    W(vector, "vector", s_st1, prev_marks["xlnst"])
                    dv.inc(vector.reciprocal(lni[i], lns[:, 5:6]))
                    W(vector, "vector", dv)
                    dv.inc(vector.tensor_scalar(xlnt[i], x1t[i], lnm[i], lni[i],
                                                ALU.subtract, ALU.mult))
                S("vector", _)
            xln_done = dv.n

            def _(scalar, t0=t0, xln_done=xln_done):
                W(scalar, "scalar", dv, xln_done)
                for i in range(GC):
                    s_st1.inc(scalar.dma_start(out=sxln_d[t0 + i * 128: t0 + (i + 1) * 128, :],
                                               in_=xlnt[i]), 16)
            S("scalar", _)
            prev_marks["x1st"] = s_st1.n
            prev_marks["xlnst"] = s_st1.n

            SLN = (4, 5, 6, 7)
            items3 = [(xlnt[i][:, k * 128:(k + 1) * 128], xlnfm[:, k * G + i * 128: k * G + (i + 1) * 128])
                      for k in range(NDK) for i in range(GC)]
            pe4, dv4 = pe.n, dv.n

            def _(tensor, items3=items3, xln_done=xln_done, dv4=dv4):
                W(tensor, "tensor", dv, xln_done)
                for j, (src, _dst) in enumerate(items3):
                    if j >= 4:
                        W(tensor, "tensor", dv, dv4 + j - 3)
                    pe.inc(TR(tensor, psb(SLN[j % 4], 128), src))
            S("tensor", _)

            def _(vector, items3=items3, pe4=pe4):
                for j, (_src, dst) in enumerate(items3):
                    W(vector, "vector", pe, pe4 + j + 1)
                    dv.inc(vector.tensor_copy(dst, psb(SLN[j % 4], 128)))
            S("vector", _)
            xlnfm_done = dv.n

            for i in range(GC):
                ci = ci0 + i
                bank = 4 + i % 2

                def _(tensor, i=i, bank=bank, xlnfm_done=xlnfm_done):
                    W(tensor, "tensor", dv, xlnfm_done)
                    for k in range(NDK):
                        ins = tensor.matmul(psb(bank, E),
                                            xlnfm[:, k * G + i * 128: k * G + (i + 1) * 128],
                                            rwt[:, k * E:(k + 1) * E],
                                            start=(k == 0), stop=(k == NDK - 1))
                    pe.inc(ins)
                S("tensor", _)
                rt_done = pe.n

                def _(vector, ci=ci, bank=bank, rt_done=rt_done):
                    W(vector, "vector", pe, rt_done)
                    dv.inc(vector.reduce_max(lgm, psb(bank, E), axis=AX))
                    W(vector, "vector", dv)
                    vector.tensor_scalar(lgnm, lgm, -1.0, None, ALU.mult)
                    dv.inc(vector.tensor_tensor(lgu, psb(bank, E),
                                                lgm.to_broadcast([128, E]), ALU.is_ge))
                    W(vector, "vector", dv)
                    dv.inc(vector.tensor_copy(lgf, lgu))
                    W(vector, "vector", dv)
                    dv.inc(vector.tensor_tensor(lgw, lgf, rev8, ALU.mult))
                    W(vector, "vector", dv)
                    dv.inc(vector.reduce_max(lgm2, lgw, axis=AX))
                    W(vector, "vector", dv)
                    dv.inc(vector.tensor_scalar(ecol[:, ci:ci + 1], lgm2, -1.0, float(E),
                                                ALU.mult, ALU.add))
                S("vector", _)
                sm1 = dv.n

                def _(scalar, bank=bank, sm1=sm1):
                    W(scalar, "scalar", dv, sm1)
                    ac.inc(scalar.activation(lge, psb(bank, E), AF.Exp, bias=lgnm))
                S("scalar", _)
                sm2 = ac.n

                def _(vector, ci=ci, sm2=sm2):
                    W(vector, "vector", ac, sm2)
                    dv.inc(vector.reduce_sum(lgs, lge, axis=AX))
                    W(vector, "vector", dv)
                    dv.inc(vector.reciprocal(pcol[:, ci:ci + 1], lgs))
                S("vector", _)

            prev_marks["wi_ev"] = [(dv, dv.n), (ac, ac.n)]
            prev_marks["gac"] = ac.n

            if (g % GPS) == GPS - 1 and g + 1 < NG:
                def _(vector):
                    W(vector, "vector", dv)
                    W(vector, "vector", pe)
                    vector.memset(hst[:], 0.0)
                    dv.inc(vector.memset(ctail[:], 0.0))
                S("vector", _)

        a_pe, a_dv, a_ac = pe.n, dv.n, ac.n
        a_st1 = s_st1.n

        eti_done = [0]
        def _(vector):
            W(vector, "vector", dv, a_dv)
            dv.inc(vector.memset(carry[:], 0.0))
        S("vector", _)

        for p2 in range(2):
            if p2 == 1:
                def _(tensor):
                    W(tensor, "tensor", dv)
                    pe.inc(TR(tensor, psb(6, E, parts=1), cnts[0:E, 0:1]))
                S("tensor", _)

                def _(vector):
                    W(vector, "vector", pe)
                    dv.inc(vector.tensor_copy(cntr, psb(6, E, parts=1)))
                S("vector", _)

                def _(tensor):
                    W(tensor, "tensor", dv)
                    pe.inc(tensor.matmul(psb(6, E, parts=NCORES), r1hot, cntr,
                                         start=True, stop=True))
                S("tensor", _)

                def _(vector):
                    W(vector, "vector", pe)
                    dv.inc(vector.tensor_copy(ccsb, psb(6, E, parts=NCORES)))
                S("vector", _)

                def _(gpsimd):
                    W(gpsimd, "gpsimd", dv)
                    s_gm.inc(gpsimd.dma_start(out=ccin_d.ap(), in_=ccsb), 16)
                    W(gpsimd, "gpsimd", s_gm)
                    cc.inc(gpsimd.collective_compute(
                        "AllReduce", ALU.add, replica_groups=[list(range(NCORES))],
                        ins=[ccin_d.ap().opt()], outs=[ccout_d.ap().opt()]))
                    W(gpsimd, "gpsimd", cc)
                    s_gm.inc(gpsimd.dma_start(out=ccall, in_=ccout_d.ap()), 16)
                S("gpsimd", _)

                def _(tensor):
                    W(tensor, "tensor", s_gm)
                    pe.inc(tensor.matmul(psb(6, E, parts=1), rpfx, ccall,
                                         start=True, stop=True))
                S("tensor", _)

                def _(vector):
                    W(vector, "vector", pe)
                    dv.inc(vector.tensor_copy(baser, psb(6, E, parts=1)))
                S("vector", _)

                def _(tensor):
                    W(tensor, "tensor", dv)
                    pe.inc(TR(tensor, psb(6, 1, parts=E), baser))
                S("tensor", _)

                def _(vector):
                    W(vector, "vector", pe)
                    dv.inc(vector.tensor_copy(basec, psb(6, 1, parts=E)))
                    W(vector, "vector", dv)
                    dv.inc(vector.tensor_scalar(pcq[:, 0:1], cnts, 1.0 / TT,
                                                -0.5 + 0.5 / TT, ALU.mult, ALU.add))
                    W(vector, "vector", dv)
                    dv.inc(vector.tensor_scalar(pcq[:, 0:1], cnts, TT - 1.0, 1.0 / TT,
                                                ALU.add, ALU.mult))
                    W(vector, "vector", dv)
                    dv.inc(vector.tensor_scalar(pcq[:, 1:2], pcq[:, 0:1],
                                                -0.5 + 1.0 / (2 * TT), None, ALU.add))
                    W(vector, "vector", dv)
                    dv.inc(vector.tensor_scalar(pcq[:, 2:3], pcq[:, 1:2], 12582912.0,
                                                None, ALU.add))
                    W(vector, "vector", dv)
                    dv.inc(vector.tensor_scalar(pcq[:, 3:4], pcq[:, 2:3], 12582912.0,
                                                None, ALU.subtract))
                    W(vector, "vector", dv)
                    dv.inc(vector.tensor_scalar(pcq[:, 0:1], pcq[:, 3:4], float(TT),
                                                None, ALU.mult))
                S("vector", _)
                pc_done = dv.n

                def _(tensor, pc_done=pc_done):
                    W(tensor, "tensor", dv, pc_done)
                    pe.inc(tensor.matmul(psb(6, 1, parts=E), tri8, pcq[:, 0:1],
                                         start=True, stop=True))
                S("tensor", _)

                def _(vector):
                    W(vector, "vector", pe)
                    dv.inc(vector.tensor_copy(poffm1, psb(6, 1, parts=E)))
                    W(vector, "vector", dv)
                    dv.inc(vector.tensor_tensor(cuminc, poffm1, pcq[:, 0:1], ALU.add))
                    W(vector, "vector", dv)
                    dv.inc(vector.tensor_scalar(poffm1, poffm1, -1.0, None, ALU.add))
                    dv.inc(vector.memset(carry[:], 0.0))
                S("vector", _)

                def _(vector):
                    W(vector, "vector", dv)
                    dv.inc(vector.tensor_tensor(etu, cuminc.to_broadcast([E, NT]),
                                                ktile, ALU.is_le))
                    W(vector, "vector", dv)
                    dv.inc(vector.tensor_copy(etf, etu))
                S("vector", _)

                def _(tensor):
                    W(tensor, "tensor", dv)
                    pe.inc(tensor.matmul(psb(6, NT, parts=1), onesc8, etf,
                                         start=True, stop=True))
                S("tensor", _)

                def _(vector):
                    W(vector, "vector", pe)
                    dv.inc(vector.tensor_copy(etr, psb(6, NT, parts=1)))
                    W(vector, "vector", dv)
                    dv.inc(vector.tensor_scalar(etfr, etr, float(E - 1), None, ALU.min))
                    W(vector, "vector", dv)
                    dv.inc(vector.tensor_copy(eti, etfr))
                S("vector", _)
                eti_done[0] = dv.n

            for tc in range(NTC):
                c0 = tc * TCH
                nqc = TCH // Q
                def _(tensor, tc=tc):
                    W(tensor, "tensor", dv)
                    pe.inc(TR(tensor, psb(7, 128, parts=nqc),
                              ecol[:, tc * nqc:(tc + 1) * nqc]))
                S("tensor", _)

                def _(vector):
                    W(vector, "vector", pe)
                    dv.inc(vector.tensor_copy(ef16[0:nqc, 0:128], psb(7, 128, parts=nqc)))
                S("vector", _)

                def _(gpsimd):
                    W(gpsimd, "gpsimd", dv)
                    s_gm.inc(gpsimd.dma_start(out=erow_c, in_=ef16[0:nqc, 0:128]), 16)
                S("gpsimd", _)
                ef_done = s_gm.n

                def _(tensor, ef_done=ef_done):
                    W(tensor, "tensor", s_gm, ef_done)
                    W(tensor, "tensor", dv)
                    for t4 in range(TCH // 512):
                        ins = tensor.matmul(PSA[0:E, t4 * 512:(t4 + 1) * 512],
                                            onesr[0:1, 0:E],
                                            erow_c[0:1, t4 * 512:(t4 + 1) * 512],
                                            start=True, stop=True)
                    pe.inc(ins)
                S("tensor", _)
                indmm = pe.n

                def _(vector, indmm=indmm):
                    W(vector, "vector", pe, indmm)
                    dv.inc(vector.tensor_tensor(tmq_c, PSA[0:E, 0:TCH],
                                                iota8c.to_broadcast([E, TCH]), ALU.is_equal))
                    W(vector, "vector", dv)
                    dv.inc(vector.tensor_copy(ind_c, tmq_c))
                    W(vector, "vector", dv)
                    dv.inc(vector.tensor_tensor_scan(po_c, ones8t[0:E, 0:TCH], ind_c,
                                                     carry[0:E, 0:1], ALU.mult, ALU.add))
                    W(vector, "vector", dv)
                    dv.inc(vector.tensor_copy(carry, po_c[0:E, TCH - 1:TCH]))
                S("vector", _)
                po_done = dv.n

                if p2 == 1:
                    def _(vector, po_done=po_done):
                        W(vector, "vector", dv, po_done)
                        dv.inc(vector.tensor_scalar(tmp_c, po_c, poffm1, None, ALU.add))
                        W(vector, "vector", dv)
                        dv.inc(vector.tensor_tensor(tmp_c, tmp_c, ind_c, ALU.mult))
                    S("vector", _)
                    srmm0 = dv.n

                    def _(tensor, srmm0=srmm0):
                        W(tensor, "tensor", dv, srmm0)
                        for t4 in range(TCH // 512):
                            ins = tensor.matmul(PSA[0:1, 2048 + t4 * 512: 2048 + (t4 + 1) * 512],
                                                onesc8, tmp_c[0:E, t4 * 512:(t4 + 1) * 512],
                                                start=True, stop=True)
                        pe.inc(ins)
                    S("tensor", _)

                    def _(vector):
                        W(vector, "vector", pe)
                        dv.inc(vector.tensor_copy(srow_c, PSA[0:1, 2048:2048 + TCH]))
                    S("vector", _)

                    def _(vector):
                        W(vector, "vector", dv)
                        dv.inc(vector.tensor_scalar(tmp_c, po_c, basec, None, ALU.add))
                        W(vector, "vector", dv)
                        dv.inc(vector.tensor_scalar(tmq_c, tmp_c, CCAP + 0.5, None, ALU.is_le))
                        W(vector, "vector", dv)
                        dv.inc(vector.tensor_copy(tmp_c, tmq_c))
                        W(vector, "vector", dv)
                        dv.inc(vector.tensor_tensor(tmp_c, tmp_c, ind_c, ALU.mult))
                    S("vector", _)
                    krmm0 = dv.n

                    def _(tensor, krmm0=krmm0):
                        W(tensor, "tensor", dv, krmm0)
                        for t4 in range(TCH // 512):
                            ins = tensor.matmul(PSA[0:1, 2048 + t4 * 512: 2048 + (t4 + 1) * 512],
                                                onesc8, tmp_c[0:E, t4 * 512:(t4 + 1) * 512],
                                                start=True, stop=True)
                        pe.inc(ins)
                    S("tensor", _)

                    def _(vector):
                        W(vector, "vector", pe)
                        dv.inc(vector.tensor_copy(krow_c, PSA[0:1, 2048:2048 + TCH]))
                    S("vector", _)
                    rows_ok = dv.n

                    def _(gpsimd, tc=tc, rows_ok=rows_ok):
                        W(gpsimd, "gpsimd", dv, rows_ok)
                        r0 = tc * (TCH // 128)
                        s_gm.inc(gpsimd.dma_start(out=srows[r0:r0 + TCH // 128, 0:128],
                                                   in_=srow_c), 16)
                        s_gm.inc(gpsimd.dma_start(out=krows[r0:r0 + TCH // 128, 0:128],
                                                   in_=krow_c), 16)
                    S("gpsimd", _)

            if p2 == 0:
                def _(vector):
                    W(vector, "vector", dv)
                    dv.inc(vector.tensor_copy(cnts, carry))
                S("vector", _)

        rows_all = s_gm.n

        def _(tensor, rows_all=rows_all):
            W(tensor, "tensor", s_gm, rows_all)
            pe.inc(TR(tensor, psb(6, NCH), srows[0:NROWB, 0:128]))
            pe.inc(TR(tensor, psb(7, NCH), krows[0:NROWB, 0:128]))
        S("tensor", _)

        def _(vector):
            W(vector, "vector", pe)
            vector.tensor_copy(scols, psb(6, NCH))
            dv.inc(vector.tensor_copy(kcols, psb(7, NCH)))
            W(vector, "vector", dv)
            vector.tensor_copy(scoli, scols)
            dv.inc(vector.tensor_tensor(scalec, kcols, pcol, ALU.mult))
        S("vector", _)
        cols_done = dv.n

        sg_scb = s_sc.n
        xl_base = [s_xl[0].n, s_xl[1].n]

        def _(sync, cols_done=cols_done, sg_scb=sg_scb):
            W(sync, "sync", dv, cols_done)
            W(sync, "sync", s_st1, a_st1)
            for c in range(NCH):
                if c >= 2:
                    W(sync, "sync", s_sc, sg_scb + ((c - 2) // 2 + 1) * 2 * 16)
                ins = sync.dma_start(out=xlt[c % 2], in_=sxln_d[c * 128:(c + 1) * 128, :])
                s_xl[c % 2].inc(ins, 16)
        S("sync", _)

        def _(gpsimd, cols_done=cols_done, xl_base=xl_base):
            W(gpsimd, "gpsimd", dv, cols_done)
            W(gpsimd, "gpsimd", s_zx, sxs_zeroed)
            for c in range(NCH):
                W(gpsimd, "gpsimd", s_xl[c % 2], xl_base[c % 2] + (c // 2 + 1) * 16)
                ins = gpsimd.indirect_dma_start(
                    out=sxs_d[0:NSLOT, :],
                    out_offset=bass.IndirectOffsetOnAxis(scoli[:, c:c + 1], 0),
                    in_=xlt[c % 2], in_offset=None)
                s_sc.inc(ins, 16)
        S("gpsimd", _)
        scat_done = s_sc.n

        tile_marks = {"g7": 0}
        for kt in range(NT):
            def _(gpsimd, kt=kt, scat_done=scat_done):
                W(gpsimd, "gpsimd", s_sc, scat_done)
                W(gpsimd, "gpsimd", ac)
                W(gpsimd, "gpsimd", dv)
                for s2 in range(NSUB):
                    s_bw.inc(gpsimd.dma_start(out=xlt[s2 % 2],
                                              in_=sxs_d[kt * TT + s2 * 128: kt * TT + (s2 + 1) * 128, :]), 16)
                s_bw.inc(dyndma(gpsimd, b1sb, b1_b, eti[0:1, kt:kt + 1],
                                128 * NFK * 4, 0, (E - 1) * 128 * NFK * 4,
                                [[NFK * 4, 128], [1, NFK * 4]]), 16)
                s_bw.inc(dyndma(gpsimd, b2sb, b2_b, eti[0:1, kt:kt + 1],
                                128 * NDK * 4, 0, (E - 1) * 128 * NDK * 4,
                                [[NDK * 4, 128], [1, NDK * 4]]), 16)
            S("gpsimd", _)
            xs_ld = s_bw.n

            SLS = (2, 3, 6, 7)
            xsbf = xsfm.bitcast(BF16)
            items4 = [(xlt[s2 % 2][:, k * 128:(k + 1) * 128],
                       k * TT + s2 * 128)
                      for s2 in range(NSUB) for k in range(NDK)]
            pe5, dv5, ac5 = pe.n, dv.n, ac.n

            def _(tensor, items4=items4, xs_ld=xs_ld, dv5=dv5):
                W(tensor, "tensor", s_bw, xs_ld)
                W(tensor, "tensor", dv)
                for j, (src, _off) in enumerate(items4):
                    if j >= 4:
                        W(tensor, "tensor", dv, dv5 + j - 3)
                    pe.inc(TR(tensor, psb(SLS[j % 4], 128), src))
            S("tensor", _)

            def _(scalar, items4=items4, pe5=pe5, xsbf=xsbf):
                for j, (_src, off) in enumerate(items4):
                    W(scalar, "scalar", pe, pe5 + j + 1)
                    ac.inc(scalar.activation(xsbf[:, off:off + 128],
                                             psb(SLS[j % 4], 128), AF.Copy))
            S("scalar", _)

            def _(vector, items4=items4, ac5=ac5, xsbf=xsbf):
                for j, (_src, off) in enumerate(items4):
                    W(vector, "vector", ac, ac5 + j + 1)
                    dv.inc(vector.tensor_tensor(
                        xsbf[:, NDK * TT + off: NDK * TT + off + 128],
                        psb(SLS[j % 4], 128), xsbf[:, off:off + 128], ALU.subtract))
            S("vector", _)
            xsfm_done = dv.n

            hbf = hhl.bitcast(BF16)
            pe_fc0 = pe.n
            gelu_marks = {}
            for fc in range(8):
                hb = 0 if fc % 2 == 0 else 3072

                def _(sync, kt=kt, fc=fc, pe_fc0=pe_fc0, g7=tile_marks["g7"]):
                    if fc >= 2:
                        W(sync, "sync", pe, pe_fc0 + 3 * (fc - 2) + 2)
                    else:
                        W(sync, "sync", ac, g7)
                    W(sync, "sync", s_sp, 10)
                    bfv = w1blk[fc % 2].bitcast(BF16)
                    RB = 2 * FF * 2
                    dst = bass.AP(bfv.tensor, bfv.offset,
                                  [list(bfv.ap[0]), [1024, NDK], [1, 1024]])
                    ins = dyndma(sync, dst, w1hl_b, eti[0:1, kt:kt + 1],
                                 DM * RB, fc * 1024 * 2, (E - 1) * DM * RB,
                                 [[RB, 128], [128 * RB, NDK], [1, 1024 * 2]])
                    s_w1[fc % 2].inc(ins, 16)
                S("sync", _)
                w1_ld = s_w1[fc % 2].n

                def _(scalar, kt=kt, fc=fc, pe_fc0=pe_fc0):
                    W(scalar, "scalar", dv, eti_done[0])
                    W(scalar, "scalar", s_sp, 11)
                    if fc >= 2:
                        W(scalar, "scalar", pe, pe_fc0 + 3 * (fc - 2) + 3)
                    bfv = w2blk[fc % 2].bitcast(BF16)
                    RB2 = 2 * DM * 2
                    dst = bass.AP(bfv.tensor, bfv.offset,
                                  [list(bfv.ap[0]), [2048, 4], [1, 2048]])
                    ins = dyndma(scalar, dst, w2hl_b, eti[0:1, kt:kt + 1],
                                 FF * RB2, fc * 512 * RB2, (E - 1) * FF * RB2,
                                 [[RB2, 128], [128 * RB2, 4], [1, RB2]])
                    s_w2[fc % 2].inc(ins, 16)
                S("scalar", _)
                w2_ld = s_w2[fc % 2].n

                def _(tensor, kt=kt, fc=fc, w1_ld=w1_ld, xsfm_done=xsfm_done, hb=hb,
                      gm2=gelu_marks.get(fc - 2, 0)):
                    W(tensor, "tensor", s_w1[fc % 2], w1_ld)
                    W(tensor, "tensor", dv, xsfm_done)
                    if fc >= 2:
                        W(tensor, "tensor", ac, gm2)
                    bfv = w1blk[fc % 2].bitcast(BF16)
                    xsbf = xsfm.bitcast(BF16)
                    for s2 in range(4):
                        bank = s2 // 2
                        off = hb + bank * 512 + (s2 % 2) * TT
                        for kc in range(NDK):
                            dst = PSA[:, off:off + TT]
                            wh = bfv[:, kc * 1024 + s2 * 128: kc * 1024 + (s2 + 1) * 128]
                            wl = bfv[:, kc * 1024 + 512 + s2 * 128:
                                     kc * 1024 + 512 + (s2 + 1) * 128]
                            xh = xsbf[:, kc * TT:(kc + 1) * TT]
                            xl = xsbf[:, NDK * TT + kc * TT: NDK * TT + (kc + 1) * TT]
                            tensor.matmul(dst, wh, xh,
                                          start=(kc == 0 and s2 % 2 == 0), stop=False)
                            tensor.matmul(dst, wh, xl, start=False, stop=False)
                            ins = tensor.matmul(
                                dst, wl, xh, start=False,
                                stop=(kc == NDK - 1 and s2 % 2 == 1))
                        if s2 % 2 == 1:
                            pe.inc(ins)
                S("tensor", _)
                hmm_done = pe.n

                gelu_ac0 = ac.n

                def _(scalar, fc=fc, hmm_done=hmm_done, xs_ld=xs_ld, hbf=hbf, hb=hb):
                    W(scalar, "scalar", s_bw, xs_ld)
                    for s2 in range(4):
                        W(scalar, "scalar", pe, hmm_done - 1 + s2 // 2)
                        bank = s2 // 2
                        off = hb + bank * 512 + (s2 % 2) * TT
                        jglob = fc * 4 + s2
                        scalar.activation(hsb[:, s2 * TT:(s2 + 1) * TT],
                                          PSA[:, off:off + TT], AF.Gelu,
                                          bias=b1sb[:, jglob:jglob + 1])
                        ac.inc(scalar.activation(hbf[:, s2 * TT:(s2 + 1) * TT],
                                                 PSA[:, off:off + TT], AF.Gelu,
                                                 bias=b1sb[:, jglob:jglob + 1]))
                S("scalar", _)
                gelu_done = ac.n
                gelu_marks[fc] = gelu_done

                dvh0 = dv.n

                def _(vector, fc=fc, gelu_ac0=gelu_ac0, hbf=hbf):
                    for s2 in range(4):
                        W(vector, "vector", ac, gelu_ac0 + s2 + 1)
                        dv.inc(vector.tensor_tensor(
                            hbf[:, 4 * TT + s2 * TT: 4 * TT + (s2 + 1) * TT],
                            hsb[:, s2 * TT:(s2 + 1) * TT],
                            hbf[:, s2 * TT:(s2 + 1) * TT], ALU.subtract))
                S("vector", _)

                def _(tensor, kt=kt, fc=fc, gelu_ac0=gelu_ac0, dvh0=dvh0, w2_ld=w2_ld,
                      hbf=hbf):
                    W(tensor, "tensor", s_w2[fc % 2], w2_ld)
                    bfv = w2blk[fc % 2].bitcast(BF16)
                    for s2 in range(4):
                        W(tensor, "tensor", ac, gelu_ac0 + s2 + 1)
                        W(tensor, "tensor", dv, dvh0 + s2 + 1)
                        hh = hbf[:, s2 * TT:(s2 + 1) * TT]
                        hl = hbf[:, 4 * TT + s2 * TT: 4 * TT + (s2 + 1) * TT]
                        for j in range(8):
                            bank = 2 + j // 2
                            off = bank * 512 + (j % 2) * TT
                            dst = PSA[:, off:off + TT]
                            wh = bfv[:, s2 * 2048 + j * 128: s2 * 2048 + (j + 1) * 128]
                            wl = bfv[:, s2 * 2048 + 1024 + j * 128:
                                     s2 * 2048 + 1024 + (j + 1) * 128]
                            tensor.matmul(dst, wh, hh,
                                          start=(fc == 0 and s2 == 0 and j % 2 == 0),
                                          stop=False)
                            tensor.matmul(dst, wh, hl, start=False, stop=False)
                            ins = tensor.matmul(
                                dst, wl, hh, start=False,
                                stop=(fc == 7 and s2 == 3 and j % 2 == 1))
                    pe.inc(ins)
                S("tensor", _)
            eomm_done = pe.n
            tile_marks["g7"] = gelu_marks[7]

            def _(vector, kt=kt, eomm_done=eomm_done, xs_ld=xs_ld):
                W(vector, "vector", pe, eomm_done)
                W(vector, "vector", s_bw, xs_ld)
                for j in range(8):
                    bank = 2 + j // 2
                    off = bank * 512 + (j % 2) * TT
                    ins = vector.tensor_scalar(eofm[:, j * TT:(j + 1) * TT],
                                               PSA[:, off:off + TT],
                                               b2sb[:, j:j + 1], None, ALU.add)
                dv.inc(ins)
            S("vector", _)
            eo_done = dv.n

            SLE = (0, 1, 6, 7)
            items5 = [(eofm[:, j * TT + s2 * 128: j * TT + (s2 + 1) * 128], s2, j)
                      for s2 in range(NSUB) for j in range(NDK)]
            pe6, dv6 = pe.n, dv.n

            def _(tensor, items5=items5, eo_done=eo_done, dv6=dv6):
                W(tensor, "tensor", dv, eo_done)
                W(tensor, "tensor", ac)
                for jj, (src, s2, j) in enumerate(items5):
                    if jj >= 4:
                        W(tensor, "tensor", dv, dv6 + jj - 3)
                    pe.inc(TR(tensor, psb(SLE[jj % 4], 128), src))
            S("tensor", _)

            def _(vector, items5=items5, pe6=pe6):
                W(vector, "vector", s_mo)
                for jj, (_src, s2, j) in enumerate(items5):
                    W(vector, "vector", pe, pe6 + jj + 1)
                    dv.inc(vector.tensor_copy(eotm[s2][:, j * 128:(j + 1) * 128],
                                              psb(SLE[jj % 4], 128)))
            S("vector", _)
            eot_done = dv.n

            def _(scalar, kt=kt, eot_done=eot_done):
                W(scalar, "scalar", dv, eot_done)
                for s2 in range(NSUB):
                    s_mo.inc(scalar.dma_start(out=smoe_d[kt * TT + s2 * 128: kt * TT + (s2 + 1) * 128, :],
                                              in_=eotm[s2]), 16)
            S("scalar", _)
        moe_st = s_mo.n

        x1f_base = [s_x1f[0].n, s_x1f[1].n]
        mot_base = [s_mg[0].n, s_mg[1].n]
        out_dvb = dv.n
        st_base = sfin.n

        def _(sync, out_dvb=out_dvb):
            W(sync, "sync", s_st1, a_st1)
            for c in range(NCH):
                if c >= 2:
                    W(sync, "sync", dv, out_dvb + c - 1)
                ins = sync.dma_start(out=x1f[c % 2], in_=sx1_d[c * 128:(c + 1) * 128, :])
                s_x1f[c % 2].inc(ins, 16)
        S("sync", _)

        def _(gpsimd, moe_st=moe_st, out_dvb=out_dvb):
            W(gpsimd, "gpsimd", s_mo, moe_st)
            for c in range(NCH):
                if c >= 2:
                    W(gpsimd, "gpsimd", dv, out_dvb + c - 1)
                ins = gpsimd.indirect_dma_start(
                    out=mot[c % 2], out_offset=None,
                    in_=smoe_d[0:NSLOT, :],
                    in_offset=bass.IndirectOffsetOnAxis(scoli[:, c:c + 1], 0))
                s_mg[c % 2].inc(ins, 16)
        S("gpsimd", _)

        def _(vector, x1f_base=x1f_base, mot_base=mot_base, st_base=st_base):
            for c in range(NCH):
                W(vector, "vector", s_x1f[c % 2], x1f_base[c % 2] + (c // 2 + 1) * 16)
                W(vector, "vector", s_mg[c % 2], mot_base[c % 2] + (c // 2 + 1) * 16)
                if c >= 2:
                    W(vector, "vector", sfin, st_base + ((c - 2) // 2 + 1) * 2 * 16)
                dv.inc(vector.scalar_tensor_tensor(outt[c % 2], mot[c % 2],
                                                   scalec[:, c:c + 1], x1f[c % 2],
                                                   ALU.mult, ALU.add))
        S("vector", _)

        def _(scalar, out_dvb=out_dvb):
            for c in range(NCH):
                W(scalar, "scalar", dv, out_dvb + c + 1)
                sfin.inc(scalar.dma_start(out=out_d[c * 128:(c + 1) * 128, :],
                                          in_=outt[c % 2]), 16)
            W(scalar, "scalar", sfin, sfin.n)
        S("scalar", _)

    return nc


_NC_CACHE = {}


def _get_nc(key, cfg):
    if key not in _NC_CACHE:
        _NC_CACHE[key] = build(cfg)
    return _NC_CACHE[key]


def prep_inputs(core, cfg, L0_attn_x, L0_attn_W_in, L0_attn_conv_w, L0_attn_conv_b,
                L0_attn_W_x, L0_attn_A_log, L0_attn_D_param, L0_attn_W_out,
                L0_ffn_router_w, L0_ffn_w1, L0_ffn_b1, L0_ffn_w2, L0_ffn_b2):
    NB = cfg.NB
    f32 = np.float32
    x = np.ascontiguousarray(L0_attn_x[core * NB:(core + 1) * NB]).reshape(cfg.TOKS, DM)
    wx = np.asarray(L0_attn_W_x, f32)
    d = {
        "x0": np.asarray(x, f32),
        "w_in": np.ascontiguousarray(np.asarray(L0_attn_W_in, f32)),
        "conv_w": np.ascontiguousarray(
            np.asarray(L0_attn_conv_w, f32).reshape(NDC, 128, DCV).transpose(1, 0, 2).reshape(128, NDC * DCV)),
        "conv_b": np.ascontiguousarray(np.asarray(L0_attn_conv_b, f32).reshape(NDC, 128).T),
        "wx_d": np.ascontiguousarray(wx[:, 0].reshape(NDC, 128).T),
        "wx_b": np.ascontiguousarray(
            wx[:, 1:1 + DS].reshape(NDC, 128, DS).transpose(1, 0, 2).reshape(128, NDC * DS)),
        "wx_c": np.ascontiguousarray(
            wx[:, 1 + DS:1 + 2 * DS].reshape(NDC, 128, DS).transpose(1, 0, 2).reshape(128, NDC * DS)),
        "a_log": np.asarray(L0_attn_A_log, f32).reshape(1, DS),
        "d_param": np.ascontiguousarray(np.asarray(L0_attn_D_param, f32).reshape(NDC, 128).T),
        "w_out": np.ascontiguousarray(np.asarray(L0_attn_W_out, f32)),
        "rw_t": np.ascontiguousarray(
            np.asarray(L0_ffn_router_w, f32).T.reshape(NDK, 128, E).transpose(1, 0, 2).reshape(128, NDK * E)),
        "w1": np.ascontiguousarray(np.asarray(L0_ffn_w1, f32).reshape(E * DM, FF)),
        "w2": np.ascontiguousarray(np.asarray(L0_ffn_w2, f32).reshape(E * FF, DM)),
        "b1t": np.ascontiguousarray(
            np.asarray(L0_ffn_b1, f32).reshape(E, NFK, 128).transpose(0, 2, 1).reshape(E * 128, NFK)),
        "b2t": np.ascontiguousarray(
            np.asarray(L0_ffn_b2, f32).reshape(E, NDK, 128).transpose(0, 2, 1).reshape(E * 128, NDK)),
        "r1hot": np.eye(NCORES, dtype=f32)[core].reshape(1, NCORES),
        "rpfx": (np.arange(NCORES) < core).astype(f32).reshape(NCORES, 1),
    }
    return d


def kernel(**inputs):
    cfg = Cfg(NB=4, L=2048, G=256, TT=256, CCAP=12288)
    nc = _get_nc("prod", cfg)
    in_maps = [prep_inputs(c, cfg, **inputs) for c in range(NCORES)]
    res = run_bass_kernel_spmd(nc, in_maps, core_ids=list(range(NCORES)))
    outs = [np.asarray(r["out"]).reshape(cfg.NB, cfg.L, DM) for r in res.results]
    return np.concatenate(outs, axis=0)
